# version 05
# baseline (speedup 1.0000x reference)
import numpy as np
from contextlib import ExitStack

import concourse.bass as bass
import concourse.mybir as mybir
from concourse.bass_utils import run_bass_kernel_spmd

F32 = mybir.dt.float32
BF16 = mybir.dt.bfloat16
AF = mybir.ActivationFunctionType
ALU = mybir.AluOpType
AX = mybir.AxisListType

DM = 1024
DIN = 2048
NS = 16
DTR = 64
NE = 8
NR = 8
KT = DM // 128
DT2 = DIN // 128
EPS = 1e-5
SM30 = 30.0
TC = 512

COMPUTE = ("p", "v", "a", "g", "s")


class Buf:
    __slots__ = ("writer", "readers", "name")

    def __init__(self, name=""):
        self.writer = None
        self.readers = []
        self.name = name


class Op:
    __slots__ = ("eng", "fn", "dma", "reads", "writes", "tag", "cost",
                 "epoch", "idx", "preds", "wait_all_dma", "start",
                 "waits", "inc")


class B:

    ENGS = ("p", "v", "a", "g", "s")

    def __init__(self):
        self.ops = []
        self.epoch = 0

    def emit(self, eng, fn, reads=(), writes=(), dma=False, noinc=False,
             extra_waits=(), war=(), tag="", cost=700, wait_all_dma=False):
        op = Op()
        op.eng, op.fn, op.dma = eng, fn, dma
        op.reads = list(reads)
        op.writes = list(writes) + list(war)
        op.tag = tag
        op.cost = cost
        op.epoch = self.epoch
        op.wait_all_dma = wait_all_dma
        op.idx = len(self.ops)
        op.preds = None
        self.ops.append(op)
        return op

    def fence(self):
        self.epoch += 1

    def barrier(self):
        self.fence()

    def close_group(self, eng, *bufs):
        pass

    def _build_graph(self):
        writer = {}
        readers = {}
        for op in self.ops:
            preds = set()
            for bb in op.reads:
                w = writer.get(id(bb))
                if w is not None:
                    preds.add(w)
            for bb in op.writes:
                w = writer.get(id(bb))
                if w is not None:
                    preds.add(w)
                for r in readers.get(id(bb), ()):
                    preds.add(r)
            for bb in op.writes:
                writer[id(bb)] = op.idx
                readers[id(bb)] = []
            for bb in op.reads:
                readers.setdefault(id(bb), []).append(op.idx)
            preds.discard(op.idx)
            op.preds = sorted(preds)

    def schedule(self):
        import heapq
        self._build_graph()
        n = len(self.ops)
        nsucc_pred = [len(op.preds) for op in self.ops]
        succs = [[] for _ in range(n)]
        for op in self.ops:
            for p in op.preds:
                succs[p].append(op.idx)
        done_t = [0.0] * n
        eng_free = {e: 0.0 for e in self.ENGS}
        q_free = {e: 0.0 for e in ("s", "a", "g")}
        epoch_end = 0.0
        order = []
        by_epoch = {}
        for op in self.ops:
            by_epoch.setdefault(op.epoch, []).append(op)
        for ep in sorted(by_epoch):
            ops_e = by_epoch[ep]
            remaining = set(o.idx for o in ops_e)
            pend = {o.idx: sum(1 for p in self.ops[o.idx].preds
                               if self.ops[p].epoch == ep) for o in ops_e}
            ready = []
            for o in ops_e:
                if pend[o.idx] == 0:
                    heapq.heappush(ready, o.idx)
            scheduled_any = True
            while remaining:
                if not ready:
                    raise RuntimeError("scheduler deadlock (cyclic deps?)")
                cand = heapq.heappop(ready)
                op = self.ops[cand]
                start = max(epoch_end, eng_free[op.eng],
                            max((done_t[p] for p in op.preds), default=0.0))
                if op.dma:
                    issue = max(start, q_free[op.eng])
                    eng_free[op.eng] = issue + 150.0
                    q_free[op.eng] = issue + max(600.0, op.cost * 0.5)
                    done_t[cand] = issue + 1500.0 + op.cost
                else:
                    eng_free[op.eng] = start + op.cost
                    done_t[cand] = start + op.cost
                op.start = start
                order.append(op)
                remaining.discard(cand)
                for sidx in succs[cand]:
                    if self.ops[sidx].epoch == ep and sidx in remaining:
                        pend[sidx] -= 1
                        if pend[sidx] == 0:
                            heapq.heappush(ready, sidx)
            epoch_end = max(max(done_t[o.idx] for o in ops_e), epoch_end)
        return order

    def lower(self, order):
        cnt = {k: 0 for k in ("p", "v", "a", "g", "sd", "ad", "gd")}
        pos = {}
        wm = {e: {} for e in self.ENGS}
        for op in order:
            semkey = (op.eng + "d") if op.dma else op.eng
            cnt[semkey] += 16 if op.dma else 1
            pos[op.idx] = (semkey, cnt[semkey])
            waits = {}
            for p in op.preds:
                s, v = pos[p]
                if wm[op.eng].get(s, 0) >= v:
                    continue
                waits[s] = max(waits.get(s, 0), v)
            if op.wait_all_dma:
                for s in ("sd", "ad", "gd"):
                    if cnt[s] > waits.get(s, 0):
                        waits[s] = cnt[s]
            op.waits = sorted(waits.items())
            op.inc = (semkey, 16 if op.dma else 1)
            for s, v in waits.items():
                wm[op.eng][s] = v
        return cnt

    def finalize(self, order):
        cnt = {k: 0 for k in ("p", "v", "a", "g", "sd", "ad", "gd")}
        epoch_counts = {}
        cur = order[0].epoch if order else 0
        for op in order:
            if op.epoch != cur:
                epoch_counts[cur] = dict(cnt)
                cur = op.epoch
            semkey = op.inc[0]
            cnt[semkey] += op.inc[1]
        seen_eng_epoch = set()
        for op in order:
            key = (op.eng, op.epoch)
            if key in seen_eng_epoch:
                continue
            seen_eng_epoch.add(key)
            prev = op.epoch - 1
            if prev in epoch_counts:
                w = dict(op.waits)
                for s, v in epoch_counts[prev].items():
                    if v > 0 and w.get(s, 0) < v:
                        w[s] = v
                op.waits = sorted(w.items())
        qmap = {"sd": "s", "ad": "a", "gd": "g"}
        ends = {k: [] for k in qmap}
        for semkey, eng in qmap.items():
            cum = 0
            open_base = 0
            for op in order:
                if op.eng != eng:
                    continue
                if op.waits:
                    if cum > open_base:
                        ends[semkey].append((open_base, cum))
                        open_base = cum
                    if cum > 0:
                        w = dict(op.waits)
                        w[semkey] = max(w.get(semkey, 0), cum)
                        op.waits = sorted(w.items())
                if op.dma and op.inc[0] == semkey:
                    cum += 16
            if cum > open_base:
                ends[semkey].append((open_base, cum))

        def patch(semkey, v):
            for lo, hi in ends[semkey]:
                if lo < v <= hi:
                    return hi
            return v

        for op in order:
            op.waits = [(s, patch(s, v) if s in qmap else v)
                        for s, v in op.waits]


class Arena:
    def __init__(self, t, n):
        self.t = t
        self.n = n
        self.off = 0
        self.stack = []

    def alloc(self, cols, rows=128):
        off = (self.off + 3) & ~3
        assert off + cols <= self.n, f"arena overflow {off}+{cols}>{self.n}"
        self.off = off + cols
        return self.t.ap()[0:rows, off:off + cols]

    def mark(self):
        self.stack.append(self.off)

    def reset(self):
        self.off = self.stack.pop()


def build_program(L=8192):
    NCH = L // TC
    nc = bass.Bass("TRN2", target_bir_lowering=False, debug=False)

    xT = nc.dram_tensor("xT", [KT, 128, L], F32, kind="ExternalInput").ap()
    win = nc.dram_tensor("win", [KT, 128, 2 * DIN], F32, kind="ExternalInput").ap()
    bwT = nc.dram_tensor("bwT", [KT, 128, DM], F32, kind="ExternalInput").ap()
    rwT = nc.dram_tensor("rwT", [KT, 128, NE], F32, kind="ExternalInput").ap()
    lA = nc.dram_tensor("lA", [KT, 128, NR], F32, kind="ExternalInput").ap()
    lB = nc.dram_tensor("lB", [NE * NR, DM], F32, kind="ExternalInput").ap()
    wx = nc.dram_tensor("wx", [DT2, 128, 96], F32, kind="ExternalInput").ap()
    wdt = nc.dram_tensor("wdt", [DTR, DIN], F32, kind="ExternalInput").ap()
    wout = nc.dram_tensor("wout", [DT2, 128, DM], F32, kind="ExternalInput").ap()
    cdiag = nc.dram_tensor("cdiag", [DT2 * 4, 128, 128], F32, kind="ExternalInput").ap()
    alog = nc.dram_tensor("alog", [DT2, 128, NS], F32, kind="ExternalInput").ap()
    vec = nc.dram_tensor("vec", [128, 64], F32, kind="ExternalInput").ap()
    bbrow = nc.dram_tensor("bbrow", [1, DM], F32, kind="ExternalInput").ap()
    rbrow = nc.dram_tensor("rbrow", [1, NE], F32, kind="ExternalInput").ap()
    ident_in = nc.dram_tensor("ident_in", [128, 128], F32, kind="ExternalInput").ap()
    repl_in = nc.dram_tensor("repl_in", [2, 8, NE * NR], F32, kind="ExternalInput").ap()

    yout = nc.dram_tensor("yout", [KT, 128, L], F32, kind="ExternalOutput").ap()

    win_bf = nc.dram_tensor("win_bf", [KT, 128, 2 * DIN], BF16)
    wout_bf = nc.dram_tensor("wout_bf", [DT2, 128, DM], BF16)
    sp_d = nc.dram_tensor("sp_d", [DT2, 128, L], BF16)
    sp_u = nc.dram_tensor("sp_u", [DT2, 128, L], BF16)
    sp_bc = nc.dram_tensor("sp_bc", [32, L], BF16)
    sp_x1 = nc.dram_tensor("sp_x1", [KT, 128, L], F32)
    sp_x2 = nc.dram_tensor("sp_x2", [KT, 128, L], F32)
    sp_g = nc.dram_tensor("sp_g", [DT2, 128, L], BF16)
    sp_du = nc.dram_tensor("sp_du", [DT2, 128, L], BF16)
    gidx = nc.dram_tensor("gidx", [128, 448], mybir.dt.uint32, kind="ExternalInput").ap()
    smask = nc.dram_tensor("smask", [128, 1], F32, kind="ExternalInput").ap()

    b = B()
    ctx = ExitStack()
    abf_t = ctx.enter_context(nc.sbuf_tensor("abf", [128, 65 * 1024], BF16))
    af_t = ctx.enter_context(nc.sbuf_tensor("af32", [128, 18 * 1024], F32))
    ABF = Arena(abf_t, 65 * 1024)
    AFA = Arena(af_t, 18 * 1024)

    gidx_t = ctx.enter_context(nc.sbuf_tensor("gidxs", [128, 448], mybir.dt.uint32))
    ps_mm = [ctx.enter_context(nc.psum_tensor(f"psmm{i}", [128, TC], F32)) for i in range(4)]
    ps_bc0 = ctx.enter_context(nc.psum_tensor("psbc0", [128, TC], F32))
    ps_bc1 = ctx.enter_context(nc.psum_tensor("psbc1", [128, TC], F32))
    ps_small = ctx.enter_context(nc.psum_tensor("pssmall", [128, TC], F32))
    ps_geb = ctx.enter_context(nc.psum_tensor("psgeb", [8, TC], BF16))
    ps_mm_b = [Buf(f"psmm{i}") for i in range(4)]
    ps_bc0_b, ps_bc1_b = Buf("psbc0"), Buf("psbc1")
    ps_small_b = Buf("pssmall")
    ps_geb_b = Buf("psgeb")

    bwT_s = ABF.alloc(KT * DM)
    rwT_s = ABF.alloc(KT * NE)
    lA_s = ABF.alloc(KT * NR)
    lB_s = ABF.alloc(DM, rows=NE * NR)
    wdt_s = ABF.alloc(DIN, rows=DTR)
    ident_s = ABF.alloc(128)
    repl_s = ABF.alloc(2 * NE * NR, rows=8)
    ones_row = ABF.alloc(TC, rows=1)
    onescol = ABF.alloc(1)
    bb_s = ABF.alloc(DM, rows=1)
    rb_s = ABF.alloc(NE, rows=1)

    onesrow_f = AFA.alloc(128, rows=1)
    vec_s = AFA.alloc(64)
    negA_s = AFA.alloc(DT2 * NS)
    posA_s = AFA.alloc(DT2 * NS)
    carry_s = AFA.alloc(DT2 * NS)
    hrun_s = AFA.alloc(DT2 * NS)
    smask_s = AFA.alloc(4)
    idf_s = AFA.alloc(128)
    atmp = AFA.alloc(NS)

    CONSTS = [0.0, 1.0, -1.0, 0.5, -0.5, 2.0, EPS, SM30, 1.0 / DM]
    const_t = AFA.alloc(len(CONSTS))
    cb = Buf("consts")
    for ci, cv in enumerate(CONSTS):
        nc.const_aps.register(F32, cv, const_t[:, ci:ci + 1])

    WB = {k: Buf(k) for k in ("bwT", "rwT", "lA", "lB", "wdt", "ident", "repl",
                              "ones", "bb", "rb", "onesf", "vec", "negA",
                              "posA", "carry", "hrun", "smask", "gidx",
                              "tails", "winbf", "woutbf", "wx",
                              "cdiag", "idf", "atmp")}

    def dma(eng, out, in_, reads=(), writes=(), tag=""):
        return b.emit(eng, lambda e, o=out, i=in_: e.dma_start(out=o, in_=i),
                      reads=reads, writes=writes, dma=True, tag=tag)

    for k in range(KT):
        dma("g", bwT_s[:, k * DM:(k + 1) * DM], bwT[k], writes=[WB["bwT"]])
        dma("g", rwT_s[:, k * NE:(k + 1) * NE], rwT[k], writes=[WB["rwT"]])
        dma("g", lA_s[:, k * NR:(k + 1) * NR], lA[k], writes=[WB["lA"]])
    dma("g", lB_s, lB, writes=[WB["lB"]])
    dma("g", wdt_s, wdt, writes=[WB["wdt"]])
    dma("g", ident_s, ident_in, writes=[WB["ident"]])
    dma("s", idf_s, ident_in, writes=[WB["idf"]])
    dma("g", repl_s[0:8, 0:64], repl_in[0], writes=[WB["repl"]])
    dma("g", repl_s[0:8, 64:128], repl_in[1], writes=[WB["repl"]])
    dma("g", bb_s, bbrow, writes=[WB["bb"]])
    dma("g", rb_s, rbrow, writes=[WB["rb"]])
    dma("s", vec_s, vec, writes=[WB["vec"]])
    dma("s", gidx_t.ap(), gidx, writes=[WB["gidx"]])
    dma("s", smask_s[:, 0:1], smask, writes=[WB["smask"]])

    for ci, cv in enumerate(CONSTS):
        b.emit("v", lambda e, ci=ci, cv=cv: e.memset(const_t[:, ci:ci + 1], cv),
               writes=[cb])
    b.emit("v", lambda e: e.memset(ones_row, 1.0), writes=[WB["ones"]])
    b.emit("v", lambda e: e.memset(onescol, 1.0), writes=[WB["ones"]])
    b.emit("v", lambda e: e.memset(onesrow_f, 1.0), writes=[WB["onesf"]])
    b.emit("v", lambda e: e.memset(carry_s, 0.0), writes=[WB["carry"]])
    b.fence()
    for d in range(DT2):
        slp = posA_s[:, d * NS:(d + 1) * NS]
        sln = negA_s[:, d * NS:(d + 1) * NS]
        dma("s", atmp, alog[d], writes=[WB["atmp"]])
        b.emit("a", lambda e, slp=slp: e.activation(out=slp, in_=atmp, func=AF.Exp),
               reads=[WB["atmp"]], writes=[WB["posA"]])
        b.emit("a", lambda e, slp=slp, sln=sln: e.mul(out=sln, in_=slp, mul=-1.0),
               reads=[WB["posA"]], writes=[WB["negA"]])

    ident8f = idf_s[0:8, 0:8]
    ident_bf = ident_s[0:128, 0:128]

    def vcol(name, j):
        base = dict(conv_b=0, b_dt=16, D=32, gamma=48, beta=56)[name]
        return vec_s[:, base + j:base + j + 1]

    def ln_rows(src_tiles, src_bufs, lnrows, lnrows_b, sq_t, sq_b, xbf_t, xbf_b, tag):
        for k in range(KT):
            b.emit("v", lambda e, k=k: e.tensor_copy(out=xbf_t, in_=src_tiles[k]),
                   reads=[src_bufs[k]], writes=[xbf_b], tag=tag + "cast")
            b.emit("a", lambda e, k=k: e.activation(out=sq_t, in_=src_tiles[k],
                   func=AF.Square), reads=[src_bufs[k]], writes=[sq_b], tag=tag + "sq")
            b.emit("p", lambda e, st=(k == 0), last=(k == KT - 1): e.matmul(
                   out=ps_small[0:1, :], lhsT=onescol, rhs=xbf_t, start=st,
                   stop=last),
                   reads=[xbf_b, WB["ones"]] if k == 0 else [xbf_b],
                   war=[ps_small_b],
                   tag=tag + "mmx")
            b.emit("p", lambda e, st=(k == 0), last=(k == KT - 1): e.matmul(
                   out=ps_bc0[0:1, :], lhsT=onescol, rhs=sq_t, start=st,
                   stop=last),
                   reads=[sq_b],
                   war=[ps_bc0_b],
                   tag=tag + "mmq")
        b.close_group("p", ps_small_b, ps_bc0_b)
        mu = lnrows[0:1, 0:TC]
        m2 = lnrows[0:1, TC:2 * TC]
        var = lnrows[0:1, 2 * TC:3 * TC]
        arow = lnrows[0:1, 3 * TC:4 * TC]
        brow = lnrows[0:1, 4 * TC:5 * TC]
        b.emit("v", lambda e: e.tensor_scalar(out=mu, in0=ps_small[0:1, :],
               scalar1=1.0 / DM, scalar2=None, op0=ALU.mult),
               reads=[ps_small_b], writes=[lnrows_b], tag=tag)
        b.emit("v", lambda e: e.tensor_scalar(out=m2, in0=ps_bc0[0:1, :],
               scalar1=1.0 / DM, scalar2=None, op0=ALU.mult),
               reads=[ps_bc0_b], writes=[lnrows_b], tag=tag)
        b.emit("v", lambda e: e.tensor_tensor(out=var, in0=mu, in1=mu, op=ALU.mult),
               reads=[lnrows_b], writes=[lnrows_b], tag=tag)
        b.emit("v", lambda e: e.tensor_tensor(out=var, in0=m2, in1=var,
               op=ALU.subtract), reads=[lnrows_b], writes=[lnrows_b], tag=tag)
        b.emit("a", lambda e: e.activation(out=arow, in_=var, func=AF.Ln, bias=EPS),
               reads=[lnrows_b], writes=[lnrows_b], tag=tag)
        b.emit("a", lambda e: e.activation(out=arow, in_=arow, func=AF.Exp, scale=-0.5),
               reads=[lnrows_b], writes=[lnrows_b], tag=tag)
        b.emit("v", lambda e: e.tensor_tensor(out=brow, in0=mu, in1=arow, op=ALU.mult),
               reads=[lnrows_b], writes=[lnrows_b], tag=tag)
        b.emit("p", lambda e: e.matmul(out=ps_bc0[:], lhsT=onesrow_f, rhs=arow,
               start=True, stop=True),
               reads=[lnrows_b, WB["onesf"]], war=[ps_bc0_b], tag=tag + "bc0")
        b.close_group("p", ps_bc0_b)
        b.emit("p", lambda e: e.matmul(out=ps_bc1[:], lhsT=onesrow_f, rhs=brow,
               start=True, stop=True),
               reads=[lnrows_b], war=[ps_bc1_b], tag=tag + "bc1")
        b.close_group("p", ps_bc1_b)

    def ln_apply(src_tiles, src_bufs, out_tiles, out_bufs, tmp_t, tmp_b, tag):
        for k in range(KT):
            b.emit("v", lambda e, k=k: e.tensor_tensor(out=tmp_t, in0=src_tiles[k],
                   in1=ps_bc0[:], op=ALU.mult),
                   reads=[src_bufs[k], ps_bc0_b], writes=[tmp_b], tag=tag + "a")
            b.emit("v", lambda e, k=k: e.tensor_tensor(out=out_tiles[k], in0=tmp_t,
                   in1=ps_bc1[:], op=ALU.subtract),
                   reads=[tmp_b, ps_bc1_b], writes=[out_bufs[k]], tag=tag + "b")

    def make_ffn_work():
        return dict(
            lgt=(AFA.alloc(TC, rows=8), Buf("lgt")),
            mid=(ABF.alloc(TC, rows=8), Buf("mid")),
            gex=(AFA.alloc(32), Buf("gex")),
            gtb=(ABF.alloc(32), Buf("gtb")),
            ge=(ABF.alloc(TC, rows=8), Buf("ge")),
            grep=(ABF.alloc(TC, rows=64), Buf("grep")),
            wmid=(ABF.alloc(TC, rows=64), Buf("wmid")),
            rows=(AFA.alloc(2), Buf("rows")),
        )

    def ffn(xin_tiles, xin_bufs, res_tiles, res_bufs, out_tiles, out_bufs, W, tag):
        lgt_t, lgt_b = W["lgt"]
        mid_t, mid_b = W["mid"]
        gex_t, gex_b = W["gex"]
        gtb_t, gtb_b = W["gtb"]
        ge_t, ge_b = W["ge"]
        grep_t, grep_b = W["grep"]
        wmid_t, wmid_b = W["wmid"]
        rows_t, rows_b = W["rows"]
        for k in range(KT):
            b.emit("p", lambda e, k=k: e.matmul(out=ps_small[0:8, :],
                   lhsT=rwT_s[:, k * NE:(k + 1) * NE], rhs=xin_tiles[k],
                   start=(k == 0), stop=False),
                   reads=[xin_bufs[k]],
                   war=[ps_small_b],
                   tag=tag + "rout")
        b.emit("p", lambda e: e.matmul(out=ps_small[0:8, :], lhsT=rb_s,
               rhs=ones_row, start=False, stop=True),
               reads=[WB["rb"], WB["ones"]], tag=tag)
        for k in range(KT):
            b.emit("p", lambda e, k=k: e.matmul(out=ps_small[32:40, :],
                   lhsT=lA_s[:, k * NR:(k + 1) * NR], rhs=xin_tiles[k],
                   start=(k == 0), stop=(k == KT - 1), tile_position=(0, 32)),
                   reads=[xin_bufs[k]],
                   war=[ps_small_b],
                   tag=tag + "mid")
        b.close_group("p", ps_small_b)
        b.emit("v", lambda e: e.tensor_copy(out=lgt_t, in_=ps_small[0:8, :]),
               reads=[ps_small_b], writes=[lgt_b], tag=tag + "lgt")
        b.emit("v", lambda e: e.tensor_copy(out=mid_t, in_=ps_small[32:40, :]),
               reads=[ps_small_b], writes=[mid_b], tag=tag + "midc")
        nj = TC // 128
        for j in range(nj):
            b.emit("p", lambda e, j=j: e.transpose(
                   out=ps_small[:, j * 8:(j + 1) * 8],
                   in_=lgt_t[0:8, j * 128:(j + 1) * 128], identity=ident8f),
                   reads=[lgt_b, WB["idf"]] if j == 0 else [lgt_b],
                   war=[ps_small_b],
                   tag=tag + "tr")
        b.close_group("p", ps_small_b)
        mxr = rows_t[0:128, 0:1]
        smr = rows_t[0:128, 1:2]
        for j in range(nj):
            sl = ps_small[:, j * 8:(j + 1) * 8]
            b.emit("v", lambda e, sl=sl: e.reduce_max(out=mxr, in_=sl, axis=AX.X),
                   reads=[ps_small_b], writes=[rows_b], tag=tag + "mx")
            b.emit("v", lambda e: e.tensor_scalar(out=mxr, in0=mxr, scalar1=-1.0,
                   scalar2=None, op0=ALU.mult), reads=[rows_b], writes=[rows_b],
                   tag=tag)
            b.emit("a", lambda e, sl=sl, j=j: e.activation(
                   out=gex_t[:, j * 8:(j + 1) * 8], in_=sl, func=AF.Exp, bias=mxr),
                   reads=[ps_small_b, rows_b], writes=[gex_b], tag=tag + "exp")
            b.emit("v", lambda e, j=j: e.reduce_sum(out=smr,
                   in_=gex_t[:, j * 8:(j + 1) * 8], axis=AX.X),
                   reads=[gex_b], writes=[rows_b], tag=tag + "sum")
            b.emit("v", lambda e: e.reciprocal(out=smr, in_=smr),
                   reads=[rows_b], writes=[rows_b], tag=tag + "rcp")
            b.emit("v", lambda e, j=j: e.tensor_scalar(
                   out=gtb_t[:, j * 8:(j + 1) * 8],
                   in0=gex_t[:, j * 8:(j + 1) * 8], scalar1=smr, scalar2=None,
                   op0=ALU.mult), reads=[gex_b, rows_b], writes=[gtb_b],
                   tag=tag + "gmul")
        for j in range(nj):
            b.emit("p", lambda e, j=j: e.transpose(
                   out=ps_geb[0:8, j * 128:(j + 1) * 128],
                   in_=gtb_t[:, j * 8:(j + 1) * 8], identity=ident_bf),
                   reads=[gtb_b, WB["ident"]] if j == 0 else [gtb_b],
                   war=[ps_geb_b],
                   tag=tag + "trb")
        b.close_group("p", ps_geb_b)
        b.emit("v", lambda e: e.tensor_copy(out=ge_t, in_=ps_geb[0:8, :]),
               reads=[ps_geb_b], writes=[ge_b], tag=tag + "gec")
        b.emit("p", lambda e: e.matmul(out=ps_bc0[0:64, :],
               lhsT=repl_s[0:8, 64:128], rhs=mid_t, start=True, stop=True),
               reads=[mid_b, WB["repl"]], war=[ps_bc0_b], tag=tag + "repm")
        b.close_group("p", ps_bc0_b)
        b.emit("p", lambda e: e.matmul(out=ps_bc1[0:64, :],
               lhsT=repl_s[0:8, 0:64], rhs=ge_t, start=True, stop=True),
               reads=[ge_b], war=[ps_bc1_b], tag=tag + "repg")
        b.close_group("p", ps_bc1_b)
        b.emit("v", lambda e: e.tensor_copy(out=grep_t, in_=ps_bc1[0:64, :]),
               reads=[ps_bc1_b], writes=[grep_b], tag=tag + "grc")
        b.emit("v", lambda e: e.scalar_tensor_tensor(out=wmid_t,
               in0=ps_bc0[0:64, :], scalar=2.0, in1=grep_t, op0=ALU.mult,
               op1=ALU.mult), reads=[ps_bc0_b, grep_b], writes=[wmid_b],
               tag=tag + "wmid")
        for g0 in range(0, KT, 4):
            for m in range(g0, g0 + 4):
                ps, psb = ps_mm[m % 4], ps_mm_b[m % 4]
                for k in range(KT):
                    b.emit("p", lambda e, m=m, k=k, ps=ps: e.matmul(out=ps[:],
                           lhsT=bwT_s[:, k * DM + m * 128:k * DM + (m + 1) * 128],
                           rhs=xin_tiles[k], start=(k == 0), stop=False),
                           reads=[xin_bufs[k]],
                           war=[psb],
                           tag=tag + f"bs{m}")
                b.emit("p", lambda e, m=m, ps=ps: e.matmul(out=ps[:],
                       lhsT=bb_s[0:1, m * 128:(m + 1) * 128], rhs=ones_row,
                       start=False, stop=False), reads=[], war=[psb], tag=tag)
                b.emit("p", lambda e, m=m, ps=ps: e.matmul(out=ps[:],
                       lhsT=lB_s[:, m * 128:(m + 1) * 128], rhs=wmid_t,
                       start=False, stop=True), reads=[wmid_b], war=[psb],
                       tag=tag + "lora")
                b.close_group("p", psb)
            for m in range(g0, g0 + 4):
                ps, psb = ps_mm[m % 4], ps_mm_b[m % 4]
                b.emit("v", lambda e, m=m, ps=ps: e.scalar_tensor_tensor(
                       out=out_tiles[m], in0=ps[:], scalar=0.5, in1=res_tiles[m],
                       op0=ALU.mult, op1=ALU.add),
                       reads=[psb, res_bufs[m]], writes=[out_bufs[m]],
                       tag=tag + "ev")

    ABF.mark(); AFA.mark()
    stage = ABF.alloc(2 * DIN)
    stage_b = Buf("stage")
    for k in range(KT):
        dma("g", stage, win[k], writes=[stage_b])
        dma("s", win_bf.ap()[k], stage, reads=[stage_b], writes=[WB["winbf"]])
    for d in range(DT2):
        dma("g", stage[:, 0:DM], wout[d], writes=[stage_b])
        dma("s", wout_bf.ap()[d], stage[:, 0:DM], reads=[stage_b],
            writes=[WB["woutbf"]])
    wx_s = ABF.alloc(DT2 * 96)
    cdiag_s = ABF.alloc(64 * 128)
    tails_s = ABF.alloc(DT2 * 4)
    for d in range(DT2):
        dma("g", wx_s[:, d * 96:(d + 1) * 96], wx[d], writes=[WB["wx"]])
    for t in range(64):
        dma("g", cdiag_s[:, t * 128:(t + 1) * 128], cdiag[t], writes=[WB["cdiag"]])
    b.emit("v", lambda e: e.memset(tails_s, 0.0), writes=[WB["tails"]])
    b.barrier()

    xf_t = [AFA.alloc(TC) for _ in range(KT)]
    xf_b = [Buf(f"xf{k}") for k in range(KT)]
    xbf_t = ABF.alloc(TC)
    xbf_b = Buf("xbf")
    sq_t = ABF.alloc(TC)
    sq_b = Buf("sq")
    lnrows_t = AFA.alloc(5 * TC, rows=1)
    lnrows_b = Buf("lnrows")
    tmp_t = AFA.alloc(TC)
    tmp_b = Buf("tmp")
    xln1_t = [ABF.alloc(TC) for _ in range(KT)]
    xln1_b = [Buf(f"xln1{k}") for k in range(KT)]
    x1_t = [AFA.alloc(TC) for _ in range(KT)]
    x1_b = [Buf(f"x1{k}") for k in range(KT)]
    xln2f_t = AFA.alloc(TC)
    xln2f_b = Buf("xln2f")
    xln2_t = [ABF.alloc(TC) for _ in range(KT)]
    xln2_b = [Buf(f"xln2{k}") for k in range(KT)]
    ffnw1 = make_ffn_work()
    wg_t = [ABF.alloc(KT * 512) for _ in range(2)]
    wg_b = [Buf(f"wg{i}") for i in range(2)]
    xin_t = [ABF.alloc(TC + 4) for _ in range(2)]
    xin_b = [Buf(f"xin{i}") for i in range(2)]
    u_t = [ABF.alloc(TC) for _ in range(DT2)]
    u_b = [Buf(f"u{d}") for d in range(DT2)]
    gz_t = ABF.alloc(TC)
    gz_b = Buf("gz")
    dtb_t = ABF.alloc(TC, rows=64)
    dtb_b = Buf("dtb")
    bct_t = ABF.alloc(TC, rows=32)
    bct_b = Buf("bct")
    cbuf_t = AFA.alloc(TC)
    cbuf_b = Buf("cbuf")
    ebuf_t = AFA.alloc(TC)
    ebuf_b = Buf("ebuf")
    lbuf_t = AFA.alloc(TC)
    lbuf_b = Buf("lbuf")
    dl_t = ABF.alloc(TC)
    dl_b = Buf("dl")
    du2_t = ABF.alloc(TC)
    du2_b = Buf("du2")

    for c in range(NCH):
        cs = slice(c * TC, (c + 1) * TC)
        for k in range(KT):
            dma("s", xf_t[k], xT[k][:, cs], writes=[xf_b[k]], tag=f"ldx{c}")
        ln_rows(xf_t, xf_b, lnrows_t, lnrows_b, sq_t, sq_b, xbf_t, xbf_b, f"L1{c}")
        ln_apply(xf_t, xf_b, xln1_t, xln1_b, tmp_t, tmp_b, f"L1a{c}")
        ffn(xln1_t, xln1_b, xf_t, xf_b, x1_t, x1_b, ffnw1, f"F1{c}")
        for k in range(KT):
            dma("s", sp_x1.ap()[k][:, cs], x1_t[k], reads=[x1_b[k]], tag=f"sx1{c}")
        ln_rows(x1_t, x1_b, lnrows_t, lnrows_b, sq_t, sq_b, xbf_t, xbf_b, f"L2{c}")
        for k in range(KT):
            b.emit("v", lambda e, k=k: e.tensor_tensor(out=tmp_t, in0=x1_t[k],
                   in1=ps_bc0[:], op=ALU.mult),
                   reads=[x1_b[k], ps_bc0_b], writes=[tmp_b], tag=f"L2a{c}")
            b.emit("v", lambda e: e.tensor_tensor(out=xln2f_t, in0=tmp_t,
                   in1=ps_bc1[:], op=ALU.subtract),
                   reads=[tmp_b, ps_bc1_b], writes=[xln2f_b], tag=f"L2b{c}")
            b.emit("v", lambda e, k=k: e.tensor_copy(out=xln2_t[k], in_=xln2f_t),
                   reads=[xln2f_b], writes=[xln2_b[k]], tag=f"L2c{c}")
            dma("s", sp_x2.ap()[k][:, cs], xln2f_t, reads=[xln2f_b], tag=f"sx2{c}")
        for g0 in range(0, 2 * DT2, 4):
            wgi = (g0 // 4) % 2
            for k in range(KT):
                dma("a", wg_t[wgi][:, k * 512:(k + 1) * 512],
                    win_bf.ap()[k][:, g0 * 128:(g0 + 4) * 128],
                    reads=[WB["winbf"]], writes=[wg_b[wgi]], tag=f"lw{c}")
            for mi in range(4):
                m = g0 + mi
                ps, psb = ps_mm[mi], ps_mm_b[mi]
                for k in range(KT):
                    b.emit("p", lambda e, k=k, mi=mi, wgi=wgi, ps=ps:
                           e.matmul(out=ps[:],
                                    lhsT=wg_t[wgi][:, k * 512 + mi * 128:k * 512 + (mi + 1) * 128],
                                    rhs=xln2_t[k], start=(k == 0), stop=(k == KT - 1)),
                           reads=[xln2_b[k], wg_b[wgi]],
                           war=[psb],
                           tag=f"wi{c}m{m}")
                b.close_group("p", psb)
                if m < DT2:
                    xt, xtb = xin_t[m % 2], xin_b[m % 2]
                    b.emit("v", lambda e, xt=xt, ps=ps: e.tensor_copy(
                           out=xt[:, 3:3 + TC], in_=ps[:]),
                           reads=[psb], writes=[xtb], tag=f"xc{c}")
                    b.emit("v", lambda e, xt=xt, m=m: e.tensor_copy(
                           out=xt[:, 0:3], in_=tails_s[:, m * 4:m * 4 + 3]),
                           reads=[WB["tails"]], writes=[xtb], tag=f"tl{c}")
                    psc, pscb = ps_mm[(mi + 2) % 4], ps_mm_b[(mi + 2) % 4]
                    for tap in range(4):
                        b.emit("p", lambda e, m=m, tap=tap, xt=xt, psc=psc:
                               e.matmul(out=psc[:],
                                        lhsT=cdiag_s[:, (m * 4 + tap) * 128:(m * 4 + tap + 1) * 128],
                                        rhs=xt[:, tap:tap + TC],
                                        start=(tap == 0), stop=(tap == 3)),
                               reads=[xtb],
                               war=[pscb],
                               tag=f"cv{c}m{m}")
                    b.close_group("p", pscb)
                    b.emit("v", lambda e, xt=xt, m=m: e.tensor_copy(
                           out=tails_s[:, m * 4:m * 4 + 3], in_=xt[:, TC:TC + 3]),
                           reads=[xtb], writes=[WB["tails"]], tag=f"tu{c}")
                    b.emit("a", lambda e, m=m, psc=psc: e.activation(
                           out=u_t[m], in_=psc[:], func=AF.Silu,
                           bias=vcol("conv_b", m)),
                           reads=[pscb, WB["vec"]], writes=[u_b[m]], tag=f"si{c}")
                    dma("s", sp_u.ap()[m][:, cs], u_t[m], reads=[u_b[m]],
                        tag=f"su{c}")
                else:
                    dz = m - DT2
                    b.emit("a", lambda e, ps=ps: e.activation(out=gz_t, in_=ps[:],
                           func=AF.Silu), reads=[psb], writes=[gz_b], tag=f"sz{c}")
                    dma("s", sp_g.ap()[dz][:, cs], gz_t, reads=[gz_b], tag=f"sg{c}")
        for d in range(DT2):
            b.emit("p", lambda e, d=d: e.matmul(out=ps_small[0:96, :],
                   lhsT=wx_s[:, d * 96:(d + 1) * 96], rhs=u_t[d],
                   start=(d == 0), stop=(d == DT2 - 1)),
                   reads=[u_b[d]],
                   war=[ps_small_b],
                   tag=f"wx{c}")
        b.close_group("p", ps_small_b)
        b.emit("v", lambda e: e.tensor_copy(out=dtb_t, in_=ps_small[0:64, :]),
               reads=[ps_small_b], writes=[dtb_b], tag=f"dtb{c}")
        b.emit("v", lambda e: e.tensor_copy(out=bct_t, in_=ps_small[64:96, :]),
               reads=[ps_small_b], writes=[bct_b], tag=f"bct{c}")
        dma("s", sp_bc.ap()[:, cs], bct_t, reads=[bct_b], tag=f"sbc{c}")
        for g0 in range(0, DT2, 4):
            for di in range(4):
                d = g0 + di
                ps, psb = ps_mm[di], ps_mm_b[di]
                b.emit("p", lambda e, d=d, ps=ps: e.matmul(out=ps[:],
                       lhsT=wdt_s[:, d * 128:(d + 1) * 128], rhs=dtb_t,
                       start=True, stop=True),
                       reads=[dtb_b], war=[psb], tag=f"wd{c}")
                b.close_group("p", psb)
            for di in range(4):
                d = g0 + di
                ps, psb = ps_mm[di], ps_mm_b[di]
                b.emit("v", lambda e, d=d, ps=ps: e.tensor_scalar(out=cbuf_t,
                       in0=ps[:], scalar1=vcol("b_dt", d), scalar2=SM30,
                       op0=ALU.add, op1=ALU.min),
                       reads=[psb, WB["vec"]], writes=[cbuf_b], tag=f"dm{c}")
                b.emit("a", lambda e: e.activation(out=ebuf_t, in_=cbuf_t,
                       func=AF.Exp), reads=[cbuf_b], writes=[ebuf_b], tag=f"de{c}")
                b.emit("a", lambda e: e.activation(out=lbuf_t, in_=ebuf_t,
                       func=AF.Ln, bias=1.0), reads=[ebuf_b], writes=[lbuf_b],
                       tag=f"dl{c}")
                b.emit("v", lambda e, d=d, ps=ps: e.scalar_tensor_tensor(
                       out=dl_t, in0=ps[:], scalar=vcol("b_dt", d), in1=lbuf_t,
                       op0=ALU.add, op1=ALU.max),
                       reads=[psb, lbuf_b, WB["vec"]], writes=[dl_b], tag=f"dx{c}")
                dma("s", sp_d.ap()[d][:, cs], dl_t, reads=[dl_b], tag=f"sd{c}")
                b.emit("v", lambda e, d=d: e.tensor_tensor(out=du2_t, in0=dl_t,
                       in1=u_t[d], op=ALU.mult),
                       reads=[dl_b, u_b[d]], writes=[du2_b], tag=f"duz{c}")
                dma("a", sp_du.ap()[d][:, cs], du2_t, reads=[du2_b], tag=f"sdu{c}")

    ABF.reset(); AFA.reset()
    b.barrier()

    ABF.mark(); AFA.mark()
    NSP = L // 1024
    NCH2 = NCH // 2
    v_d = sp_d.ap().rearrange("d p (s n) -> (d p s) n", n=TC)
    v_du = sp_du.ap().rearrange("d p (s n) -> (d p s) n", n=TC)
    v_u = sp_u.ap().rearrange("d p (s n) -> (d p s) n", n=TC)
    v_g = sp_g.ap().rearrange("d p (s n) -> (d p s) n", n=TC)
    v_bc = sp_bc.ap().rearrange("r (c n) -> (r c) n", n=TC)
    v_x1 = sp_x1.ap().rearrange("k p (c n) -> (k p c) n", n=TC)
    v_x2 = sp_x2.ap().rearrange("k p (c n) -> (k p c) n", n=TC)
    v_yo = yout.rearrange("k p (c n) -> (k p c) n", n=TC)

    def gdt(dt, j):
        c0 = dt * 8 + j
        return gidx_t.ap()[:, c0:c0 + 1]

    def gbc(r, j):
        c0 = 128 + r * 8 + j
        return gidx_t.ap()[:, c0:c0 + 1]

    def gk(k, j):
        c0 = 384 + k * 8 + j
        return gidx_t.ap()[:, c0:c0 + 1]

    def ig(out, view, idx, reads=(), writes=(), tag=""):
        return b.emit("g", lambda e, o=out, v=view, i=idx: e.indirect_dma_start(
            out=o, out_offset=None, in_=v,
            in_offset=bass.IndirectOffsetOnAxis(ap=i, axis=0)),
            reads=reads, writes=writes, dma=True, tag=tag)

    def isc(view, idx, in_, reads=(), tag=""):
        return b.emit("g", lambda e, v=view, i=idx, s=in_: e.indirect_dma_start(
            out=v, out_offset=bass.IndirectOffsetOnAxis(ap=i, axis=0),
            in_=s, in_offset=None),
            reads=reads, dma=True, tag=tag)

    Bbc_t = [ABF.alloc(TC) for _ in range(NS)]
    Bbc_b = [Buf(f"Bb{s}") for s in range(NS)]
    Cbc_t = [ABF.alloc(TC) for _ in range(NS)]
    Cbc_b = [Buf(f"Cb{s}") for s in range(NS)]
    dld_t = [ABF.alloc(TC) for _ in range(2)]
    dld_b = [Buf(f"dld{i}") for i in range(2)]
    duld_t = [ABF.alloc(TC) for _ in range(2)]
    duld_b = [Buf(f"duld{i}") for i in range(2)]
    uld_t = [ABF.alloc(TC) for _ in range(2)]
    uld_b = [Buf(f"uld{i}") for i in range(2)]
    gld_t = [ABF.alloc(TC) for _ in range(2)]
    gld_b = [Buf(f"gld{i}") for i in range(2)]
    wbuf_t = [ABF.alloc(TC) for _ in range(2)]
    wbuf_b = [Buf(f"wb{i}") for i in range(2)]
    scr_t = ABF.alloc(TC)
    scr_b = Buf("scr")
    dA_t = [AFA.alloc(TC) for _ in range(2)]
    dA_b = [Buf(f"dA{i}") for i in range(2)]
    dBu_t = [ABF.alloc(TC) for _ in range(2)]
    dBu_b = [Buf(f"dBu{i}") for i in range(2)]
    h_t = [ABF.alloc(TC) for _ in range(2)]
    h_b = [Buf(f"h{i}") for i in range(2)]
    p_t = [ABF.alloc(TC) for _ in range(NS)]
    p_b = [Buf(f"p{s}") for s in range(NS)]
    ym_t = [ABF.alloc(TC) for _ in range(DT2)]
    ym_b = [Buf(f"ym{d}") for d in range(DT2)]
    cdl_t = AFA.alloc(TC)
    cdl_b = Buf("cdl")
    rb_t2 = AFA.alloc(TC)
    rb_b2 = Buf("rb2")
    gam_t = AFA.alloc(NS)
    gam_b = Buf("gam")
    part_t = AFA.alloc(NS)
    part_b = Buf("part")
    t1_t = AFA.alloc(TC)
    t1_b = Buf("t1")
    wo_t = [ABF.alloc(DT2 * 256) for _ in range(2)]
    wo_b = [Buf(f"wo{i}") for i in range(2)]
    x1l_t = [AFA.alloc(TC) for _ in range(2)]
    x1l_b = [Buf(f"x1l{i}") for i in range(2)]
    xl2l_t = [AFA.alloc(TC) for _ in range(2)]
    xl2l_b = [Buf(f"xl2l{i}") for i in range(2)]
    h3_t = [AFA.alloc(TC) for _ in range(KT)]
    h3_b = [Buf(f"h3{k}") for k in range(KT)]
    x2_t = [AFA.alloc(TC) for _ in range(KT)]
    x2_b = [Buf(f"x2{k}") for k in range(KT)]
    xln4_t = [ym_t[k] for k in range(KT)]
    xln4_b = [ym_b[k] for k in range(KT)]
    outst_t = AFA.alloc(TC)
    outst_b = Buf("outst")
    xbf2_t = ABF.alloc(TC)
    xbf2_b = Buf("xbf2")
    sq2_t = ABF.alloc(TC)
    sq2_b = Buf("sq2")
    lnrows2_t = AFA.alloc(5 * TC, rows=1)
    lnrows2_b = Buf("lnrows2")
    tmp2_t = AFA.alloc(TC)
    tmp2_b = Buf("tmp2")
    ffnw2 = make_ffn_work()

    b.emit("v", lambda e: e.memset(hrun_s, 0.0), writes=[WB["hrun"]])

    for c in range(NCH2):
        cs = slice(c * TC, (c + 1) * TC)
        for s in range(NS):
            q = "s" if s % 2 == 0 else "a"
            dma(q, Bbc_t[s], sp_bc.ap()[s, cs].partition_broadcast(128),
                writes=[Bbc_b[s]], tag=f"tBb{c}")
        for dt in range(DT2):
            sl = dt % 2
            dma("s", dld_t[sl], sp_d.ap()[dt][:, cs],
                writes=[dld_b[sl]], tag=f"tld{c}")
            dma("a", duld_t[sl], sp_du.ap()[dt][:, cs],
                writes=[duld_b[sl]], tag=f"tlu{c}")
            b.emit("v", lambda e, sl=sl: e.tensor_tensor_scan(
                   out=cdl_t, data0=dld_t[sl], data1=dld_t[sl],
                   initial=0.0, op0=ALU.add, op1=ALU.bypass),
                   reads=[dld_b[sl]], writes=[cdl_b], tag=f"tcs{c}")
            b.emit("v", lambda e: e.tensor_scalar(out=rb_t2, in0=cdl_t,
                   scalar1=cdl_t[:, TC - 1:TC], scalar2=None, op0=ALU.subtract),
                   reads=[cdl_b], writes=[rb_b2], tag=f"tr{c}")
            b.emit("v", lambda e, dt=dt: e.tensor_scalar(
                   out=gam_t, in0=posA_s[:, dt * NS:(dt + 1) * NS],
                   scalar1=cdl_t[:, TC - 1:TC], scalar2=-1.0,
                   op0=ALU.mult, op1=ALU.mult),
                   reads=[cdl_b, WB["posA"]], writes=[gam_b], tag=f"tg{c}")
            b.emit("a", lambda e: e.activation(out=gam_t, in_=gam_t, func=AF.Exp),
                   reads=[gam_b], writes=[gam_b], tag=f"tge{c}")
            hsl = hrun_s[:, dt * NS:(dt + 1) * NS]
            b.emit("v", lambda e, hsl=hsl: e.tensor_tensor(out=hsl, in0=hsl,
                   in1=gam_t, op=ALU.mult),
                   reads=[gam_b, WB["hrun"]], writes=[WB["hrun"]], tag=f"thg{c}")
            for s in range(NS):
                r2 = s % 2
                b.emit("a", lambda e, dt=dt, s=s, r2=r2: e.activation(
                       out=wbuf_t[r2], in_=rb_t2, func=AF.Exp,
                       scale=posA_s[:, dt * NS + s:dt * NS + s + 1]),
                       reads=[rb_b2, WB["posA"]], writes=[wbuf_b[r2]], tag=f"tw{c}")
                b.emit("g", lambda e, s=s, sl=sl, r2=r2: e.tensor_tensor(
                       out=dBu_t[r2], in0=duld_t[sl], in1=Bbc_t[s],
                       op=ALU.mult),
                       reads=[duld_b[sl], Bbc_b[s]], writes=[dBu_b[r2]],
                       tag=f"tdb{c}")
                b.emit("v", lambda e, s=s, r2=r2: e.scalar_tensor_tensor(
                       out=scr_t, in0=wbuf_t[r2], scalar=1.0, in1=dBu_t[r2],
                       op0=ALU.mult, op1=ALU.mult,
                       accum_out=part_t[:, s:s + 1]),
                       reads=[wbuf_b[r2], dBu_b[r2]],
                       writes=[scr_b, part_b], tag=f"ttr{c}")
            hsl2 = hrun_s[:, dt * NS:(dt + 1) * NS]
            b.emit("v", lambda e, hsl2=hsl2: e.tensor_tensor(out=hsl2,
                   in0=hsl2, in1=part_t, op=ALU.add),
                   reads=[part_b, WB["hrun"]], writes=[WB["hrun"]],
                   tag=f"tha{c}")
    b.emit("v", lambda e: e.tensor_scalar(out=carry_s, in0=hrun_s,
           scalar1=smask_s[:, 0:1], scalar2=None, op0=ALU.mult),
           reads=[WB["hrun"], WB["smask"]], writes=[WB["carry"]], tag="cinit")

    for j in range(NCH2):
        for s in range(NS):
            ig(Bbc_t[s], v_bc, gbc(s, j), writes=[Bbc_b[s]], tag=f"sBb{j}")
            ig(Cbc_t[s], v_bc, gbc(16 + s, j), writes=[Cbc_b[s]], tag=f"sCb{j}")
        if True:
            for dt in range(DT2):
                sl = dt % 2
                psy, psyb = (ps_bc0, ps_bc0_b) if dt % 2 == 0 else (ps_bc1, ps_bc1_b)
                ig(dld_t[sl], v_d, gdt(dt, j), writes=[dld_b[sl]], tag=f"gd{j}")
                ig(duld_t[sl], v_du, gdt(dt, j), writes=[duld_b[sl]], tag=f"gdu{j}")
                ig(uld_t[sl], v_u, gdt(dt, j), writes=[uld_b[sl]], tag=f"gu{j}")
                ig(gld_t[sl], v_g, gdt(dt, j), writes=[gld_b[sl]], tag=f"gg{j}")
                for s in range(NS):
                    r2 = s % 2
                    b.emit("a", lambda e, dt=dt, s=s, sl=sl, r2=r2:
                           e.activation(out=dA_t[r2], in_=dld_t[sl],
                                        func=AF.Exp,
                                        scale=negA_s[:, dt * NS + s:dt * NS + s + 1]),
                           reads=[dld_b[sl], WB["negA"]], writes=[dA_b[r2]],
                           tag=f"dA{j}")
                    b.emit("g", lambda e, s=s, sl=sl, r2=r2: e.tensor_tensor(
                           out=dBu_t[r2], in0=duld_t[sl], in1=Bbc_t[s],
                           op=ALU.mult),
                           reads=[duld_b[sl], Bbc_b[s]], writes=[dBu_b[r2]],
                           tag=f"dB{j}")
                    b.emit("v", lambda e, dt=dt, s=s, r2=r2: e.tensor_tensor_scan(
                           out=h_t[r2], data0=dA_t[r2], data1=dBu_t[r2],
                           initial=carry_s[:, dt * NS + s:dt * NS + s + 1],
                           op0=ALU.mult, op1=ALU.add),
                           reads=[dA_b[r2], dBu_b[r2], WB["carry"]],
                           writes=[h_b[r2]], tag=f"ts{j}")
                    b.emit("v", lambda e, dt=dt, s=s, r2=r2: e.tensor_copy(
                           out=carry_s[:, dt * NS + s:dt * NS + s + 1],
                           in_=h_t[r2][:, TC - 1:TC]),
                           reads=[h_b[r2]], writes=[WB["carry"]], tag=f"cr{j}")
                    b.emit("v", lambda e, s=s, r2=r2: e.tensor_tensor(out=p_t[s],
                           in0=h_t[r2], in1=Cbc_t[s], op=ALU.mult),
                           reads=[h_b[r2], Cbc_b[s]], writes=[p_b[s]], tag=f"cm{j}")
                    b.emit("p", lambda e, s=s, psy=psy: e.matmul(out=psy[:],
                           lhsT=ident_bf, rhs=p_t[s], start=(s == 0),
                           stop=(s == NS - 1)),
                           reads=[p_b[s]],
                           war=[psyb],
                           tag=f"sm{j}")
                b.close_group("p", psyb)
                b.emit("v", lambda e, dt=dt, sl=sl, psy=psy:
                       e.scalar_tensor_tensor(
                       out=t1_t, in0=uld_t[sl], scalar=vcol("D", dt),
                       in1=psy[:], op0=ALU.mult, op1=ALU.add),
                       reads=[uld_b[sl], psyb, WB["vec"]], writes=[t1_b],
                       tag=f"y1{j}")
                b.emit("v", lambda e, dt=dt, sl=sl: e.tensor_tensor(
                       out=ym_t[dt], in0=t1_t, in1=gld_t[sl], op=ALU.mult),
                       reads=[t1_b, gld_b[sl]], writes=[ym_b[dt]], tag=f"y2{j}")
            for g0 in range(0, KT, 2):
                wgi = (g0 // 2) % 2
                for d in range(DT2):
                    dma("a", wo_t[wgi][:, d * 256:(d + 1) * 256],
                        wout_bf.ap()[d][:, g0 * 128:(g0 + 2) * 128],
                        reads=[WB["woutbf"]], writes=[wo_b[wgi]], tag=f"lwo{j}")
                for mi in range(2):
                    m = g0 + mi
                    ps, psb = ps_mm[mi], ps_mm_b[mi]
                    for d in range(DT2):
                        b.emit("p", lambda e, m=m, d=d, mi=mi, wgi=wgi, ps=ps:
                               e.matmul(out=ps[:],
                                        lhsT=wo_t[wgi][:, d * 256 + mi * 128:d * 256 + (mi + 1) * 128],
                                        rhs=ym_t[d], start=(d == 0),
                                        stop=(d == DT2 - 1)),
                               reads=[ym_b[d], wo_b[wgi]],
                               war=[psb],
                               tag=f"wo{j}m{m}")
                    b.close_group("p", psb)
                for mi in range(2):
                    m = g0 + mi
                    ps, psb = ps_mm[mi], ps_mm_b[mi]
                    ig(xl2l_t[mi], v_x2, gk(m, j), writes=[xl2l_b[mi]], tag=f"lx2{j}")
                    b.emit("v", lambda e, m=m, mi=mi, ps=ps: e.tensor_tensor(
                           out=h3_t[m], in0=ps[:], in1=xl2l_t[mi], op=ALU.add),
                           reads=[psb, xl2l_b[mi]], writes=[h3_b[m]], tag=f"h3{j}")
            ln_rows(h3_t, h3_b, lnrows2_t, lnrows2_b, sq2_t, sq2_b, xbf2_t,
                    xbf2_b, f"L3{j}")
            for k in range(KT):
                kl = k % 2
                ig(x1l_t[kl], v_x1, gk(k, j), writes=[x1l_b[kl]], tag=f"lx1{j}")
                b.emit("v", lambda e, k=k: e.tensor_tensor(out=tmp2_t,
                       in0=h3_t[k], in1=ps_bc0[:], op=ALU.mult),
                       reads=[h3_b[k], ps_bc0_b], writes=[tmp2_b], tag=f"L3a{j}")
                b.emit("v", lambda e: e.tensor_tensor(out=tmp2_t, in0=tmp2_t,
                       in1=ps_bc1[:], op=ALU.subtract),
                       reads=[tmp2_b, ps_bc1_b], writes=[tmp2_b], tag=f"L3b{j}")
                b.emit("v", lambda e, k=k, kl=kl: e.scalar_tensor_tensor(
                       out=x2_t[k], in0=tmp2_t, scalar=vcol("gamma", k),
                       in1=x1l_t[kl], op0=ALU.mult, op1=ALU.add),
                       reads=[tmp2_b, x1l_b[kl], WB["vec"]], writes=[x2_b[k]],
                       tag=f"L3g{j}")
                b.emit("v", lambda e, k=k: e.tensor_scalar(out=x2_t[k],
                       in0=x2_t[k], scalar1=vcol("beta", k), scalar2=None,
                       op0=ALU.add),
                       reads=[x2_b[k], WB["vec"]], writes=[x2_b[k]], tag=f"L3bt{j}")
            ln_rows(x2_t, x2_b, lnrows2_t, lnrows2_b, sq2_t, sq2_b, xbf2_t,
                    xbf2_b, f"L4{j}")
            ln_apply(x2_t, x2_b, xln4_t, xln4_b, tmp2_t, tmp2_b, f"L4a{j}")
            ffn(xln4_t, xln4_b, x2_t, x2_b, x2_t, x2_b, ffnw2, f"F2{j}")
            for k in range(KT):
                isc(v_yo, gk(k, j), x2_t[k], reads=[x2_b[k]], tag=f"out{j}")

    ABF.reset(); AFA.reset()
    b.fence()
    b.emit("g", lambda e: e.nop(), wait_all_dma=True, tag="final")
    order = b.schedule()
    b.lower(order)
    b.finalize(order)

    with ctx:
        with ExitStack() as sctx:
            sems = {}
            for n in ("p", "v", "a", "g", "sd", "ad", "gd"):
                sems[n] = sctx.enter_context(nc.semaphore(n))
            with nc.Block() as block:
                def mk(eng_key):
                    def run(engine):
                        for op in order:
                            if op.eng != eng_key:
                                continue
                            for s, v in op.waits:
                                engine.wait_ge(sems[s], v)
                            inst = op.fn(engine)
                            if op.inc is not None and inst is not None:
                                inst.then_inc(sems[op.inc[0]], op.inc[1])
                    return run
                block.tensor(mk("p"))
                block.vector(mk("v"))
                block.scalar(mk("a"))
                block.gpsimd(mk("g"))
                block.sync(mk("s"))
    return nc


def prep_inputs(inputs, L=8192):
    f = np.ascontiguousarray
    W_in = np.asarray(inputs["L0_attn_W_in"])
    conv_w = np.asarray(inputs["L0_attn_conv_w"])
    d = {}
    d["win"] = f(W_in.reshape(KT, 128, 2 * DIN))
    d["bwT"] = f(np.asarray(inputs["L0_ffn_base_w"]).T.reshape(KT, 128, DM))
    d["rwT"] = f(np.asarray(inputs["L0_ffn_router_w"]).T.reshape(KT, 128, NE))
    d["lA"] = f(np.asarray(inputs["L0_ffn_lora_A"]).reshape(KT, 128, NR))
    d["lB"] = f(np.asarray(inputs["L0_ffn_lora_B"]).reshape(NE * NR, DM))
    d["wx"] = f(np.asarray(inputs["L0_attn_W_x"]).reshape(DT2, 128, 96))
    d["wdt"] = f(np.asarray(inputs["L0_attn_W_dt"]))
    d["wout"] = f(np.asarray(inputs["L0_attn_W_out"]).reshape(DT2, 128, DM))
    cd = np.zeros((DT2 * 4, 128, 128), np.float32)
    idx = np.arange(128)
    for dt in range(DT2):
        for k in range(4):
            cd[dt * 4 + k, idx, idx] = conv_w[dt * 128 + idx, k]
    d["cdiag"] = cd
    d["alog"] = f(np.asarray(inputs["L0_attn_A_log"]).reshape(DT2, 128, NS))
    vec = np.zeros((128, 64), np.float32)
    vec[:, 0:16] = np.asarray(inputs["L0_attn_conv_b"]).reshape(DT2, 128).T
    vec[:, 16:32] = np.asarray(inputs["L0_attn_b_dt"]).reshape(DT2, 128).T
    vec[:, 32:48] = np.asarray(inputs["L0_attn_D"]).reshape(DT2, 128).T
    vec[:, 48:56] = np.asarray(inputs["L0_attn_gamma"]).reshape(KT, 128).T
    vec[:, 56:64] = np.asarray(inputs["L0_attn_beta"]).reshape(KT, 128).T
    d["vec"] = vec
    d["bbrow"] = f(np.asarray(inputs["L0_ffn_base_b"]).reshape(1, DM))
    d["rbrow"] = f(np.asarray(inputs["L0_ffn_router_b"]).reshape(1, NE))
    d["ident_in"] = np.eye(128, dtype=np.float32)
    p = np.arange(128, dtype=np.uint32)
    nch = L // TC
    nc2 = nch // 2
    for sigma in (0, 1):
        gi = np.zeros((128, 448), np.uint32)
        for dt in range(16):
            for j in range(nc2):
                gi[:, dt * 8 + j] = (dt * 128 + p) * nch + (sigma * nc2 + j)
        for r in range(32):
            for j in range(nc2):
                gi[:, 128 + r * 8 + j] = r * nch + (sigma * nc2 + j)
        for k in range(8):
            for j in range(nc2):
                gi[:, 384 + k * 8 + j] = (k * 128 + p) * nch + (sigma * nc2 + j)
        d[f"gidx{sigma}"] = gi
        d[f"smask{sigma}"] = np.full((128, 1), float(sigma), np.float32)
    repl = np.zeros((2, 8, NE * NR), np.float32)
    for ee in range(NE):
        for r in range(NR):
            repl[0, ee, ee * NR + r] = 1.0
            repl[1, r, ee * NR + r] = 1.0
    d["repl_in"] = repl
    return d


def kernel(**inputs):
    x = np.asarray(inputs["L0_attn_x"])
    L = x.shape[1]
    shared = prep_inputs(inputs, L=L)
    nc = build_program(L=L)
    in_maps = []
    for core in range(8):
        bidx = core // 2
        sigma = core % 2
        m = {k: v for k, v in shared.items()
             if not k.startswith(("gidx", "smask"))}
        m["gidx"] = shared[f"gidx{sigma}"]
        m["smask"] = shared[f"smask{sigma}"]
        m["xT"] = np.ascontiguousarray(x[bidx].T).reshape(KT, 128, L)
        in_maps.append(m)
    res = run_bass_kernel_spmd(nc, in_maps, core_ids=list(range(8)))
    out = np.empty((x.shape[0], L, DM), np.float32)
    half = L // 2
    for bidx in range(x.shape[0]):
        y0 = np.asarray(res.results[2 * bidx]["yout"]).reshape(DM, L)
        y1 = np.asarray(res.results[2 * bidx + 1]["yout"]).reshape(DM, L)
        out[bidx, :half] = y0[:, :half].T
        out[bidx, half:] = y1[:, half:].T
    return out


# version 07
# speedup vs baseline: 1.0332x; 1.0332x over previous
import numpy as np
from contextlib import ExitStack

import concourse.bass as bass
import concourse.mybir as mybir
from concourse.bass_utils import run_bass_kernel_spmd

F32 = mybir.dt.float32
BF16 = mybir.dt.bfloat16
AF = mybir.ActivationFunctionType
ALU = mybir.AluOpType
AX = mybir.AxisListType

DM = 1024
DIN = 2048
NS = 16
DTR = 64
NE = 8
NR = 8
KT = DM // 128
DT2 = DIN // 128
EPS = 1e-5
SM30 = 30.0
TC = 512

COMPUTE = ("p", "v", "a", "g", "s")


class Buf:
    __slots__ = ("writer", "readers", "name", "dsem")

    def __init__(self, name=""):
        self.writer = None
        self.readers = []
        self.name = name
        self.dsem = {}


class Op:
    __slots__ = ("eng", "fn", "dma", "reads", "writes", "tag", "cost",
                 "epoch", "idx", "preds", "wait_all_dma", "start",
                 "waits", "inc", "dsem")


class B:

    ENGS = ("p", "v", "a", "g", "s")

    def __init__(self):
        self.ops = []
        self.epoch = 0

    def emit(self, eng, fn, reads=(), writes=(), dma=False, noinc=False,
             extra_waits=(), war=(), tag="", cost=700, wait_all_dma=False,
             dsem=None):
        op = Op()
        op.eng, op.fn, op.dma = eng, fn, dma
        op.reads = list(reads)
        op.writes = list(writes) + list(war)
        op.tag = tag
        op.cost = cost
        op.epoch = self.epoch
        op.wait_all_dma = wait_all_dma
        op.dsem = dsem
        op.idx = len(self.ops)
        op.preds = None
        self.ops.append(op)
        return op

    def fence(self):
        self.epoch += 1

    def barrier(self):
        self.fence()

    def close_group(self, eng, *bufs):
        pass

    def _build_graph(self):
        writer = {}
        readers = {}
        for op in self.ops:
            preds = set()
            for bb in op.reads:
                w = writer.get(id(bb))
                if w is not None:
                    preds.add(w)
            for bb in op.writes:
                w = writer.get(id(bb))
                if w is not None:
                    preds.add(w)
                for r in readers.get(id(bb), ()):
                    preds.add(r)
            for bb in op.writes:
                writer[id(bb)] = op.idx
                readers[id(bb)] = []
            for bb in op.reads:
                readers.setdefault(id(bb), []).append(op.idx)
            preds.discard(op.idx)
            op.preds = sorted(preds)

    COST = {"p": 320, "v": 750, "a": 780, "g": 1200, "s": 300}

    def _estimate_costs(self):
        for op in self.ops:
            if op.cost != 700:
                continue
            if op.dma:
                op.cost = 1200
            else:
                c = self.COST.get(op.eng, 700)
                t = op.tag
                if op.eng == "v":
                    if t.startswith("ts"):
                        c = 1300
                    elif t.startswith(("cm", "dB")):
                        c = 550
                    elif t.startswith("cr"):
                        c = 160
                    elif t.startswith("ttr"):
                        c = 1050
                op.cost = c

    def schedule(self):
        import heapq
        self._estimate_costs()
        self._build_graph()
        n = len(self.ops)
        nsucc_pred = [len(op.preds) for op in self.ops]
        succs = [[] for _ in range(n)]
        for op in self.ops:
            for p in op.preds:
                succs[p].append(op.idx)
        done_t = [0.0] * n
        eng_free = {e: 0.0 for e in self.ENGS}
        q_free = {e: 0.0 for e in ("s", "a", "g")}
        epoch_end = 0.0
        order = []
        by_epoch = {}
        for op in self.ops:
            by_epoch.setdefault(op.epoch, []).append(op)
        for ep in sorted(by_epoch):
            ops_e = by_epoch[ep]
            remaining = set(o.idx for o in ops_e)
            pend = {o.idx: sum(1 for p in self.ops[o.idx].preds
                               if self.ops[p].epoch == ep) for o in ops_e}
            ready = []
            for o in ops_e:
                if pend[o.idx] == 0:
                    heapq.heappush(ready, o.idx)
            scheduled_any = True
            while remaining:
                if not ready:
                    raise RuntimeError("scheduler deadlock (cyclic deps?)")
                cand = heapq.heappop(ready)
                op = self.ops[cand]
                start = max(epoch_end, eng_free[op.eng],
                            max((done_t[p] for p in op.preds), default=0.0))
                if op.dma:
                    issue = max(start, q_free[op.eng])
                    eng_free[op.eng] = issue + 150.0
                    q_free[op.eng] = issue + max(600.0, op.cost * 0.5)
                    done_t[cand] = issue + 1500.0 + op.cost
                else:
                    eng_free[op.eng] = start + op.cost
                    done_t[cand] = start + op.cost
                op.start = start
                order.append(op)
                remaining.discard(cand)
                for sidx in succs[cand]:
                    if self.ops[sidx].epoch == ep and sidx in remaining:
                        pend[sidx] -= 1
                        if pend[sidx] == 0:
                            heapq.heappush(ready, sidx)
            epoch_end = max(max(done_t[o.idx] for o in ops_e), epoch_end)
        return order

    def lower(self, order):
        self.dsem_n = 0
        cnt = {k: 0 for k in ("p", "v", "a", "g")}
        pos = {}
        wm = {e: {} for e in self.ENGS}

        def dma_sem(op):
            if op.dsem is None:
                tgt = None
                for bb in op.writes:
                    tgt = bb
                    break
                if tgt is None:
                    op.dsem = "dmisc_" + op.eng
                else:
                    if op.eng not in tgt.dsem:
                        tgt.dsem[op.eng] = f"d{self.dsem_n}_{op.eng}"
                        self.dsem_n += 1
                    op.dsem = tgt.dsem[op.eng]
            if op.dsem not in cnt:
                cnt[op.dsem] = 0
            return op.dsem

        for op in order:
            semkey = dma_sem(op) if op.dma else op.eng
            cnt[semkey] += 16 if op.dma else 1
            pos[op.idx] = (semkey, cnt[semkey])
            waits = {}
            for p in op.preds:
                s, v = pos[p]
                if wm[op.eng].get(s, 0) >= v:
                    continue
                waits[s] = max(waits.get(s, 0), v)
            if op.wait_all_dma:
                for s, v in cnt.items():
                    if s not in ("p", "v", "a", "g") and v > waits.get(s, 0):
                        waits[s] = v
            op.waits = sorted(waits.items())
            op.inc = (semkey, 16 if op.dma else 1)
            for s, v in waits.items():
                wm[op.eng][s] = v
        self.all_sems = sorted(cnt)
        return cnt

    def finalize(self, order):
        cnt = {}
        epoch_counts = {}
        cur = order[0].epoch if order else 0
        for op in order:
            if op.epoch != cur:
                epoch_counts[cur] = dict(cnt)
                cur = op.epoch
            semkey = op.inc[0]
            cnt[semkey] = cnt.get(semkey, 0) + op.inc[1]
        seen = set()
        for op in order:
            key = (op.eng, op.epoch)
            if key in seen:
                continue
            seen.add(key)
            prev = op.epoch - 1
            if prev in epoch_counts:
                w = dict(op.waits)
                for s, v in epoch_counts[prev].items():
                    if v > 0 and w.get(s, 0) < v:
                        w[s] = v
                op.waits = sorted(w.items())


class Arena:
    def __init__(self, t, n):
        self.t = t
        self.n = n
        self.off = 0
        self.stack = []

    def alloc(self, cols, rows=128):
        off = (self.off + 3) & ~3
        assert off + cols <= self.n, f"arena overflow {off}+{cols}>{self.n}"
        self.off = off + cols
        return self.t.ap()[0:rows, off:off + cols]

    def mark(self):
        self.stack.append(self.off)

    def reset(self):
        self.off = self.stack.pop()


def build_program(L=8192):
    NCH = L // TC
    nc = bass.Bass("TRN2", target_bir_lowering=False, debug=False)

    xT = nc.dram_tensor("xT", [KT, 128, L], F32, kind="ExternalInput").ap()
    win = nc.dram_tensor("win", [KT, 128, 2 * DIN], F32, kind="ExternalInput").ap()
    bwT = nc.dram_tensor("bwT", [KT, 128, DM], F32, kind="ExternalInput").ap()
    rwT = nc.dram_tensor("rwT", [KT, 128, NE], F32, kind="ExternalInput").ap()
    lA = nc.dram_tensor("lA", [KT, 128, NR], F32, kind="ExternalInput").ap()
    lB = nc.dram_tensor("lB", [NE * NR, DM], F32, kind="ExternalInput").ap()
    wx = nc.dram_tensor("wx", [DT2, 128, 96], F32, kind="ExternalInput").ap()
    wdt = nc.dram_tensor("wdt", [DTR, DIN], F32, kind="ExternalInput").ap()
    wout = nc.dram_tensor("wout", [DT2, 128, DM], F32, kind="ExternalInput").ap()
    cdiag = nc.dram_tensor("cdiag", [DT2 * 4, 128, 128], F32, kind="ExternalInput").ap()
    alog = nc.dram_tensor("alog", [DT2, 128, NS], F32, kind="ExternalInput").ap()
    vec = nc.dram_tensor("vec", [128, 64], F32, kind="ExternalInput").ap()
    bbrow = nc.dram_tensor("bbrow", [1, DM], F32, kind="ExternalInput").ap()
    rbrow = nc.dram_tensor("rbrow", [1, NE], F32, kind="ExternalInput").ap()
    ident_in = nc.dram_tensor("ident_in", [128, 128], F32, kind="ExternalInput").ap()
    repl_in = nc.dram_tensor("repl_in", [2, 8, NE * NR], F32, kind="ExternalInput").ap()

    yout = nc.dram_tensor("yout", [KT, 128, L], F32, kind="ExternalOutput").ap()

    win_bf = nc.dram_tensor("win_bf", [KT, 128, 2 * DIN], BF16)
    wout_bf = nc.dram_tensor("wout_bf", [DT2, 128, DM], BF16)
    sp_d = nc.dram_tensor("sp_d", [DT2, 128, L], BF16)
    sp_u = nc.dram_tensor("sp_u", [DT2, 128, L], BF16)
    sp_bc = nc.dram_tensor("sp_bc", [32, L], BF16)
    sp_x1 = nc.dram_tensor("sp_x1", [KT, 128, L], F32)
    sp_x2 = nc.dram_tensor("sp_x2", [KT, 128, L], F32)
    sp_g = nc.dram_tensor("sp_g", [DT2, 128, L], BF16)
    sp_du = nc.dram_tensor("sp_du", [DT2, 128, L], BF16)
    gidx = nc.dram_tensor("gidx", [128, 448], mybir.dt.uint32, kind="ExternalInput").ap()
    smask = nc.dram_tensor("smask", [128, 1], F32, kind="ExternalInput").ap()

    b = B()
    ctx = ExitStack()
    abf_t = ctx.enter_context(nc.sbuf_tensor("abf", [128, 59 * 1024], BF16))
    af_t = ctx.enter_context(nc.sbuf_tensor("af32", [128, 22 * 1024], F32))
    ABF = Arena(abf_t, 59 * 1024)
    AFA = Arena(af_t, 22 * 1024)

    gidx_t = ctx.enter_context(nc.sbuf_tensor("gidxs", [128, 448], mybir.dt.uint32))
    ps_mm = [ctx.enter_context(nc.psum_tensor(f"psmm{i}", [128, TC], F32)) for i in range(4)]
    ps_bc0 = ctx.enter_context(nc.psum_tensor("psbc0", [128, TC], F32))
    ps_bc1 = ctx.enter_context(nc.psum_tensor("psbc1", [128, TC], F32))
    ps_small = ctx.enter_context(nc.psum_tensor("pssmall", [128, TC], F32))
    ps_geb = ctx.enter_context(nc.psum_tensor("psgeb", [8, TC], BF16))
    ps_mm_b = [Buf(f"psmm{i}") for i in range(4)]
    ps_bc0_b, ps_bc1_b = Buf("psbc0"), Buf("psbc1")
    ps_small_b = Buf("pssmall")
    ps_geb_b = Buf("psgeb")

    bwT_s = ABF.alloc(KT * DM)
    rwT_s = ABF.alloc(KT * NE)
    lA_s = ABF.alloc(KT * NR)
    lB_s = ABF.alloc(DM, rows=NE * NR)
    wdt_s = ABF.alloc(DIN, rows=DTR)
    ident_s = ABF.alloc(128)
    repl_s = ABF.alloc(2 * NE * NR, rows=8)
    ones_row = ABF.alloc(TC, rows=1)
    onescol = ABF.alloc(1)
    bb_s = ABF.alloc(DM, rows=1)
    rb_s = ABF.alloc(NE, rows=1)

    onesrow_f = AFA.alloc(128, rows=1)
    vec_s = AFA.alloc(64)
    negA_s = AFA.alloc(DT2 * NS)
    posA_s = AFA.alloc(DT2 * NS)
    carry_s = AFA.alloc(DT2 * NS)
    hrun_s = AFA.alloc(DT2 * NS)
    smask_s = AFA.alloc(4)
    idf_s = AFA.alloc(128)
    atmp = AFA.alloc(NS)

    CONSTS = [0.0, 1.0, -1.0, 0.5, -0.5, 2.0, EPS, SM30, 1.0 / DM]
    const_t = AFA.alloc(len(CONSTS))
    cb = Buf("consts")
    for ci, cv in enumerate(CONSTS):
        nc.const_aps.register(F32, cv, const_t[:, ci:ci + 1])

    WB = {k: Buf(k) for k in ("bwT", "rwT", "lA", "lB", "wdt", "ident", "repl",
                              "ones", "bb", "rb", "onesf", "vec", "negA",
                              "posA", "carry", "hrun", "smask", "gidx",
                              "tails", "winbf", "woutbf", "wx",
                              "cdiag", "idf", "atmp")}

    def dma(eng, out, in_, reads=(), writes=(), tag="", dsem=None):
        return b.emit(eng, lambda e, o=out, i=in_: e.dma_start(out=o, in_=i),
                      reads=reads, writes=writes, dma=True, tag=tag, dsem=dsem)

    for k in range(KT):
        dma("g", bwT_s[:, k * DM:(k + 1) * DM], bwT[k], dsem="boot_g")
        dma("g", rwT_s[:, k * NE:(k + 1) * NE], rwT[k], dsem="boot_g")
        dma("g", lA_s[:, k * NR:(k + 1) * NR], lA[k], dsem="boot_g")
    dma("g", lB_s, lB, dsem="boot_g")
    dma("g", wdt_s, wdt, dsem="boot_g")
    dma("g", ident_s, ident_in, dsem="boot_g")
    dma("s", idf_s, ident_in, dsem="boot_s")
    dma("g", repl_s[0:8, 0:64], repl_in[0], dsem="boot_g")
    dma("g", repl_s[0:8, 64:128], repl_in[1], dsem="boot_g")
    dma("g", bb_s, bbrow, dsem="boot_g")
    dma("g", rb_s, rbrow, dsem="boot_g")
    dma("s", vec_s, vec, dsem="boot_s")
    dma("s", gidx_t.ap(), gidx, dsem="boot_s")
    dma("s", smask_s[:, 0:1], smask, dsem="boot_s")

    for ci, cv in enumerate(CONSTS):
        b.emit("v", lambda e, ci=ci, cv=cv: e.memset(const_t[:, ci:ci + 1], cv),
               writes=[cb])
    b.emit("v", lambda e: e.memset(ones_row, 1.0), writes=[WB["ones"]])
    b.emit("v", lambda e: e.memset(onescol, 1.0), writes=[WB["ones"]])
    b.emit("v", lambda e: e.memset(onesrow_f, 1.0), writes=[WB["onesf"]])
    b.emit("v", lambda e: e.memset(carry_s, 0.0), writes=[WB["carry"]])
    b.fence()
    for d in range(DT2):
        slp = posA_s[:, d * NS:(d + 1) * NS]
        sln = negA_s[:, d * NS:(d + 1) * NS]
        dma("s", atmp, alog[d], writes=[WB["atmp"]])
        b.emit("a", lambda e, slp=slp: e.activation(out=slp, in_=atmp, func=AF.Exp),
               reads=[WB["atmp"]], writes=[WB["posA"]])
        b.emit("a", lambda e, slp=slp, sln=sln: e.mul(out=sln, in_=slp, mul=-1.0),
               reads=[WB["posA"]], writes=[WB["negA"]])

    ident8f = idf_s[0:8, 0:8]
    ident_bf = ident_s[0:128, 0:128]

    def vcol(name, j):
        base = dict(conv_b=0, b_dt=16, D=32, gamma=48, beta=56)[name]
        return vec_s[:, base + j:base + j + 1]

    _lnrot = [0]

    def ln_rows(src_tiles, src_bufs, lnrows_sets, lnrows_bs, sq_t, sq_b, xbf_t, xbf_b, tag):
        _lnrot[0] ^= 1
        lnrows = lnrows_sets[_lnrot[0]] if isinstance(lnrows_sets, (list, tuple)) else lnrows_sets
        lnrows_b = lnrows_bs[_lnrot[0]] if isinstance(lnrows_bs, (list, tuple)) else lnrows_bs
        for k in range(KT):
            b.emit("v", lambda e, k=k: e.tensor_copy(out=xbf_t, in_=src_tiles[k]),
                   reads=[src_bufs[k]], writes=[xbf_b], tag=tag + "cast")
            b.emit("a", lambda e, k=k: e.activation(out=sq_t, in_=src_tiles[k],
                   func=AF.Square), reads=[src_bufs[k]], writes=[sq_b], tag=tag + "sq")
            b.emit("p", lambda e, st=(k == 0), last=(k == KT - 1): e.matmul(
                   out=ps_small[0:1, :], lhsT=onescol, rhs=xbf_t, start=st,
                   stop=last),
                   reads=[xbf_b, WB["ones"]] if k == 0 else [xbf_b],
                   war=[ps_small_b],
                   tag=tag + "mmx")
            b.emit("p", lambda e, st=(k == 0), last=(k == KT - 1): e.matmul(
                   out=ps_bc0[0:1, :], lhsT=onescol, rhs=sq_t, start=st,
                   stop=last),
                   reads=[sq_b],
                   war=[ps_bc0_b],
                   tag=tag + "mmq")
        b.close_group("p", ps_small_b, ps_bc0_b)
        mu = lnrows[0:1, 0:TC]
        m2 = lnrows[0:1, TC:2 * TC]
        var = lnrows[0:1, 2 * TC:3 * TC]
        arow = lnrows[0:1, 3 * TC:4 * TC]
        brow = lnrows[0:1, 4 * TC:5 * TC]
        b.emit("v", lambda e: e.tensor_scalar(out=mu, in0=ps_small[0:1, :],
               scalar1=1.0 / DM, scalar2=None, op0=ALU.mult),
               reads=[ps_small_b], writes=[lnrows_b], tag=tag)
        b.emit("v", lambda e: e.tensor_scalar(out=m2, in0=ps_bc0[0:1, :],
               scalar1=1.0 / DM, scalar2=None, op0=ALU.mult),
               reads=[ps_bc0_b], writes=[lnrows_b], tag=tag)
        b.emit("v", lambda e: e.tensor_tensor(out=var, in0=mu, in1=mu, op=ALU.mult),
               reads=[lnrows_b], writes=[lnrows_b], tag=tag)
        b.emit("v", lambda e: e.tensor_tensor(out=var, in0=m2, in1=var,
               op=ALU.subtract), reads=[lnrows_b], writes=[lnrows_b], tag=tag)
        b.emit("a", lambda e: e.activation(out=arow, in_=var, func=AF.Ln, bias=EPS),
               reads=[lnrows_b], writes=[lnrows_b], tag=tag)
        b.emit("a", lambda e: e.activation(out=arow, in_=arow, func=AF.Exp, scale=-0.5),
               reads=[lnrows_b], writes=[lnrows_b], tag=tag)
        b.emit("v", lambda e: e.tensor_tensor(out=brow, in0=mu, in1=arow, op=ALU.mult),
               reads=[lnrows_b], writes=[lnrows_b], tag=tag)
        b.emit("p", lambda e: e.matmul(out=ps_bc0[:], lhsT=onesrow_f, rhs=arow,
               start=True, stop=True),
               reads=[lnrows_b, WB["onesf"]], war=[ps_bc0_b], tag=tag + "bc0")
        b.close_group("p", ps_bc0_b)
        b.emit("p", lambda e: e.matmul(out=ps_bc1[:], lhsT=onesrow_f, rhs=brow,
               start=True, stop=True),
               reads=[lnrows_b], war=[ps_bc1_b], tag=tag + "bc1")
        b.close_group("p", ps_bc1_b)

    def ln_apply(src_tiles, src_bufs, out_tiles, out_bufs, tmp_t, tmp_b, tag):
        for k in range(KT):
            b.emit("v", lambda e, k=k: e.tensor_tensor(out=tmp_t, in0=src_tiles[k],
                   in1=ps_bc0[:], op=ALU.mult),
                   reads=[src_bufs[k], ps_bc0_b], writes=[tmp_b], tag=tag + "a")
            b.emit("v", lambda e, k=k: e.tensor_tensor(out=out_tiles[k], in0=tmp_t,
                   in1=ps_bc1[:], op=ALU.subtract),
                   reads=[tmp_b, ps_bc1_b], writes=[out_bufs[k]], tag=tag + "b")

    def make_ffn_work():
        return dict(
            lgt=(AFA.alloc(TC, rows=8), Buf("lgt")),
            mid=(ABF.alloc(TC, rows=8), Buf("mid")),
            gex=(AFA.alloc(32), Buf("gex")),
            gtb=(ABF.alloc(32), Buf("gtb")),
            ge=(ABF.alloc(TC, rows=8), Buf("ge")),
            grep=(ABF.alloc(TC, rows=64), Buf("grep")),
            wmid=(ABF.alloc(TC, rows=64), Buf("wmid")),
            rows=(AFA.alloc(2), Buf("rows")),
        )

    def ffn(xin_tiles, xin_bufs, res_tiles, res_bufs, out_tiles, out_bufs, W, tag):
        lgt_t, lgt_b = W["lgt"]
        mid_t, mid_b = W["mid"]
        gex_t, gex_b = W["gex"]
        gtb_t, gtb_b = W["gtb"]
        ge_t, ge_b = W["ge"]
        grep_t, grep_b = W["grep"]
        wmid_t, wmid_b = W["wmid"]
        rows_t, rows_b = W["rows"]
        for k in range(KT):
            b.emit("p", lambda e, k=k: e.matmul(out=ps_small[0:8, :],
                   lhsT=rwT_s[:, k * NE:(k + 1) * NE], rhs=xin_tiles[k],
                   start=(k == 0), stop=False),
                   reads=[xin_bufs[k]],
                   war=[ps_small_b],
                   tag=tag + "rout")
        b.emit("p", lambda e: e.matmul(out=ps_small[0:8, :], lhsT=rb_s,
               rhs=ones_row, start=False, stop=True),
               reads=[WB["rb"], WB["ones"]], tag=tag)
        for k in range(KT):
            b.emit("p", lambda e, k=k: e.matmul(out=ps_small[32:40, :],
                   lhsT=lA_s[:, k * NR:(k + 1) * NR], rhs=xin_tiles[k],
                   start=(k == 0), stop=(k == KT - 1), tile_position=(0, 32)),
                   reads=[xin_bufs[k]],
                   war=[ps_small_b],
                   tag=tag + "mid")
        b.close_group("p", ps_small_b)
        b.emit("v", lambda e: e.tensor_copy(out=lgt_t, in_=ps_small[0:8, :]),
               reads=[ps_small_b], writes=[lgt_b], tag=tag + "lgt")
        b.emit("v", lambda e: e.tensor_copy(out=mid_t, in_=ps_small[32:40, :]),
               reads=[ps_small_b], writes=[mid_b], tag=tag + "midc")
        nj = TC // 128
        for j in range(nj):
            b.emit("p", lambda e, j=j: e.transpose(
                   out=ps_small[:, j * 8:(j + 1) * 8],
                   in_=lgt_t[0:8, j * 128:(j + 1) * 128], identity=ident8f),
                   reads=[lgt_b, WB["idf"]] if j == 0 else [lgt_b],
                   war=[ps_small_b],
                   tag=tag + "tr")
        b.close_group("p", ps_small_b)
        mxr = rows_t[0:128, 0:1]
        smr = rows_t[0:128, 1:2]
        for j in range(nj):
            sl = ps_small[:, j * 8:(j + 1) * 8]
            b.emit("v", lambda e, sl=sl: e.reduce_max(out=mxr, in_=sl, axis=AX.X),
                   reads=[ps_small_b], writes=[rows_b], tag=tag + "mx")
            b.emit("v", lambda e: e.tensor_scalar(out=mxr, in0=mxr, scalar1=-1.0,
                   scalar2=None, op0=ALU.mult), reads=[rows_b], writes=[rows_b],
                   tag=tag)
            b.emit("a", lambda e, sl=sl, j=j: e.activation(
                   out=gex_t[:, j * 8:(j + 1) * 8], in_=sl, func=AF.Exp, bias=mxr),
                   reads=[ps_small_b, rows_b], writes=[gex_b], tag=tag + "exp")
            b.emit("v", lambda e, j=j: e.reduce_sum(out=smr,
                   in_=gex_t[:, j * 8:(j + 1) * 8], axis=AX.X),
                   reads=[gex_b], writes=[rows_b], tag=tag + "sum")
            b.emit("v", lambda e: e.reciprocal(out=smr, in_=smr),
                   reads=[rows_b], writes=[rows_b], tag=tag + "rcp")
            b.emit("v", lambda e, j=j: e.tensor_scalar(
                   out=gtb_t[:, j * 8:(j + 1) * 8],
                   in0=gex_t[:, j * 8:(j + 1) * 8], scalar1=smr, scalar2=None,
                   op0=ALU.mult), reads=[gex_b, rows_b], writes=[gtb_b],
                   tag=tag + "gmul")
        for j in range(nj):
            b.emit("p", lambda e, j=j: e.transpose(
                   out=ps_geb[0:8, j * 128:(j + 1) * 128],
                   in_=gtb_t[:, j * 8:(j + 1) * 8], identity=ident_bf),
                   reads=[gtb_b, WB["ident"]] if j == 0 else [gtb_b],
                   war=[ps_geb_b],
                   tag=tag + "trb")
        b.close_group("p", ps_geb_b)
        b.emit("v", lambda e: e.tensor_copy(out=ge_t, in_=ps_geb[0:8, :]),
               reads=[ps_geb_b], writes=[ge_b], tag=tag + "gec")
        b.emit("p", lambda e: e.matmul(out=ps_bc0[0:64, :],
               lhsT=repl_s[0:8, 64:128], rhs=mid_t, start=True, stop=True),
               reads=[mid_b, WB["repl"]], war=[ps_bc0_b], tag=tag + "repm")
        b.close_group("p", ps_bc0_b)
        b.emit("p", lambda e: e.matmul(out=ps_bc1[0:64, :],
               lhsT=repl_s[0:8, 0:64], rhs=ge_t, start=True, stop=True),
               reads=[ge_b], war=[ps_bc1_b], tag=tag + "repg")
        b.close_group("p", ps_bc1_b)
        b.emit("v", lambda e: e.tensor_copy(out=grep_t, in_=ps_bc1[0:64, :]),
               reads=[ps_bc1_b], writes=[grep_b], tag=tag + "grc")
        b.emit("v", lambda e: e.scalar_tensor_tensor(out=wmid_t,
               in0=ps_bc0[0:64, :], scalar=2.0, in1=grep_t, op0=ALU.mult,
               op1=ALU.mult), reads=[ps_bc0_b, grep_b], writes=[wmid_b],
               tag=tag + "wmid")
        for g0 in range(0, KT, 4):
            for m in range(g0, g0 + 4):
                ps, psb = ps_mm[m % 4], ps_mm_b[m % 4]
                for k in range(KT):
                    b.emit("p", lambda e, m=m, k=k, ps=ps: e.matmul(out=ps[:],
                           lhsT=bwT_s[:, k * DM + m * 128:k * DM + (m + 1) * 128],
                           rhs=xin_tiles[k], start=(k == 0), stop=False),
                           reads=[xin_bufs[k]],
                           war=[psb],
                           tag=tag + f"bs{m}")
                b.emit("p", lambda e, m=m, ps=ps: e.matmul(out=ps[:],
                       lhsT=bb_s[0:1, m * 128:(m + 1) * 128], rhs=ones_row,
                       start=False, stop=False), reads=[], war=[psb], tag=tag)
                b.emit("p", lambda e, m=m, ps=ps: e.matmul(out=ps[:],
                       lhsT=lB_s[:, m * 128:(m + 1) * 128], rhs=wmid_t,
                       start=False, stop=True), reads=[wmid_b], war=[psb],
                       tag=tag + "lora")
                b.close_group("p", psb)
            for m in range(g0, g0 + 4):
                ps, psb = ps_mm[m % 4], ps_mm_b[m % 4]
                b.emit("v", lambda e, m=m, ps=ps: e.scalar_tensor_tensor(
                       out=out_tiles[m], in0=ps[:], scalar=0.5, in1=res_tiles[m],
                       op0=ALU.mult, op1=ALU.add),
                       reads=[psb, res_bufs[m]], writes=[out_bufs[m]],
                       tag=tag + "ev")

    ffnw1 = make_ffn_work()

    ABF.mark(); AFA.mark()
    wx_s = ABF.alloc(DT2 * 96)
    cdiag_s = ABF.alloc(64 * 128)
    tails_s = ABF.alloc(DT2 * 4)
    for d in range(DT2):
        dma("g", wx_s[:, d * 96:(d + 1) * 96], wx[d], dsem="boot_g")
    for t in range(64):
        dma("g", cdiag_s[:, t * 128:(t + 1) * 128], cdiag[t],
            dsem="boot_g")
    b.emit("v", lambda e: e.memset(tails_s, 0.0), writes=[WB["tails"]])
    ABF.mark()
    stage = ABF.alloc(2 * DIN)
    stage_b = Buf("stage")
    for k in range(KT):
        dma("g", stage, win[k], writes=[stage_b])
        dma("s", win_bf.ap()[k], stage, reads=[stage_b], writes=[WB["winbf"]])
    for d in range(DT2):
        dma("g", stage[:, 0:DM], wout[d], writes=[stage_b])
        dma("s", wout_bf.ap()[d], stage[:, 0:DM], reads=[stage_b],
            writes=[WB["woutbf"]])
    ABF.reset()
    b.barrier()

    xf_t = [AFA.alloc(TC) for _ in range(KT)]
    xf_b = [Buf(f"xf{k}") for k in range(KT)]
    xbf_t = ABF.alloc(TC)
    xbf_b = Buf("xbf")
    sq_t = ABF.alloc(TC)
    sq_b = Buf("sq")
    lnrows_t = [AFA.alloc(5 * TC, rows=1) for _ in range(2)]
    lnrows_b = [Buf(f"lnrows{i}") for i in range(2)]
    tmp_t = AFA.alloc(TC)
    tmp_b = Buf("tmp")
    xln1_t = [ABF.alloc(TC) for _ in range(KT)]
    xln1_b = [Buf(f"xln1{k}") for k in range(KT)]
    x1_t = [AFA.alloc(TC) for _ in range(KT)]
    x1_b = [Buf(f"x1{k}") for k in range(KT)]
    xln2f_t = AFA.alloc(TC)
    xln2f_b = Buf("xln2f")
    xln2_t = [ABF.alloc(TC) for _ in range(KT)]
    xln2_b = [Buf(f"xln2{k}") for k in range(KT)]

    wg_t = [ABF.alloc(KT * 512) for _ in range(2)]
    wg_b = [Buf(f"wg{i}") for i in range(2)]
    xin_t = [ABF.alloc(TC + 4) for _ in range(2)]
    xin_b = [Buf(f"xin{i}") for i in range(2)]
    u_t = [ABF.alloc(TC) for _ in range(DT2)]
    u_b = [Buf(f"u{d}") for d in range(DT2)]
    gz_t = ABF.alloc(TC)
    gz_b = Buf("gz")
    dtb_t = ABF.alloc(TC, rows=64)
    dtb_b = Buf("dtb")
    bct_t = ABF.alloc(TC, rows=32)
    bct_b = Buf("bct")
    cbuf_t = AFA.alloc(TC)
    cbuf_b = Buf("cbuf")
    ebuf_t = AFA.alloc(TC)
    ebuf_b = Buf("ebuf")
    lbuf_t = AFA.alloc(TC)
    lbuf_b = Buf("lbuf")
    dl_t = ABF.alloc(TC)
    dl_b = Buf("dl")
    du2_t = ABF.alloc(TC)
    du2_b = Buf("du2")
    _spd = Buf("spd"); sb_spd = [_spd] * DT2
    _spdu = Buf("spdu"); sb_spdu = [_spdu] * DT2
    _spu = Buf("spu"); sb_spu = [_spu] * DT2
    _spg = Buf("spg"); sb_spg = [_spg] * DT2
    sb_spbc = Buf("spbc")
    _spx1 = Buf("spx1"); sb_spx1 = [_spx1] * KT
    _spx2 = Buf("spx2"); sb_spx2 = [_spx2] * KT

    for c in range(NCH):
        cs = slice(c * TC, (c + 1) * TC)
        for k in range(KT):
            dma("s", xf_t[k], xT[k][:, cs], writes=[xf_b[k]], tag=f"ldx{c}")
        ln_rows(xf_t, xf_b, lnrows_t, lnrows_b, sq_t, sq_b, xbf_t, xbf_b, f"L1{c}")
        ln_apply(xf_t, xf_b, xln1_t, xln1_b, tmp_t, tmp_b, f"L1a{c}")
        ffn(xln1_t, xln1_b, xf_t, xf_b, x1_t, x1_b, ffnw1, f"F1{c}")
        for k in range(KT):
            dma("s", sp_x1.ap()[k][:, cs], x1_t[k], reads=[x1_b[k]], writes=[sb_spx1[k]], tag=f"sx1{c}")
        ln_rows(x1_t, x1_b, lnrows_t, lnrows_b, sq_t, sq_b, xbf_t, xbf_b, f"L2{c}")
        for k in range(KT):
            b.emit("v", lambda e, k=k: e.tensor_tensor(out=tmp_t, in0=x1_t[k],
                   in1=ps_bc0[:], op=ALU.mult),
                   reads=[x1_b[k], ps_bc0_b], writes=[tmp_b], tag=f"L2a{c}")
            b.emit("v", lambda e: e.tensor_tensor(out=xln2f_t, in0=tmp_t,
                   in1=ps_bc1[:], op=ALU.subtract),
                   reads=[tmp_b, ps_bc1_b], writes=[xln2f_b], tag=f"L2b{c}")
            b.emit("v", lambda e, k=k: e.tensor_copy(out=xln2_t[k], in_=xln2f_t),
                   reads=[xln2f_b], writes=[xln2_b[k]], tag=f"L2c{c}")
            dma("s", sp_x2.ap()[k][:, cs], xln2f_t, reads=[xln2f_b], writes=[sb_spx2[k]], tag=f"sx2{c}")
        for g0 in range(0, 2 * DT2, 4):
            wgi = (g0 // 4) % 2
            for k in range(KT):
                dma("s" if k % 2 == 0 else "a", wg_t[wgi][:, k * 512:(k + 1) * 512],
                    win_bf.ap()[k][:, g0 * 128:(g0 + 4) * 128],
                    reads=[WB["winbf"]], writes=[wg_b[wgi]], tag=f"lw{c}")
            for mi in range(4):
                m = g0 + mi
                ps, psb = ps_mm[mi % 2], ps_mm_b[mi % 2]
                for k in range(KT):
                    b.emit("p", lambda e, k=k, mi=mi, wgi=wgi, ps=ps:
                           e.matmul(out=ps[:],
                                    lhsT=wg_t[wgi][:, k * 512 + mi * 128:k * 512 + (mi + 1) * 128],
                                    rhs=xln2_t[k], start=(k == 0), stop=(k == KT - 1)),
                           reads=[xln2_b[k], wg_b[wgi]],
                           war=[psb],
                           tag=f"wi{c}m{m}")
                b.close_group("p", psb)
                if m < DT2:
                    xt, xtb = xin_t[m % 2], xin_b[m % 2]
                    b.emit("v", lambda e, xt=xt, ps=ps: e.tensor_copy(
                           out=xt[:, 3:3 + TC], in_=ps[:]),
                           reads=[psb], writes=[xtb], tag=f"xc{c}")
                    b.emit("v", lambda e, xt=xt, m=m: e.tensor_copy(
                           out=xt[:, 0:3], in_=tails_s[:, m * 4:m * 4 + 3]),
                           reads=[WB["tails"]], writes=[xtb], tag=f"tl{c}")
                    psc, pscb = ps_mm[(mi + 2) % 4], ps_mm_b[(mi + 2) % 4]
                    for tap in range(4):
                        b.emit("p", lambda e, m=m, tap=tap, xt=xt, psc=psc:
                               e.matmul(out=psc[:],
                                        lhsT=cdiag_s[:, (m * 4 + tap) * 128:(m * 4 + tap + 1) * 128],
                                        rhs=xt[:, tap:tap + TC],
                                        start=(tap == 0), stop=(tap == 3)),
                               reads=[xtb],
                               war=[pscb],
                               tag=f"cv{c}m{m}")
                    b.close_group("p", pscb)
                    b.emit("v", lambda e, xt=xt, m=m: e.tensor_copy(
                           out=tails_s[:, m * 4:m * 4 + 3], in_=xt[:, TC:TC + 3]),
                           reads=[xtb], writes=[WB["tails"]], tag=f"tu{c}")
                    b.emit("a", lambda e, m=m, psc=psc: e.activation(
                           out=u_t[m], in_=psc[:], func=AF.Silu,
                           bias=vcol("conv_b", m)),
                           reads=[pscb, WB["vec"]], writes=[u_b[m]], tag=f"si{c}")
                    dma("s", sp_u.ap()[m][:, cs], u_t[m], reads=[u_b[m]],
                        writes=[sb_spu[m]], tag=f"su{c}")
                else:
                    dz = m - DT2
                    b.emit("a", lambda e, ps=ps: e.activation(out=gz_t, in_=ps[:],
                           func=AF.Silu), reads=[psb], writes=[gz_b], tag=f"sz{c}")
                    dma("s", sp_g.ap()[dz][:, cs], gz_t, reads=[gz_b], writes=[sb_spg[dz]], tag=f"sg{c}")
        for d in range(DT2):
            b.emit("p", lambda e, d=d: e.matmul(out=ps_small[0:96, :],
                   lhsT=wx_s[:, d * 96:(d + 1) * 96], rhs=u_t[d],
                   start=(d == 0), stop=(d == DT2 - 1)),
                   reads=[u_b[d]],
                   war=[ps_small_b],
                   tag=f"wx{c}")
        b.close_group("p", ps_small_b)
        b.emit("v", lambda e: e.tensor_copy(out=dtb_t, in_=ps_small[0:64, :]),
               reads=[ps_small_b], writes=[dtb_b], tag=f"dtb{c}")
        b.emit("v", lambda e: e.tensor_copy(out=bct_t, in_=ps_small[64:96, :]),
               reads=[ps_small_b], writes=[bct_b], tag=f"bct{c}")
        dma("s", sp_bc.ap()[:, cs], bct_t, reads=[bct_b], writes=[sb_spbc], tag=f"sbc{c}")
        for g0 in range(0, DT2, 4):
            for di in range(4):
                d = g0 + di
                ps, psb = ps_mm[di], ps_mm_b[di]
                b.emit("p", lambda e, d=d, ps=ps: e.matmul(out=ps[:],
                       lhsT=wdt_s[:, d * 128:(d + 1) * 128], rhs=dtb_t,
                       start=True, stop=True),
                       reads=[dtb_b], war=[psb], tag=f"wd{c}")
                b.close_group("p", psb)
            for di in range(4):
                d = g0 + di
                ps, psb = ps_mm[di], ps_mm_b[di]
                b.emit("v", lambda e, d=d, ps=ps: e.tensor_scalar(out=cbuf_t,
                       in0=ps[:], scalar1=vcol("b_dt", d), scalar2=SM30,
                       op0=ALU.add, op1=ALU.min),
                       reads=[psb, WB["vec"]], writes=[cbuf_b], tag=f"dm{c}")
                b.emit("a", lambda e: e.activation(out=ebuf_t, in_=cbuf_t,
                       func=AF.Exp), reads=[cbuf_b], writes=[ebuf_b], tag=f"de{c}")
                b.emit("a", lambda e: e.activation(out=lbuf_t, in_=ebuf_t,
                       func=AF.Ln, bias=1.0), reads=[ebuf_b], writes=[lbuf_b],
                       tag=f"dl{c}")
                b.emit("v", lambda e, d=d, ps=ps: e.scalar_tensor_tensor(
                       out=dl_t, in0=ps[:], scalar=vcol("b_dt", d), in1=lbuf_t,
                       op0=ALU.add, op1=ALU.max),
                       reads=[psb, lbuf_b, WB["vec"]], writes=[dl_b], tag=f"dx{c}")
                dma("s", sp_d.ap()[d][:, cs], dl_t, reads=[dl_b], writes=[sb_spd[d]], tag=f"sd{c}")
                b.emit("v", lambda e, d=d: e.tensor_tensor(out=du2_t, in0=dl_t,
                       in1=u_t[d], op=ALU.mult),
                       reads=[dl_b, u_b[d]], writes=[du2_b], tag=f"duz{c}")
                dma("a", sp_du.ap()[d][:, cs], du2_t, reads=[du2_b], writes=[sb_spdu[d]], tag=f"sdu{c}")

    ABF.reset(); AFA.reset()
    b.barrier()

    ABF.mark(); AFA.mark()
    NSP = L // 1024
    NCH2 = NCH // 2
    v_d = sp_d.ap().rearrange("d p (s n) -> (d p s) n", n=TC)
    v_du = sp_du.ap().rearrange("d p (s n) -> (d p s) n", n=TC)
    v_u = sp_u.ap().rearrange("d p (s n) -> (d p s) n", n=TC)
    v_g = sp_g.ap().rearrange("d p (s n) -> (d p s) n", n=TC)
    v_bc = sp_bc.ap().rearrange("r (c n) -> (r c) n", n=TC)
    v_x1 = sp_x1.ap().rearrange("k p (c n) -> (k p c) n", n=TC)
    v_x2 = sp_x2.ap().rearrange("k p (c n) -> (k p c) n", n=TC)
    v_yo = yout.rearrange("k p (c n) -> (k p c) n", n=TC)

    def gdt(dt, j):
        c0 = dt * 8 + j
        return gidx_t.ap()[:, c0:c0 + 1]

    def gbc(r, j):
        c0 = 128 + r * 8 + j
        return gidx_t.ap()[:, c0:c0 + 1]

    def gk(k, j):
        c0 = 384 + k * 8 + j
        return gidx_t.ap()[:, c0:c0 + 1]

    def ig(out, view, idx, reads=(), writes=(), tag=""):
        return b.emit("g", lambda e, o=out, v=view, i=idx: e.indirect_dma_start(
            out=o, out_offset=None, in_=v,
            in_offset=bass.IndirectOffsetOnAxis(ap=i, axis=0)),
            reads=reads, writes=writes, dma=True, tag=tag)

    def isc(view, idx, in_, reads=(), writes=(), tag=""):
        return b.emit("g", lambda e, v=view, i=idx, s=in_: e.indirect_dma_start(
            out=v, out_offset=bass.IndirectOffsetOnAxis(ap=i, axis=0),
            in_=s, in_offset=None),
            reads=reads, writes=writes, dma=True, tag=tag)

    Bbc_t = [ABF.alloc(TC) for _ in range(NS)]
    Bbc_b = [Buf(f"Bb{s}") for s in range(NS)]
    Cbc_t = [ABF.alloc(TC) for _ in range(NS)]
    Cbc_b = [Buf(f"Cb{s}") for s in range(NS)]
    bchain = [Buf(f"bch{i}") for i in range(4)]
    cchain = [Buf(f"cch{i}") for i in range(4)]
    dld_t = [ABF.alloc(TC) for _ in range(2)]
    dld_b = [Buf(f"dld{i}") for i in range(2)]
    duld_t = [ABF.alloc(TC) for _ in range(2)]
    duld_b = [Buf(f"duld{i}") for i in range(2)]
    _u1 = ABF.alloc(TC); uld_t = [_u1, _u1]
    _u1b = Buf("uld"); uld_b = [_u1b, _u1b]
    _g1 = ABF.alloc(TC); gld_t = [_g1, _g1]
    _g1b = Buf("gld"); gld_b = [_g1b, _g1b]
    wbuf_t = [ABF.alloc(TC) for _ in range(2)]
    wbuf_b = [Buf(f"wb{i}") for i in range(2)]
    scr_t = AFA.alloc(TC)
    scr_b = Buf("scr")
    dA_t = [AFA.alloc(TC) for _ in range(4)]
    dA_b = [Buf(f"dA{i}") for i in range(4)]
    dBu_t = [ABF.alloc(TC) for _ in range(4)]
    dBu_b = [Buf(f"dBu{i}") for i in range(4)]
    h_t = [ABF.alloc(TC) for _ in range(4)]
    h_b = [Buf(f"h{i}") for i in range(4)]
    p_t = [ABF.alloc(TC) for _ in range(4)]
    p_b = [Buf(f"p{s}") for s in range(4)]
    ym_t = [ABF.alloc(TC) for _ in range(DT2)]
    ym_b = [Buf(f"ym{d}") for d in range(DT2)]
    cdl_t = AFA.alloc(TC)
    cdl_b = Buf("cdl")
    rb_t2 = AFA.alloc(TC)
    rb_b2 = Buf("rb2")
    gam_t = AFA.alloc(NS)
    gam_b = Buf("gam")
    part_t = AFA.alloc(NS)
    part_b = Buf("part")
    t1_t = AFA.alloc(TC)
    t1_b = Buf("t1")
    wo2_t = [ABF.alloc(DT2 * 256) for _ in range(2)]
    wo2_b = [Buf(f"wo2{i}") for i in range(2)]
    x1l_t = [AFA.alloc(TC) for _ in range(2)]
    x1l_b = [Buf(f"x1l{i}") for i in range(2)]
    xl2l_t = [AFA.alloc(TC) for _ in range(2)]
    xl2l_b = [Buf(f"xl2l{i}") for i in range(2)]
    h3_t = [AFA.alloc(TC) for _ in range(KT)]
    h3_b = [Buf(f"h3{k}") for k in range(KT)]
    x2_t = [AFA.alloc(TC) for _ in range(KT)]
    x2_b = [Buf(f"x2{k}") for k in range(KT)]
    xln4_t = [ym_t[k] for k in range(KT)]
    xln4_b = [ym_b[k] for k in range(KT)]
    outst_t = AFA.alloc(TC)
    outst_b = Buf("outst")
    xbf2_t = ABF.alloc(TC)
    xbf2_b = Buf("xbf2")
    sq2_t = ABF.alloc(TC)
    sq2_b = Buf("sq2")
    lnrows2_t = [AFA.alloc(5 * TC, rows=1) for _ in range(2)]
    lnrows2_b = [Buf(f"lnrows2{i}") for i in range(2)]
    tmp2_t = AFA.alloc(TC)
    tmp2_b = Buf("tmp2")
    ffnw2 = ffnw1
    sb_yo = [Buf(f"yo{k}") for k in range(4)] * 2

    b.emit("v", lambda e: e.memset(hrun_s, 0.0), writes=[WB["hrun"]])

    for c in range(NCH2):
        cs = slice(c * TC, (c + 1) * TC)
        for s in range(NS):
            q = "s" if s % 2 == 0 else "a"
            dma(q, Bbc_t[s], sp_bc.ap()[s, cs].partition_broadcast(128),
                writes=[bchain[s % 4], Bbc_b[s]], tag=f"tBb{c}")
        for dt in range(DT2):
            sl = dt % 2
            dma("s", dld_t[sl], sp_d.ap()[dt][:, cs],
                writes=[dld_b[sl]], tag=f"tld{c}")
            dma("a", duld_t[sl], sp_du.ap()[dt][:, cs],
                writes=[duld_b[sl]], tag=f"tlu{c}")
            b.emit("v", lambda e, sl=sl: e.tensor_tensor_scan(
                   out=cdl_t, data0=dld_t[sl], data1=dld_t[sl],
                   initial=0.0, op0=ALU.add, op1=ALU.bypass),
                   reads=[dld_b[sl]], writes=[cdl_b], tag=f"tcs{c}")
            b.emit("v", lambda e: e.tensor_scalar(out=rb_t2, in0=cdl_t,
                   scalar1=cdl_t[:, TC - 1:TC], scalar2=None, op0=ALU.subtract),
                   reads=[cdl_b], writes=[rb_b2], tag=f"tr{c}")
            b.emit("v", lambda e, dt=dt: e.tensor_scalar(
                   out=gam_t, in0=posA_s[:, dt * NS:(dt + 1) * NS],
                   scalar1=cdl_t[:, TC - 1:TC], scalar2=-1.0,
                   op0=ALU.mult, op1=ALU.mult),
                   reads=[cdl_b, WB["posA"]], writes=[gam_b], tag=f"tg{c}")
            b.emit("a", lambda e: e.activation(out=gam_t, in_=gam_t, func=AF.Exp),
                   reads=[gam_b], writes=[gam_b], tag=f"tge{c}")
            hsl = hrun_s[:, dt * NS:(dt + 1) * NS]
            b.emit("v", lambda e, hsl=hsl: e.tensor_tensor(out=hsl, in0=hsl,
                   in1=gam_t, op=ALU.mult),
                   reads=[gam_b, WB["hrun"]], writes=[WB["hrun"]], tag=f"thg{c}")
            for s in range(NS):
                r2 = s % 4
                b.emit("a", lambda e, dt=dt, s=s, r2=r2 % 2: e.activation(
                       out=wbuf_t[r2 % 2], in_=rb_t2, func=AF.Exp,
                       scale=posA_s[:, dt * NS + s:dt * NS + s + 1]),
                       reads=[rb_b2, WB["posA"]], writes=[wbuf_b[r2 % 2]], tag=f"tw{c}")
                b.emit("g", lambda e, s=s, sl=sl, r2=r2: e.tensor_tensor(
                       out=dBu_t[r2], in0=duld_t[sl], in1=Bbc_t[s],
                       op=ALU.mult),
                       reads=[duld_b[sl], Bbc_b[s]], writes=[dBu_b[r2]],
                       tag=f"tdb{c}")
                b.emit("v", lambda e, s=s, r2=r2: e.scalar_tensor_tensor(
                       out=scr_t, in0=wbuf_t[r2 % 2], scalar=1.0, in1=dBu_t[r2],
                       op0=ALU.mult, op1=ALU.mult,
                       accum_out=part_t[:, s:s + 1]),
                       reads=[wbuf_b[r2 % 2], dBu_b[r2]],
                       writes=[scr_b, part_b], tag=f"ttr{c}")
            hsl2 = hrun_s[:, dt * NS:(dt + 1) * NS]
            b.emit("v", lambda e, hsl2=hsl2: e.tensor_tensor(out=hsl2,
                   in0=hsl2, in1=part_t, op=ALU.add),
                   reads=[part_b, WB["hrun"]], writes=[WB["hrun"]],
                   tag=f"tha{c}")
    b.emit("v", lambda e: e.tensor_scalar(out=carry_s, in0=hrun_s,
           scalar1=smask_s[:, 0:1], scalar2=None, op0=ALU.mult),
           reads=[WB["hrun"], WB["smask"]], writes=[WB["carry"]], tag="cinit")

    for j in range(NCH2):
        for s in range(NS):
            ig(Bbc_t[s], v_bc, gbc(s, j),
               writes=[bchain[s % 4], Bbc_b[s]], tag=f"sBb{j}")
            ig(Cbc_t[s], v_bc, gbc(16 + s, j),
               writes=[cchain[s % 4], Cbc_b[s]], tag=f"sCb{j}")
        if True:
            for dt in range(DT2):
                sl = dt % 2
                psy, psyb = (ps_bc0, ps_bc0_b) if dt % 2 == 0 else (ps_bc1, ps_bc1_b)
                ig(dld_t[sl], v_d, gdt(dt, j), writes=[dld_b[sl]], tag=f"gd{j}")
                ig(duld_t[sl], v_du, gdt(dt, j), writes=[duld_b[sl]], tag=f"gdu{j}")
                ig(uld_t[sl], v_u, gdt(dt, j), writes=[uld_b[sl]], tag=f"gu{j}")
                ig(gld_t[sl], v_g, gdt(dt, j), writes=[gld_b[sl]], tag=f"gg{j}")
                for s in range(NS):
                    r2 = s % 4
                    b.emit("a", lambda e, dt=dt, s=s, sl=sl, r2=r2:
                           e.activation(out=dA_t[r2], in_=dld_t[sl],
                                        func=AF.Exp,
                                        scale=negA_s[:, dt * NS + s:dt * NS + s + 1]),
                           reads=[dld_b[sl], WB["negA"]], writes=[dA_b[r2]],
                           tag=f"dA{j}")
                    b.emit("g", lambda e, s=s, sl=sl, r2=r2: e.tensor_tensor(
                           out=dBu_t[r2], in0=duld_t[sl], in1=Bbc_t[s],
                           op=ALU.mult),
                           reads=[duld_b[sl], Bbc_b[s]], writes=[dBu_b[r2]],
                           tag=f"dB{j}")
                    b.emit("v", lambda e, dt=dt, s=s, r2=r2: e.tensor_tensor_scan(
                           out=h_t[r2], data0=dA_t[r2], data1=dBu_t[r2],
                           initial=carry_s[:, dt * NS + s:dt * NS + s + 1],
                           op0=ALU.mult, op1=ALU.add),
                           reads=[dA_b[r2], dBu_b[r2], WB["carry"]],
                           writes=[h_b[r2]], tag=f"ts{j}")
                    b.emit("v", lambda e, dt=dt, s=s, r2=r2: e.tensor_copy(
                           out=carry_s[:, dt * NS + s:dt * NS + s + 1],
                           in_=h_t[r2][:, TC - 1:TC]),
                           reads=[h_b[r2]], writes=[WB["carry"]], tag=f"cr{j}")
                    b.emit("v", lambda e, s=s, r2=r2: e.tensor_tensor(
                           out=p_t[s % 4],
                           in0=h_t[r2], in1=Cbc_t[s], op=ALU.mult),
                           reads=[h_b[r2], Cbc_b[s]], writes=[p_b[s % 4]],
                           tag=f"cm{j}")
                    b.emit("p", lambda e, s=s, psy=psy: e.matmul(out=psy[:],
                           lhsT=ident_bf, rhs=p_t[s % 4], start=(s == 0),
                           stop=(s == NS - 1)),
                           reads=[p_b[s % 4]],
                           war=[psyb],
                           tag=f"sm{j}")
                b.close_group("p", psyb)
                b.emit("v", lambda e, dt=dt, sl=sl, psy=psy:
                       e.scalar_tensor_tensor(
                       out=t1_t, in0=uld_t[sl], scalar=vcol("D", dt),
                       in1=psy[:], op0=ALU.mult, op1=ALU.add),
                       reads=[uld_b[sl], psyb, WB["vec"]], writes=[t1_b],
                       tag=f"y1{j}")
                b.emit("v", lambda e, dt=dt, sl=sl: e.tensor_tensor(
                       out=ym_t[dt], in0=t1_t, in1=gld_t[sl], op=ALU.mult),
                       reads=[t1_b, gld_b[sl]], writes=[ym_b[dt]], tag=f"y2{j}")
            for m in range(KT):
                wgi = m % 2
                if m % 2 == 0:
                    for d in range(DT2):
                        dma("s" if d % 2 == 0 else "a",
                            wo2_t[(m // 2) % 2][:, d * 256:(d + 1) * 256],
                            wout_bf.ap()[d][:, m * 128:(m + 2) * 128],
                            reads=[WB["woutbf"]], writes=[wo2_b[(m // 2) % 2]],
                            tag=f"lwo{j}")
                ps, psb = ps_mm[m % 2], ps_mm_b[m % 2]
                for d in range(DT2):
                    b.emit("p", lambda e, d=d, m=m, ps=ps:
                           e.matmul(out=ps[:],
                                    lhsT=wo2_t[(m // 2) % 2][:, d * 256 + (m % 2) * 128:d * 256 + (m % 2 + 1) * 128],
                                    rhs=ym_t[d], start=(d == 0),
                                    stop=(d == DT2 - 1)),
                           reads=[ym_b[d], wo2_b[(m // 2) % 2]],
                           war=[psb],
                           tag=f"wo{j}m{m}")
                b.close_group("p", psb)
                mi = m % 2
                ig(xl2l_t[mi], v_x2, gk(m, j), writes=[xl2l_b[mi]], tag=f"lx2{j}")
                b.emit("v", lambda e, m=m, mi=mi, ps=ps: e.tensor_tensor(
                       out=h3_t[m], in0=ps[:], in1=xl2l_t[mi], op=ALU.add),
                       reads=[psb, xl2l_b[mi]], writes=[h3_b[m]], tag=f"h3{j}")
            ln_rows(h3_t, h3_b, lnrows2_t, lnrows2_b, sq2_t, sq2_b, xbf2_t,
                    xbf2_b, f"L3{j}")
            for k in range(KT):
                kl = k % 2
                ig(x1l_t[kl], v_x1, gk(k, j), writes=[x1l_b[kl]], tag=f"lx1{j}")
                b.emit("v", lambda e, k=k: e.tensor_tensor(out=tmp2_t,
                       in0=h3_t[k], in1=ps_bc0[:], op=ALU.mult),
                       reads=[h3_b[k], ps_bc0_b], writes=[tmp2_b], tag=f"L3a{j}")
                b.emit("v", lambda e: e.tensor_tensor(out=tmp2_t, in0=tmp2_t,
                       in1=ps_bc1[:], op=ALU.subtract),
                       reads=[tmp2_b, ps_bc1_b], writes=[tmp2_b], tag=f"L3b{j}")
                b.emit("v", lambda e, k=k, kl=kl: e.scalar_tensor_tensor(
                       out=x2_t[k], in0=tmp2_t, scalar=vcol("gamma", k),
                       in1=x1l_t[kl], op0=ALU.mult, op1=ALU.add),
                       reads=[tmp2_b, x1l_b[kl], WB["vec"]], writes=[x2_b[k]],
                       tag=f"L3g{j}")
                b.emit("v", lambda e, k=k: e.tensor_scalar(out=x2_t[k],
                       in0=x2_t[k], scalar1=vcol("beta", k), scalar2=None,
                       op0=ALU.add),
                       reads=[x2_b[k], WB["vec"]], writes=[x2_b[k]], tag=f"L3bt{j}")
            ln_rows(x2_t, x2_b, lnrows2_t, lnrows2_b, sq2_t, sq2_b, xbf2_t,
                    xbf2_b, f"L4{j}")
            ln_apply(x2_t, x2_b, xln4_t, xln4_b, tmp2_t, tmp2_b, f"L4a{j}")
            ffn(xln4_t, xln4_b, x2_t, x2_b, x2_t, x2_b, ffnw2, f"F2{j}")
            for k in range(KT):
                isc(v_yo, gk(k, j), x2_t[k], reads=[x2_b[k]],
                    writes=[sb_yo[k]], tag=f"out{j}")

    ABF.reset(); AFA.reset()
    b.fence()
    b.emit("g", lambda e: e.nop(), wait_all_dma=True, tag="final")
    order = b.schedule()
    b.lower(order)
    b.finalize(order)

    with ctx:
        with ExitStack() as sctx:
            sems = {}
            for n in b.all_sems:
                sems[n] = sctx.enter_context(nc.semaphore(n))
            with nc.Block() as block:
                def mk(eng_key):
                    def run(engine):
                        for op in order:
                            if op.eng != eng_key:
                                continue
                            for s, v in op.waits:
                                engine.wait_ge(sems[s], v)
                            inst = op.fn(engine)
                            if op.inc is not None and inst is not None:
                                inst.then_inc(sems[op.inc[0]], op.inc[1])
                    return run
                block.tensor(mk("p"))
                block.vector(mk("v"))
                block.scalar(mk("a"))
                block.gpsimd(mk("g"))
                block.sync(mk("s"))
    return nc


def prep_inputs(inputs, L=8192):
    f = np.ascontiguousarray
    W_in = np.asarray(inputs["L0_attn_W_in"])
    conv_w = np.asarray(inputs["L0_attn_conv_w"])
    d = {}
    d["win"] = f(W_in.reshape(KT, 128, 2 * DIN))
    d["bwT"] = f(np.asarray(inputs["L0_ffn_base_w"]).T.reshape(KT, 128, DM))
    d["rwT"] = f(np.asarray(inputs["L0_ffn_router_w"]).T.reshape(KT, 128, NE))
    d["lA"] = f(np.asarray(inputs["L0_ffn_lora_A"]).reshape(KT, 128, NR))
    d["lB"] = f(np.asarray(inputs["L0_ffn_lora_B"]).reshape(NE * NR, DM))
    d["wx"] = f(np.asarray(inputs["L0_attn_W_x"]).reshape(DT2, 128, 96))
    d["wdt"] = f(np.asarray(inputs["L0_attn_W_dt"]))
    d["wout"] = f(np.asarray(inputs["L0_attn_W_out"]).reshape(DT2, 128, DM))
    cd = np.zeros((DT2 * 4, 128, 128), np.float32)
    idx = np.arange(128)
    for dt in range(DT2):
        for k in range(4):
            cd[dt * 4 + k, idx, idx] = conv_w[dt * 128 + idx, k]
    d["cdiag"] = cd
    d["alog"] = f(np.asarray(inputs["L0_attn_A_log"]).reshape(DT2, 128, NS))
    vec = np.zeros((128, 64), np.float32)
    vec[:, 0:16] = np.asarray(inputs["L0_attn_conv_b"]).reshape(DT2, 128).T
    vec[:, 16:32] = np.asarray(inputs["L0_attn_b_dt"]).reshape(DT2, 128).T
    vec[:, 32:48] = np.asarray(inputs["L0_attn_D"]).reshape(DT2, 128).T
    vec[:, 48:56] = np.asarray(inputs["L0_attn_gamma"]).reshape(KT, 128).T
    vec[:, 56:64] = np.asarray(inputs["L0_attn_beta"]).reshape(KT, 128).T
    d["vec"] = vec
    d["bbrow"] = f(np.asarray(inputs["L0_ffn_base_b"]).reshape(1, DM))
    d["rbrow"] = f(np.asarray(inputs["L0_ffn_router_b"]).reshape(1, NE))
    d["ident_in"] = np.eye(128, dtype=np.float32)
    p = np.arange(128, dtype=np.uint32)
    nch = L // TC
    nc2 = nch // 2
    for sigma in (0, 1):
        gi = np.zeros((128, 448), np.uint32)
        for dt in range(16):
            for j in range(nc2):
                gi[:, dt * 8 + j] = (dt * 128 + p) * nch + (sigma * nc2 + j)
        for r in range(32):
            for j in range(nc2):
                gi[:, 128 + r * 8 + j] = r * nch + (sigma * nc2 + j)
        for k in range(8):
            for j in range(nc2):
                gi[:, 384 + k * 8 + j] = (k * 128 + p) * nch + (sigma * nc2 + j)
        d[f"gidx{sigma}"] = gi
        d[f"smask{sigma}"] = np.full((128, 1), float(sigma), np.float32)
    repl = np.zeros((2, 8, NE * NR), np.float32)
    for ee in range(NE):
        for r in range(NR):
            repl[0, ee, ee * NR + r] = 1.0
            repl[1, r, ee * NR + r] = 1.0
    d["repl_in"] = repl
    return d


def kernel(**inputs):
    x = np.asarray(inputs["L0_attn_x"])
    L = x.shape[1]
    shared = prep_inputs(inputs, L=L)
    nc = build_program(L=L)
    in_maps = []
    for core in range(8):
        bidx = core // 2
        sigma = core % 2
        m = {k: v for k, v in shared.items()
             if not k.startswith(("gidx", "smask"))}
        m["gidx"] = shared[f"gidx{sigma}"]
        m["smask"] = shared[f"smask{sigma}"]
        m["xT"] = np.ascontiguousarray(x[bidx].T).reshape(KT, 128, L)
        in_maps.append(m)
    res = run_bass_kernel_spmd(nc, in_maps, core_ids=list(range(8)))
    out = np.empty((x.shape[0], L, DM), np.float32)
    half = L // 2
    for bidx in range(x.shape[0]):
        y0 = np.asarray(res.results[2 * bidx]["yout"]).reshape(DM, L)
        y1 = np.asarray(res.results[2 * bidx + 1]["yout"]).reshape(DM, L)
        out[bidx, :half] = y0[:, :half].T
        out[bidx, half:] = y1[:, half:].T
    return out


# version 08
# speedup vs baseline: 1.0448x; 1.0112x over previous
import numpy as np
from contextlib import ExitStack

import concourse.bass as bass
import concourse.mybir as mybir
from concourse.bass_utils import run_bass_kernel_spmd

F32 = mybir.dt.float32
BF16 = mybir.dt.bfloat16
AF = mybir.ActivationFunctionType
ALU = mybir.AluOpType
AX = mybir.AxisListType

DM = 1024
DIN = 2048
NS = 16
DTR = 64
NE = 8
NR = 8
KT = DM // 128
DT2 = DIN // 128
EPS = 1e-5
SM30 = 30.0
TC = 512

COMPUTE = ("p", "v", "a", "g", "s")


class Buf:
    __slots__ = ("writer", "readers", "name", "dsem")

    def __init__(self, name=""):
        self.writer = None
        self.readers = []
        self.name = name
        self.dsem = {}


class Op:
    __slots__ = ("eng", "fn", "dma", "reads", "writes", "tag", "cost",
                 "epoch", "idx", "preds", "wait_all_dma", "start",
                 "waits", "inc", "dsem")


class B:

    ENGS = ("p", "v", "a", "g", "s")

    def __init__(self):
        self.ops = []
        self.epoch = 0

    def emit(self, eng, fn, reads=(), writes=(), dma=False, noinc=False,
             extra_waits=(), war=(), tag="", cost=700, wait_all_dma=False,
             dsem=None):
        op = Op()
        op.eng, op.fn, op.dma = eng, fn, dma
        op.reads = list(reads)
        op.writes = list(writes) + list(war)
        op.tag = tag
        op.cost = cost
        op.epoch = self.epoch
        op.wait_all_dma = wait_all_dma
        op.dsem = dsem
        op.idx = len(self.ops)
        op.preds = None
        self.ops.append(op)
        return op

    def fence(self):
        self.epoch += 1

    def barrier(self):
        self.fence()

    def close_group(self, eng, *bufs):
        pass

    def _build_graph(self):
        writer = {}
        readers = {}
        for op in self.ops:
            preds = set()
            for bb in op.reads:
                w = writer.get(id(bb))
                if w is not None:
                    preds.add(w)
            for bb in op.writes:
                w = writer.get(id(bb))
                if w is not None:
                    preds.add(w)
                for r in readers.get(id(bb), ()):
                    preds.add(r)
            for bb in op.writes:
                writer[id(bb)] = op.idx
                readers[id(bb)] = []
            for bb in op.reads:
                readers.setdefault(id(bb), []).append(op.idx)
            preds.discard(op.idx)
            op.preds = sorted(preds)

    COST = {"p": 320, "v": 750, "a": 780, "g": 1200, "s": 300}

    def _estimate_costs(self):
        for op in self.ops:
            if op.cost != 700:
                continue
            if op.dma:
                op.cost = 1200
            else:
                c = self.COST.get(op.eng, 700)
                t = op.tag
                if op.eng == "v":
                    if t.startswith("ts"):
                        c = 1300
                    elif t.startswith(("cm", "dB")):
                        c = 550
                    elif t.startswith("cr"):
                        c = 160
                    elif t.startswith("ttr"):
                        c = 1050
                op.cost = c

    def schedule(self):
        import heapq
        self._estimate_costs()
        self._build_graph()
        n = len(self.ops)
        nsucc_pred = [len(op.preds) for op in self.ops]
        succs = [[] for _ in range(n)]
        for op in self.ops:
            for p in op.preds:
                succs[p].append(op.idx)
        done_t = [0.0] * n
        eng_free = {e: 0.0 for e in self.ENGS}
        q_free = {e: 0.0 for e in ("s", "a", "g")}
        epoch_end = 0.0
        order = []
        by_epoch = {}
        for op in self.ops:
            by_epoch.setdefault(op.epoch, []).append(op)
        for ep in sorted(by_epoch):
            ops_e = by_epoch[ep]
            remaining = set(o.idx for o in ops_e)
            pend = {o.idx: sum(1 for p in self.ops[o.idx].preds
                               if self.ops[p].epoch == ep) for o in ops_e}
            ready = []
            for o in ops_e:
                if pend[o.idx] == 0:
                    heapq.heappush(ready, o.idx)
            scheduled_any = True
            while remaining:
                if not ready:
                    raise RuntimeError("scheduler deadlock (cyclic deps?)")
                cand = heapq.heappop(ready)
                op = self.ops[cand]
                start = max(epoch_end, eng_free[op.eng],
                            max((done_t[p] for p in op.preds), default=0.0))
                if op.dma:
                    issue = max(start, q_free[op.eng])
                    eng_free[op.eng] = issue + 150.0
                    q_free[op.eng] = issue + max(600.0, op.cost * 0.5)
                    done_t[cand] = issue + 1500.0 + op.cost
                else:
                    eng_free[op.eng] = start + op.cost
                    done_t[cand] = start + op.cost
                op.start = start
                order.append(op)
                remaining.discard(cand)
                for sidx in succs[cand]:
                    if self.ops[sidx].epoch == ep and sidx in remaining:
                        pend[sidx] -= 1
                        if pend[sidx] == 0:
                            heapq.heappush(ready, sidx)
            epoch_end = max(max(done_t[o.idx] for o in ops_e), epoch_end)
        return order

    def lower(self, order):
        self.dsem_n = 0
        self.posmap = {}
        cnt = {k: 0 for k in ("p", "v", "a", "g")}
        pos = {}
        wm = {e: {} for e in self.ENGS}

        def dma_sem(op):
            if op.dsem is None:
                tgt = None
                for bb in op.writes:
                    tgt = bb
                    break
                if tgt is None:
                    op.dsem = "dmisc_" + op.eng
                else:
                    if op.eng not in tgt.dsem:
                        tgt.dsem[op.eng] = f"d{self.dsem_n}_{op.eng}"
                        self.dsem_n += 1
                    op.dsem = tgt.dsem[op.eng]
            if op.dsem not in cnt:
                cnt[op.dsem] = 0
            return op.dsem

        for op in order:
            semkey = dma_sem(op) if op.dma else op.eng
            cnt[semkey] += 16 if op.dma else 1
            pos[op.idx] = (semkey, cnt[semkey])
            waits = {}
            for p in op.preds:
                s, v = pos[p]
                if wm[op.eng].get(s, 0) >= v:
                    continue
                waits[s] = max(waits.get(s, 0), v)
            if op.wait_all_dma:
                for s, v in cnt.items():
                    if s not in ("p", "v", "a", "g") and v > waits.get(s, 0):
                        waits[s] = v
            op.waits = sorted(waits.items())
            op.inc = (semkey, 16 if op.dma else 1)
            self.posmap[(semkey, cnt[semkey])] = op
            for s, v in waits.items():
                wm[op.eng][s] = v
        self.all_sems = sorted(cnt)
        return cnt

    def coalesce_waits(self, order, margin=4000.0):
        per_eng = {}
        for op in order:
            per_eng.setdefault(op.eng, []).append(op)
        for eng, ops in per_eng.items():
            carrier = None
            carrier_limit = 0
            for op in ops:
                if not op.waits:
                    continue
                keep = []
                hoist = []
                for s, v in op.waits:
                    p = self.posmap.get((s, v))
                    if (carrier is not None and p is not None
                            and p.start + p.cost + margin <= carrier.start
                            and carrier_limit < 24):
                        hoist.append((s, v))
                    else:
                        keep.append((s, v))
                if hoist and carrier is not None:
                    w = dict(carrier.waits)
                    for s, v in hoist:
                        w[s] = max(w.get(s, 0), v)
                    carrier.waits = sorted(w.items())
                    carrier_limit += len(hoist)
                op.waits = keep
                if keep:
                    carrier = op
                    carrier_limit = 0
                elif carrier is None:
                    carrier = op
                    carrier_limit = 0

    def finalize(self, order):
        cnt = {}
        epoch_counts = {}
        cur = order[0].epoch if order else 0
        for op in order:
            if op.epoch != cur:
                epoch_counts[cur] = dict(cnt)
                cur = op.epoch
            semkey = op.inc[0]
            cnt[semkey] = cnt.get(semkey, 0) + op.inc[1]
        seen = set()
        for op in order:
            key = (op.eng, op.epoch)
            if key in seen:
                continue
            seen.add(key)
            prev = op.epoch - 1
            if prev in epoch_counts:
                w = dict(op.waits)
                for s, v in epoch_counts[prev].items():
                    if v > 0 and w.get(s, 0) < v:
                        w[s] = v
                op.waits = sorted(w.items())


class Arena:
    def __init__(self, t, n):
        self.t = t
        self.n = n
        self.off = 0
        self.stack = []

    def alloc(self, cols, rows=128):
        off = (self.off + 3) & ~3
        assert off + cols <= self.n, f"arena overflow {off}+{cols}>{self.n}"
        self.off = off + cols
        return self.t.ap()[0:rows, off:off + cols]

    def mark(self):
        self.stack.append(self.off)

    def reset(self):
        self.off = self.stack.pop()


def build_program(L=8192):
    NCH = L // TC
    nc = bass.Bass("TRN2", target_bir_lowering=False, debug=False)

    xT = nc.dram_tensor("xT", [KT, 128, L], F32, kind="ExternalInput").ap()
    win = nc.dram_tensor("win", [KT, 128, 2 * DIN], F32, kind="ExternalInput").ap()
    bwT = nc.dram_tensor("bwT", [KT, 128, DM], F32, kind="ExternalInput").ap()
    rwT = nc.dram_tensor("rwT", [KT, 128, NE], F32, kind="ExternalInput").ap()
    lA = nc.dram_tensor("lA", [KT, 128, NR], F32, kind="ExternalInput").ap()
    lB = nc.dram_tensor("lB", [NE * NR, DM], F32, kind="ExternalInput").ap()
    wx = nc.dram_tensor("wx", [DT2, 128, 96], F32, kind="ExternalInput").ap()
    wdt = nc.dram_tensor("wdt", [DTR, DIN], F32, kind="ExternalInput").ap()
    wout = nc.dram_tensor("wout", [DT2, 128, DM], F32, kind="ExternalInput").ap()
    cdiag = nc.dram_tensor("cdiag", [DT2 * 4, 128, 128], F32, kind="ExternalInput").ap()
    alog = nc.dram_tensor("alog", [DT2, 128, NS], F32, kind="ExternalInput").ap()
    vec = nc.dram_tensor("vec", [128, 64], F32, kind="ExternalInput").ap()
    bbrow = nc.dram_tensor("bbrow", [1, DM], F32, kind="ExternalInput").ap()
    rbrow = nc.dram_tensor("rbrow", [1, NE], F32, kind="ExternalInput").ap()
    ident_in = nc.dram_tensor("ident_in", [128, 128], F32, kind="ExternalInput").ap()
    repl_in = nc.dram_tensor("repl_in", [2, 8, NE * NR], F32, kind="ExternalInput").ap()

    yout = nc.dram_tensor("yout", [KT, 128, L], F32, kind="ExternalOutput").ap()

    win_bf = nc.dram_tensor("win_bf", [KT, 128, 2 * DIN], BF16)
    wout_bf = nc.dram_tensor("wout_bf", [DT2, 128, DM], BF16)
    sp_d = nc.dram_tensor("sp_d", [DT2, 128, L], BF16)
    sp_u = nc.dram_tensor("sp_u", [DT2, 128, L], BF16)
    sp_bc = nc.dram_tensor("sp_bc", [32, L], BF16)
    sp_x1 = nc.dram_tensor("sp_x1", [KT, 128, L], F32)
    sp_x2 = nc.dram_tensor("sp_x2", [KT, 128, L], F32)
    sp_g = nc.dram_tensor("sp_g", [DT2, 128, L], BF16)
    sp_du = nc.dram_tensor("sp_du", [DT2, 128, L], BF16)
    gidx = nc.dram_tensor("gidx", [128, 448], mybir.dt.uint32, kind="ExternalInput").ap()
    smask = nc.dram_tensor("smask", [128, 1], F32, kind="ExternalInput").ap()

    b = B()
    ctx = ExitStack()
    abf_t = ctx.enter_context(nc.sbuf_tensor("abf", [128, 59 * 1024], BF16))
    af_t = ctx.enter_context(nc.sbuf_tensor("af32", [128, 22 * 1024], F32))
    ABF = Arena(abf_t, 59 * 1024)
    AFA = Arena(af_t, 22 * 1024)

    gidx_t = ctx.enter_context(nc.sbuf_tensor("gidxs", [128, 448], mybir.dt.uint32))
    ps_mm = [ctx.enter_context(nc.psum_tensor(f"psmm{i}", [128, TC], F32)) for i in range(4)]
    ps_bc0 = ctx.enter_context(nc.psum_tensor("psbc0", [128, TC], F32))
    ps_bc1 = ctx.enter_context(nc.psum_tensor("psbc1", [128, TC], F32))
    ps_small = ctx.enter_context(nc.psum_tensor("pssmall", [128, TC], F32))
    ps_geb = ctx.enter_context(nc.psum_tensor("psgeb", [8, TC], BF16))
    ps_mm_b = [Buf(f"psmm{i}") for i in range(4)]
    ps_bc0_b, ps_bc1_b = Buf("psbc0"), Buf("psbc1")
    ps_small_b = Buf("pssmall")
    ps_geb_b = Buf("psgeb")

    bwT_s = ABF.alloc(KT * DM)
    rwT_s = ABF.alloc(KT * NE)
    lA_s = ABF.alloc(KT * NR)
    lB_s = ABF.alloc(DM, rows=NE * NR)
    wdt_s = ABF.alloc(DIN, rows=DTR)
    ident_s = ABF.alloc(128)
    repl_s = ABF.alloc(2 * NE * NR, rows=8)
    ones_row = ABF.alloc(TC, rows=1)
    onescol = ABF.alloc(1)
    bb_s = ABF.alloc(DM, rows=1)
    rb_s = ABF.alloc(NE, rows=1)

    onesrow_f = AFA.alloc(128, rows=1)
    vec_s = AFA.alloc(64)
    negA_s = AFA.alloc(DT2 * NS)
    posA_s = AFA.alloc(DT2 * NS)
    carry_s = AFA.alloc(DT2 * NS)
    hrun_s = AFA.alloc(DT2 * NS)
    smask_s = AFA.alloc(4)
    idf_s = AFA.alloc(128)
    atmp = AFA.alloc(NS)

    CONSTS = [0.0, 1.0, -1.0, 0.5, -0.5, 2.0, EPS, SM30, 1.0 / DM]
    const_t = AFA.alloc(len(CONSTS))
    cb = Buf("consts")
    for ci, cv in enumerate(CONSTS):
        nc.const_aps.register(F32, cv, const_t[:, ci:ci + 1])

    WB = {k: Buf(k) for k in ("bwT", "rwT", "lA", "lB", "wdt", "ident", "repl",
                              "ones", "bb", "rb", "onesf", "vec", "negA",
                              "posA", "carry", "hrun", "smask", "gidx",
                              "tails", "winbf", "woutbf", "wx",
                              "cdiag", "idf", "atmp")}

    def dma(eng, out, in_, reads=(), writes=(), tag="", dsem=None):
        return b.emit(eng, lambda e, o=out, i=in_: e.dma_start(out=o, in_=i),
                      reads=reads, writes=writes, dma=True, tag=tag, dsem=dsem)

    for k in range(KT):
        dma("g", bwT_s[:, k * DM:(k + 1) * DM], bwT[k], dsem="boot_g")
        dma("g", rwT_s[:, k * NE:(k + 1) * NE], rwT[k], dsem="boot_g")
        dma("g", lA_s[:, k * NR:(k + 1) * NR], lA[k], dsem="boot_g")
    dma("g", lB_s, lB, dsem="boot_g")
    dma("g", wdt_s, wdt, dsem="boot_g")
    dma("g", ident_s, ident_in, dsem="boot_g")
    dma("s", idf_s, ident_in, dsem="boot_s")
    dma("g", repl_s[0:8, 0:64], repl_in[0], dsem="boot_g")
    dma("g", repl_s[0:8, 64:128], repl_in[1], dsem="boot_g")
    dma("g", bb_s, bbrow, dsem="boot_g")
    dma("g", rb_s, rbrow, dsem="boot_g")
    dma("s", vec_s, vec, dsem="boot_s")
    dma("s", gidx_t.ap(), gidx, dsem="boot_s")
    dma("s", smask_s[:, 0:1], smask, dsem="boot_s")

    for ci, cv in enumerate(CONSTS):
        b.emit("v", lambda e, ci=ci, cv=cv: e.memset(const_t[:, ci:ci + 1], cv),
               writes=[cb])
    b.emit("v", lambda e: e.memset(ones_row, 1.0), writes=[WB["ones"]])
    b.emit("v", lambda e: e.memset(onescol, 1.0), writes=[WB["ones"]])
    b.emit("v", lambda e: e.memset(onesrow_f, 1.0), writes=[WB["onesf"]])
    b.emit("v", lambda e: e.memset(carry_s, 0.0), writes=[WB["carry"]])
    b.fence()
    for d in range(DT2):
        slp = posA_s[:, d * NS:(d + 1) * NS]
        sln = negA_s[:, d * NS:(d + 1) * NS]
        dma("s", atmp, alog[d], writes=[WB["atmp"]])
        b.emit("a", lambda e, slp=slp: e.activation(out=slp, in_=atmp, func=AF.Exp),
               reads=[WB["atmp"]], writes=[WB["posA"]])
        b.emit("a", lambda e, slp=slp, sln=sln: e.mul(out=sln, in_=slp, mul=-1.0),
               reads=[WB["posA"]], writes=[WB["negA"]])

    ident8f = idf_s[0:8, 0:8]
    ident_bf = ident_s[0:128, 0:128]

    def vcol(name, j):
        base = dict(conv_b=0, b_dt=16, D=32, gamma=48, beta=56)[name]
        return vec_s[:, base + j:base + j + 1]

    _lnrot = [0]

    def ln_rows(src_tiles, src_bufs, lnrows_sets, lnrows_bs, sq_t, sq_b, xbf_t, xbf_b, tag):
        _lnrot[0] ^= 1
        lnrows = lnrows_sets[_lnrot[0]] if isinstance(lnrows_sets, (list, tuple)) else lnrows_sets
        lnrows_b = lnrows_bs[_lnrot[0]] if isinstance(lnrows_bs, (list, tuple)) else lnrows_bs
        for k in range(KT):
            b.emit("v", lambda e, k=k: e.tensor_copy(out=xbf_t, in_=src_tiles[k]),
                   reads=[src_bufs[k]], writes=[xbf_b], tag=tag + "cast")
            b.emit("a", lambda e, k=k: e.activation(out=sq_t, in_=src_tiles[k],
                   func=AF.Square), reads=[src_bufs[k]], writes=[sq_b], tag=tag + "sq")
            b.emit("p", lambda e, st=(k == 0), last=(k == KT - 1): e.matmul(
                   out=ps_small[0:1, :], lhsT=onescol, rhs=xbf_t, start=st,
                   stop=last),
                   reads=[xbf_b, WB["ones"]] if k == 0 else [xbf_b],
                   war=[ps_small_b],
                   tag=tag + "mmx")
            b.emit("p", lambda e, st=(k == 0), last=(k == KT - 1): e.matmul(
                   out=ps_bc0[0:1, :], lhsT=onescol, rhs=sq_t, start=st,
                   stop=last),
                   reads=[sq_b],
                   war=[ps_bc0_b],
                   tag=tag + "mmq")
        b.close_group("p", ps_small_b, ps_bc0_b)
        mu = lnrows[0:1, 0:TC]
        m2 = lnrows[0:1, TC:2 * TC]
        var = lnrows[0:1, 2 * TC:3 * TC]
        arow = lnrows[0:1, 3 * TC:4 * TC]
        brow = lnrows[0:1, 4 * TC:5 * TC]
        b.emit("v", lambda e: e.tensor_scalar(out=mu, in0=ps_small[0:1, :],
               scalar1=1.0 / DM, scalar2=None, op0=ALU.mult),
               reads=[ps_small_b], writes=[lnrows_b], tag=tag)
        b.emit("v", lambda e: e.tensor_scalar(out=m2, in0=ps_bc0[0:1, :],
               scalar1=1.0 / DM, scalar2=None, op0=ALU.mult),
               reads=[ps_bc0_b], writes=[lnrows_b], tag=tag)
        b.emit("v", lambda e: e.tensor_tensor(out=var, in0=mu, in1=mu, op=ALU.mult),
               reads=[lnrows_b], writes=[lnrows_b], tag=tag)
        b.emit("v", lambda e: e.tensor_tensor(out=var, in0=m2, in1=var,
               op=ALU.subtract), reads=[lnrows_b], writes=[lnrows_b], tag=tag)
        b.emit("a", lambda e: e.activation(out=arow, in_=var, func=AF.Ln, bias=EPS),
               reads=[lnrows_b], writes=[lnrows_b], tag=tag)
        b.emit("a", lambda e: e.activation(out=arow, in_=arow, func=AF.Exp, scale=-0.5),
               reads=[lnrows_b], writes=[lnrows_b], tag=tag)
        b.emit("v", lambda e: e.tensor_tensor(out=brow, in0=mu, in1=arow, op=ALU.mult),
               reads=[lnrows_b], writes=[lnrows_b], tag=tag)
        b.emit("p", lambda e: e.matmul(out=ps_bc0[:], lhsT=onesrow_f, rhs=arow,
               start=True, stop=True),
               reads=[lnrows_b, WB["onesf"]], war=[ps_bc0_b], tag=tag + "bc0")
        b.close_group("p", ps_bc0_b)
        b.emit("p", lambda e: e.matmul(out=ps_bc1[:], lhsT=onesrow_f, rhs=brow,
               start=True, stop=True),
               reads=[lnrows_b], war=[ps_bc1_b], tag=tag + "bc1")
        b.close_group("p", ps_bc1_b)

    def ln_apply(src_tiles, src_bufs, out_tiles, out_bufs, tmp_t, tmp_b, tag):
        for k in range(KT):
            b.emit("v", lambda e, k=k: e.tensor_tensor(out=tmp_t, in0=src_tiles[k],
                   in1=ps_bc0[:], op=ALU.mult),
                   reads=[src_bufs[k], ps_bc0_b], writes=[tmp_b], tag=tag + "a")
            b.emit("v", lambda e, k=k: e.tensor_tensor(out=out_tiles[k], in0=tmp_t,
                   in1=ps_bc1[:], op=ALU.subtract),
                   reads=[tmp_b, ps_bc1_b], writes=[out_bufs[k]], tag=tag + "b")

    def make_ffn_work():
        return dict(
            lgt=(AFA.alloc(TC, rows=8), Buf("lgt")),
            mid=(ABF.alloc(TC, rows=8), Buf("mid")),
            gex=(AFA.alloc(32), Buf("gex")),
            gtb=(ABF.alloc(32), Buf("gtb")),
            ge=(ABF.alloc(TC, rows=8), Buf("ge")),
            grep=(ABF.alloc(TC, rows=64), Buf("grep")),
            wmid=(ABF.alloc(TC, rows=64), Buf("wmid")),
            rows=(AFA.alloc(2), Buf("rows")),
        )

    def ffn(xin_tiles, xin_bufs, res_tiles, res_bufs, out_tiles, out_bufs, W, tag):
        lgt_t, lgt_b = W["lgt"]
        mid_t, mid_b = W["mid"]
        gex_t, gex_b = W["gex"]
        gtb_t, gtb_b = W["gtb"]
        ge_t, ge_b = W["ge"]
        grep_t, grep_b = W["grep"]
        wmid_t, wmid_b = W["wmid"]
        rows_t, rows_b = W["rows"]
        for k in range(KT):
            b.emit("p", lambda e, k=k: e.matmul(out=ps_small[0:8, :],
                   lhsT=rwT_s[:, k * NE:(k + 1) * NE], rhs=xin_tiles[k],
                   start=(k == 0), stop=False),
                   reads=[xin_bufs[k]],
                   war=[ps_small_b],
                   tag=tag + "rout")
        b.emit("p", lambda e: e.matmul(out=ps_small[0:8, :], lhsT=rb_s,
               rhs=ones_row, start=False, stop=True),
               reads=[WB["rb"], WB["ones"]], tag=tag)
        for k in range(KT):
            b.emit("p", lambda e, k=k: e.matmul(out=ps_small[32:40, :],
                   lhsT=lA_s[:, k * NR:(k + 1) * NR], rhs=xin_tiles[k],
                   start=(k == 0), stop=(k == KT - 1), tile_position=(0, 32)),
                   reads=[xin_bufs[k]],
                   war=[ps_small_b],
                   tag=tag + "mid")
        b.close_group("p", ps_small_b)
        b.emit("v", lambda e: e.tensor_copy(out=lgt_t, in_=ps_small[0:8, :]),
               reads=[ps_small_b], writes=[lgt_b], tag=tag + "lgt")
        b.emit("v", lambda e: e.tensor_copy(out=mid_t, in_=ps_small[32:40, :]),
               reads=[ps_small_b], writes=[mid_b], tag=tag + "midc")
        nj = TC // 128
        for j in range(nj):
            b.emit("p", lambda e, j=j: e.transpose(
                   out=ps_small[:, j * 8:(j + 1) * 8],
                   in_=lgt_t[0:8, j * 128:(j + 1) * 128], identity=ident8f),
                   reads=[lgt_b, WB["idf"]] if j == 0 else [lgt_b],
                   war=[ps_small_b],
                   tag=tag + "tr")
        b.close_group("p", ps_small_b)
        mxr = rows_t[0:128, 0:1]
        smr = rows_t[0:128, 1:2]
        for j in range(nj):
            sl = ps_small[:, j * 8:(j + 1) * 8]
            b.emit("v", lambda e, sl=sl: e.reduce_max(out=mxr, in_=sl, axis=AX.X),
                   reads=[ps_small_b], writes=[rows_b], tag=tag + "mx")
            b.emit("v", lambda e: e.tensor_scalar(out=mxr, in0=mxr, scalar1=-1.0,
                   scalar2=None, op0=ALU.mult), reads=[rows_b], writes=[rows_b],
                   tag=tag)
            b.emit("a", lambda e, sl=sl, j=j: e.activation(
                   out=gex_t[:, j * 8:(j + 1) * 8], in_=sl, func=AF.Exp, bias=mxr),
                   reads=[ps_small_b, rows_b], writes=[gex_b], tag=tag + "exp")
            b.emit("v", lambda e, j=j: e.reduce_sum(out=smr,
                   in_=gex_t[:, j * 8:(j + 1) * 8], axis=AX.X),
                   reads=[gex_b], writes=[rows_b], tag=tag + "sum")
            b.emit("v", lambda e: e.reciprocal(out=smr, in_=smr),
                   reads=[rows_b], writes=[rows_b], tag=tag + "rcp")
            b.emit("v", lambda e, j=j: e.tensor_scalar(
                   out=gtb_t[:, j * 8:(j + 1) * 8],
                   in0=gex_t[:, j * 8:(j + 1) * 8], scalar1=smr, scalar2=None,
                   op0=ALU.mult), reads=[gex_b, rows_b], writes=[gtb_b],
                   tag=tag + "gmul")
        for j in range(nj):
            b.emit("p", lambda e, j=j: e.transpose(
                   out=ps_geb[0:8, j * 128:(j + 1) * 128],
                   in_=gtb_t[:, j * 8:(j + 1) * 8], identity=ident_bf),
                   reads=[gtb_b, WB["ident"]] if j == 0 else [gtb_b],
                   war=[ps_geb_b],
                   tag=tag + "trb")
        b.close_group("p", ps_geb_b)
        b.emit("v", lambda e: e.tensor_copy(out=ge_t, in_=ps_geb[0:8, :]),
               reads=[ps_geb_b], writes=[ge_b], tag=tag + "gec")
        b.emit("p", lambda e: e.matmul(out=ps_bc0[0:64, :],
               lhsT=repl_s[0:8, 64:128], rhs=mid_t, start=True, stop=True),
               reads=[mid_b, WB["repl"]], war=[ps_bc0_b], tag=tag + "repm")
        b.close_group("p", ps_bc0_b)
        b.emit("p", lambda e: e.matmul(out=ps_bc1[0:64, :],
               lhsT=repl_s[0:8, 0:64], rhs=ge_t, start=True, stop=True),
               reads=[ge_b], war=[ps_bc1_b], tag=tag + "repg")
        b.close_group("p", ps_bc1_b)
        b.emit("v", lambda e: e.tensor_copy(out=grep_t, in_=ps_bc1[0:64, :]),
               reads=[ps_bc1_b], writes=[grep_b], tag=tag + "grc")
        b.emit("v", lambda e: e.scalar_tensor_tensor(out=wmid_t,
               in0=ps_bc0[0:64, :], scalar=2.0, in1=grep_t, op0=ALU.mult,
               op1=ALU.mult), reads=[ps_bc0_b, grep_b], writes=[wmid_b],
               tag=tag + "wmid")
        for g0 in range(0, KT, 4):
            for m in range(g0, g0 + 4):
                ps, psb = ps_mm[m % 4], ps_mm_b[m % 4]
                for k in range(KT):
                    b.emit("p", lambda e, m=m, k=k, ps=ps: e.matmul(out=ps[:],
                           lhsT=bwT_s[:, k * DM + m * 128:k * DM + (m + 1) * 128],
                           rhs=xin_tiles[k], start=(k == 0), stop=False),
                           reads=[xin_bufs[k]],
                           war=[psb],
                           tag=tag + f"bs{m}")
                b.emit("p", lambda e, m=m, ps=ps: e.matmul(out=ps[:],
                       lhsT=bb_s[0:1, m * 128:(m + 1) * 128], rhs=ones_row,
                       start=False, stop=False), reads=[], war=[psb], tag=tag)
                b.emit("p", lambda e, m=m, ps=ps: e.matmul(out=ps[:],
                       lhsT=lB_s[:, m * 128:(m + 1) * 128], rhs=wmid_t,
                       start=False, stop=True), reads=[wmid_b], war=[psb],
                       tag=tag + "lora")
                b.close_group("p", psb)
            for m in range(g0, g0 + 4):
                ps, psb = ps_mm[m % 4], ps_mm_b[m % 4]
                b.emit("v", lambda e, m=m, ps=ps: e.scalar_tensor_tensor(
                       out=out_tiles[m], in0=ps[:], scalar=0.5, in1=res_tiles[m],
                       op0=ALU.mult, op1=ALU.add),
                       reads=[psb, res_bufs[m]], writes=[out_bufs[m]],
                       tag=tag + "ev")

    ffnw1 = make_ffn_work()

    ABF.mark(); AFA.mark()
    wx_s = ABF.alloc(DT2 * 96)
    cdiag_s = ABF.alloc(64 * 128)
    tails_s = ABF.alloc(DT2 * 4)
    for d in range(DT2):
        dma("g", wx_s[:, d * 96:(d + 1) * 96], wx[d], dsem="boot_g")
    for t in range(64):
        dma("g", cdiag_s[:, t * 128:(t + 1) * 128], cdiag[t],
            dsem="boot_g")
    b.emit("v", lambda e: e.memset(tails_s, 0.0), writes=[WB["tails"]])
    ABF.mark()
    stage = ABF.alloc(2 * DIN)
    stage_b = Buf("stage")
    for k in range(KT):
        dma("g", stage, win[k], writes=[stage_b])
        dma("s", win_bf.ap()[k], stage, reads=[stage_b], writes=[WB["winbf"]])
    for d in range(DT2):
        dma("g", stage[:, 0:DM], wout[d], writes=[stage_b])
        dma("s", wout_bf.ap()[d], stage[:, 0:DM], reads=[stage_b],
            writes=[WB["woutbf"]])
    ABF.reset()
    b.barrier()

    xf_t = [AFA.alloc(TC) for _ in range(KT)]
    xf_b = [Buf(f"xf{k}") for k in range(KT)]
    xbf_t = ABF.alloc(TC)
    xbf_b = Buf("xbf")
    sq_t = ABF.alloc(TC)
    sq_b = Buf("sq")
    lnrows_t = [AFA.alloc(5 * TC, rows=1) for _ in range(2)]
    lnrows_b = [Buf(f"lnrows{i}") for i in range(2)]
    tmp_t = AFA.alloc(TC)
    tmp_b = Buf("tmp")
    xln1_t = [ABF.alloc(TC) for _ in range(KT)]
    xln1_b = [Buf(f"xln1{k}") for k in range(KT)]
    x1_t = [AFA.alloc(TC) for _ in range(KT)]
    x1_b = [Buf(f"x1{k}") for k in range(KT)]
    xln2f_t = AFA.alloc(TC)
    xln2f_b = Buf("xln2f")
    xln2_t = [ABF.alloc(TC) for _ in range(KT)]
    xln2_b = [Buf(f"xln2{k}") for k in range(KT)]

    wg_t = [ABF.alloc(KT * 512) for _ in range(2)]
    wg_b = [Buf(f"wg{i}") for i in range(2)]
    xin_t = [ABF.alloc(TC + 4) for _ in range(2)]
    xin_b = [Buf(f"xin{i}") for i in range(2)]
    u_t = [ABF.alloc(TC) for _ in range(DT2)]
    u_b = [Buf(f"u{d}") for d in range(DT2)]
    gz_t = ABF.alloc(TC)
    gz_b = Buf("gz")
    dtb_t = ABF.alloc(TC, rows=64)
    dtb_b = Buf("dtb")
    bct_t = ABF.alloc(TC, rows=32)
    bct_b = Buf("bct")
    cbuf_t = AFA.alloc(TC)
    cbuf_b = Buf("cbuf")
    ebuf_t = AFA.alloc(TC)
    ebuf_b = Buf("ebuf")
    lbuf_t = AFA.alloc(TC)
    lbuf_b = Buf("lbuf")
    dl_t = ABF.alloc(TC)
    dl_b = Buf("dl")
    du2_t = ABF.alloc(TC)
    du2_b = Buf("du2")
    _spd = Buf("spd"); sb_spd = [_spd] * DT2
    _spdu = Buf("spdu"); sb_spdu = [_spdu] * DT2
    _spu = Buf("spu"); sb_spu = [_spu] * DT2
    _spg = Buf("spg"); sb_spg = [_spg] * DT2
    sb_spbc = Buf("spbc")
    _spx1 = Buf("spx1"); sb_spx1 = [_spx1] * KT
    _spx2 = Buf("spx2"); sb_spx2 = [_spx2] * KT

    for c in range(NCH):
        cs = slice(c * TC, (c + 1) * TC)
        for k in range(KT):
            dma("s", xf_t[k], xT[k][:, cs], writes=[xf_b[k]], tag=f"ldx{c}")
        ln_rows(xf_t, xf_b, lnrows_t, lnrows_b, sq_t, sq_b, xbf_t, xbf_b, f"L1{c}")
        ln_apply(xf_t, xf_b, xln1_t, xln1_b, tmp_t, tmp_b, f"L1a{c}")
        ffn(xln1_t, xln1_b, xf_t, xf_b, x1_t, x1_b, ffnw1, f"F1{c}")
        for k in range(KT):
            dma("s", sp_x1.ap()[k][:, cs], x1_t[k], reads=[x1_b[k]], writes=[sb_spx1[k]], tag=f"sx1{c}")
        ln_rows(x1_t, x1_b, lnrows_t, lnrows_b, sq_t, sq_b, xbf_t, xbf_b, f"L2{c}")
        for k in range(KT):
            b.emit("v", lambda e, k=k: e.tensor_tensor(out=tmp_t, in0=x1_t[k],
                   in1=ps_bc0[:], op=ALU.mult),
                   reads=[x1_b[k], ps_bc0_b], writes=[tmp_b], tag=f"L2a{c}")
            b.emit("v", lambda e: e.tensor_tensor(out=xln2f_t, in0=tmp_t,
                   in1=ps_bc1[:], op=ALU.subtract),
                   reads=[tmp_b, ps_bc1_b], writes=[xln2f_b], tag=f"L2b{c}")
            b.emit("v", lambda e, k=k: e.tensor_copy(out=xln2_t[k], in_=xln2f_t),
                   reads=[xln2f_b], writes=[xln2_b[k]], tag=f"L2c{c}")
            dma("s", sp_x2.ap()[k][:, cs], xln2f_t, reads=[xln2f_b], writes=[sb_spx2[k]], tag=f"sx2{c}")
        for g0 in range(0, 2 * DT2, 4):
            wgi = (g0 // 4) % 2
            for k in range(KT):
                dma("s" if k % 2 == 0 else "a", wg_t[wgi][:, k * 512:(k + 1) * 512],
                    win_bf.ap()[k][:, g0 * 128:(g0 + 4) * 128],
                    reads=[WB["winbf"]], writes=[wg_b[wgi]], tag=f"lw{c}")
            for mi in range(4):
                m = g0 + mi
                ps, psb = ps_mm[mi % 2], ps_mm_b[mi % 2]
                for k in range(KT):
                    b.emit("p", lambda e, k=k, mi=mi, wgi=wgi, ps=ps:
                           e.matmul(out=ps[:],
                                    lhsT=wg_t[wgi][:, k * 512 + mi * 128:k * 512 + (mi + 1) * 128],
                                    rhs=xln2_t[k], start=(k == 0), stop=(k == KT - 1)),
                           reads=[xln2_b[k], wg_b[wgi]],
                           war=[psb],
                           tag=f"wi{c}m{m}")
                b.close_group("p", psb)
                if m < DT2:
                    xt, xtb = xin_t[m % 2], xin_b[m % 2]
                    b.emit("v", lambda e, xt=xt, ps=ps: e.tensor_copy(
                           out=xt[:, 3:3 + TC], in_=ps[:]),
                           reads=[psb], writes=[xtb], tag=f"xc{c}")
                    b.emit("v", lambda e, xt=xt, m=m: e.tensor_copy(
                           out=xt[:, 0:3], in_=tails_s[:, m * 4:m * 4 + 3]),
                           reads=[WB["tails"]], writes=[xtb], tag=f"tl{c}")
                    psc, pscb = ps_mm[(mi + 2) % 4], ps_mm_b[(mi + 2) % 4]
                    for tap in range(4):
                        b.emit("p", lambda e, m=m, tap=tap, xt=xt, psc=psc:
                               e.matmul(out=psc[:],
                                        lhsT=cdiag_s[:, (m * 4 + tap) * 128:(m * 4 + tap + 1) * 128],
                                        rhs=xt[:, tap:tap + TC],
                                        start=(tap == 0), stop=(tap == 3)),
                               reads=[xtb],
                               war=[pscb],
                               tag=f"cv{c}m{m}")
                    b.close_group("p", pscb)
                    b.emit("v", lambda e, xt=xt, m=m: e.tensor_copy(
                           out=tails_s[:, m * 4:m * 4 + 3], in_=xt[:, TC:TC + 3]),
                           reads=[xtb], writes=[WB["tails"]], tag=f"tu{c}")
                    b.emit("a", lambda e, m=m, psc=psc: e.activation(
                           out=u_t[m], in_=psc[:], func=AF.Silu,
                           bias=vcol("conv_b", m)),
                           reads=[pscb, WB["vec"]], writes=[u_b[m]], tag=f"si{c}")
                    dma("s", sp_u.ap()[m][:, cs], u_t[m], reads=[u_b[m]],
                        writes=[sb_spu[m]], tag=f"su{c}")
                else:
                    dz = m - DT2
                    b.emit("a", lambda e, ps=ps: e.activation(out=gz_t, in_=ps[:],
                           func=AF.Silu), reads=[psb], writes=[gz_b], tag=f"sz{c}")
                    dma("s", sp_g.ap()[dz][:, cs], gz_t, reads=[gz_b], writes=[sb_spg[dz]], tag=f"sg{c}")
        for d in range(DT2):
            b.emit("p", lambda e, d=d: e.matmul(out=ps_small[0:96, :],
                   lhsT=wx_s[:, d * 96:(d + 1) * 96], rhs=u_t[d],
                   start=(d == 0), stop=(d == DT2 - 1)),
                   reads=[u_b[d]],
                   war=[ps_small_b],
                   tag=f"wx{c}")
        b.close_group("p", ps_small_b)
        b.emit("v", lambda e: e.tensor_copy(out=dtb_t, in_=ps_small[0:64, :]),
               reads=[ps_small_b], writes=[dtb_b], tag=f"dtb{c}")
        b.emit("v", lambda e: e.tensor_copy(out=bct_t, in_=ps_small[64:96, :]),
               reads=[ps_small_b], writes=[bct_b], tag=f"bct{c}")
        dma("s", sp_bc.ap()[:, cs], bct_t, reads=[bct_b], writes=[sb_spbc], tag=f"sbc{c}")
        for g0 in range(0, DT2, 4):
            for di in range(4):
                d = g0 + di
                ps, psb = ps_mm[di], ps_mm_b[di]
                b.emit("p", lambda e, d=d, ps=ps: e.matmul(out=ps[:],
                       lhsT=wdt_s[:, d * 128:(d + 1) * 128], rhs=dtb_t,
                       start=True, stop=True),
                       reads=[dtb_b], war=[psb], tag=f"wd{c}")
                b.close_group("p", psb)
            for di in range(4):
                d = g0 + di
                ps, psb = ps_mm[di], ps_mm_b[di]
                b.emit("v", lambda e, d=d, ps=ps: e.tensor_scalar(out=cbuf_t,
                       in0=ps[:], scalar1=vcol("b_dt", d), scalar2=SM30,
                       op0=ALU.add, op1=ALU.min),
                       reads=[psb, WB["vec"]], writes=[cbuf_b], tag=f"dm{c}")
                b.emit("a", lambda e: e.activation(out=ebuf_t, in_=cbuf_t,
                       func=AF.Exp), reads=[cbuf_b], writes=[ebuf_b], tag=f"de{c}")
                b.emit("a", lambda e: e.activation(out=lbuf_t, in_=ebuf_t,
                       func=AF.Ln, bias=1.0), reads=[ebuf_b], writes=[lbuf_b],
                       tag=f"dl{c}")
                b.emit("v", lambda e, d=d, ps=ps: e.scalar_tensor_tensor(
                       out=dl_t, in0=ps[:], scalar=vcol("b_dt", d), in1=lbuf_t,
                       op0=ALU.add, op1=ALU.max),
                       reads=[psb, lbuf_b, WB["vec"]], writes=[dl_b], tag=f"dx{c}")
                dma("s", sp_d.ap()[d][:, cs], dl_t, reads=[dl_b], writes=[sb_spd[d]], tag=f"sd{c}")
                b.emit("v", lambda e, d=d: e.tensor_tensor(out=du2_t, in0=dl_t,
                       in1=u_t[d], op=ALU.mult),
                       reads=[dl_b, u_b[d]], writes=[du2_b], tag=f"duz{c}")
                dma("a", sp_du.ap()[d][:, cs], du2_t, reads=[du2_b], writes=[sb_spdu[d]], tag=f"sdu{c}")

    ABF.reset(); AFA.reset()
    b.barrier()

    ABF.mark(); AFA.mark()
    NSP = L // 1024
    NCH2 = NCH // 2
    v_d = sp_d.ap().rearrange("d p (s n) -> (d p s) n", n=TC)
    v_du = sp_du.ap().rearrange("d p (s n) -> (d p s) n", n=TC)
    v_u = sp_u.ap().rearrange("d p (s n) -> (d p s) n", n=TC)
    v_g = sp_g.ap().rearrange("d p (s n) -> (d p s) n", n=TC)
    v_bc = sp_bc.ap().rearrange("r (c n) -> (r c) n", n=TC)
    v_x1 = sp_x1.ap().rearrange("k p (c n) -> (k p c) n", n=TC)
    v_x2 = sp_x2.ap().rearrange("k p (c n) -> (k p c) n", n=TC)
    v_yo = yout.rearrange("k p (c n) -> (k p c) n", n=TC)

    def gdt(dt, j):
        c0 = dt * 8 + j
        return gidx_t.ap()[:, c0:c0 + 1]

    def gbc(r, j):
        c0 = 128 + r * 8 + j
        return gidx_t.ap()[:, c0:c0 + 1]

    def gk(k, j):
        c0 = 384 + k * 8 + j
        return gidx_t.ap()[:, c0:c0 + 1]

    def ig(out, view, idx, reads=(), writes=(), tag=""):
        return b.emit("g", lambda e, o=out, v=view, i=idx: e.indirect_dma_start(
            out=o, out_offset=None, in_=v,
            in_offset=bass.IndirectOffsetOnAxis(ap=i, axis=0)),
            reads=reads, writes=writes, dma=True, tag=tag)

    def isc(view, idx, in_, reads=(), writes=(), tag=""):
        return b.emit("g", lambda e, v=view, i=idx, s=in_: e.indirect_dma_start(
            out=v, out_offset=bass.IndirectOffsetOnAxis(ap=i, axis=0),
            in_=s, in_offset=None),
            reads=reads, writes=writes, dma=True, tag=tag)

    Bbc_t = [ABF.alloc(TC) for _ in range(NS)]
    Bbc_b = [Buf(f"Bb{s}") for s in range(NS)]
    Cbc_t = [ABF.alloc(TC) for _ in range(NS)]
    Cbc_b = [Buf(f"Cb{s}") for s in range(NS)]
    bchain = [Buf(f"bch{i}") for i in range(4)]
    cchain = [Buf(f"cch{i}") for i in range(4)]
    dld_t = [ABF.alloc(TC) for _ in range(2)]
    dld_b = [Buf(f"dld{i}") for i in range(2)]
    duld_t = [ABF.alloc(TC) for _ in range(2)]
    duld_b = [Buf(f"duld{i}") for i in range(2)]
    _u1 = ABF.alloc(TC); uld_t = [_u1, _u1]
    _u1b = Buf("uld"); uld_b = [_u1b, _u1b]
    _g1 = ABF.alloc(TC); gld_t = [_g1, _g1]
    _g1b = Buf("gld"); gld_b = [_g1b, _g1b]
    wbuf_t = [ABF.alloc(TC) for _ in range(2)]
    wbuf_b = [Buf(f"wb{i}") for i in range(2)]
    scr_t = AFA.alloc(TC)
    scr_b = Buf("scr")
    dA_t = [AFA.alloc(TC) for _ in range(4)]
    dA_b = [Buf(f"dA{i}") for i in range(4)]
    dBu_t = [ABF.alloc(TC) for _ in range(4)]
    dBu_b = [Buf(f"dBu{i}") for i in range(4)]
    h_all = ABF.alloc(4 * TC)
    h_t = [h_all[:, i * TC:(i + 1) * TC] for i in range(4)]
    h_b = [Buf(f"h{i}") for i in range(4)]
    h_last = h_all.tensor.ap()

    p_t = [ABF.alloc(TC) for _ in range(4)]
    p_b = [Buf(f"p{s}") for s in range(4)]
    ym_t = [ABF.alloc(TC) for _ in range(DT2)]
    ym_b = [Buf(f"ym{d}") for d in range(DT2)]
    cdl_t = AFA.alloc(TC)
    cdl_b = Buf("cdl")
    rb_t2 = AFA.alloc(TC)
    rb_b2 = Buf("rb2")
    gam_t = AFA.alloc(NS)
    gam_b = Buf("gam")
    part_t = AFA.alloc(NS)
    part_b = Buf("part")
    t1_t = AFA.alloc(TC)
    t1_b = Buf("t1")
    wo2_t = [ABF.alloc(DT2 * 256) for _ in range(2)]
    wo2_b = [Buf(f"wo2{i}") for i in range(2)]
    x1l_t = [AFA.alloc(TC) for _ in range(2)]
    x1l_b = [Buf(f"x1l{i}") for i in range(2)]
    xl2l_t = [AFA.alloc(TC) for _ in range(2)]
    xl2l_b = [Buf(f"xl2l{i}") for i in range(2)]
    h3_t = [AFA.alloc(TC) for _ in range(KT)]
    h3_b = [Buf(f"h3{k}") for k in range(KT)]
    x2_t = [AFA.alloc(TC) for _ in range(KT)]
    x2_b = [Buf(f"x2{k}") for k in range(KT)]
    xln4_t = [ym_t[k] for k in range(KT)]
    xln4_b = [ym_b[k] for k in range(KT)]
    outst_t = AFA.alloc(TC)
    outst_b = Buf("outst")
    xbf2_t = ABF.alloc(TC)
    xbf2_b = Buf("xbf2")
    sq2_t = ABF.alloc(TC)
    sq2_b = Buf("sq2")
    lnrows2_t = [AFA.alloc(5 * TC, rows=1) for _ in range(2)]
    lnrows2_b = [Buf(f"lnrows2{i}") for i in range(2)]
    tmp2_t = AFA.alloc(TC)
    tmp2_b = Buf("tmp2")
    ffnw2 = ffnw1
    sb_yo = [Buf(f"yo{k}") for k in range(4)] * 2

    b.emit("v", lambda e: e.memset(hrun_s, 0.0), writes=[WB["hrun"]])

    for c in range(NCH2):
        cs = slice(c * TC, (c + 1) * TC)
        for s in range(NS):
            q = "s" if s % 2 == 0 else "a"
            dma(q, Bbc_t[s], sp_bc.ap()[s, cs].partition_broadcast(128),
                writes=[bchain[s % 4], Bbc_b[s]], tag=f"tBb{c}")
        for dt in range(DT2):
            sl = dt % 2
            dma("s", dld_t[sl], sp_d.ap()[dt][:, cs],
                writes=[dld_b[sl]], tag=f"tld{c}")
            dma("a", duld_t[sl], sp_du.ap()[dt][:, cs],
                writes=[duld_b[sl]], tag=f"tlu{c}")
            b.emit("v", lambda e, sl=sl: e.tensor_tensor_scan(
                   out=cdl_t, data0=dld_t[sl], data1=dld_t[sl],
                   initial=0.0, op0=ALU.add, op1=ALU.bypass),
                   reads=[dld_b[sl]], writes=[cdl_b], tag=f"tcs{c}")
            b.emit("v", lambda e: e.tensor_scalar(out=rb_t2, in0=cdl_t,
                   scalar1=cdl_t[:, TC - 1:TC], scalar2=None, op0=ALU.subtract),
                   reads=[cdl_b], writes=[rb_b2], tag=f"tr{c}")
            b.emit("v", lambda e, dt=dt: e.tensor_scalar(
                   out=gam_t, in0=posA_s[:, dt * NS:(dt + 1) * NS],
                   scalar1=cdl_t[:, TC - 1:TC], scalar2=-1.0,
                   op0=ALU.mult, op1=ALU.mult),
                   reads=[cdl_b, WB["posA"]], writes=[gam_b], tag=f"tg{c}")
            b.emit("a", lambda e: e.activation(out=gam_t, in_=gam_t, func=AF.Exp),
                   reads=[gam_b], writes=[gam_b], tag=f"tge{c}")
            hsl = hrun_s[:, dt * NS:(dt + 1) * NS]
            b.emit("v", lambda e, hsl=hsl: e.tensor_tensor(out=hsl, in0=hsl,
                   in1=gam_t, op=ALU.mult),
                   reads=[gam_b, WB["hrun"]], writes=[WB["hrun"]], tag=f"thg{c}")
            for s in range(NS):
                r2 = s % 4
                b.emit("a", lambda e, dt=dt, s=s, r2=r2 % 2: e.activation(
                       out=wbuf_t[r2 % 2], in_=rb_t2, func=AF.Exp,
                       scale=posA_s[:, dt * NS + s:dt * NS + s + 1]),
                       reads=[rb_b2, WB["posA"]], writes=[wbuf_b[r2 % 2]], tag=f"tw{c}")
                b.emit("g", lambda e, s=s, sl=sl, r2=r2: e.tensor_tensor(
                       out=dBu_t[r2], in0=duld_t[sl], in1=Bbc_t[s],
                       op=ALU.mult),
                       reads=[duld_b[sl], Bbc_b[s]], writes=[dBu_b[r2]],
                       tag=f"tdb{c}")
                b.emit("v", lambda e, s=s, r2=r2: e.scalar_tensor_tensor(
                       out=scr_t, in0=wbuf_t[r2 % 2], scalar=1.0, in1=dBu_t[r2],
                       op0=ALU.mult, op1=ALU.mult,
                       accum_out=part_t[:, s:s + 1]),
                       reads=[wbuf_b[r2 % 2], dBu_b[r2]],
                       writes=[scr_b, part_b], tag=f"ttr{c}")
            hsl2 = hrun_s[:, dt * NS:(dt + 1) * NS]
            b.emit("v", lambda e, hsl2=hsl2: e.tensor_tensor(out=hsl2,
                   in0=hsl2, in1=part_t, op=ALU.add),
                   reads=[part_b, WB["hrun"]], writes=[WB["hrun"]],
                   tag=f"tha{c}")
    b.emit("v", lambda e: e.tensor_scalar(out=carry_s, in0=hrun_s,
           scalar1=smask_s[:, 0:1], scalar2=None, op0=ALU.mult),
           reads=[WB["hrun"], WB["smask"]], writes=[WB["carry"]], tag="cinit")

    for j in range(NCH2):
        for s in range(NS):
            ig(Bbc_t[s], v_bc, gbc(s, j),
               writes=[bchain[s % 4], Bbc_b[s]], tag=f"sBb{j}")
            ig(Cbc_t[s], v_bc, gbc(16 + s, j),
               writes=[cchain[s % 4], Cbc_b[s]], tag=f"sCb{j}")
        if True:
            for dt in range(DT2):
                sl = dt % 2
                psy, psyb = (ps_bc0, ps_bc0_b) if dt % 2 == 0 else (ps_bc1, ps_bc1_b)
                ig(dld_t[sl], v_d, gdt(dt, j), writes=[dld_b[sl]], tag=f"gd{j}")
                ig(duld_t[sl], v_du, gdt(dt, j), writes=[duld_b[sl]], tag=f"gdu{j}")
                ig(uld_t[sl], v_u, gdt(dt, j), writes=[uld_b[sl]], tag=f"gu{j}")
                ig(gld_t[sl], v_g, gdt(dt, j), writes=[gld_b[sl]], tag=f"gg{j}")
                for s in range(NS):
                    r2 = s % 4
                    b.emit("a", lambda e, dt=dt, s=s, sl=sl, r2=r2:
                           e.activation(out=dA_t[r2], in_=dld_t[sl],
                                        func=AF.Exp,
                                        scale=negA_s[:, dt * NS + s:dt * NS + s + 1]),
                           reads=[dld_b[sl], WB["negA"]], writes=[dA_b[r2]],
                           tag=f"dA{j}")
                    b.emit("g", lambda e, s=s, sl=sl, r2=r2: e.tensor_tensor(
                           out=dBu_t[r2], in0=duld_t[sl], in1=Bbc_t[s],
                           op=ALU.mult),
                           reads=[duld_b[sl], Bbc_b[s]], writes=[dBu_b[r2]],
                           tag=f"dB{j}")
                    b.emit("v", lambda e, dt=dt, s=s, r2=r2: e.tensor_tensor_scan(
                           out=h_t[r2], data0=dA_t[r2], data1=dBu_t[r2],
                           initial=carry_s[:, dt * NS + s:dt * NS + s + 1],
                           op0=ALU.mult, op1=ALU.add),
                           reads=[dA_b[r2], dBu_b[r2], WB["carry"]],
                           writes=[h_b[r2]], tag=f"ts{j}")
                    if s % 4 == 3:
                        hoff = h_t[0].offset % (128 * 1024 * 64)
                        b.emit("v", lambda e, dt=dt, s=s: e.tensor_copy(
                               out=carry_s[:, dt * NS + s - 3:dt * NS + s + 1],
                               in_=h_all[:, TC - 1::TC]),
                               reads=[h_b[0], h_b[1], h_b[2], h_b[3]],
                               writes=[WB["carry"]], tag=f"cr{j}", cost=200)
                    b.emit("v", lambda e, s=s, r2=r2: e.tensor_tensor(
                           out=p_t[s % 4],
                           in0=h_t[r2], in1=Cbc_t[s], op=ALU.mult),
                           reads=[h_b[r2], Cbc_b[s]], writes=[p_b[s % 4]],
                           tag=f"cm{j}")
                    b.emit("p", lambda e, s=s, psy=psy: e.matmul(out=psy[:],
                           lhsT=ident_bf, rhs=p_t[s % 4], start=(s == 0),
                           stop=(s == NS - 1)),
                           reads=[p_b[s % 4]],
                           war=[psyb],
                           tag=f"sm{j}")
                b.close_group("p", psyb)
                b.emit("v", lambda e, dt=dt, sl=sl, psy=psy:
                       e.scalar_tensor_tensor(
                       out=t1_t, in0=uld_t[sl], scalar=vcol("D", dt),
                       in1=psy[:], op0=ALU.mult, op1=ALU.add),
                       reads=[uld_b[sl], psyb, WB["vec"]], writes=[t1_b],
                       tag=f"y1{j}")
                b.emit("v", lambda e, dt=dt, sl=sl: e.tensor_tensor(
                       out=ym_t[dt], in0=t1_t, in1=gld_t[sl], op=ALU.mult),
                       reads=[t1_b, gld_b[sl]], writes=[ym_b[dt]], tag=f"y2{j}")
            for m in range(KT):
                wgi = m % 2
                if m % 2 == 0:
                    for d in range(DT2):
                        dma("s" if d % 2 == 0 else "a",
                            wo2_t[(m // 2) % 2][:, d * 256:(d + 1) * 256],
                            wout_bf.ap()[d][:, m * 128:(m + 2) * 128],
                            reads=[WB["woutbf"]], writes=[wo2_b[(m // 2) % 2]],
                            tag=f"lwo{j}")
                ps, psb = ps_mm[m % 2], ps_mm_b[m % 2]
                for d in range(DT2):
                    b.emit("p", lambda e, d=d, m=m, ps=ps:
                           e.matmul(out=ps[:],
                                    lhsT=wo2_t[(m // 2) % 2][:, d * 256 + (m % 2) * 128:d * 256 + (m % 2 + 1) * 128],
                                    rhs=ym_t[d], start=(d == 0),
                                    stop=(d == DT2 - 1)),
                           reads=[ym_b[d], wo2_b[(m // 2) % 2]],
                           war=[psb],
                           tag=f"wo{j}m{m}")
                b.close_group("p", psb)
                mi = m % 2
                ig(xl2l_t[mi], v_x2, gk(m, j), writes=[xl2l_b[mi]], tag=f"lx2{j}")
                b.emit("v", lambda e, m=m, mi=mi, ps=ps: e.tensor_tensor(
                       out=h3_t[m], in0=ps[:], in1=xl2l_t[mi], op=ALU.add),
                       reads=[psb, xl2l_b[mi]], writes=[h3_b[m]], tag=f"h3{j}")
            ln_rows(h3_t, h3_b, lnrows2_t, lnrows2_b, sq2_t, sq2_b, xbf2_t,
                    xbf2_b, f"L3{j}")
            for k in range(KT):
                kl = k % 2
                ig(x1l_t[kl], v_x1, gk(k, j), writes=[x1l_b[kl]], tag=f"lx1{j}")
                b.emit("v", lambda e, k=k: e.tensor_tensor(out=tmp2_t,
                       in0=h3_t[k], in1=ps_bc0[:], op=ALU.mult),
                       reads=[h3_b[k], ps_bc0_b], writes=[tmp2_b], tag=f"L3a{j}")
                b.emit("v", lambda e: e.tensor_tensor(out=tmp2_t, in0=tmp2_t,
                       in1=ps_bc1[:], op=ALU.subtract),
                       reads=[tmp2_b, ps_bc1_b], writes=[tmp2_b], tag=f"L3b{j}")
                b.emit("v", lambda e, k=k, kl=kl: e.scalar_tensor_tensor(
                       out=x2_t[k], in0=tmp2_t, scalar=vcol("gamma", k),
                       in1=x1l_t[kl], op0=ALU.mult, op1=ALU.add),
                       reads=[tmp2_b, x1l_b[kl], WB["vec"]], writes=[x2_b[k]],
                       tag=f"L3g{j}")
                b.emit("v", lambda e, k=k: e.tensor_scalar(out=x2_t[k],
                       in0=x2_t[k], scalar1=vcol("beta", k), scalar2=None,
                       op0=ALU.add),
                       reads=[x2_b[k], WB["vec"]], writes=[x2_b[k]], tag=f"L3bt{j}")
            ln_rows(x2_t, x2_b, lnrows2_t, lnrows2_b, sq2_t, sq2_b, xbf2_t,
                    xbf2_b, f"L4{j}")
            ln_apply(x2_t, x2_b, xln4_t, xln4_b, tmp2_t, tmp2_b, f"L4a{j}")
            ffn(xln4_t, xln4_b, x2_t, x2_b, x2_t, x2_b, ffnw2, f"F2{j}")
            for k in range(KT):
                isc(v_yo, gk(k, j), x2_t[k], reads=[x2_b[k]],
                    writes=[sb_yo[k]], tag=f"out{j}")

    ABF.reset(); AFA.reset()
    b.fence()
    b.emit("g", lambda e: e.nop(), wait_all_dma=True, tag="final")
    order = b.schedule()
    b.lower(order)
    b.finalize(order)
    b.coalesce_waits(order)

    with ctx:
        with ExitStack() as sctx:
            sems = {}
            for n in b.all_sems:
                sems[n] = sctx.enter_context(nc.semaphore(n))
            with nc.Block() as block:
                def mk(eng_key):
                    def run(engine):
                        for op in order:
                            if op.eng != eng_key:
                                continue
                            for s, v in op.waits:
                                engine.wait_ge(sems[s], v)
                            inst = op.fn(engine)
                            if op.inc is not None and inst is not None:
                                inst.then_inc(sems[op.inc[0]], op.inc[1])
                    return run
                block.tensor(mk("p"))
                block.vector(mk("v"))
                block.scalar(mk("a"))
                block.gpsimd(mk("g"))
                block.sync(mk("s"))
    return nc


def prep_inputs(inputs, L=8192):
    f = np.ascontiguousarray
    W_in = np.asarray(inputs["L0_attn_W_in"])
    conv_w = np.asarray(inputs["L0_attn_conv_w"])
    d = {}
    d["win"] = f(W_in.reshape(KT, 128, 2 * DIN))
    d["bwT"] = f(np.asarray(inputs["L0_ffn_base_w"]).T.reshape(KT, 128, DM))
    d["rwT"] = f(np.asarray(inputs["L0_ffn_router_w"]).T.reshape(KT, 128, NE))
    d["lA"] = f(np.asarray(inputs["L0_ffn_lora_A"]).reshape(KT, 128, NR))
    d["lB"] = f(np.asarray(inputs["L0_ffn_lora_B"]).reshape(NE * NR, DM))
    d["wx"] = f(np.asarray(inputs["L0_attn_W_x"]).reshape(DT2, 128, 96))
    d["wdt"] = f(np.asarray(inputs["L0_attn_W_dt"]))
    d["wout"] = f(np.asarray(inputs["L0_attn_W_out"]).reshape(DT2, 128, DM))
    cd = np.zeros((DT2 * 4, 128, 128), np.float32)
    idx = np.arange(128)
    for dt in range(DT2):
        for k in range(4):
            cd[dt * 4 + k, idx, idx] = conv_w[dt * 128 + idx, k]
    d["cdiag"] = cd
    d["alog"] = f(np.asarray(inputs["L0_attn_A_log"]).reshape(DT2, 128, NS))
    vec = np.zeros((128, 64), np.float32)
    vec[:, 0:16] = np.asarray(inputs["L0_attn_conv_b"]).reshape(DT2, 128).T
    vec[:, 16:32] = np.asarray(inputs["L0_attn_b_dt"]).reshape(DT2, 128).T
    vec[:, 32:48] = np.asarray(inputs["L0_attn_D"]).reshape(DT2, 128).T
    vec[:, 48:56] = np.asarray(inputs["L0_attn_gamma"]).reshape(KT, 128).T
    vec[:, 56:64] = np.asarray(inputs["L0_attn_beta"]).reshape(KT, 128).T
    d["vec"] = vec
    d["bbrow"] = f(np.asarray(inputs["L0_ffn_base_b"]).reshape(1, DM))
    d["rbrow"] = f(np.asarray(inputs["L0_ffn_router_b"]).reshape(1, NE))
    d["ident_in"] = np.eye(128, dtype=np.float32)
    p = np.arange(128, dtype=np.uint32)
    nch = L // TC
    nc2 = nch // 2
    for sigma in (0, 1):
        gi = np.zeros((128, 448), np.uint32)
        for dt in range(16):
            for j in range(nc2):
                gi[:, dt * 8 + j] = (dt * 128 + p) * nch + (sigma * nc2 + j)
        for r in range(32):
            for j in range(nc2):
                gi[:, 128 + r * 8 + j] = r * nch + (sigma * nc2 + j)
        for k in range(8):
            for j in range(nc2):
                gi[:, 384 + k * 8 + j] = (k * 128 + p) * nch + (sigma * nc2 + j)
        d[f"gidx{sigma}"] = gi
        d[f"smask{sigma}"] = np.full((128, 1), float(sigma), np.float32)
    repl = np.zeros((2, 8, NE * NR), np.float32)
    for ee in range(NE):
        for r in range(NR):
            repl[0, ee, ee * NR + r] = 1.0
            repl[1, r, ee * NR + r] = 1.0
    d["repl_in"] = repl
    return d


def kernel(**inputs):
    x = np.asarray(inputs["L0_attn_x"])
    L = x.shape[1]
    shared = prep_inputs(inputs, L=L)
    nc = build_program(L=L)
    in_maps = []
    for core in range(8):
        bidx = core // 2
        sigma = core % 2
        m = {k: v for k, v in shared.items()
             if not k.startswith(("gidx", "smask"))}
        m["gidx"] = shared[f"gidx{sigma}"]
        m["smask"] = shared[f"smask{sigma}"]
        m["xT"] = np.ascontiguousarray(x[bidx].T).reshape(KT, 128, L)
        in_maps.append(m)
    res = run_bass_kernel_spmd(nc, in_maps, core_ids=list(range(8)))
    out = np.empty((x.shape[0], L, DM), np.float32)
    half = L // 2
    for bidx in range(x.shape[0]):
        y0 = np.asarray(res.results[2 * bidx]["yout"]).reshape(DM, L)
        y1 = np.asarray(res.results[2 * bidx + 1]["yout"]).reshape(DM, L)
        out[bidx, :half] = y0[:, :half].T
        out[bidx, half:] = y1[:, half:].T
    return out


# version 09
# speedup vs baseline: 1.1934x; 1.1423x over previous
import numpy as np
from contextlib import ExitStack

import concourse.bass as bass
import concourse.mybir as mybir
from concourse.bass_utils import run_bass_kernel_spmd

F32 = mybir.dt.float32
BF16 = mybir.dt.bfloat16
AF = mybir.ActivationFunctionType
ALU = mybir.AluOpType
AX = mybir.AxisListType

DM = 1024
DIN = 2048
NS = 16
DTR = 64
NE = 8
NR = 8
KT = DM // 128
DT2 = DIN // 128
EPS = 1e-5
SM30 = 30.0
TC = 512

COMPUTE = ("p", "v", "a", "g", "s")


class Buf:
    __slots__ = ("writer", "readers", "name", "dsem")

    def __init__(self, name=""):
        self.writer = None
        self.readers = []
        self.name = name
        self.dsem = {}


class Op:
    __slots__ = ("eng", "fn", "dma", "reads", "writes", "tag", "cost",
                 "epoch", "idx", "preds", "wait_all_dma", "start",
                 "waits", "inc", "dsem")


class B:

    ENGS = ("p", "v", "a", "g", "s")

    def __init__(self):
        self.ops = []
        self.epoch = 0

    def emit(self, eng, fn, reads=(), writes=(), dma=False, noinc=False,
             extra_waits=(), war=(), tag="", cost=700, wait_all_dma=False,
             dsem=None):
        op = Op()
        op.eng, op.fn, op.dma = eng, fn, dma
        op.reads = list(reads)
        op.writes = list(writes) + list(war)
        op.tag = tag
        op.cost = cost
        op.epoch = self.epoch
        op.wait_all_dma = wait_all_dma
        op.dsem = dsem
        op.idx = len(self.ops)
        op.preds = None
        self.ops.append(op)
        return op

    def fence(self):
        self.epoch += 1

    def barrier(self):
        self.fence()

    def close_group(self, eng, *bufs):
        pass

    def _build_graph(self):
        writer = {}
        readers = {}
        for op in self.ops:
            preds = set()
            for bb in op.reads:
                w = writer.get(id(bb))
                if w is not None:
                    preds.add(w)
            for bb in op.writes:
                w = writer.get(id(bb))
                if w is not None:
                    preds.add(w)
                for r in readers.get(id(bb), ()):
                    preds.add(r)
            for bb in op.writes:
                writer[id(bb)] = op.idx
                readers[id(bb)] = []
            for bb in op.reads:
                readers.setdefault(id(bb), []).append(op.idx)
            preds.discard(op.idx)
            op.preds = sorted(preds)

    COST = {"p": 320, "v": 750, "a": 780, "g": 1200, "s": 300}

    def _estimate_costs(self):
        for op in self.ops:
            if op.cost != 700:
                continue
            if op.dma:
                op.cost = 1200
            else:
                c = self.COST.get(op.eng, 700)
                t = op.tag
                if op.eng == "v":
                    if t.startswith("ts"):
                        c = 1300
                    elif t.startswith(("cm", "dB")):
                        c = 550
                    elif t.startswith("cr"):
                        c = 160
                    elif t.startswith("ttr"):
                        c = 1050
                op.cost = c

    def schedule(self):
        import heapq
        self._estimate_costs()
        self._build_graph()
        n = len(self.ops)
        nsucc_pred = [len(op.preds) for op in self.ops]
        succs = [[] for _ in range(n)]
        for op in self.ops:
            for p in op.preds:
                succs[p].append(op.idx)
        done_t = [0.0] * n
        eng_free = {e: 0.0 for e in self.ENGS}
        q_free = {e: 0.0 for e in ("s", "a", "g")}
        epoch_end = 0.0
        order = []
        by_epoch = {}
        for op in self.ops:
            by_epoch.setdefault(op.epoch, []).append(op)
        for ep in sorted(by_epoch):
            ops_e = by_epoch[ep]
            remaining = set(o.idx for o in ops_e)
            pend = {o.idx: sum(1 for p in self.ops[o.idx].preds
                               if self.ops[p].epoch == ep) for o in ops_e}
            ready = []
            for o in ops_e:
                if pend[o.idx] == 0:
                    heapq.heappush(ready, o.idx)
            scheduled_any = True
            while remaining:
                if not ready:
                    raise RuntimeError("scheduler deadlock (cyclic deps?)")
                cand = heapq.heappop(ready)
                op = self.ops[cand]
                start = max(epoch_end, eng_free[op.eng],
                            max((done_t[p] for p in op.preds), default=0.0))
                if op.dma:
                    issue = max(start, q_free[op.eng])
                    eng_free[op.eng] = issue + 150.0
                    q_free[op.eng] = issue + max(600.0, op.cost * 0.5)
                    done_t[cand] = issue + 1500.0 + op.cost
                else:
                    eng_free[op.eng] = start + op.cost
                    done_t[cand] = start + op.cost
                op.start = start
                order.append(op)
                remaining.discard(cand)
                for sidx in succs[cand]:
                    if self.ops[sidx].epoch == ep and sidx in remaining:
                        pend[sidx] -= 1
                        if pend[sidx] == 0:
                            heapq.heappush(ready, sidx)
            epoch_end = max(max(done_t[o.idx] for o in ops_e), epoch_end)
        return order

    def lower(self, order):
        self.dsem_n = 0
        self.posmap = {}
        cnt = {k: 0 for k in ("p", "v", "a", "g")}
        pos = {}
        wm = {e: {} for e in self.ENGS}

        def dma_sem(op):
            if op.dsem is None:
                tgt = None
                for bb in op.writes:
                    tgt = bb
                    break
                if tgt is None:
                    op.dsem = "dmisc_" + op.eng
                else:
                    if op.eng not in tgt.dsem:
                        tgt.dsem[op.eng] = f"d{self.dsem_n}_{op.eng}"
                        self.dsem_n += 1
                    op.dsem = tgt.dsem[op.eng]
            if op.dsem not in cnt:
                cnt[op.dsem] = 0
            return op.dsem

        for op in order:
            semkey = dma_sem(op) if op.dma else op.eng
            cnt[semkey] += 16 if op.dma else 1
            pos[op.idx] = (semkey, cnt[semkey])
            waits = {}
            for p in op.preds:
                s, v = pos[p]
                if s == "p" and op.eng == "p":
                    continue
                if wm[op.eng].get(s, 0) >= v:
                    continue
                waits[s] = max(waits.get(s, 0), v)
            if op.wait_all_dma:
                for s, v in cnt.items():
                    if s not in ("p", "v", "a", "g") and v > waits.get(s, 0):
                        waits[s] = v
            op.waits = sorted(waits.items())
            op.inc = (semkey, 16 if op.dma else 1)
            self.posmap[(semkey, cnt[semkey])] = op
            for s, v in waits.items():
                wm[op.eng][s] = v
        self.all_sems = sorted(cnt)
        return cnt

    def coalesce_waits(self, order, margin=4000.0):
        per_eng = {}
        for op in order:
            per_eng.setdefault(op.eng, []).append(op)
        for eng, ops in per_eng.items():
            carrier = None
            carrier_limit = 0
            for op in ops:
                if not op.waits:
                    continue
                keep = []
                hoist = []
                for s, v in op.waits:
                    p = self.posmap.get((s, v))
                    if (carrier is not None and p is not None
                            and p.start + p.cost + margin <= carrier.start
                            and carrier_limit < 24):
                        hoist.append((s, v))
                    else:
                        keep.append((s, v))
                if hoist and carrier is not None:
                    w = dict(carrier.waits)
                    for s, v in hoist:
                        w[s] = max(w.get(s, 0), v)
                    carrier.waits = sorted(w.items())
                    carrier_limit += len(hoist)
                op.waits = keep
                if keep:
                    carrier = op
                    carrier_limit = 0
                elif carrier is None:
                    carrier = op
                    carrier_limit = 0

    def finalize(self, order):
        cnt = {}
        epoch_counts = {}
        cur = order[0].epoch if order else 0
        for op in order:
            if op.epoch != cur:
                epoch_counts[cur] = dict(cnt)
                cur = op.epoch
            semkey = op.inc[0]
            cnt[semkey] = cnt.get(semkey, 0) + op.inc[1]
        seen = set()
        for op in order:
            key = (op.eng, op.epoch)
            if key in seen:
                continue
            seen.add(key)
            prev = op.epoch - 1
            if prev in epoch_counts:
                w = dict(op.waits)
                for s, v in epoch_counts[prev].items():
                    if v > 0 and w.get(s, 0) < v:
                        w[s] = v
                op.waits = sorted(w.items())


class Arena:
    def __init__(self, t, n):
        self.t = t
        self.n = n
        self.off = 0
        self.stack = []

    def alloc(self, cols, rows=128):
        off = (self.off + 3) & ~3
        assert off + cols <= self.n, f"arena overflow {off}+{cols}>{self.n}"
        self.off = off + cols
        return self.t.ap()[0:rows, off:off + cols]

    def mark(self):
        self.stack.append(self.off)

    def reset(self):
        self.off = self.stack.pop()


def build_program(L=8192):
    NCH = L // TC
    nc = bass.Bass("TRN2", target_bir_lowering=False, debug=False)

    xT = nc.dram_tensor("xT", [KT, 128, L], F32, kind="ExternalInput").ap()
    win = nc.dram_tensor("win", [KT, 128, 2 * DIN], F32, kind="ExternalInput").ap()
    bwT = nc.dram_tensor("bwT", [KT, 128, DM], F32, kind="ExternalInput").ap()
    rwT = nc.dram_tensor("rwT", [KT, 128, NE], F32, kind="ExternalInput").ap()
    lA = nc.dram_tensor("lA", [KT, 128, NR], F32, kind="ExternalInput").ap()
    lB = nc.dram_tensor("lB", [NE * NR, DM], F32, kind="ExternalInput").ap()
    wx = nc.dram_tensor("wx", [DT2, 128, 96], F32, kind="ExternalInput").ap()
    wdt = nc.dram_tensor("wdt", [DTR, DIN], F32, kind="ExternalInput").ap()
    wout = nc.dram_tensor("wout", [DT2, 128, DM], F32, kind="ExternalInput").ap()
    cdiag = nc.dram_tensor("cdiag", [DT2 * 4, 128, 128], F32, kind="ExternalInput").ap()
    alog = nc.dram_tensor("alog", [DT2, 128, NS], F32, kind="ExternalInput").ap()
    vec = nc.dram_tensor("vec", [128, 64], F32, kind="ExternalInput").ap()
    bbrow = nc.dram_tensor("bbrow", [1, DM], F32, kind="ExternalInput").ap()
    rbrow = nc.dram_tensor("rbrow", [1, NE], F32, kind="ExternalInput").ap()
    ident_in = nc.dram_tensor("ident_in", [128, 128], F32, kind="ExternalInput").ap()
    repl_in = nc.dram_tensor("repl_in", [2, 8, NE * NR], F32, kind="ExternalInput").ap()

    yout = nc.dram_tensor("yout", [KT, 128, L], F32, kind="ExternalOutput").ap()

    win_bf = nc.dram_tensor("win_bf", [KT, 128, 2 * DIN], BF16)
    wout_bf = nc.dram_tensor("wout_bf", [DT2, 128, DM], BF16)
    sp_d = nc.dram_tensor("sp_d", [DT2, 128, L], BF16)
    sp_u = nc.dram_tensor("sp_u", [DT2, 128, L], BF16)
    sp_bc = nc.dram_tensor("sp_bc", [32, L], BF16)
    sp_x1 = nc.dram_tensor("sp_x1", [KT, 128, L], F32)
    sp_x2 = nc.dram_tensor("sp_x2", [KT, 128, L], F32)
    sp_g = nc.dram_tensor("sp_g", [DT2, 128, L], BF16)
    sp_du = nc.dram_tensor("sp_du", [DT2, 128, L], BF16)
    gidx = nc.dram_tensor("gidx", [128, 448], mybir.dt.uint32, kind="ExternalInput").ap()
    smask = nc.dram_tensor("smask", [128, 1], F32, kind="ExternalInput").ap()

    b = B()
    ctx = ExitStack()
    abf_t = ctx.enter_context(nc.sbuf_tensor("abf", [128, 59 * 1024], BF16))
    af_t = ctx.enter_context(nc.sbuf_tensor("af32", [128, 22 * 1024], F32))
    ABF = Arena(abf_t, 59 * 1024)
    AFA = Arena(af_t, 22 * 1024)

    gidx_t = ctx.enter_context(nc.sbuf_tensor("gidxs", [128, 448], mybir.dt.uint32))
    ps_mm = [ctx.enter_context(nc.psum_tensor(f"psmm{i}", [128, TC], F32)) for i in range(4)]
    ps_bc0 = ctx.enter_context(nc.psum_tensor("psbc0", [128, TC], F32))
    ps_bc1 = ctx.enter_context(nc.psum_tensor("psbc1", [128, TC], F32))
    ps_small = ctx.enter_context(nc.psum_tensor("pssmall", [128, TC], F32))
    ps_geb = ctx.enter_context(nc.psum_tensor("psgeb", [8, TC], BF16))
    ps_mm_b = [Buf(f"psmm{i}") for i in range(4)]
    ps_bc0_b, ps_bc1_b = Buf("psbc0"), Buf("psbc1")
    ps_small_b = Buf("pssmall")
    ps_geb_b = Buf("psgeb")

    bwT_s = ABF.alloc(KT * DM)
    rwT_s = ABF.alloc(KT * NE)
    lA_s = ABF.alloc(KT * NR)
    lB_s = ABF.alloc(DM, rows=NE * NR)
    wdt_s = ABF.alloc(DIN, rows=DTR)
    ident_s = ABF.alloc(128)
    repl_s = ABF.alloc(2 * NE * NR, rows=8)
    ones_row = ABF.alloc(TC, rows=1)
    onescol = ABF.alloc(1)
    bb_s = ABF.alloc(DM, rows=1)
    rb_s = ABF.alloc(NE, rows=1)

    onesrow_f = AFA.alloc(128, rows=1)
    vec_s = AFA.alloc(64)
    negA_s = AFA.alloc(DT2 * NS)
    posA_s = AFA.alloc(DT2 * NS)
    carry_s = AFA.alloc(DT2 * NS)
    hrun_s = AFA.alloc(DT2 * NS)
    smask_s = AFA.alloc(4)
    idf_s = AFA.alloc(128)
    atmp = AFA.alloc(NS)

    CONSTS = [0.0, 1.0, -1.0, 0.5, -0.5, 2.0, EPS, SM30, 1.0 / DM]
    const_t = AFA.alloc(len(CONSTS))
    cb = Buf("consts")
    for ci, cv in enumerate(CONSTS):
        nc.const_aps.register(F32, cv, const_t[:, ci:ci + 1])

    WB = {k: Buf(k) for k in ("bwT", "rwT", "lA", "lB", "wdt", "ident", "repl",
                              "ones", "bb", "rb", "onesf", "vec", "negA",
                              "posA", "carry", "hrun", "smask", "gidx",
                              "tails", "winbf", "woutbf", "wx",
                              "cdiag", "idf", "atmp")}

    def dma(eng, out, in_, reads=(), writes=(), tag="", dsem=None):
        return b.emit(eng, lambda e, o=out, i=in_: e.dma_start(out=o, in_=i),
                      reads=reads, writes=writes, dma=True, tag=tag, dsem=dsem)

    for k in range(KT):
        dma("g", bwT_s[:, k * DM:(k + 1) * DM], bwT[k], dsem="boot_g")
        dma("g", rwT_s[:, k * NE:(k + 1) * NE], rwT[k], dsem="boot_g")
        dma("g", lA_s[:, k * NR:(k + 1) * NR], lA[k], dsem="boot_g")
    dma("g", lB_s, lB, dsem="boot_g")
    dma("g", wdt_s, wdt, dsem="boot_g")
    dma("g", ident_s, ident_in, dsem="boot_g")
    dma("s", idf_s, ident_in, dsem="boot_s")
    dma("g", repl_s[0:8, 0:64], repl_in[0], dsem="boot_g")
    dma("g", repl_s[0:8, 64:128], repl_in[1], dsem="boot_g")
    dma("g", bb_s, bbrow, dsem="boot_g")
    dma("g", rb_s, rbrow, dsem="boot_g")
    dma("s", vec_s, vec, dsem="boot_s")
    dma("s", gidx_t.ap(), gidx, dsem="boot_s")
    dma("s", smask_s[:, 0:1], smask, dsem="boot_s")

    for ci, cv in enumerate(CONSTS):
        b.emit("v", lambda e, ci=ci, cv=cv: e.memset(const_t[:, ci:ci + 1], cv),
               writes=[cb])
    b.emit("v", lambda e: e.memset(ones_row, 1.0), writes=[WB["ones"]])
    b.emit("v", lambda e: e.memset(onescol, 1.0), writes=[WB["ones"]])
    b.emit("v", lambda e: e.memset(onesrow_f, 1.0), writes=[WB["onesf"]])
    b.emit("v", lambda e: e.memset(carry_s, 0.0), writes=[WB["carry"]])
    b.fence()
    for d in range(DT2):
        slp = posA_s[:, d * NS:(d + 1) * NS]
        sln = negA_s[:, d * NS:(d + 1) * NS]
        dma("s", atmp, alog[d], writes=[WB["atmp"]])
        b.emit("a", lambda e, slp=slp: e.activation(out=slp, in_=atmp, func=AF.Exp),
               reads=[WB["atmp"]], writes=[WB["posA"]])
        b.emit("a", lambda e, slp=slp, sln=sln: e.mul(out=sln, in_=slp, mul=-1.0),
               reads=[WB["posA"]], writes=[WB["negA"]])

    ident8f = idf_s[0:8, 0:8]
    ident_bf = ident_s[0:128, 0:128]

    def vcol(name, j):
        base = dict(conv_b=0, b_dt=16, D=32, gamma=48, beta=56)[name]
        return vec_s[:, base + j:base + j + 1]

    _lnrot = [0]

    def ln_rows(src_tiles, src_bufs, lnrows_sets, lnrows_bs, sq_t, sq_b, xbf_t, xbf_b, tag):
        _lnrot[0] ^= 1
        lnrows = lnrows_sets[_lnrot[0]] if isinstance(lnrows_sets, (list, tuple)) else lnrows_sets
        lnrows_b = lnrows_bs[_lnrot[0]] if isinstance(lnrows_bs, (list, tuple)) else lnrows_bs
        for k in range(KT):
            b.emit("v", lambda e, k=k: e.tensor_copy(out=xbf_t, in_=src_tiles[k]),
                   reads=[src_bufs[k]], writes=[xbf_b], tag=tag + "cast")
            b.emit("a", lambda e, k=k: e.activation(out=sq_t, in_=src_tiles[k],
                   func=AF.Square), reads=[src_bufs[k]], writes=[sq_b], tag=tag + "sq")
            b.emit("p", lambda e, st=(k == 0), last=(k == KT - 1): e.matmul(
                   out=ps_small[0:1, :], lhsT=onescol, rhs=xbf_t, start=st,
                   stop=last),
                   reads=[xbf_b, WB["ones"]] if k == 0 else [xbf_b],
                   war=[ps_small_b],
                   tag=tag + "mmx")
            b.emit("p", lambda e, st=(k == 0), last=(k == KT - 1): e.matmul(
                   out=ps_bc0[0:1, :], lhsT=onescol, rhs=sq_t, start=st,
                   stop=last),
                   reads=[sq_b],
                   war=[ps_bc0_b],
                   tag=tag + "mmq")
        b.close_group("p", ps_small_b, ps_bc0_b)
        mu = lnrows[0:1, 0:TC]
        m2 = lnrows[0:1, TC:2 * TC]
        var = lnrows[0:1, 2 * TC:3 * TC]
        arow = lnrows[0:1, 3 * TC:4 * TC]
        brow = lnrows[0:1, 4 * TC:5 * TC]
        b.emit("v", lambda e: e.tensor_scalar(out=mu, in0=ps_small[0:1, :],
               scalar1=1.0 / DM, scalar2=None, op0=ALU.mult),
               reads=[ps_small_b], writes=[lnrows_b], tag=tag)
        b.emit("v", lambda e: e.tensor_scalar(out=m2, in0=ps_bc0[0:1, :],
               scalar1=1.0 / DM, scalar2=None, op0=ALU.mult),
               reads=[ps_bc0_b], writes=[lnrows_b], tag=tag)
        b.emit("v", lambda e: e.tensor_tensor(out=var, in0=mu, in1=mu, op=ALU.mult),
               reads=[lnrows_b], writes=[lnrows_b], tag=tag)
        b.emit("v", lambda e: e.tensor_tensor(out=var, in0=m2, in1=var,
               op=ALU.subtract), reads=[lnrows_b], writes=[lnrows_b], tag=tag)
        b.emit("a", lambda e: e.activation(out=arow, in_=var, func=AF.Ln, bias=EPS),
               reads=[lnrows_b], writes=[lnrows_b], tag=tag)
        b.emit("a", lambda e: e.activation(out=arow, in_=arow, func=AF.Exp, scale=-0.5),
               reads=[lnrows_b], writes=[lnrows_b], tag=tag)
        b.emit("v", lambda e: e.tensor_tensor(out=brow, in0=mu, in1=arow, op=ALU.mult),
               reads=[lnrows_b], writes=[lnrows_b], tag=tag)
        b.emit("p", lambda e: e.matmul(out=ps_bc0[:], lhsT=onesrow_f, rhs=arow,
               start=True, stop=True),
               reads=[lnrows_b, WB["onesf"]], war=[ps_bc0_b], tag=tag + "bc0")
        b.close_group("p", ps_bc0_b)
        b.emit("p", lambda e: e.matmul(out=ps_bc1[:], lhsT=onesrow_f, rhs=brow,
               start=True, stop=True),
               reads=[lnrows_b], war=[ps_bc1_b], tag=tag + "bc1")
        b.close_group("p", ps_bc1_b)

    def ln_apply(src_tiles, src_bufs, out_tiles, out_bufs, tmp_t, tmp_b, tag):
        for k in range(KT):
            b.emit("v", lambda e, k=k: e.tensor_tensor(out=tmp_t, in0=src_tiles[k],
                   in1=ps_bc0[:], op=ALU.mult),
                   reads=[src_bufs[k], ps_bc0_b], writes=[tmp_b], tag=tag + "a")
            b.emit("v", lambda e, k=k: e.tensor_tensor(out=out_tiles[k], in0=tmp_t,
                   in1=ps_bc1[:], op=ALU.subtract),
                   reads=[tmp_b, ps_bc1_b], writes=[out_bufs[k]], tag=tag + "b")

    def make_ffn_work():
        return dict(
            lgt=(AFA.alloc(TC, rows=8), Buf("lgt")),
            mid=(ABF.alloc(TC, rows=8), Buf("mid")),
            gex=(AFA.alloc(32), Buf("gex")),
            gtb=(ABF.alloc(32), Buf("gtb")),
            ge=(ABF.alloc(TC, rows=8), Buf("ge")),
            grep=(ABF.alloc(TC, rows=64), Buf("grep")),
            wmid=(ABF.alloc(TC, rows=64), Buf("wmid")),
            rows=(AFA.alloc(2), Buf("rows")),
        )

    def ffn(xin_tiles, xin_bufs, res_tiles, res_bufs, out_tiles, out_bufs, W, tag):
        lgt_t, lgt_b = W["lgt"]
        mid_t, mid_b = W["mid"]
        gex_t, gex_b = W["gex"]
        gtb_t, gtb_b = W["gtb"]
        ge_t, ge_b = W["ge"]
        grep_t, grep_b = W["grep"]
        wmid_t, wmid_b = W["wmid"]
        rows_t, rows_b = W["rows"]
        for k in range(KT):
            b.emit("p", lambda e, k=k: e.matmul(out=ps_small[0:8, :],
                   lhsT=rwT_s[:, k * NE:(k + 1) * NE], rhs=xin_tiles[k],
                   start=(k == 0), stop=False),
                   reads=[xin_bufs[k]],
                   war=[ps_small_b],
                   tag=tag + "rout")
        b.emit("p", lambda e: e.matmul(out=ps_small[0:8, :], lhsT=rb_s,
               rhs=ones_row, start=False, stop=True),
               reads=[WB["rb"], WB["ones"]], tag=tag)
        for k in range(KT):
            b.emit("p", lambda e, k=k: e.matmul(out=ps_small[32:40, :],
                   lhsT=lA_s[:, k * NR:(k + 1) * NR], rhs=xin_tiles[k],
                   start=(k == 0), stop=(k == KT - 1), tile_position=(0, 32)),
                   reads=[xin_bufs[k]],
                   war=[ps_small_b],
                   tag=tag + "mid")
        b.close_group("p", ps_small_b)
        b.emit("v", lambda e: e.tensor_copy(out=lgt_t, in_=ps_small[0:8, :]),
               reads=[ps_small_b], writes=[lgt_b], tag=tag + "lgt")
        b.emit("v", lambda e: e.tensor_copy(out=mid_t, in_=ps_small[32:40, :]),
               reads=[ps_small_b], writes=[mid_b], tag=tag + "midc")
        nj = TC // 128
        for j in range(nj):
            b.emit("p", lambda e, j=j: e.transpose(
                   out=ps_small[:, j * 8:(j + 1) * 8],
                   in_=lgt_t[0:8, j * 128:(j + 1) * 128], identity=ident8f),
                   reads=[lgt_b, WB["idf"]] if j == 0 else [lgt_b],
                   war=[ps_small_b],
                   tag=tag + "tr")
        b.close_group("p", ps_small_b)
        mxr = rows_t[0:128, 0:1]
        smr = rows_t[0:128, 1:2]
        for j in range(nj):
            sl = ps_small[:, j * 8:(j + 1) * 8]
            b.emit("v", lambda e, sl=sl: e.reduce_max(out=mxr, in_=sl, axis=AX.X),
                   reads=[ps_small_b], writes=[rows_b], tag=tag + "mx")
            b.emit("v", lambda e: e.tensor_scalar(out=mxr, in0=mxr, scalar1=-1.0,
                   scalar2=None, op0=ALU.mult), reads=[rows_b], writes=[rows_b],
                   tag=tag)
            b.emit("a", lambda e, sl=sl, j=j: e.activation(
                   out=gex_t[:, j * 8:(j + 1) * 8], in_=sl, func=AF.Exp, bias=mxr),
                   reads=[ps_small_b, rows_b], writes=[gex_b], tag=tag + "exp")
            b.emit("v", lambda e, j=j: e.reduce_sum(out=smr,
                   in_=gex_t[:, j * 8:(j + 1) * 8], axis=AX.X),
                   reads=[gex_b], writes=[rows_b], tag=tag + "sum")
            b.emit("v", lambda e: e.reciprocal(out=smr, in_=smr),
                   reads=[rows_b], writes=[rows_b], tag=tag + "rcp")
            b.emit("v", lambda e, j=j: e.tensor_scalar(
                   out=gtb_t[:, j * 8:(j + 1) * 8],
                   in0=gex_t[:, j * 8:(j + 1) * 8], scalar1=smr, scalar2=None,
                   op0=ALU.mult), reads=[gex_b, rows_b], writes=[gtb_b],
                   tag=tag + "gmul")
        for j in range(nj):
            b.emit("p", lambda e, j=j: e.transpose(
                   out=ps_geb[0:8, j * 128:(j + 1) * 128],
                   in_=gtb_t[:, j * 8:(j + 1) * 8], identity=ident_bf),
                   reads=[gtb_b, WB["ident"]] if j == 0 else [gtb_b],
                   war=[ps_geb_b],
                   tag=tag + "trb")
        b.close_group("p", ps_geb_b)
        b.emit("v", lambda e: e.tensor_copy(out=ge_t, in_=ps_geb[0:8, :]),
               reads=[ps_geb_b], writes=[ge_b], tag=tag + "gec")
        b.emit("p", lambda e: e.matmul(out=ps_bc0[0:64, :],
               lhsT=repl_s[0:8, 64:128], rhs=mid_t, start=True, stop=True),
               reads=[mid_b, WB["repl"]], war=[ps_bc0_b], tag=tag + "repm")
        b.close_group("p", ps_bc0_b)
        b.emit("p", lambda e: e.matmul(out=ps_bc1[0:64, :],
               lhsT=repl_s[0:8, 0:64], rhs=ge_t, start=True, stop=True),
               reads=[ge_b], war=[ps_bc1_b], tag=tag + "repg")
        b.close_group("p", ps_bc1_b)
        b.emit("v", lambda e: e.tensor_copy(out=grep_t, in_=ps_bc1[0:64, :]),
               reads=[ps_bc1_b], writes=[grep_b], tag=tag + "grc")
        b.emit("v", lambda e: e.scalar_tensor_tensor(out=wmid_t,
               in0=ps_bc0[0:64, :], scalar=2.0, in1=grep_t, op0=ALU.mult,
               op1=ALU.mult), reads=[ps_bc0_b, grep_b], writes=[wmid_b],
               tag=tag + "wmid")
        for g0 in range(0, KT, 4):
            for m in range(g0, g0 + 4):
                ps, psb = ps_mm[m % 4], ps_mm_b[m % 4]
                for k in range(KT):
                    b.emit("p", lambda e, m=m, k=k, ps=ps: e.matmul(out=ps[:],
                           lhsT=bwT_s[:, k * DM + m * 128:k * DM + (m + 1) * 128],
                           rhs=xin_tiles[k], start=(k == 0), stop=False),
                           reads=[xin_bufs[k]],
                           war=[psb],
                           tag=tag + f"bs{m}")
                b.emit("p", lambda e, m=m, ps=ps: e.matmul(out=ps[:],
                       lhsT=bb_s[0:1, m * 128:(m + 1) * 128], rhs=ones_row,
                       start=False, stop=False), reads=[], war=[psb], tag=tag)
                b.emit("p", lambda e, m=m, ps=ps: e.matmul(out=ps[:],
                       lhsT=lB_s[:, m * 128:(m + 1) * 128], rhs=wmid_t,
                       start=False, stop=True), reads=[wmid_b], war=[psb],
                       tag=tag + "lora")
                b.close_group("p", psb)
            for m in range(g0, g0 + 4):
                ps, psb = ps_mm[m % 4], ps_mm_b[m % 4]
                b.emit("v", lambda e, m=m, ps=ps: e.scalar_tensor_tensor(
                       out=out_tiles[m], in0=ps[:], scalar=0.5, in1=res_tiles[m],
                       op0=ALU.mult, op1=ALU.add),
                       reads=[psb, res_bufs[m]], writes=[out_bufs[m]],
                       tag=tag + "ev")

    ffnw1 = make_ffn_work()

    ABF.mark(); AFA.mark()
    wx_s = ABF.alloc(DT2 * 96)
    cdiag_s = ABF.alloc(64 * 128)
    tails_s = ABF.alloc(DT2 * 4)
    for d in range(DT2):
        dma("g", wx_s[:, d * 96:(d + 1) * 96], wx[d], dsem="boot_g")
    for t in range(64):
        dma("g", cdiag_s[:, t * 128:(t + 1) * 128], cdiag[t],
            dsem="boot_g")
    b.emit("v", lambda e: e.memset(tails_s, 0.0), writes=[WB["tails"]])
    ABF.mark()
    stage = ABF.alloc(2 * DIN)
    stage_b = Buf("stage")
    for k in range(KT):
        dma("g", stage, win[k], writes=[stage_b])
        dma("s", win_bf.ap()[k], stage, reads=[stage_b], writes=[WB["winbf"]])
    for d in range(DT2):
        dma("g", stage[:, 0:DM], wout[d], writes=[stage_b])
        dma("s", wout_bf.ap()[d], stage[:, 0:DM], reads=[stage_b],
            writes=[WB["woutbf"]])
    ABF.reset()
    b.barrier()

    xf_t = [AFA.alloc(TC) for _ in range(KT)]
    xf_b = [Buf(f"xf{k}") for k in range(KT)]
    xbf_t = ABF.alloc(TC)
    xbf_b = Buf("xbf")
    sq_t = ABF.alloc(TC)
    sq_b = Buf("sq")
    lnrows_t = [AFA.alloc(5 * TC, rows=1) for _ in range(2)]
    lnrows_b = [Buf(f"lnrows{i}") for i in range(2)]
    tmp_t = AFA.alloc(TC)
    tmp_b = Buf("tmp")
    xln1_t = [ABF.alloc(TC) for _ in range(KT)]
    xln1_b = [Buf(f"xln1{k}") for k in range(KT)]
    x1_t = [AFA.alloc(TC) for _ in range(KT)]
    x1_b = [Buf(f"x1{k}") for k in range(KT)]
    xln2f_t = AFA.alloc(TC)
    xln2f_b = Buf("xln2f")
    xln2_t = [ABF.alloc(TC) for _ in range(KT)]
    xln2_b = [Buf(f"xln2{k}") for k in range(KT)]

    wg_t = [ABF.alloc(KT * 512) for _ in range(2)]
    wg_b = [Buf(f"wg{i}") for i in range(2)]
    xin_t = [ABF.alloc(TC + 4) for _ in range(2)]
    xin_b = [Buf(f"xin{i}") for i in range(2)]
    u_t = [ABF.alloc(TC) for _ in range(DT2)]
    u_b = [Buf(f"u{d}") for d in range(DT2)]
    gz_t = ABF.alloc(TC)
    gz_b = Buf("gz")
    dtb_t = ABF.alloc(TC, rows=64)
    dtb_b = Buf("dtb")
    bct_t = ABF.alloc(TC, rows=32)
    bct_b = Buf("bct")
    cbuf_t = AFA.alloc(TC)
    cbuf_b = Buf("cbuf")
    ebuf_t = AFA.alloc(TC)
    ebuf_b = Buf("ebuf")
    lbuf_t = AFA.alloc(TC)
    lbuf_b = Buf("lbuf")
    dl_t = ABF.alloc(TC)
    dl_b = Buf("dl")
    du2_t = ABF.alloc(TC)
    du2_b = Buf("du2")
    _spd = Buf("spd"); sb_spd = [_spd] * DT2
    _spdu = Buf("spdu"); sb_spdu = [_spdu] * DT2
    _spu = Buf("spu"); sb_spu = [_spu] * DT2
    _spg = Buf("spg"); sb_spg = [_spg] * DT2
    sb_spbc = Buf("spbc")
    _spx1 = Buf("spx1"); sb_spx1 = [_spx1] * KT
    _spx2 = Buf("spx2"); sb_spx2 = [_spx2] * KT

    for c in range(NCH):
        cs = slice(c * TC, (c + 1) * TC)
        for k in range(KT):
            dma("s", xf_t[k], xT[k][:, cs], writes=[xf_b[k]], tag=f"ldx{c}")
        ln_rows(xf_t, xf_b, lnrows_t, lnrows_b, sq_t, sq_b, xbf_t, xbf_b, f"L1{c}")
        ln_apply(xf_t, xf_b, xln1_t, xln1_b, tmp_t, tmp_b, f"L1a{c}")
        ffn(xln1_t, xln1_b, xf_t, xf_b, x1_t, x1_b, ffnw1, f"F1{c}")
        for k in range(KT):
            dma("s", sp_x1.ap()[k][:, cs], x1_t[k], reads=[x1_b[k]], writes=[sb_spx1[k]], tag=f"sx1{c}")
        ln_rows(x1_t, x1_b, lnrows_t, lnrows_b, sq_t, sq_b, xbf_t, xbf_b, f"L2{c}")
        for k in range(KT):
            b.emit("v", lambda e, k=k: e.tensor_tensor(out=tmp_t, in0=x1_t[k],
                   in1=ps_bc0[:], op=ALU.mult),
                   reads=[x1_b[k], ps_bc0_b], writes=[tmp_b], tag=f"L2a{c}")
            b.emit("v", lambda e: e.tensor_tensor(out=xln2f_t, in0=tmp_t,
                   in1=ps_bc1[:], op=ALU.subtract),
                   reads=[tmp_b, ps_bc1_b], writes=[xln2f_b], tag=f"L2b{c}")
            b.emit("v", lambda e, k=k: e.tensor_copy(out=xln2_t[k], in_=xln2f_t),
                   reads=[xln2f_b], writes=[xln2_b[k]], tag=f"L2c{c}")
            dma("s", sp_x2.ap()[k][:, cs], xln2f_t, reads=[xln2f_b], writes=[sb_spx2[k]], tag=f"sx2{c}")
        for g0 in range(0, 2 * DT2, 4):
            wgi = (g0 // 4) % 2
            for k in range(KT):
                dma("s" if k % 2 == 0 else "a", wg_t[wgi][:, k * 512:(k + 1) * 512],
                    win_bf.ap()[k][:, g0 * 128:(g0 + 4) * 128],
                    reads=[WB["winbf"]], writes=[wg_b[wgi]], tag=f"lw{c}")
            for mi in range(4):
                m = g0 + mi
                ps, psb = ps_mm[mi % 2], ps_mm_b[mi % 2]
                for k in range(KT):
                    b.emit("p", lambda e, k=k, mi=mi, wgi=wgi, ps=ps:
                           e.matmul(out=ps[:],
                                    lhsT=wg_t[wgi][:, k * 512 + mi * 128:k * 512 + (mi + 1) * 128],
                                    rhs=xln2_t[k], start=(k == 0), stop=(k == KT - 1)),
                           reads=[xln2_b[k], wg_b[wgi]],
                           war=[psb],
                           tag=f"wi{c}m{m}")
                b.close_group("p", psb)
                if m < DT2:
                    xt, xtb = xin_t[m % 2], xin_b[m % 2]
                    b.emit("v", lambda e, xt=xt, ps=ps: e.tensor_copy(
                           out=xt[:, 3:3 + TC], in_=ps[:]),
                           reads=[psb], writes=[xtb], tag=f"xc{c}")
                    b.emit("v", lambda e, xt=xt, m=m: e.tensor_copy(
                           out=xt[:, 0:3], in_=tails_s[:, m * 4:m * 4 + 3]),
                           reads=[WB["tails"]], writes=[xtb], tag=f"tl{c}")
                    psc, pscb = ps_mm[(mi + 2) % 4], ps_mm_b[(mi + 2) % 4]
                    for tap in range(4):
                        b.emit("p", lambda e, m=m, tap=tap, xt=xt, psc=psc:
                               e.matmul(out=psc[:],
                                        lhsT=cdiag_s[:, (m * 4 + tap) * 128:(m * 4 + tap + 1) * 128],
                                        rhs=xt[:, tap:tap + TC],
                                        start=(tap == 0), stop=(tap == 3)),
                               reads=[xtb],
                               war=[pscb],
                               tag=f"cv{c}m{m}")
                    b.close_group("p", pscb)
                    b.emit("v", lambda e, xt=xt, m=m: e.tensor_copy(
                           out=tails_s[:, m * 4:m * 4 + 3], in_=xt[:, TC:TC + 3]),
                           reads=[xtb], writes=[WB["tails"]], tag=f"tu{c}")
                    b.emit("a", lambda e, m=m, psc=psc: e.activation(
                           out=u_t[m], in_=psc[:], func=AF.Silu,
                           bias=vcol("conv_b", m)),
                           reads=[pscb, WB["vec"]], writes=[u_b[m]], tag=f"si{c}")
                    dma("s", sp_u.ap()[m][:, cs], u_t[m], reads=[u_b[m]],
                        writes=[sb_spu[m]], tag=f"su{c}")
                else:
                    dz = m - DT2
                    b.emit("a", lambda e, ps=ps: e.activation(out=gz_t, in_=ps[:],
                           func=AF.Silu), reads=[psb], writes=[gz_b], tag=f"sz{c}")
                    dma("s", sp_g.ap()[dz][:, cs], gz_t, reads=[gz_b], writes=[sb_spg[dz]], tag=f"sg{c}")
        for d in range(DT2):
            b.emit("p", lambda e, d=d: e.matmul(out=ps_small[0:96, :],
                   lhsT=wx_s[:, d * 96:(d + 1) * 96], rhs=u_t[d],
                   start=(d == 0), stop=(d == DT2 - 1)),
                   reads=[u_b[d]],
                   war=[ps_small_b],
                   tag=f"wx{c}")
        b.close_group("p", ps_small_b)
        b.emit("v", lambda e: e.tensor_copy(out=dtb_t, in_=ps_small[0:64, :]),
               reads=[ps_small_b], writes=[dtb_b], tag=f"dtb{c}")
        b.emit("v", lambda e: e.tensor_copy(out=bct_t, in_=ps_small[64:96, :]),
               reads=[ps_small_b], writes=[bct_b], tag=f"bct{c}")
        dma("s", sp_bc.ap()[:, cs], bct_t, reads=[bct_b], writes=[sb_spbc], tag=f"sbc{c}")
        for g0 in range(0, DT2, 4):
            for di in range(4):
                d = g0 + di
                ps, psb = ps_mm[di], ps_mm_b[di]
                b.emit("p", lambda e, d=d, ps=ps: e.matmul(out=ps[:],
                       lhsT=wdt_s[:, d * 128:(d + 1) * 128], rhs=dtb_t,
                       start=True, stop=True),
                       reads=[dtb_b], war=[psb], tag=f"wd{c}")
                b.close_group("p", psb)
            for di in range(4):
                d = g0 + di
                ps, psb = ps_mm[di], ps_mm_b[di]
                b.emit("v", lambda e, d=d, ps=ps: e.tensor_scalar(out=cbuf_t,
                       in0=ps[:], scalar1=vcol("b_dt", d), scalar2=SM30,
                       op0=ALU.add, op1=ALU.min),
                       reads=[psb, WB["vec"]], writes=[cbuf_b], tag=f"dm{c}")
                b.emit("a", lambda e: e.activation(out=ebuf_t, in_=cbuf_t,
                       func=AF.Exp), reads=[cbuf_b], writes=[ebuf_b], tag=f"de{c}")
                b.emit("a", lambda e: e.activation(out=lbuf_t, in_=ebuf_t,
                       func=AF.Ln, bias=1.0), reads=[ebuf_b], writes=[lbuf_b],
                       tag=f"dl{c}")
                b.emit("v", lambda e, d=d, ps=ps: e.scalar_tensor_tensor(
                       out=dl_t, in0=ps[:], scalar=vcol("b_dt", d), in1=lbuf_t,
                       op0=ALU.add, op1=ALU.max),
                       reads=[psb, lbuf_b, WB["vec"]], writes=[dl_b], tag=f"dx{c}")
                dma("s", sp_d.ap()[d][:, cs], dl_t, reads=[dl_b], writes=[sb_spd[d]], tag=f"sd{c}")
                b.emit("v", lambda e, d=d: e.tensor_tensor(out=du2_t, in0=dl_t,
                       in1=u_t[d], op=ALU.mult),
                       reads=[dl_b, u_b[d]], writes=[du2_b], tag=f"duz{c}")
                dma("a", sp_du.ap()[d][:, cs], du2_t, reads=[du2_b], writes=[sb_spdu[d]], tag=f"sdu{c}")

    ABF.reset(); AFA.reset()
    b.barrier()

    ABF.mark(); AFA.mark()
    NSP = L // 1024
    NCH2 = NCH // 2
    v_d = sp_d.ap().rearrange("d p (s n) -> (d p s) n", n=TC)
    v_du = sp_du.ap().rearrange("d p (s n) -> (d p s) n", n=TC)
    v_u = sp_u.ap().rearrange("d p (s n) -> (d p s) n", n=TC)
    v_g = sp_g.ap().rearrange("d p (s n) -> (d p s) n", n=TC)
    v_bc = sp_bc.ap().rearrange("r (c n) -> (r c) n", n=TC)
    v_x1 = sp_x1.ap().rearrange("k p (c n) -> (k p c) n", n=TC)
    v_x2 = sp_x2.ap().rearrange("k p (c n) -> (k p c) n", n=TC)
    v_yo = yout.rearrange("k p (c n) -> (k p c) n", n=TC)

    def gdt(dt, j):
        c0 = dt * 8 + j
        return gidx_t.ap()[:, c0:c0 + 1]

    def gbc(r, j):
        c0 = 128 + r * 8 + j
        return gidx_t.ap()[:, c0:c0 + 1]

    def gk(k, j):
        c0 = 384 + k * 8 + j
        return gidx_t.ap()[:, c0:c0 + 1]

    def ig(out, view, idx, reads=(), writes=(), tag=""):
        return b.emit("g", lambda e, o=out, v=view, i=idx: e.indirect_dma_start(
            out=o, out_offset=None, in_=v,
            in_offset=bass.IndirectOffsetOnAxis(ap=i, axis=0)),
            reads=reads, writes=writes, dma=True, tag=tag)

    def isc(view, idx, in_, reads=(), writes=(), tag=""):
        return b.emit("g", lambda e, v=view, i=idx, s=in_: e.indirect_dma_start(
            out=v, out_offset=bass.IndirectOffsetOnAxis(ap=i, axis=0),
            in_=s, in_offset=None),
            reads=reads, writes=writes, dma=True, tag=tag)

    Bbc_t = [ABF.alloc(TC) for _ in range(NS)]
    Bbc_b = [Buf(f"Bb{s}") for s in range(NS)]
    Cbc_t = [ABF.alloc(TC) for _ in range(NS)]
    Cbc_b = [Buf(f"Cb{s}") for s in range(NS)]
    bchain = [Buf(f"bch{i}") for i in range(4)]
    cchain = [Buf(f"cch{i}") for i in range(4)]
    dld_t = [ABF.alloc(TC) for _ in range(2)]
    dld_b = [Buf(f"dld{i}") for i in range(2)]
    duld_t = [ABF.alloc(TC) for _ in range(2)]
    duld_b = [Buf(f"duld{i}") for i in range(2)]
    _u1 = ABF.alloc(TC); uld_t = [_u1, _u1]
    _u1b = Buf("uld"); uld_b = [_u1b, _u1b]
    _g1 = ABF.alloc(TC); gld_t = [_g1, _g1]
    _g1b = Buf("gld"); gld_b = [_g1b, _g1b]
    wbuf_t = [ABF.alloc(TC) for _ in range(2)]
    wbuf_b = [Buf(f"wb{i}") for i in range(2)]
    scr_t = AFA.alloc(TC)
    scr_b = Buf("scr")
    dA_t = [AFA.alloc(TC) for _ in range(4)]
    dA_b = [Buf(f"dA{i}") for i in range(4)]
    dBu_t = [ABF.alloc(TC) for _ in range(4)]
    dBu_b = [Buf(f"dBu{i}") for i in range(4)]
    h_all = ABF.alloc(4 * TC)
    h_t = [h_all[:, i * TC:(i + 1) * TC] for i in range(4)]
    h_b = [Buf(f"h{i}") for i in range(4)]
    h_last = h_all.tensor.ap()

    p_t = [ABF.alloc(TC) for _ in range(4)]
    p_b = [Buf(f"p{s}") for s in range(4)]
    ym_t = [ABF.alloc(TC) for _ in range(DT2)]
    ym_b = [Buf(f"ym{d}") for d in range(DT2)]
    cdl_t = AFA.alloc(TC)
    cdl_b = Buf("cdl")
    rb_t2 = AFA.alloc(TC)
    rb_b2 = Buf("rb2")
    gam_t = AFA.alloc(NS)
    gam_b = Buf("gam")
    part_t = AFA.alloc(NS)
    part_b = Buf("part")
    t1_t = AFA.alloc(TC)
    t1_b = Buf("t1")
    wo2_t = [ABF.alloc(DT2 * 256) for _ in range(2)]
    wo2_b = [Buf(f"wo2{i}") for i in range(2)]
    x1l_t = [AFA.alloc(TC) for _ in range(2)]
    x1l_b = [Buf(f"x1l{i}") for i in range(2)]
    xl2l_t = [AFA.alloc(TC) for _ in range(2)]
    xl2l_b = [Buf(f"xl2l{i}") for i in range(2)]
    h3_t = [AFA.alloc(TC) for _ in range(KT)]
    h3_b = [Buf(f"h3{k}") for k in range(KT)]
    x2_t = [AFA.alloc(TC) for _ in range(KT)]
    x2_b = [Buf(f"x2{k}") for k in range(KT)]
    xln4_t = [ym_t[k] for k in range(KT)]
    xln4_b = [ym_b[k] for k in range(KT)]
    outst_t = AFA.alloc(TC)
    outst_b = Buf("outst")
    xbf2_t = ABF.alloc(TC)
    xbf2_b = Buf("xbf2")
    sq2_t = ABF.alloc(TC)
    sq2_b = Buf("sq2")
    lnrows2_t = [AFA.alloc(5 * TC, rows=1) for _ in range(2)]
    lnrows2_b = [Buf(f"lnrows2{i}") for i in range(2)]
    tmp2_t = AFA.alloc(TC)
    tmp2_b = Buf("tmp2")
    ffnw2 = ffnw1
    sb_yo = [Buf(f"yo{k}") for k in range(4)] * 2

    b.emit("v", lambda e: e.memset(hrun_s, 0.0), writes=[WB["hrun"]])

    for c in range(NCH2):
        cs = slice(c * TC, (c + 1) * TC)
        for s in range(NS):
            q = "s" if s % 2 == 0 else "a"
            dma(q, Bbc_t[s], sp_bc.ap()[s, cs].partition_broadcast(128),
                writes=[bchain[s % 4], Bbc_b[s]], tag=f"tBb{c}")
        for dt in range(DT2):
            sl = dt % 2
            dma("s", dld_t[sl], sp_d.ap()[dt][:, cs],
                writes=[dld_b[sl]], tag=f"tld{c}")
            dma("a", duld_t[sl], sp_du.ap()[dt][:, cs],
                writes=[duld_b[sl]], tag=f"tlu{c}")
            b.emit("v", lambda e, sl=sl: e.tensor_tensor_scan(
                   out=cdl_t, data0=dld_t[sl], data1=dld_t[sl],
                   initial=0.0, op0=ALU.add, op1=ALU.bypass),
                   reads=[dld_b[sl]], writes=[cdl_b], tag=f"tcs{c}")
            b.emit("v", lambda e: e.tensor_scalar(out=rb_t2, in0=cdl_t,
                   scalar1=cdl_t[:, TC - 1:TC], scalar2=None, op0=ALU.subtract),
                   reads=[cdl_b], writes=[rb_b2], tag=f"tr{c}")
            b.emit("v", lambda e, dt=dt: e.tensor_scalar(
                   out=gam_t, in0=posA_s[:, dt * NS:(dt + 1) * NS],
                   scalar1=cdl_t[:, TC - 1:TC], scalar2=-1.0,
                   op0=ALU.mult, op1=ALU.mult),
                   reads=[cdl_b, WB["posA"]], writes=[gam_b], tag=f"tg{c}")
            b.emit("a", lambda e: e.activation(out=gam_t, in_=gam_t, func=AF.Exp),
                   reads=[gam_b], writes=[gam_b], tag=f"tge{c}")
            hsl = hrun_s[:, dt * NS:(dt + 1) * NS]
            b.emit("v", lambda e, hsl=hsl: e.tensor_tensor(out=hsl, in0=hsl,
                   in1=gam_t, op=ALU.mult),
                   reads=[gam_b, WB["hrun"]], writes=[WB["hrun"]], tag=f"thg{c}")
            for s in range(NS):
                r2 = s % 4
                b.emit("a", lambda e, dt=dt, s=s, r2=r2 % 2: e.activation(
                       out=wbuf_t[r2 % 2], in_=rb_t2, func=AF.Exp,
                       scale=posA_s[:, dt * NS + s:dt * NS + s + 1]),
                       reads=[rb_b2, WB["posA"]], writes=[wbuf_b[r2 % 2]], tag=f"tw{c}")
                b.emit("g", lambda e, s=s, sl=sl, r2=r2: e.tensor_tensor(
                       out=dBu_t[r2], in0=duld_t[sl], in1=Bbc_t[s],
                       op=ALU.mult),
                       reads=[duld_b[sl], Bbc_b[s]], writes=[dBu_b[r2]],
                       tag=f"tdb{c}")
                b.emit("v", lambda e, s=s, r2=r2: e.scalar_tensor_tensor(
                       out=scr_t, in0=wbuf_t[r2 % 2], scalar=1.0, in1=dBu_t[r2],
                       op0=ALU.mult, op1=ALU.mult,
                       accum_out=part_t[:, s:s + 1]),
                       reads=[wbuf_b[r2 % 2], dBu_b[r2]],
                       writes=[scr_b, part_b], tag=f"ttr{c}")
            hsl2 = hrun_s[:, dt * NS:(dt + 1) * NS]
            b.emit("v", lambda e, hsl2=hsl2: e.tensor_tensor(out=hsl2,
                   in0=hsl2, in1=part_t, op=ALU.add),
                   reads=[part_b, WB["hrun"]], writes=[WB["hrun"]],
                   tag=f"tha{c}")
    b.emit("v", lambda e: e.tensor_scalar(out=carry_s, in0=hrun_s,
           scalar1=smask_s[:, 0:1], scalar2=None, op0=ALU.mult),
           reads=[WB["hrun"], WB["smask"]], writes=[WB["carry"]], tag="cinit")

    for j in range(NCH2):
        for s in range(NS):
            ig(Bbc_t[s], v_bc, gbc(s, j),
               writes=[bchain[s % 4], Bbc_b[s]], tag=f"sBb{j}")
            ig(Cbc_t[s], v_bc, gbc(16 + s, j),
               writes=[cchain[s % 4], Cbc_b[s]], tag=f"sCb{j}")
        if True:
            for dt in range(DT2):
                sl = dt % 2
                psy, psyb = (ps_bc0, ps_bc0_b) if dt % 2 == 0 else (ps_bc1, ps_bc1_b)
                ig(dld_t[sl], v_d, gdt(dt, j), writes=[dld_b[sl]], tag=f"gd{j}")
                ig(duld_t[sl], v_du, gdt(dt, j), writes=[duld_b[sl]], tag=f"gdu{j}")
                ig(uld_t[sl], v_u, gdt(dt, j), writes=[uld_b[sl]], tag=f"gu{j}")
                ig(gld_t[sl], v_g, gdt(dt, j), writes=[gld_b[sl]], tag=f"gg{j}")
                for s in range(NS):
                    r2 = s % 4
                    b.emit("a", lambda e, dt=dt, s=s, sl=sl, r2=r2:
                           e.activation(out=dA_t[r2], in_=dld_t[sl],
                                        func=AF.Exp,
                                        scale=negA_s[:, dt * NS + s:dt * NS + s + 1]),
                           reads=[dld_b[sl], WB["negA"]], writes=[dA_b[r2]],
                           tag=f"dA{j}")
                    b.emit("g", lambda e, s=s, sl=sl, r2=r2: e.tensor_tensor(
                           out=dBu_t[r2], in0=duld_t[sl], in1=Bbc_t[s],
                           op=ALU.mult),
                           reads=[duld_b[sl], Bbc_b[s]], writes=[dBu_b[r2]],
                           tag=f"dB{j}")
                    b.emit("v", lambda e, dt=dt, s=s, r2=r2: e.tensor_tensor_scan(
                           out=h_t[r2], data0=dA_t[r2], data1=dBu_t[r2],
                           initial=carry_s[:, dt * NS + s:dt * NS + s + 1],
                           op0=ALU.mult, op1=ALU.add),
                           reads=[dA_b[r2], dBu_b[r2], WB["carry"]],
                           writes=[h_b[r2]], tag=f"ts{j}")
                    if s % 4 == 3:
                        hoff = h_t[0].offset % (128 * 1024 * 64)
                        b.emit("v", lambda e, dt=dt, s=s: e.tensor_copy(
                               out=carry_s[:, dt * NS + s - 3:dt * NS + s + 1],
                               in_=h_all[:, TC - 1::TC]),
                               reads=[h_b[0], h_b[1], h_b[2], h_b[3]],
                               writes=[WB["carry"]], tag=f"cr{j}", cost=200)
                    b.emit("v", lambda e, s=s, r2=r2: e.tensor_tensor(
                           out=p_t[s % 4],
                           in0=h_t[r2], in1=Cbc_t[s], op=ALU.mult),
                           reads=[h_b[r2], Cbc_b[s]], writes=[p_b[s % 4]],
                           tag=f"cm{j}")
                    b.emit("p", lambda e, s=s, psy=psy: e.matmul(out=psy[:],
                           lhsT=ident_bf, rhs=p_t[s % 4], start=(s == 0),
                           stop=(s == NS - 1)),
                           reads=[p_b[s % 4]],
                           war=[psyb],
                           tag=f"sm{j}")
                b.close_group("p", psyb)
                b.emit("v", lambda e, dt=dt, sl=sl, psy=psy:
                       e.scalar_tensor_tensor(
                       out=t1_t, in0=uld_t[sl], scalar=vcol("D", dt),
                       in1=psy[:], op0=ALU.mult, op1=ALU.add),
                       reads=[uld_b[sl], psyb, WB["vec"]], writes=[t1_b],
                       tag=f"y1{j}")
                b.emit("v", lambda e, dt=dt, sl=sl: e.tensor_tensor(
                       out=ym_t[dt], in0=t1_t, in1=gld_t[sl], op=ALU.mult),
                       reads=[t1_b, gld_b[sl]], writes=[ym_b[dt]], tag=f"y2{j}")
            for m in range(KT):
                wgi = m % 2
                if m % 2 == 0:
                    for d in range(DT2):
                        dma("s" if d % 2 == 0 else "a",
                            wo2_t[(m // 2) % 2][:, d * 256:(d + 1) * 256],
                            wout_bf.ap()[d][:, m * 128:(m + 2) * 128],
                            reads=[WB["woutbf"]], writes=[wo2_b[(m // 2) % 2]],
                            tag=f"lwo{j}")
                ps, psb = ps_mm[m % 2], ps_mm_b[m % 2]
                for d in range(DT2):
                    b.emit("p", lambda e, d=d, m=m, ps=ps:
                           e.matmul(out=ps[:],
                                    lhsT=wo2_t[(m // 2) % 2][:, d * 256 + (m % 2) * 128:d * 256 + (m % 2 + 1) * 128],
                                    rhs=ym_t[d], start=(d == 0),
                                    stop=(d == DT2 - 1)),
                           reads=[ym_b[d], wo2_b[(m // 2) % 2]],
                           war=[psb],
                           tag=f"wo{j}m{m}")
                b.close_group("p", psb)
                mi = m % 2
                ig(xl2l_t[mi], v_x2, gk(m, j), writes=[xl2l_b[mi]], tag=f"lx2{j}")
                b.emit("v", lambda e, m=m, mi=mi, ps=ps: e.tensor_tensor(
                       out=h3_t[m], in0=ps[:], in1=xl2l_t[mi], op=ALU.add),
                       reads=[psb, xl2l_b[mi]], writes=[h3_b[m]], tag=f"h3{j}")
            ln_rows(h3_t, h3_b, lnrows2_t, lnrows2_b, sq2_t, sq2_b, xbf2_t,
                    xbf2_b, f"L3{j}")
            for k in range(KT):
                kl = k % 2
                ig(x1l_t[kl], v_x1, gk(k, j), writes=[x1l_b[kl]], tag=f"lx1{j}")
                b.emit("v", lambda e, k=k: e.tensor_tensor(out=tmp2_t,
                       in0=h3_t[k], in1=ps_bc0[:], op=ALU.mult),
                       reads=[h3_b[k], ps_bc0_b], writes=[tmp2_b], tag=f"L3a{j}")
                b.emit("v", lambda e: e.tensor_tensor(out=tmp2_t, in0=tmp2_t,
                       in1=ps_bc1[:], op=ALU.subtract),
                       reads=[tmp2_b, ps_bc1_b], writes=[tmp2_b], tag=f"L3b{j}")
                b.emit("v", lambda e, k=k, kl=kl: e.scalar_tensor_tensor(
                       out=x2_t[k], in0=tmp2_t, scalar=vcol("gamma", k),
                       in1=x1l_t[kl], op0=ALU.mult, op1=ALU.add),
                       reads=[tmp2_b, x1l_b[kl], WB["vec"]], writes=[x2_b[k]],
                       tag=f"L3g{j}")
                b.emit("v", lambda e, k=k: e.tensor_scalar(out=x2_t[k],
                       in0=x2_t[k], scalar1=vcol("beta", k), scalar2=None,
                       op0=ALU.add),
                       reads=[x2_b[k], WB["vec"]], writes=[x2_b[k]], tag=f"L3bt{j}")
            ln_rows(x2_t, x2_b, lnrows2_t, lnrows2_b, sq2_t, sq2_b, xbf2_t,
                    xbf2_b, f"L4{j}")
            ln_apply(x2_t, x2_b, xln4_t, xln4_b, tmp2_t, tmp2_b, f"L4a{j}")
            ffn(xln4_t, xln4_b, x2_t, x2_b, x2_t, x2_b, ffnw2, f"F2{j}")
            for k in range(KT):
                isc(v_yo, gk(k, j), x2_t[k], reads=[x2_b[k]],
                    writes=[sb_yo[k]], tag=f"out{j}")

    ABF.reset(); AFA.reset()
    b.fence()
    b.emit("g", lambda e: e.nop(), wait_all_dma=True, tag="final")
    order = b.schedule()
    b.lower(order)
    b.finalize(order)
    b.coalesce_waits(order)

    with ctx:
        with ExitStack() as sctx:
            sems = {}
            for n in b.all_sems:
                sems[n] = sctx.enter_context(nc.semaphore(n))
            with nc.Block() as block:
                def mk(eng_key):
                    def run(engine):
                        for op in order:
                            if op.eng != eng_key:
                                continue
                            for s, v in op.waits:
                                engine.wait_ge(sems[s], v)
                            inst = op.fn(engine)
                            if op.inc is not None and inst is not None:
                                inst.then_inc(sems[op.inc[0]], op.inc[1])
                    return run
                block.tensor(mk("p"))
                block.vector(mk("v"))
                block.scalar(mk("a"))
                block.gpsimd(mk("g"))
                block.sync(mk("s"))
    return nc


def prep_inputs(inputs, L=8192):
    f = np.ascontiguousarray
    W_in = np.asarray(inputs["L0_attn_W_in"])
    conv_w = np.asarray(inputs["L0_attn_conv_w"])
    d = {}
    d["win"] = f(W_in.reshape(KT, 128, 2 * DIN))
    d["bwT"] = f(np.asarray(inputs["L0_ffn_base_w"]).T.reshape(KT, 128, DM))
    d["rwT"] = f(np.asarray(inputs["L0_ffn_router_w"]).T.reshape(KT, 128, NE))
    d["lA"] = f(np.asarray(inputs["L0_ffn_lora_A"]).reshape(KT, 128, NR))
    d["lB"] = f(np.asarray(inputs["L0_ffn_lora_B"]).reshape(NE * NR, DM))
    d["wx"] = f(np.asarray(inputs["L0_attn_W_x"]).reshape(DT2, 128, 96))
    d["wdt"] = f(np.asarray(inputs["L0_attn_W_dt"]))
    d["wout"] = f(np.asarray(inputs["L0_attn_W_out"]).reshape(DT2, 128, DM))
    cd = np.zeros((DT2 * 4, 128, 128), np.float32)
    idx = np.arange(128)
    for dt in range(DT2):
        for k in range(4):
            cd[dt * 4 + k, idx, idx] = conv_w[dt * 128 + idx, k]
    d["cdiag"] = cd
    d["alog"] = f(np.asarray(inputs["L0_attn_A_log"]).reshape(DT2, 128, NS))
    vec = np.zeros((128, 64), np.float32)
    vec[:, 0:16] = np.asarray(inputs["L0_attn_conv_b"]).reshape(DT2, 128).T
    vec[:, 16:32] = np.asarray(inputs["L0_attn_b_dt"]).reshape(DT2, 128).T
    vec[:, 32:48] = np.asarray(inputs["L0_attn_D"]).reshape(DT2, 128).T
    vec[:, 48:56] = np.asarray(inputs["L0_attn_gamma"]).reshape(KT, 128).T
    vec[:, 56:64] = np.asarray(inputs["L0_attn_beta"]).reshape(KT, 128).T
    d["vec"] = vec
    d["bbrow"] = f(np.asarray(inputs["L0_ffn_base_b"]).reshape(1, DM))
    d["rbrow"] = f(np.asarray(inputs["L0_ffn_router_b"]).reshape(1, NE))
    d["ident_in"] = np.eye(128, dtype=np.float32)
    p = np.arange(128, dtype=np.uint32)
    nch = L // TC
    nc2 = nch // 2
    for sigma in (0, 1):
        gi = np.zeros((128, 448), np.uint32)
        for dt in range(16):
            for j in range(nc2):
                gi[:, dt * 8 + j] = (dt * 128 + p) * nch + (sigma * nc2 + j)
        for r in range(32):
            for j in range(nc2):
                gi[:, 128 + r * 8 + j] = r * nch + (sigma * nc2 + j)
        for k in range(8):
            for j in range(nc2):
                gi[:, 384 + k * 8 + j] = (k * 128 + p) * nch + (sigma * nc2 + j)
        d[f"gidx{sigma}"] = gi
        d[f"smask{sigma}"] = np.full((128, 1), float(sigma), np.float32)
    repl = np.zeros((2, 8, NE * NR), np.float32)
    for ee in range(NE):
        for r in range(NR):
            repl[0, ee, ee * NR + r] = 1.0
            repl[1, r, ee * NR + r] = 1.0
    d["repl_in"] = repl
    return d


def kernel(**inputs):
    x = np.asarray(inputs["L0_attn_x"])
    L = x.shape[1]
    shared = prep_inputs(inputs, L=L)
    nc = build_program(L=L)
    in_maps = []
    for core in range(8):
        bidx = core // 2
        sigma = core % 2
        m = {k: v for k, v in shared.items()
             if not k.startswith(("gidx", "smask"))}
        m["gidx"] = shared[f"gidx{sigma}"]
        m["smask"] = shared[f"smask{sigma}"]
        m["xT"] = np.ascontiguousarray(x[bidx].T).reshape(KT, 128, L)
        in_maps.append(m)
    res = run_bass_kernel_spmd(nc, in_maps, core_ids=list(range(8)))
    out = np.empty((x.shape[0], L, DM), np.float32)
    half = L // 2
    for bidx in range(x.shape[0]):
        y0 = np.asarray(res.results[2 * bidx]["yout"]).reshape(DM, L)
        y1 = np.asarray(res.results[2 * bidx + 1]["yout"]).reshape(DM, L)
        out[bidx, :half] = y0[:, :half].T
        out[bidx, half:] = y1[:, half:].T
    return out


# version 10
# speedup vs baseline: 1.3153x; 1.1022x over previous
import numpy as np
from contextlib import ExitStack

import concourse.bass as bass
import concourse.mybir as mybir
from concourse.bass_utils import run_bass_kernel_spmd

F32 = mybir.dt.float32
BF16 = mybir.dt.bfloat16
AF = mybir.ActivationFunctionType
ALU = mybir.AluOpType
AX = mybir.AxisListType

DM = 1024
DIN = 2048
NS = 16
DTR = 64
NE = 8
NR = 8
KT = DM // 128
DT2 = DIN // 128
EPS = 1e-5
SM30 = 30.0
TC = 512

COMPUTE = ("p", "v", "a", "g", "s")


class Buf:
    __slots__ = ("writer", "readers", "name", "dsem")

    def __init__(self, name=""):
        self.writer = None
        self.readers = []
        self.name = name
        self.dsem = {}


class Op:
    __slots__ = ("eng", "fn", "dma", "reads", "writes", "tag", "cost",
                 "epoch", "idx", "preds", "wait_all_dma", "start",
                 "waits", "inc", "dsem")


class B:

    ENGS = ("p", "v", "a", "g", "s")

    def __init__(self):
        self.ops = []
        self.epoch = 0

    def emit(self, eng, fn, reads=(), writes=(), dma=False, noinc=False,
             extra_waits=(), war=(), tag="", cost=700, wait_all_dma=False,
             dsem=None):
        op = Op()
        op.eng, op.fn, op.dma = eng, fn, dma
        op.reads = list(reads)
        op.writes = list(writes) + list(war)
        op.tag = tag
        op.cost = cost
        op.epoch = self.epoch
        op.wait_all_dma = wait_all_dma
        op.dsem = dsem
        op.idx = len(self.ops)
        op.preds = None
        self.ops.append(op)
        return op

    def fence(self):
        self.epoch += 1

    def barrier(self):
        self.fence()

    def close_group(self, eng, *bufs):
        pass

    def _build_graph(self):
        writer = {}
        readers = {}
        for op in self.ops:
            preds = set()
            for bb in op.reads:
                w = writer.get(id(bb))
                if w is not None:
                    preds.add(w)
            for bb in op.writes:
                w = writer.get(id(bb))
                if w is not None:
                    preds.add(w)
                for r in readers.get(id(bb), ()):
                    preds.add(r)
            for bb in op.writes:
                writer[id(bb)] = op.idx
                readers[id(bb)] = []
            for bb in op.reads:
                readers.setdefault(id(bb), []).append(op.idx)
            preds.discard(op.idx)
            op.preds = sorted(preds)

    COST = {"p": 320, "v": 750, "a": 780, "g": 1200, "s": 300}

    def _estimate_costs(self):
        for op in self.ops:
            if op.cost != 700:
                continue
            if op.dma:
                op.cost = 1200
            else:
                c = self.COST.get(op.eng, 700)
                t = op.tag
                if op.eng == "v":
                    if t.startswith("ts"):
                        c = 1300
                    elif t.startswith(("cm", "dB")):
                        c = 550
                    elif t.startswith("cr"):
                        c = 160
                    elif t.startswith("ttr"):
                        c = 1050
                op.cost = c

    def schedule(self):
        import heapq
        self._estimate_costs()
        self._build_graph()
        n = len(self.ops)
        nsucc_pred = [len(op.preds) for op in self.ops]
        succs = [[] for _ in range(n)]
        for op in self.ops:
            for p in op.preds:
                succs[p].append(op.idx)
        done_t = [0.0] * n
        eng_free = {e: 0.0 for e in self.ENGS}
        q_free = {e: 0.0 for e in ("s", "a", "g")}
        epoch_end = 0.0
        order = []
        by_epoch = {}
        for op in self.ops:
            by_epoch.setdefault(op.epoch, []).append(op)
        for ep in sorted(by_epoch):
            ops_e = by_epoch[ep]
            remaining = set(o.idx for o in ops_e)
            pend = {o.idx: sum(1 for p in self.ops[o.idx].preds
                               if self.ops[p].epoch == ep) for o in ops_e}
            ready = []
            for o in ops_e:
                if pend[o.idx] == 0:
                    heapq.heappush(ready, o.idx)
            scheduled_any = True
            while remaining:
                if not ready:
                    raise RuntimeError("scheduler deadlock (cyclic deps?)")
                cand = heapq.heappop(ready)
                op = self.ops[cand]
                start = max(epoch_end, eng_free[op.eng],
                            max((done_t[p] for p in op.preds), default=0.0))
                if op.dma:
                    issue = max(start, q_free[op.eng])
                    eng_free[op.eng] = issue + 150.0
                    q_free[op.eng] = issue + max(600.0, op.cost * 0.5)
                    done_t[cand] = issue + 1500.0 + op.cost
                else:
                    eng_free[op.eng] = start + op.cost
                    done_t[cand] = start + op.cost
                op.start = start
                order.append(op)
                remaining.discard(cand)
                for sidx in succs[cand]:
                    if self.ops[sidx].epoch == ep and sidx in remaining:
                        pend[sidx] -= 1
                        if pend[sidx] == 0:
                            heapq.heappush(ready, sidx)
            epoch_end = max(max(done_t[o.idx] for o in ops_e), epoch_end)
        return order

    def lower(self, order):
        self.dsem_n = 0
        self.posmap = {}
        cnt = {k: 0 for k in ("p", "v", "a", "g")}
        pos = {}
        wm = {e: {} for e in self.ENGS}

        def dma_sem(op):
            if op.dsem is None:
                tgt = None
                for bb in op.writes:
                    tgt = bb
                    break
                if tgt is None:
                    op.dsem = "dmisc_" + op.eng
                else:
                    if op.eng not in tgt.dsem:
                        tgt.dsem[op.eng] = f"d{self.dsem_n}_{op.eng}"
                        self.dsem_n += 1
                    op.dsem = tgt.dsem[op.eng]
            if op.dsem not in cnt:
                cnt[op.dsem] = 0
            return op.dsem

        for op in order:
            semkey = dma_sem(op) if op.dma else op.eng
            cnt[semkey] += 16 if op.dma else 1
            pos[op.idx] = (semkey, cnt[semkey])
            waits = {}
            for p in op.preds:
                s, v = pos[p]
                if s == "p" and op.eng == "p":
                    continue
                if wm[op.eng].get(s, 0) >= v:
                    continue
                waits[s] = max(waits.get(s, 0), v)
            if op.wait_all_dma:
                for s, v in cnt.items():
                    if s not in ("p", "v", "a", "g") and v > waits.get(s, 0):
                        waits[s] = v
            op.waits = sorted(waits.items())
            op.inc = (semkey, 16 if op.dma else 1)
            self.posmap[(semkey, cnt[semkey])] = op
            for s, v in waits.items():
                wm[op.eng][s] = v
        self.all_sems = sorted(cnt)
        return cnt

    def coalesce_waits(self, order, margin=4000.0):
        per_eng = {}
        for op in order:
            per_eng.setdefault(op.eng, []).append(op)
        for eng, ops in per_eng.items():
            carrier = None
            carrier_limit = 0
            for op in ops:
                if not op.waits:
                    continue
                keep = []
                hoist = []
                for s, v in op.waits:
                    p = self.posmap.get((s, v))
                    if (carrier is not None and p is not None
                            and p.start + p.cost + margin <= carrier.start
                            and carrier_limit < 24):
                        hoist.append((s, v))
                    else:
                        keep.append((s, v))
                if hoist and carrier is not None:
                    w = dict(carrier.waits)
                    for s, v in hoist:
                        w[s] = max(w.get(s, 0), v)
                    carrier.waits = sorted(w.items())
                    carrier_limit += len(hoist)
                op.waits = keep
                if keep:
                    carrier = op
                    carrier_limit = 0
                elif carrier is None:
                    carrier = op
                    carrier_limit = 0

    def finalize(self, order):
        cnt = {}
        epoch_counts = {}
        cur = order[0].epoch if order else 0
        for op in order:
            if op.epoch != cur:
                epoch_counts[cur] = dict(cnt)
                cur = op.epoch
            semkey = op.inc[0]
            cnt[semkey] = cnt.get(semkey, 0) + op.inc[1]
        seen = set()
        for op in order:
            key = (op.eng, op.epoch)
            if key in seen:
                continue
            seen.add(key)
            prev = op.epoch - 1
            if prev in epoch_counts:
                w = dict(op.waits)
                for s, v in epoch_counts[prev].items():
                    if v > 0 and w.get(s, 0) < v:
                        w[s] = v
                op.waits = sorted(w.items())


class Arena:
    def __init__(self, t, n):
        self.t = t
        self.n = n
        self.off = 0
        self.stack = []

    def alloc(self, cols, rows=128):
        off = (self.off + 3) & ~3
        assert off + cols <= self.n, f"arena overflow {off}+{cols}>{self.n}"
        self.off = off + cols
        return self.t.ap()[0:rows, off:off + cols]

    def mark(self):
        self.stack.append(self.off)

    def reset(self):
        self.off = self.stack.pop()


def build_program(L=8192):
    NCH = L // TC
    nc = bass.Bass("TRN2", target_bir_lowering=False, debug=False)

    xT = nc.dram_tensor("xT", [KT, 128, L], F32, kind="ExternalInput").ap()
    win = nc.dram_tensor("win", [KT, 128, 2 * DIN], F32, kind="ExternalInput").ap()
    bwT = nc.dram_tensor("bwT", [KT, 128, DM], F32, kind="ExternalInput").ap()
    rwT = nc.dram_tensor("rwT", [KT, 128, NE], F32, kind="ExternalInput").ap()
    lA = nc.dram_tensor("lA", [KT, 128, NR], F32, kind="ExternalInput").ap()
    lB = nc.dram_tensor("lB", [NE * NR, DM], F32, kind="ExternalInput").ap()
    wx = nc.dram_tensor("wx", [DT2, 128, 96], F32, kind="ExternalInput").ap()
    wdt = nc.dram_tensor("wdt", [DTR, DIN], F32, kind="ExternalInput").ap()
    wout = nc.dram_tensor("wout", [DT2, 128, DM], F32, kind="ExternalInput").ap()
    cdiag = nc.dram_tensor("cdiag", [DT2 * 4, 128, 128], F32, kind="ExternalInput").ap()
    alog = nc.dram_tensor("alog", [DT2, 128, NS], F32, kind="ExternalInput").ap()
    vec = nc.dram_tensor("vec", [128, 64], F32, kind="ExternalInput").ap()
    bbrow = nc.dram_tensor("bbrow", [1, DM], F32, kind="ExternalInput").ap()
    rbrow = nc.dram_tensor("rbrow", [1, NE], F32, kind="ExternalInput").ap()
    ident_in = nc.dram_tensor("ident_in", [128, 128], F32, kind="ExternalInput").ap()
    repl_in = nc.dram_tensor("repl_in", [2, 8, NE * NR], F32, kind="ExternalInput").ap()

    yout = nc.dram_tensor("yout", [KT, 128, L], F32, kind="ExternalOutput").ap()

    win_bf = nc.dram_tensor("win_bf", [KT, 128, 2 * DIN], BF16)
    wout_bf = nc.dram_tensor("wout_bf", [DT2, 128, DM], BF16)
    sp_d = nc.dram_tensor("sp_d", [DT2, 128, L], BF16)
    sp_u = nc.dram_tensor("sp_u", [DT2, 128, L], BF16)
    sp_bc = nc.dram_tensor("sp_bc", [32, L], BF16)
    sp_x1 = nc.dram_tensor("sp_x1", [KT, 128, L], F32)
    sp_x2 = nc.dram_tensor("sp_x2", [KT, 128, L], F32)
    sp_g = nc.dram_tensor("sp_g", [DT2, 128, L], BF16)
    sp_du = nc.dram_tensor("sp_du", [DT2, 128, L], BF16)
    gidx = nc.dram_tensor("gidx", [128, 448], mybir.dt.uint32, kind="ExternalInput").ap()
    smask = nc.dram_tensor("smask", [128, 1], F32, kind="ExternalInput").ap()

    b = B()
    ctx = ExitStack()
    abf_t = ctx.enter_context(nc.sbuf_tensor("abf", [128, 59 * 1024], BF16))
    af_t = ctx.enter_context(nc.sbuf_tensor("af32", [128, 22 * 1024], F32))
    ABF = Arena(abf_t, 59 * 1024)
    AFA = Arena(af_t, 22 * 1024)

    gidx_t = ctx.enter_context(nc.sbuf_tensor("gidxs", [128, 448], mybir.dt.uint32))
    ps_mm = [ctx.enter_context(nc.psum_tensor(f"psmm{i}", [128, TC], F32)) for i in range(4)]
    ps_bc0 = ctx.enter_context(nc.psum_tensor("psbc0", [128, TC], F32))
    ps_bc1 = ctx.enter_context(nc.psum_tensor("psbc1", [128, TC], F32))
    ps_small = ctx.enter_context(nc.psum_tensor("pssmall", [128, TC], F32))
    ps_geb = ctx.enter_context(nc.psum_tensor("psgeb", [8, TC], BF16))
    ps_mm_b = [Buf(f"psmm{i}") for i in range(4)]
    ps_bc0_b, ps_bc1_b = Buf("psbc0"), Buf("psbc1")
    ps_small_b = Buf("pssmall")
    ps_geb_b = Buf("psgeb")

    bwT_s = ABF.alloc(KT * DM)
    rwT_s = ABF.alloc(KT * NE)
    lA_s = ABF.alloc(KT * NR)
    lB_s = ABF.alloc(DM, rows=NE * NR)
    wdt_s = ABF.alloc(DIN, rows=DTR)
    ident_s = ABF.alloc(128)
    repl_s = ABF.alloc(2 * NE * NR, rows=8)
    ones_row = ABF.alloc(TC, rows=1)
    onescol = ABF.alloc(1)
    bb_s = ABF.alloc(DM, rows=1)
    rb_s = ABF.alloc(NE, rows=1)

    onesrow_f = AFA.alloc(128, rows=1)
    vec_s = AFA.alloc(64)
    negA_s = AFA.alloc(DT2 * NS)
    posA_s = AFA.alloc(DT2 * NS)
    carry_s = AFA.alloc(DT2 * NS)
    hrun_s = AFA.alloc(DT2 * NS)
    smask_s = AFA.alloc(4)
    idf_s = AFA.alloc(128)
    atmp = AFA.alloc(NS)

    CONSTS = [0.0, 1.0, -1.0, 0.5, -0.5, 2.0, EPS, SM30, 1.0 / DM]
    const_t = AFA.alloc(len(CONSTS))
    cb = Buf("consts")
    for ci, cv in enumerate(CONSTS):
        nc.const_aps.register(F32, cv, const_t[:, ci:ci + 1])

    WB = {k: Buf(k) for k in ("bwT", "rwT", "lA", "lB", "wdt", "ident", "repl",
                              "ones", "bb", "rb", "onesf", "vec", "negA",
                              "posA", "carry", "hrun", "smask", "gidx",
                              "tails", "winbf", "woutbf", "wx",
                              "cdiag", "idf", "atmp")}

    def dma(eng, out, in_, reads=(), writes=(), tag="", dsem=None):
        return b.emit(eng, lambda e, o=out, i=in_: e.dma_start(out=o, in_=i),
                      reads=reads, writes=writes, dma=True, tag=tag, dsem=dsem)

    for k in range(KT):
        dma("g", bwT_s[:, k * DM:(k + 1) * DM], bwT[k], dsem="boot_g")
        dma("g", rwT_s[:, k * NE:(k + 1) * NE], rwT[k], dsem="boot_g")
        dma("g", lA_s[:, k * NR:(k + 1) * NR], lA[k], dsem="boot_g")
    dma("g", lB_s, lB, dsem="boot_g")
    dma("g", wdt_s, wdt, dsem="boot_g")
    dma("g", ident_s, ident_in, dsem="boot_g")
    dma("s", idf_s, ident_in, dsem="boot_s")
    dma("g", repl_s[0:8, 0:64], repl_in[0], dsem="boot_g")
    dma("g", repl_s[0:8, 64:128], repl_in[1], dsem="boot_g")
    dma("g", bb_s, bbrow, dsem="boot_g")
    dma("g", rb_s, rbrow, dsem="boot_g")
    dma("s", vec_s, vec, dsem="boot_s")
    dma("s", gidx_t.ap(), gidx, dsem="boot_s")
    dma("s", smask_s[:, 0:1], smask, dsem="boot_s")

    for ci, cv in enumerate(CONSTS):
        b.emit("v", lambda e, ci=ci, cv=cv: e.memset(const_t[:, ci:ci + 1], cv),
               writes=[cb])
    b.emit("v", lambda e: e.memset(ones_row, 1.0), writes=[WB["ones"]])
    b.emit("v", lambda e: e.memset(onescol, 1.0), writes=[WB["ones"]])
    b.emit("v", lambda e: e.memset(onesrow_f, 1.0), writes=[WB["onesf"]])
    b.emit("v", lambda e: e.memset(carry_s, 0.0), writes=[WB["carry"]])
    b.fence()
    for d in range(DT2):
        slp = posA_s[:, d * NS:(d + 1) * NS]
        sln = negA_s[:, d * NS:(d + 1) * NS]
        dma("s", atmp, alog[d], writes=[WB["atmp"]])
        b.emit("a", lambda e, slp=slp: e.activation(out=slp, in_=atmp, func=AF.Exp),
               reads=[WB["atmp"]], writes=[WB["posA"]])
        b.emit("a", lambda e, slp=slp, sln=sln: e.mul(out=sln, in_=slp, mul=-1.0),
               reads=[WB["posA"]], writes=[WB["negA"]])

    ident8f = idf_s[0:8, 0:8]
    ident_bf = ident_s[0:128, 0:128]

    def vcol(name, j):
        base = dict(conv_b=0, b_dt=16, D=32, gamma=48, beta=56)[name]
        return vec_s[:, base + j:base + j + 1]

    _lnrot = [0]

    def ln_rows(src_tiles, src_bufs, lnrows_sets, lnrows_bs, sq_t, sq_b, xbf_t, xbf_b, tag):
        _lnrot[0] ^= 1
        lnrows = lnrows_sets[_lnrot[0]] if isinstance(lnrows_sets, (list, tuple)) else lnrows_sets
        lnrows_b = lnrows_bs[_lnrot[0]] if isinstance(lnrows_bs, (list, tuple)) else lnrows_bs
        for k in range(KT):
            b.emit("v", lambda e, k=k: e.tensor_copy(out=xbf_t, in_=src_tiles[k]),
                   reads=[src_bufs[k]], writes=[xbf_b], tag=tag + "cast")
            b.emit("a", lambda e, k=k: e.activation(out=sq_t, in_=src_tiles[k],
                   func=AF.Square), reads=[src_bufs[k]], writes=[sq_b], tag=tag + "sq")
            b.emit("p", lambda e, st=(k == 0), last=(k == KT - 1): e.matmul(
                   out=ps_small[0:1, :], lhsT=onescol, rhs=xbf_t, start=st,
                   stop=last),
                   reads=[xbf_b, WB["ones"]] if k == 0 else [xbf_b],
                   war=[ps_small_b],
                   tag=tag + "mmx")
            b.emit("p", lambda e, st=(k == 0), last=(k == KT - 1): e.matmul(
                   out=ps_bc0[0:1, :], lhsT=onescol, rhs=sq_t, start=st,
                   stop=last),
                   reads=[sq_b],
                   war=[ps_bc0_b],
                   tag=tag + "mmq")
        b.close_group("p", ps_small_b, ps_bc0_b)
        mu = lnrows[0:1, 0:TC]
        m2 = lnrows[0:1, TC:2 * TC]
        var = lnrows[0:1, 2 * TC:3 * TC]
        arow = lnrows[0:1, 3 * TC:4 * TC]
        brow = lnrows[0:1, 4 * TC:5 * TC]
        b.emit("v", lambda e: e.tensor_scalar(out=mu, in0=ps_small[0:1, :],
               scalar1=1.0 / DM, scalar2=None, op0=ALU.mult),
               reads=[ps_small_b], writes=[lnrows_b], tag=tag)
        b.emit("v", lambda e: e.tensor_scalar(out=m2, in0=ps_bc0[0:1, :],
               scalar1=1.0 / DM, scalar2=None, op0=ALU.mult),
               reads=[ps_bc0_b], writes=[lnrows_b], tag=tag)
        b.emit("v", lambda e: e.tensor_tensor(out=var, in0=mu, in1=mu, op=ALU.mult),
               reads=[lnrows_b], writes=[lnrows_b], tag=tag)
        b.emit("v", lambda e: e.tensor_tensor(out=var, in0=m2, in1=var,
               op=ALU.subtract), reads=[lnrows_b], writes=[lnrows_b], tag=tag)
        b.emit("a", lambda e: e.activation(out=arow, in_=var, func=AF.Ln, bias=EPS),
               reads=[lnrows_b], writes=[lnrows_b], tag=tag)
        b.emit("a", lambda e: e.activation(out=arow, in_=arow, func=AF.Exp, scale=-0.5),
               reads=[lnrows_b], writes=[lnrows_b], tag=tag)
        b.emit("v", lambda e: e.tensor_tensor(out=brow, in0=mu, in1=arow, op=ALU.mult),
               reads=[lnrows_b], writes=[lnrows_b], tag=tag)
        b.emit("p", lambda e: e.matmul(out=ps_bc0[:], lhsT=onesrow_f, rhs=arow,
               start=True, stop=True),
               reads=[lnrows_b, WB["onesf"]], war=[ps_bc0_b], tag=tag + "bc0")
        b.close_group("p", ps_bc0_b)
        b.emit("p", lambda e: e.matmul(out=ps_bc1[:], lhsT=onesrow_f, rhs=brow,
               start=True, stop=True),
               reads=[lnrows_b], war=[ps_bc1_b], tag=tag + "bc1")
        b.close_group("p", ps_bc1_b)

    def ln_apply(src_tiles, src_bufs, out_tiles, out_bufs, tmp_t, tmp_b, tag):
        for k in range(KT):
            b.emit("v", lambda e, k=k: e.tensor_tensor(out=tmp_t, in0=src_tiles[k],
                   in1=ps_bc0[:], op=ALU.mult),
                   reads=[src_bufs[k], ps_bc0_b], writes=[tmp_b], tag=tag + "a")
            b.emit("v", lambda e, k=k: e.tensor_tensor(out=out_tiles[k], in0=tmp_t,
                   in1=ps_bc1[:], op=ALU.subtract),
                   reads=[tmp_b, ps_bc1_b], writes=[out_bufs[k]], tag=tag + "b")

    def make_ffn_work():
        return dict(
            lgt=(AFA.alloc(TC, rows=8), Buf("lgt")),
            mid=(ABF.alloc(TC, rows=8), Buf("mid")),
            gex=(AFA.alloc(32), Buf("gex")),
            gtb=(ABF.alloc(32), Buf("gtb")),
            ge=(ABF.alloc(TC, rows=8), Buf("ge")),
            grep=(ABF.alloc(TC, rows=64), Buf("grep")),
            wmid=(ABF.alloc(TC, rows=64), Buf("wmid")),
            rows=(AFA.alloc(2), Buf("rows")),
        )

    def ffn(xin_tiles, xin_bufs, res_tiles, res_bufs, out_tiles, out_bufs, W, tag):
        lgt_t, lgt_b = W["lgt"]
        mid_t, mid_b = W["mid"]
        gex_t, gex_b = W["gex"]
        gtb_t, gtb_b = W["gtb"]
        ge_t, ge_b = W["ge"]
        grep_t, grep_b = W["grep"]
        wmid_t, wmid_b = W["wmid"]
        rows_t, rows_b = W["rows"]
        for k in range(KT):
            b.emit("p", lambda e, k=k: e.matmul(out=ps_small[0:8, :],
                   lhsT=rwT_s[:, k * NE:(k + 1) * NE], rhs=xin_tiles[k],
                   start=(k == 0), stop=False),
                   reads=[xin_bufs[k]],
                   war=[ps_small_b],
                   tag=tag + "rout")
        b.emit("p", lambda e: e.matmul(out=ps_small[0:8, :], lhsT=rb_s,
               rhs=ones_row, start=False, stop=True),
               reads=[WB["rb"], WB["ones"]], tag=tag)
        for k in range(KT):
            b.emit("p", lambda e, k=k: e.matmul(out=ps_small[32:40, :],
                   lhsT=lA_s[:, k * NR:(k + 1) * NR], rhs=xin_tiles[k],
                   start=(k == 0), stop=(k == KT - 1), tile_position=(0, 32)),
                   reads=[xin_bufs[k]],
                   war=[ps_small_b],
                   tag=tag + "mid")
        b.close_group("p", ps_small_b)
        b.emit("v", lambda e: e.tensor_copy(out=lgt_t, in_=ps_small[0:8, :]),
               reads=[ps_small_b], writes=[lgt_b], tag=tag + "lgt")
        b.emit("v", lambda e: e.tensor_copy(out=mid_t, in_=ps_small[32:40, :]),
               reads=[ps_small_b], writes=[mid_b], tag=tag + "midc")
        nj = TC // 128
        for j in range(nj):
            b.emit("p", lambda e, j=j: e.transpose(
                   out=ps_small[:, j * 8:(j + 1) * 8],
                   in_=lgt_t[0:8, j * 128:(j + 1) * 128], identity=ident8f),
                   reads=[lgt_b, WB["idf"]] if j == 0 else [lgt_b],
                   war=[ps_small_b],
                   tag=tag + "tr")
        b.close_group("p", ps_small_b)
        mxr = rows_t[0:128, 0:1]
        smr = rows_t[0:128, 1:2]
        for j in range(nj):
            sl = ps_small[:, j * 8:(j + 1) * 8]
            b.emit("v", lambda e, sl=sl: e.reduce_max(out=mxr, in_=sl, axis=AX.X),
                   reads=[ps_small_b], writes=[rows_b], tag=tag + "mx")
            b.emit("v", lambda e: e.tensor_scalar(out=mxr, in0=mxr, scalar1=-1.0,
                   scalar2=None, op0=ALU.mult), reads=[rows_b], writes=[rows_b],
                   tag=tag)
            b.emit("a", lambda e, sl=sl, j=j: e.activation(
                   out=gex_t[:, j * 8:(j + 1) * 8], in_=sl, func=AF.Exp, bias=mxr),
                   reads=[ps_small_b, rows_b], writes=[gex_b], tag=tag + "exp")
            b.emit("v", lambda e, j=j: e.reduce_sum(out=smr,
                   in_=gex_t[:, j * 8:(j + 1) * 8], axis=AX.X),
                   reads=[gex_b], writes=[rows_b], tag=tag + "sum")
            b.emit("v", lambda e: e.reciprocal(out=smr, in_=smr),
                   reads=[rows_b], writes=[rows_b], tag=tag + "rcp")
            b.emit("v", lambda e, j=j: e.tensor_scalar(
                   out=gtb_t[:, j * 8:(j + 1) * 8],
                   in0=gex_t[:, j * 8:(j + 1) * 8], scalar1=smr, scalar2=None,
                   op0=ALU.mult), reads=[gex_b, rows_b], writes=[gtb_b],
                   tag=tag + "gmul")
        for j in range(nj):
            b.emit("p", lambda e, j=j: e.transpose(
                   out=ps_geb[0:8, j * 128:(j + 1) * 128],
                   in_=gtb_t[:, j * 8:(j + 1) * 8], identity=ident_bf),
                   reads=[gtb_b, WB["ident"]] if j == 0 else [gtb_b],
                   war=[ps_geb_b],
                   tag=tag + "trb")
        b.close_group("p", ps_geb_b)
        b.emit("v", lambda e: e.tensor_copy(out=ge_t, in_=ps_geb[0:8, :]),
               reads=[ps_geb_b], writes=[ge_b], tag=tag + "gec")
        b.emit("p", lambda e: e.matmul(out=ps_bc0[0:64, :],
               lhsT=repl_s[0:8, 64:128], rhs=mid_t, start=True, stop=True),
               reads=[mid_b, WB["repl"]], war=[ps_bc0_b], tag=tag + "repm")
        b.close_group("p", ps_bc0_b)
        b.emit("p", lambda e: e.matmul(out=ps_bc1[0:64, :],
               lhsT=repl_s[0:8, 0:64], rhs=ge_t, start=True, stop=True),
               reads=[ge_b], war=[ps_bc1_b], tag=tag + "repg")
        b.close_group("p", ps_bc1_b)
        b.emit("v", lambda e: e.tensor_copy(out=grep_t, in_=ps_bc1[0:64, :]),
               reads=[ps_bc1_b], writes=[grep_b], tag=tag + "grc")
        b.emit("v", lambda e: e.scalar_tensor_tensor(out=wmid_t,
               in0=ps_bc0[0:64, :], scalar=2.0, in1=grep_t, op0=ALU.mult,
               op1=ALU.mult), reads=[ps_bc0_b, grep_b], writes=[wmid_b],
               tag=tag + "wmid")
        for g0 in range(0, KT, 4):
            for m in range(g0, g0 + 4):
                ps, psb = ps_mm[m % 4], ps_mm_b[m % 4]
                for k in range(KT):
                    b.emit("p", lambda e, m=m, k=k, ps=ps: e.matmul(out=ps[:],
                           lhsT=bwT_s[:, k * DM + m * 128:k * DM + (m + 1) * 128],
                           rhs=xin_tiles[k], start=(k == 0), stop=False),
                           reads=[xin_bufs[k]],
                           war=[psb],
                           tag=tag + f"bs{m}")
                b.emit("p", lambda e, m=m, ps=ps: e.matmul(out=ps[:],
                       lhsT=bb_s[0:1, m * 128:(m + 1) * 128], rhs=ones_row,
                       start=False, stop=False), reads=[], war=[psb], tag=tag)
                b.emit("p", lambda e, m=m, ps=ps: e.matmul(out=ps[:],
                       lhsT=lB_s[:, m * 128:(m + 1) * 128], rhs=wmid_t,
                       start=False, stop=True), reads=[wmid_b], war=[psb],
                       tag=tag + "lora")
                b.close_group("p", psb)
            for m in range(g0, g0 + 4):
                ps, psb = ps_mm[m % 4], ps_mm_b[m % 4]
                b.emit("v", lambda e, m=m, ps=ps: e.scalar_tensor_tensor(
                       out=out_tiles[m], in0=ps[:], scalar=0.5, in1=res_tiles[m],
                       op0=ALU.mult, op1=ALU.add),
                       reads=[psb, res_bufs[m]], writes=[out_bufs[m]],
                       tag=tag + "ev")

    ffnw1 = make_ffn_work()

    ABF.mark(); AFA.mark()
    wx_s = ABF.alloc(DT2 * 96)
    cdiag_s = ABF.alloc(64 * 128)
    tails_s = ABF.alloc(DT2 * 4)
    for d in range(DT2):
        dma("g", wx_s[:, d * 96:(d + 1) * 96], wx[d], dsem="boot_g")
    for t in range(64):
        dma("g", cdiag_s[:, t * 128:(t + 1) * 128], cdiag[t],
            dsem="boot_g")
    b.emit("v", lambda e: e.memset(tails_s, 0.0), writes=[WB["tails"]])
    ABF.mark()
    stage = ABF.alloc(2 * DIN)
    stage_b = Buf("stage")
    for k in range(KT):
        dma("g", stage, win[k], writes=[stage_b])
        dma("s", win_bf.ap()[k], stage, reads=[stage_b], writes=[WB["winbf"]])
    for d in range(DT2):
        dma("g", stage[:, 0:DM], wout[d], writes=[stage_b])
        dma("s", wout_bf.ap()[d], stage[:, 0:DM], reads=[stage_b],
            writes=[WB["woutbf"]])
    ABF.reset()
    b.barrier()

    xf_t = [AFA.alloc(TC) for _ in range(KT)]
    xf_b = [Buf(f"xf{k}") for k in range(KT)]
    xbf_t = ABF.alloc(TC)
    xbf_b = Buf("xbf")
    sq_t = ABF.alloc(TC)
    sq_b = Buf("sq")
    lnrows_t = [AFA.alloc(5 * TC, rows=1) for _ in range(2)]
    lnrows_b = [Buf(f"lnrows{i}") for i in range(2)]
    tmp_t = AFA.alloc(TC)
    tmp_b = Buf("tmp")
    xln1_t = [ABF.alloc(TC) for _ in range(KT)]
    xln1_b = [Buf(f"xln1{k}") for k in range(KT)]
    x1_t = [AFA.alloc(TC) for _ in range(KT)]
    x1_b = [Buf(f"x1{k}") for k in range(KT)]
    xln2f_t = AFA.alloc(TC)
    xln2f_b = Buf("xln2f")
    xln2_t = [ABF.alloc(TC) for _ in range(KT)]
    xln2_b = [Buf(f"xln2{k}") for k in range(KT)]

    wg_t = [ABF.alloc(KT * 512) for _ in range(2)]
    wg_b = [Buf(f"wg{i}") for i in range(2)]
    xin_t = [ABF.alloc(TC + 4) for _ in range(2)]
    xin_b = [Buf(f"xin{i}") for i in range(2)]
    u_t = [ABF.alloc(TC) for _ in range(DT2)]
    u_b = [Buf(f"u{d}") for d in range(DT2)]
    gz_t = ABF.alloc(TC)
    gz_b = Buf("gz")
    dtb_t = ABF.alloc(TC, rows=64)
    dtb_b = Buf("dtb")
    bct_t = ABF.alloc(TC, rows=32)
    bct_b = Buf("bct")
    cbuf_t = AFA.alloc(TC)
    cbuf_b = Buf("cbuf")
    ebuf_t = AFA.alloc(TC)
    ebuf_b = Buf("ebuf")
    lbuf_t = AFA.alloc(TC)
    lbuf_b = Buf("lbuf")
    dl_t = ABF.alloc(TC)
    dl_b = Buf("dl")
    du2_t = ABF.alloc(TC)
    du2_b = Buf("du2")
    _spd = Buf("spd"); sb_spd = [_spd] * DT2
    _spdu = Buf("spdu"); sb_spdu = [_spdu] * DT2
    _spu = Buf("spu"); sb_spu = [_spu] * DT2
    _spg = Buf("spg"); sb_spg = [_spg] * DT2
    sb_spbc = Buf("spbc")
    _spx1 = Buf("spx1"); sb_spx1 = [_spx1] * KT
    _spx2 = Buf("spx2"); sb_spx2 = [_spx2] * KT

    for c in range(NCH):
        cs = slice(c * TC, (c + 1) * TC)
        for k in range(KT):
            dma("s", xf_t[k], xT[k][:, cs], writes=[xf_b[k]], tag=f"ldx{c}")
        ln_rows(xf_t, xf_b, lnrows_t, lnrows_b, sq_t, sq_b, xbf_t, xbf_b, f"L1{c}")
        ln_apply(xf_t, xf_b, xln1_t, xln1_b, tmp_t, tmp_b, f"L1a{c}")
        ffn(xln1_t, xln1_b, xf_t, xf_b, x1_t, x1_b, ffnw1, f"F1{c}")
        for k in range(KT):
            dma("s", sp_x1.ap()[k][:, cs], x1_t[k], reads=[x1_b[k]], writes=[sb_spx1[k]], tag=f"sx1{c}")
        ln_rows(x1_t, x1_b, lnrows_t, lnrows_b, sq_t, sq_b, xbf_t, xbf_b, f"L2{c}")
        for k in range(KT):
            b.emit("v", lambda e, k=k: e.tensor_tensor(out=tmp_t, in0=x1_t[k],
                   in1=ps_bc0[:], op=ALU.mult),
                   reads=[x1_b[k], ps_bc0_b], writes=[tmp_b], tag=f"L2a{c}")
            b.emit("v", lambda e: e.tensor_tensor(out=xln2f_t, in0=tmp_t,
                   in1=ps_bc1[:], op=ALU.subtract),
                   reads=[tmp_b, ps_bc1_b], writes=[xln2f_b], tag=f"L2b{c}")
            b.emit("v", lambda e, k=k: e.tensor_copy(out=xln2_t[k], in_=xln2f_t),
                   reads=[xln2f_b], writes=[xln2_b[k]], tag=f"L2c{c}")
            dma("s", sp_x2.ap()[k][:, cs], xln2f_t, reads=[xln2f_b], writes=[sb_spx2[k]], tag=f"sx2{c}")
        for g0 in range(0, 2 * DT2, 4):
            wgi = (g0 // 4) % 2
            for k in range(KT):
                dma("s" if k % 2 == 0 else "a", wg_t[wgi][:, k * 512:(k + 1) * 512],
                    win_bf.ap()[k][:, g0 * 128:(g0 + 4) * 128],
                    reads=[WB["winbf"]], writes=[wg_b[wgi]], tag=f"lw{c}")
            for mi in range(4):
                m = g0 + mi
                ps, psb = ps_mm[mi % 2], ps_mm_b[mi % 2]
                for k in range(KT):
                    b.emit("p", lambda e, k=k, mi=mi, wgi=wgi, ps=ps:
                           e.matmul(out=ps[:],
                                    lhsT=wg_t[wgi][:, k * 512 + mi * 128:k * 512 + (mi + 1) * 128],
                                    rhs=xln2_t[k], start=(k == 0), stop=(k == KT - 1)),
                           reads=[xln2_b[k], wg_b[wgi]],
                           war=[psb],
                           tag=f"wi{c}m{m}")
                b.close_group("p", psb)
                if m < DT2:
                    xt, xtb = xin_t[m % 2], xin_b[m % 2]
                    b.emit("v", lambda e, xt=xt, ps=ps: e.tensor_copy(
                           out=xt[:, 3:3 + TC], in_=ps[:]),
                           reads=[psb], writes=[xtb], tag=f"xc{c}")
                    b.emit("v", lambda e, xt=xt, m=m: e.tensor_copy(
                           out=xt[:, 0:3], in_=tails_s[:, m * 4:m * 4 + 3]),
                           reads=[WB["tails"]], writes=[xtb], tag=f"tl{c}")
                    psc, pscb = ps_mm[(mi + 2) % 4], ps_mm_b[(mi + 2) % 4]
                    for tap in range(4):
                        b.emit("p", lambda e, m=m, tap=tap, xt=xt, psc=psc:
                               e.matmul(out=psc[:],
                                        lhsT=cdiag_s[:, (m * 4 + tap) * 128:(m * 4 + tap + 1) * 128],
                                        rhs=xt[:, tap:tap + TC],
                                        start=(tap == 0), stop=(tap == 3)),
                               reads=[xtb],
                               war=[pscb],
                               tag=f"cv{c}m{m}")
                    b.close_group("p", pscb)
                    b.emit("v", lambda e, xt=xt, m=m: e.tensor_copy(
                           out=tails_s[:, m * 4:m * 4 + 3], in_=xt[:, TC:TC + 3]),
                           reads=[xtb], writes=[WB["tails"]], tag=f"tu{c}")
                    b.emit("a", lambda e, m=m, psc=psc: e.activation(
                           out=u_t[m], in_=psc[:], func=AF.Silu,
                           bias=vcol("conv_b", m)),
                           reads=[pscb, WB["vec"]], writes=[u_b[m]], tag=f"si{c}")
                    dma("s", sp_u.ap()[m][:, cs], u_t[m], reads=[u_b[m]],
                        writes=[sb_spu[m]], tag=f"su{c}")
                else:
                    dz = m - DT2
                    b.emit("a", lambda e, ps=ps: e.activation(out=gz_t, in_=ps[:],
                           func=AF.Silu), reads=[psb], writes=[gz_b], tag=f"sz{c}")
                    dma("s", sp_g.ap()[dz][:, cs], gz_t, reads=[gz_b], writes=[sb_spg[dz]], tag=f"sg{c}")
        for d in range(DT2):
            b.emit("p", lambda e, d=d: e.matmul(out=ps_small[0:96, :],
                   lhsT=wx_s[:, d * 96:(d + 1) * 96], rhs=u_t[d],
                   start=(d == 0), stop=(d == DT2 - 1)),
                   reads=[u_b[d]],
                   war=[ps_small_b],
                   tag=f"wx{c}")
        b.close_group("p", ps_small_b)
        b.emit("v", lambda e: e.tensor_copy(out=dtb_t, in_=ps_small[0:64, :]),
               reads=[ps_small_b], writes=[dtb_b], tag=f"dtb{c}")
        b.emit("v", lambda e: e.tensor_copy(out=bct_t, in_=ps_small[64:96, :]),
               reads=[ps_small_b], writes=[bct_b], tag=f"bct{c}")
        dma("s", sp_bc.ap()[:, cs], bct_t, reads=[bct_b], writes=[sb_spbc], tag=f"sbc{c}")
        for g0 in range(0, DT2, 4):
            for di in range(4):
                d = g0 + di
                ps, psb = ps_mm[di], ps_mm_b[di]
                b.emit("p", lambda e, d=d, ps=ps: e.matmul(out=ps[:],
                       lhsT=wdt_s[:, d * 128:(d + 1) * 128], rhs=dtb_t,
                       start=True, stop=True),
                       reads=[dtb_b], war=[psb], tag=f"wd{c}")
                b.close_group("p", psb)
            for di in range(4):
                d = g0 + di
                ps, psb = ps_mm[di], ps_mm_b[di]
                b.emit("v", lambda e, d=d, ps=ps: e.tensor_scalar(out=cbuf_t,
                       in0=ps[:], scalar1=vcol("b_dt", d), scalar2=SM30,
                       op0=ALU.add, op1=ALU.min),
                       reads=[psb, WB["vec"]], writes=[cbuf_b], tag=f"dm{c}")
                b.emit("a", lambda e: e.activation(out=ebuf_t, in_=cbuf_t,
                       func=AF.Exp), reads=[cbuf_b], writes=[ebuf_b], tag=f"de{c}")
                b.emit("a", lambda e: e.activation(out=lbuf_t, in_=ebuf_t,
                       func=AF.Ln, bias=1.0), reads=[ebuf_b], writes=[lbuf_b],
                       tag=f"dl{c}")
                b.emit("v", lambda e, d=d, ps=ps: e.scalar_tensor_tensor(
                       out=dl_t, in0=ps[:], scalar=vcol("b_dt", d), in1=lbuf_t,
                       op0=ALU.add, op1=ALU.max),
                       reads=[psb, lbuf_b, WB["vec"]], writes=[dl_b], tag=f"dx{c}")
                dma("s", sp_d.ap()[d][:, cs], dl_t, reads=[dl_b], writes=[sb_spd[d]], tag=f"sd{c}")
                b.emit("v", lambda e, d=d: e.tensor_tensor(out=du2_t, in0=dl_t,
                       in1=u_t[d], op=ALU.mult),
                       reads=[dl_b, u_b[d]], writes=[du2_b], tag=f"duz{c}")
                dma("a", sp_du.ap()[d][:, cs], du2_t, reads=[du2_b], writes=[sb_spdu[d]], tag=f"sdu{c}")

    ABF.reset(); AFA.reset()
    b.barrier()

    ABF.mark(); AFA.mark()
    NSP = L // 1024
    NCH2 = NCH // 2
    v_d = sp_d.ap().rearrange("d p (s n) -> (d p s) n", n=TC)
    v_du = sp_du.ap().rearrange("d p (s n) -> (d p s) n", n=TC)
    v_u = sp_u.ap().rearrange("d p (s n) -> (d p s) n", n=TC)
    v_g = sp_g.ap().rearrange("d p (s n) -> (d p s) n", n=TC)
    v_bc = sp_bc.ap().rearrange("r (c n) -> (r c) n", n=TC)
    v_x1 = sp_x1.ap().rearrange("k p (c n) -> (k p c) n", n=TC)
    v_x2 = sp_x2.ap().rearrange("k p (c n) -> (k p c) n", n=TC)
    v_yo = yout.rearrange("k p (c n) -> (k p c) n", n=TC)

    def gdt(dt, j):
        c0 = dt * 8 + j
        return gidx_t.ap()[:, c0:c0 + 1]

    def gbc(r, j):
        c0 = 128 + r * 8 + j
        return gidx_t.ap()[:, c0:c0 + 1]

    def gk(k, j):
        c0 = 384 + k * 8 + j
        return gidx_t.ap()[:, c0:c0 + 1]

    def ig(out, view, idx, reads=(), writes=(), tag=""):
        return b.emit("g", lambda e, o=out, v=view, i=idx: e.indirect_dma_start(
            out=o, out_offset=None, in_=v,
            in_offset=bass.IndirectOffsetOnAxis(ap=i, axis=0)),
            reads=reads, writes=writes, dma=True, tag=tag)

    def isc(view, idx, in_, reads=(), writes=(), tag=""):
        return b.emit("g", lambda e, v=view, i=idx, s=in_: e.indirect_dma_start(
            out=v, out_offset=bass.IndirectOffsetOnAxis(ap=i, axis=0),
            in_=s, in_offset=None),
            reads=reads, writes=writes, dma=True, tag=tag)

    Bbc_t = [ABF.alloc(TC) for _ in range(NS)]
    Bbc_b = [Buf(f"Bb{s}") for s in range(NS)]
    Cbc_t = [ABF.alloc(TC) for _ in range(NS)]
    Cbc_b = [Buf(f"Cb{s}") for s in range(NS)]
    bchain = [Buf(f"bch{i}") for i in range(4)]
    cchain = [Buf(f"cch{i}") for i in range(4)]
    dld_t = [ABF.alloc(TC) for _ in range(2)]
    dld_b = [Buf(f"dld{i}") for i in range(2)]
    duld_t = [ABF.alloc(TC) for _ in range(2)]
    duld_b = [Buf(f"duld{i}") for i in range(2)]
    _u1 = ABF.alloc(TC); uld_t = [_u1, _u1]
    _u1b = Buf("uld"); uld_b = [_u1b, _u1b]
    _g1 = ABF.alloc(TC); gld_t = [_g1, _g1]
    _g1b = Buf("gld"); gld_b = [_g1b, _g1b]
    wbuf_t = [ABF.alloc(TC) for _ in range(2)]
    wbuf_b = [Buf(f"wb{i}") for i in range(2)]
    scr_t = AFA.alloc(TC)
    scr_b = Buf("scr")
    dA_t = [AFA.alloc(TC) for _ in range(4)]
    dA_b = [Buf(f"dA{i}") for i in range(4)]
    dBu_t = [ABF.alloc(TC) for _ in range(4)]
    dBu_b = [Buf(f"dBu{i}") for i in range(4)]
    h_all = ABF.alloc(4 * TC)
    h_t = [h_all[:, i * TC:(i + 1) * TC] for i in range(4)]
    h_b = [Buf(f"h{i}") for i in range(4)]
    h_last = h_all.tensor.ap()

    p_t = [ABF.alloc(TC) for _ in range(4)]
    p_b = [Buf(f"p{s}") for s in range(4)]
    ym_t = [ABF.alloc(TC) for _ in range(DT2)]
    ym_b = [Buf(f"ym{d}") for d in range(DT2)]
    cdl_t = AFA.alloc(TC)
    cdl_b = Buf("cdl")
    rb_t2 = AFA.alloc(TC)
    rb_b2 = Buf("rb2")
    gam_t = AFA.alloc(NS)
    gam_b = Buf("gam")
    part_t = AFA.alloc(NS)
    part_b = Buf("part")
    t1_t = AFA.alloc(TC)
    t1_b = Buf("t1")
    wo2_t = [ABF.alloc(DT2 * 256) for _ in range(2)]
    wo2_b = [Buf(f"wo2{i}") for i in range(2)]
    x1l_t = [AFA.alloc(TC) for _ in range(2)]
    x1l_b = [Buf(f"x1l{i}") for i in range(2)]
    xl2l_t = [AFA.alloc(TC) for _ in range(2)]
    xl2l_b = [Buf(f"xl2l{i}") for i in range(2)]
    h3_t = [AFA.alloc(TC) for _ in range(KT)]
    h3_b = [Buf(f"h3{k}") for k in range(KT)]
    x2_t = [AFA.alloc(TC) for _ in range(KT)]
    x2_b = [Buf(f"x2{k}") for k in range(KT)]
    xln4_t = [ym_t[k] for k in range(KT)]
    xln4_b = [ym_b[k] for k in range(KT)]
    outst_t = AFA.alloc(TC)
    outst_b = Buf("outst")
    xbf2_t = ABF.alloc(TC)
    xbf2_b = Buf("xbf2")
    sq2_t = ABF.alloc(TC)
    sq2_b = Buf("sq2")
    lnrows2_t = [AFA.alloc(5 * TC, rows=1) for _ in range(2)]
    lnrows2_b = [Buf(f"lnrows2{i}") for i in range(2)]
    tmp2_t = AFA.alloc(TC)
    tmp2_b = Buf("tmp2")
    ffnw2 = ffnw1
    sb_yo = [Buf(f"yo{k}") for k in range(4)] * 2

    b.emit("v", lambda e: e.memset(hrun_s, 0.0), writes=[WB["hrun"]])

    for c in range(NCH2):
        cs = slice(c * TC, (c + 1) * TC)
        for s in range(NS):
            q = "s" if s % 2 == 0 else "a"
            dma(q, Bbc_t[s], sp_bc.ap()[s, cs].partition_broadcast(128),
                writes=[bchain[s % 4], Bbc_b[s]], tag=f"tBb{c}")
        for dt in range(DT2):
            sl = dt % 2
            dma("s", dld_t[sl], sp_d.ap()[dt][:, cs],
                writes=[dld_b[sl]], tag=f"tld{c}")
            dma("a", duld_t[sl], sp_du.ap()[dt][:, cs],
                writes=[duld_b[sl]], tag=f"tlu{c}")
            b.emit("v", lambda e, sl=sl: e.tensor_tensor_scan(
                   out=cdl_t, data0=dld_t[sl], data1=dld_t[sl],
                   initial=0.0, op0=ALU.add, op1=ALU.bypass),
                   reads=[dld_b[sl]], writes=[cdl_b], tag=f"tcs{c}")
            b.emit("v", lambda e: e.tensor_scalar(out=rb_t2, in0=cdl_t,
                   scalar1=cdl_t[:, TC - 1:TC], scalar2=None, op0=ALU.subtract),
                   reads=[cdl_b], writes=[rb_b2], tag=f"tr{c}")
            b.emit("v", lambda e, dt=dt: e.tensor_scalar(
                   out=gam_t, in0=posA_s[:, dt * NS:(dt + 1) * NS],
                   scalar1=cdl_t[:, TC - 1:TC], scalar2=-1.0,
                   op0=ALU.mult, op1=ALU.mult),
                   reads=[cdl_b, WB["posA"]], writes=[gam_b], tag=f"tg{c}")
            b.emit("a", lambda e: e.activation(out=gam_t, in_=gam_t, func=AF.Exp),
                   reads=[gam_b], writes=[gam_b], tag=f"tge{c}")
            hsl = hrun_s[:, dt * NS:(dt + 1) * NS]
            b.emit("v", lambda e, hsl=hsl: e.tensor_tensor(out=hsl, in0=hsl,
                   in1=gam_t, op=ALU.mult),
                   reads=[gam_b, WB["hrun"]], writes=[WB["hrun"]], tag=f"thg{c}")
            for s in range(NS):
                r2 = s % 4
                b.emit("a", lambda e, dt=dt, s=s, r2=r2 % 2: e.activation(
                       out=wbuf_t[r2 % 2], in_=rb_t2, func=AF.Exp,
                       scale=posA_s[:, dt * NS + s:dt * NS + s + 1]),
                       reads=[rb_b2, WB["posA"]], writes=[wbuf_b[r2 % 2]], tag=f"tw{c}")
                b.emit("v", lambda e, s=s, sl=sl, r2=r2: e.tensor_tensor(
                       out=dBu_t[r2], in0=duld_t[sl], in1=Bbc_t[s],
                       op=ALU.mult),
                       reads=[duld_b[sl], Bbc_b[s]], writes=[dBu_b[r2]],
                       tag=f"tdb{c}", cost=550)
                b.emit("v", lambda e, s=s, r2=r2: e.scalar_tensor_tensor(
                       out=scr_t, in0=wbuf_t[r2 % 2], scalar=1.0, in1=dBu_t[r2],
                       op0=ALU.mult, op1=ALU.mult,
                       accum_out=part_t[:, s:s + 1]),
                       reads=[wbuf_b[r2 % 2], dBu_b[r2]],
                       writes=[scr_b, part_b], tag=f"ttr{c}")
            hsl2 = hrun_s[:, dt * NS:(dt + 1) * NS]
            b.emit("v", lambda e, hsl2=hsl2: e.tensor_tensor(out=hsl2,
                   in0=hsl2, in1=part_t, op=ALU.add),
                   reads=[part_b, WB["hrun"]], writes=[WB["hrun"]],
                   tag=f"tha{c}")
    b.emit("v", lambda e: e.tensor_scalar(out=carry_s, in0=hrun_s,
           scalar1=smask_s[:, 0:1], scalar2=None, op0=ALU.mult),
           reads=[WB["hrun"], WB["smask"]], writes=[WB["carry"]], tag="cinit")

    for j in range(NCH2):
        for s in range(NS):
            ig(Bbc_t[s], v_bc, gbc(s, j),
               writes=[bchain[s % 4], Bbc_b[s]], tag=f"sBb{j}")
            ig(Cbc_t[s], v_bc, gbc(16 + s, j),
               writes=[cchain[s % 4], Cbc_b[s]], tag=f"sCb{j}")
        if True:
            for dt in range(DT2):
                sl = dt % 2
                psy, psyb = (ps_bc0, ps_bc0_b) if dt % 2 == 0 else (ps_bc1, ps_bc1_b)
                ig(dld_t[sl], v_d, gdt(dt, j), writes=[dld_b[sl]], tag=f"gd{j}")
                ig(duld_t[sl], v_du, gdt(dt, j), writes=[duld_b[sl]], tag=f"gdu{j}")
                ig(uld_t[sl], v_u, gdt(dt, j), writes=[uld_b[sl]], tag=f"gu{j}")
                ig(gld_t[sl], v_g, gdt(dt, j), writes=[gld_b[sl]], tag=f"gg{j}")
                for s in range(NS):
                    r2 = s % 4
                    b.emit("a", lambda e, dt=dt, s=s, sl=sl, r2=r2:
                           e.activation(out=dA_t[r2], in_=dld_t[sl],
                                        func=AF.Exp,
                                        scale=negA_s[:, dt * NS + s:dt * NS + s + 1]),
                           reads=[dld_b[sl], WB["negA"]], writes=[dA_b[r2]],
                           tag=f"dA{j}")
                    b.emit("v", lambda e, s=s, sl=sl, r2=r2: e.tensor_tensor(
                           out=dBu_t[r2], in0=duld_t[sl], in1=Bbc_t[s],
                           op=ALU.mult),
                           reads=[duld_b[sl], Bbc_b[s]], writes=[dBu_b[r2]],
                           tag=f"dB{j}", cost=550)
                    b.emit("v", lambda e, dt=dt, s=s, r2=r2: e.tensor_tensor_scan(
                           out=h_t[r2], data0=dA_t[r2], data1=dBu_t[r2],
                           initial=carry_s[:, dt * NS + s:dt * NS + s + 1],
                           op0=ALU.mult, op1=ALU.add),
                           reads=[dA_b[r2], dBu_b[r2], WB["carry"]],
                           writes=[h_b[r2]], tag=f"ts{j}")
                    if s % 4 == 3:
                        hoff = h_t[0].offset % (128 * 1024 * 64)
                        b.emit("v", lambda e, dt=dt, s=s: e.tensor_copy(
                               out=carry_s[:, dt * NS + s - 3:dt * NS + s + 1],
                               in_=h_all[:, TC - 1::TC]),
                               reads=[h_b[0], h_b[1], h_b[2], h_b[3]],
                               writes=[WB["carry"]], tag=f"cr{j}", cost=200)
                    b.emit("v", lambda e, s=s, r2=r2: e.tensor_tensor(
                           out=p_t[s % 4],
                           in0=h_t[r2], in1=Cbc_t[s], op=ALU.mult),
                           reads=[h_b[r2], Cbc_b[s]], writes=[p_b[s % 4]],
                           tag=f"cm{j}")
                    b.emit("p", lambda e, s=s, psy=psy: e.matmul(out=psy[:],
                           lhsT=ident_bf, rhs=p_t[s % 4], start=(s == 0),
                           stop=(s == NS - 1)),
                           reads=[p_b[s % 4]],
                           war=[psyb],
                           tag=f"sm{j}")
                b.close_group("p", psyb)
                b.emit("v", lambda e, dt=dt, sl=sl, psy=psy:
                       e.scalar_tensor_tensor(
                       out=t1_t, in0=uld_t[sl], scalar=vcol("D", dt),
                       in1=psy[:], op0=ALU.mult, op1=ALU.add),
                       reads=[uld_b[sl], psyb, WB["vec"]], writes=[t1_b],
                       tag=f"y1{j}")
                b.emit("v", lambda e, dt=dt, sl=sl: e.tensor_tensor(
                       out=ym_t[dt], in0=t1_t, in1=gld_t[sl], op=ALU.mult),
                       reads=[t1_b, gld_b[sl]], writes=[ym_b[dt]], tag=f"y2{j}")
            for m in range(KT):
                wgi = m % 2
                if m % 2 == 0:
                    for d in range(DT2):
                        dma("s" if d % 2 == 0 else "a",
                            wo2_t[(m // 2) % 2][:, d * 256:(d + 1) * 256],
                            wout_bf.ap()[d][:, m * 128:(m + 2) * 128],
                            reads=[WB["woutbf"]], writes=[wo2_b[(m // 2) % 2]],
                            tag=f"lwo{j}")
                ps, psb = ps_mm[m % 2], ps_mm_b[m % 2]
                for d in range(DT2):
                    b.emit("p", lambda e, d=d, m=m, ps=ps:
                           e.matmul(out=ps[:],
                                    lhsT=wo2_t[(m // 2) % 2][:, d * 256 + (m % 2) * 128:d * 256 + (m % 2 + 1) * 128],
                                    rhs=ym_t[d], start=(d == 0),
                                    stop=(d == DT2 - 1)),
                           reads=[ym_b[d], wo2_b[(m // 2) % 2]],
                           war=[psb],
                           tag=f"wo{j}m{m}")
                b.close_group("p", psb)
                mi = m % 2
                ig(xl2l_t[mi], v_x2, gk(m, j), writes=[xl2l_b[mi]], tag=f"lx2{j}")
                b.emit("v", lambda e, m=m, mi=mi, ps=ps: e.tensor_tensor(
                       out=h3_t[m], in0=ps[:], in1=xl2l_t[mi], op=ALU.add),
                       reads=[psb, xl2l_b[mi]], writes=[h3_b[m]], tag=f"h3{j}")
            ln_rows(h3_t, h3_b, lnrows2_t, lnrows2_b, sq2_t, sq2_b, xbf2_t,
                    xbf2_b, f"L3{j}")
            for k in range(KT):
                kl = k % 2
                ig(x1l_t[kl], v_x1, gk(k, j), writes=[x1l_b[kl]], tag=f"lx1{j}")
                b.emit("v", lambda e, k=k: e.tensor_tensor(out=tmp2_t,
                       in0=h3_t[k], in1=ps_bc0[:], op=ALU.mult),
                       reads=[h3_b[k], ps_bc0_b], writes=[tmp2_b], tag=f"L3a{j}")
                b.emit("v", lambda e: e.tensor_tensor(out=tmp2_t, in0=tmp2_t,
                       in1=ps_bc1[:], op=ALU.subtract),
                       reads=[tmp2_b, ps_bc1_b], writes=[tmp2_b], tag=f"L3b{j}")
                b.emit("v", lambda e, k=k, kl=kl: e.scalar_tensor_tensor(
                       out=x2_t[k], in0=tmp2_t, scalar=vcol("gamma", k),
                       in1=x1l_t[kl], op0=ALU.mult, op1=ALU.add),
                       reads=[tmp2_b, x1l_b[kl], WB["vec"]], writes=[x2_b[k]],
                       tag=f"L3g{j}")
                b.emit("v", lambda e, k=k: e.tensor_scalar(out=x2_t[k],
                       in0=x2_t[k], scalar1=vcol("beta", k), scalar2=None,
                       op0=ALU.add),
                       reads=[x2_b[k], WB["vec"]], writes=[x2_b[k]], tag=f"L3bt{j}")
            ln_rows(x2_t, x2_b, lnrows2_t, lnrows2_b, sq2_t, sq2_b, xbf2_t,
                    xbf2_b, f"L4{j}")
            ln_apply(x2_t, x2_b, xln4_t, xln4_b, tmp2_t, tmp2_b, f"L4a{j}")
            ffn(xln4_t, xln4_b, x2_t, x2_b, x2_t, x2_b, ffnw2, f"F2{j}")
            for k in range(KT):
                isc(v_yo, gk(k, j), x2_t[k], reads=[x2_b[k]],
                    writes=[sb_yo[k]], tag=f"out{j}")

    ABF.reset(); AFA.reset()
    b.fence()
    b.emit("g", lambda e: e.nop(), wait_all_dma=True, tag="final")
    order = b.schedule()
    b.lower(order)
    b.finalize(order)
    b.coalesce_waits(order)

    with ctx:
        with ExitStack() as sctx:
            sems = {}
            for n in b.all_sems:
                sems[n] = sctx.enter_context(nc.semaphore(n))
            with nc.Block() as block:
                def mk(eng_key):
                    def run(engine):
                        for op in order:
                            if op.eng != eng_key:
                                continue
                            for s, v in op.waits:
                                engine.wait_ge(sems[s], v)
                            inst = op.fn(engine)
                            if op.inc is not None and inst is not None:
                                inst.then_inc(sems[op.inc[0]], op.inc[1])
                    return run
                block.tensor(mk("p"))
                block.vector(mk("v"))
                block.scalar(mk("a"))
                block.gpsimd(mk("g"))
                block.sync(mk("s"))
    return nc


def prep_inputs(inputs, L=8192):
    f = np.ascontiguousarray
    W_in = np.asarray(inputs["L0_attn_W_in"])
    conv_w = np.asarray(inputs["L0_attn_conv_w"])
    d = {}
    d["win"] = f(W_in.reshape(KT, 128, 2 * DIN))
    d["bwT"] = f(np.asarray(inputs["L0_ffn_base_w"]).T.reshape(KT, 128, DM))
    d["rwT"] = f(np.asarray(inputs["L0_ffn_router_w"]).T.reshape(KT, 128, NE))
    d["lA"] = f(np.asarray(inputs["L0_ffn_lora_A"]).reshape(KT, 128, NR))
    d["lB"] = f(np.asarray(inputs["L0_ffn_lora_B"]).reshape(NE * NR, DM))
    d["wx"] = f(np.asarray(inputs["L0_attn_W_x"]).reshape(DT2, 128, 96))
    d["wdt"] = f(np.asarray(inputs["L0_attn_W_dt"]))
    d["wout"] = f(np.asarray(inputs["L0_attn_W_out"]).reshape(DT2, 128, DM))
    cd = np.zeros((DT2 * 4, 128, 128), np.float32)
    idx = np.arange(128)
    for dt in range(DT2):
        for k in range(4):
            cd[dt * 4 + k, idx, idx] = conv_w[dt * 128 + idx, k]
    d["cdiag"] = cd
    d["alog"] = f(np.asarray(inputs["L0_attn_A_log"]).reshape(DT2, 128, NS))
    vec = np.zeros((128, 64), np.float32)
    vec[:, 0:16] = np.asarray(inputs["L0_attn_conv_b"]).reshape(DT2, 128).T
    vec[:, 16:32] = np.asarray(inputs["L0_attn_b_dt"]).reshape(DT2, 128).T
    vec[:, 32:48] = np.asarray(inputs["L0_attn_D"]).reshape(DT2, 128).T
    vec[:, 48:56] = np.asarray(inputs["L0_attn_gamma"]).reshape(KT, 128).T
    vec[:, 56:64] = np.asarray(inputs["L0_attn_beta"]).reshape(KT, 128).T
    d["vec"] = vec
    d["bbrow"] = f(np.asarray(inputs["L0_ffn_base_b"]).reshape(1, DM))
    d["rbrow"] = f(np.asarray(inputs["L0_ffn_router_b"]).reshape(1, NE))
    d["ident_in"] = np.eye(128, dtype=np.float32)
    p = np.arange(128, dtype=np.uint32)
    nch = L // TC
    nc2 = nch // 2
    for sigma in (0, 1):
        gi = np.zeros((128, 448), np.uint32)
        for dt in range(16):
            for j in range(nc2):
                gi[:, dt * 8 + j] = (dt * 128 + p) * nch + (sigma * nc2 + j)
        for r in range(32):
            for j in range(nc2):
                gi[:, 128 + r * 8 + j] = r * nch + (sigma * nc2 + j)
        for k in range(8):
            for j in range(nc2):
                gi[:, 384 + k * 8 + j] = (k * 128 + p) * nch + (sigma * nc2 + j)
        d[f"gidx{sigma}"] = gi
        d[f"smask{sigma}"] = np.full((128, 1), float(sigma), np.float32)
    repl = np.zeros((2, 8, NE * NR), np.float32)
    for ee in range(NE):
        for r in range(NR):
            repl[0, ee, ee * NR + r] = 1.0
            repl[1, r, ee * NR + r] = 1.0
    d["repl_in"] = repl
    return d


def kernel(**inputs):
    x = np.asarray(inputs["L0_attn_x"])
    L = x.shape[1]
    shared = prep_inputs(inputs, L=L)
    nc = build_program(L=L)
    in_maps = []
    for core in range(8):
        bidx = core // 2
        sigma = core % 2
        m = {k: v for k, v in shared.items()
             if not k.startswith(("gidx", "smask"))}
        m["gidx"] = shared[f"gidx{sigma}"]
        m["smask"] = shared[f"smask{sigma}"]
        m["xT"] = np.ascontiguousarray(x[bidx].T).reshape(KT, 128, L)
        in_maps.append(m)
    res = run_bass_kernel_spmd(nc, in_maps, core_ids=list(range(8)))
    out = np.empty((x.shape[0], L, DM), np.float32)
    half = L // 2
    for bidx in range(x.shape[0]):
        y0 = np.asarray(res.results[2 * bidx]["yout"]).reshape(DM, L)
        y1 = np.asarray(res.results[2 * bidx + 1]["yout"]).reshape(DM, L)
        out[bidx, :half] = y0[:, :half].T
        out[bidx, half:] = y1[:, half:].T
    return out


# version 11
# speedup vs baseline: 1.3236x; 1.0063x over previous
import numpy as np
from contextlib import ExitStack

import concourse.bass as bass
import concourse.mybir as mybir
from concourse.bass_utils import run_bass_kernel_spmd

F32 = mybir.dt.float32
BF16 = mybir.dt.bfloat16
AF = mybir.ActivationFunctionType
ALU = mybir.AluOpType
AX = mybir.AxisListType

DM = 1024
DIN = 2048
NS = 16
DTR = 64
NE = 8
NR = 8
KT = DM // 128
DT2 = DIN // 128
EPS = 1e-5
SM30 = 30.0
TC = 512

COMPUTE = ("p", "v", "a", "g", "s")


class Buf:
    __slots__ = ("writer", "readers", "name", "dsem")

    def __init__(self, name=""):
        self.writer = None
        self.readers = []
        self.name = name
        self.dsem = {}


class Op:
    __slots__ = ("eng", "fn", "dma", "reads", "writes", "tag", "cost",
                 "epoch", "idx", "preds", "wait_all_dma", "start",
                 "waits", "inc", "dsem")


class B:

    ENGS = ("p", "v", "a", "g", "s")

    def __init__(self):
        self.ops = []
        self.epoch = 0

    def emit(self, eng, fn, reads=(), writes=(), dma=False, noinc=False,
             extra_waits=(), war=(), tag="", cost=700, wait_all_dma=False,
             dsem=None):
        op = Op()
        op.eng, op.fn, op.dma = eng, fn, dma
        op.reads = list(reads)
        op.writes = list(writes) + list(war)
        op.tag = tag
        op.cost = cost
        op.epoch = self.epoch
        op.wait_all_dma = wait_all_dma
        op.dsem = dsem
        op.idx = len(self.ops)
        op.preds = None
        self.ops.append(op)
        return op

    def fence(self):
        self.epoch += 1

    def barrier(self):
        self.fence()

    def close_group(self, eng, *bufs):
        pass

    def _build_graph(self):
        writer = {}
        readers = {}
        for op in self.ops:
            preds = set()
            for bb in op.reads:
                w = writer.get(id(bb))
                if w is not None:
                    preds.add(w)
            for bb in op.writes:
                w = writer.get(id(bb))
                if w is not None:
                    preds.add(w)
                for r in readers.get(id(bb), ()):
                    preds.add(r)
            for bb in op.writes:
                writer[id(bb)] = op.idx
                readers[id(bb)] = []
            for bb in op.reads:
                readers.setdefault(id(bb), []).append(op.idx)
            preds.discard(op.idx)
            op.preds = sorted(preds)

    COST = {"p": 320, "v": 750, "a": 780, "g": 1200, "s": 300}

    def _estimate_costs(self):
        for op in self.ops:
            if op.cost != 700:
                continue
            if op.dma:
                op.cost = 1200
            else:
                c = self.COST.get(op.eng, 700)
                t = op.tag
                if op.eng == "v":
                    if t.startswith("ts"):
                        c = 1300
                    elif t.startswith(("cm", "dB")):
                        c = 550
                    elif t.startswith("cr"):
                        c = 160
                    elif t.startswith("ttr"):
                        c = 1050
                op.cost = c

    def schedule(self):
        import heapq
        self._estimate_costs()
        self._build_graph()
        n = len(self.ops)
        nsucc_pred = [len(op.preds) for op in self.ops]
        succs = [[] for _ in range(n)]
        for op in self.ops:
            for p in op.preds:
                succs[p].append(op.idx)
        done_t = [0.0] * n
        eng_free = {e: 0.0 for e in self.ENGS}
        q_free = {e: 0.0 for e in ("s", "a", "g")}
        epoch_end = 0.0
        order = []
        by_epoch = {}
        for op in self.ops:
            by_epoch.setdefault(op.epoch, []).append(op)
        for ep in sorted(by_epoch):
            ops_e = by_epoch[ep]
            remaining = set(o.idx for o in ops_e)
            pend = {o.idx: sum(1 for p in self.ops[o.idx].preds
                               if self.ops[p].epoch == ep) for o in ops_e}
            ready = []
            for o in ops_e:
                if pend[o.idx] == 0:
                    heapq.heappush(ready, o.idx)
            scheduled_any = True
            while remaining:
                if not ready:
                    raise RuntimeError("scheduler deadlock (cyclic deps?)")
                cand = heapq.heappop(ready)
                op = self.ops[cand]
                start = max(epoch_end, eng_free[op.eng],
                            max((done_t[p] for p in op.preds), default=0.0))
                if op.dma:
                    issue = max(start, q_free[op.eng])
                    eng_free[op.eng] = issue + 150.0
                    q_free[op.eng] = issue + max(600.0, op.cost * 0.5)
                    done_t[cand] = issue + 1500.0 + op.cost
                else:
                    eng_free[op.eng] = start + op.cost
                    done_t[cand] = start + op.cost
                op.start = start
                order.append(op)
                remaining.discard(cand)
                for sidx in succs[cand]:
                    if self.ops[sidx].epoch == ep and sidx in remaining:
                        pend[sidx] -= 1
                        if pend[sidx] == 0:
                            heapq.heappush(ready, sidx)
            epoch_end = max(max(done_t[o.idx] for o in ops_e), epoch_end)
        return order

    def lower(self, order):
        self.dsem_n = 0
        self.posmap = {}
        cnt = {k: 0 for k in ("p", "v", "a", "g")}
        pos = {}
        wm = {e: {} for e in self.ENGS}

        def dma_sem(op):
            if op.dsem is None:
                tgt = None
                for bb in op.writes:
                    tgt = bb
                    break
                if tgt is None:
                    op.dsem = "dmisc_" + op.eng
                else:
                    if op.eng not in tgt.dsem:
                        tgt.dsem[op.eng] = f"d{self.dsem_n}_{op.eng}"
                        self.dsem_n += 1
                    op.dsem = tgt.dsem[op.eng]
            if op.dsem not in cnt:
                cnt[op.dsem] = 0
            return op.dsem

        for op in order:
            semkey = dma_sem(op) if op.dma else op.eng
            cnt[semkey] += 16 if op.dma else 1
            pos[op.idx] = (semkey, cnt[semkey])
            waits = {}
            for p in op.preds:
                s, v = pos[p]
                if s == "p" and op.eng == "p":
                    continue
                if wm[op.eng].get(s, 0) >= v:
                    continue
                waits[s] = max(waits.get(s, 0), v)
            if op.wait_all_dma:
                for s, v in cnt.items():
                    if s not in ("p", "v", "a", "g") and v > waits.get(s, 0):
                        waits[s] = v
            op.waits = sorted(waits.items())
            op.inc = (semkey, 16 if op.dma else 1)
            self.posmap[(semkey, cnt[semkey])] = op
            for s, v in waits.items():
                wm[op.eng][s] = v
        self.all_sems = sorted(cnt)
        return cnt

    def coalesce_waits(self, order, margin=4000.0):
        per_eng = {}
        for op in order:
            per_eng.setdefault(op.eng, []).append(op)
        for eng, ops in per_eng.items():
            carrier = None
            carrier_limit = 0
            for op in ops:
                if not op.waits:
                    continue
                keep = []
                hoist = []
                for s, v in op.waits:
                    p = self.posmap.get((s, v))
                    if (carrier is not None and p is not None
                            and p.start + p.cost + margin <= carrier.start
                            and carrier_limit < 24):
                        hoist.append((s, v))
                    else:
                        keep.append((s, v))
                if hoist and carrier is not None:
                    w = dict(carrier.waits)
                    for s, v in hoist:
                        w[s] = max(w.get(s, 0), v)
                    carrier.waits = sorted(w.items())
                    carrier_limit += len(hoist)
                op.waits = keep
                if keep:
                    carrier = op
                    carrier_limit = 0
                elif carrier is None:
                    carrier = op
                    carrier_limit = 0

    def finalize(self, order):
        cnt = {}
        epoch_counts = {}
        cur = order[0].epoch if order else 0
        for op in order:
            if op.epoch != cur:
                epoch_counts[cur] = dict(cnt)
                cur = op.epoch
            semkey = op.inc[0]
            cnt[semkey] = cnt.get(semkey, 0) + op.inc[1]
        seen = set()
        for op in order:
            key = (op.eng, op.epoch)
            if key in seen:
                continue
            seen.add(key)
            prev = op.epoch - 1
            if prev in epoch_counts:
                w = dict(op.waits)
                for s, v in epoch_counts[prev].items():
                    if v > 0 and w.get(s, 0) < v:
                        w[s] = v
                op.waits = sorted(w.items())


class Arena:
    def __init__(self, t, n):
        self.t = t
        self.n = n
        self.off = 0
        self.stack = []

    def alloc(self, cols, rows=128):
        off = (self.off + 3) & ~3
        assert off + cols <= self.n, f"arena overflow {off}+{cols}>{self.n}"
        self.off = off + cols
        return self.t.ap()[0:rows, off:off + cols]

    def mark(self):
        self.stack.append(self.off)

    def reset(self):
        self.off = self.stack.pop()


def build_program(L=8192):
    NCH = L // TC
    nc = bass.Bass("TRN2", target_bir_lowering=False, debug=False)

    xT = nc.dram_tensor("xT", [KT, 128, L], F32, kind="ExternalInput").ap()
    win = nc.dram_tensor("win", [KT, 128, 2 * DIN], F32, kind="ExternalInput").ap()
    bwT = nc.dram_tensor("bwT", [KT, 128, DM], F32, kind="ExternalInput").ap()
    rwT = nc.dram_tensor("rwT", [KT, 128, NE], F32, kind="ExternalInput").ap()
    lA = nc.dram_tensor("lA", [KT, 128, NR], F32, kind="ExternalInput").ap()
    lB = nc.dram_tensor("lB", [NE * NR, DM], F32, kind="ExternalInput").ap()
    wx = nc.dram_tensor("wx", [DT2, 128, 96], F32, kind="ExternalInput").ap()
    wdt = nc.dram_tensor("wdt", [DTR, DIN], F32, kind="ExternalInput").ap()
    wout = nc.dram_tensor("wout", [DT2, 128, DM], F32, kind="ExternalInput").ap()
    cdiag = nc.dram_tensor("cdiag", [DT2 * 4, 128, 128], F32, kind="ExternalInput").ap()
    alog = nc.dram_tensor("alog", [DT2, 128, NS], F32, kind="ExternalInput").ap()
    vec = nc.dram_tensor("vec", [128, 64], F32, kind="ExternalInput").ap()
    bbrow = nc.dram_tensor("bbrow", [1, DM], F32, kind="ExternalInput").ap()
    rbrow = nc.dram_tensor("rbrow", [1, NE], F32, kind="ExternalInput").ap()
    ident_in = nc.dram_tensor("ident_in", [128, 128], F32, kind="ExternalInput").ap()
    repl_in = nc.dram_tensor("repl_in", [2, 8, NE * NR], F32, kind="ExternalInput").ap()

    yout = nc.dram_tensor("yout", [KT, 128, L], F32, kind="ExternalOutput").ap()

    win_bf = nc.dram_tensor("win_bf", [KT, 128, 2 * DIN], BF16)
    wout_bf = nc.dram_tensor("wout_bf", [DT2, 128, DM], BF16)
    sp_d = nc.dram_tensor("sp_d", [DT2, 128, L], BF16)
    sp_u = nc.dram_tensor("sp_u", [DT2, 128, L], BF16)
    sp_bc = nc.dram_tensor("sp_bc", [32, L], BF16)
    sp_x1 = nc.dram_tensor("sp_x1", [KT, 128, L], F32)
    sp_x2 = nc.dram_tensor("sp_x2", [KT, 128, L], F32)
    sp_g = nc.dram_tensor("sp_g", [DT2, 128, L], BF16)
    sp_du = nc.dram_tensor("sp_du", [DT2, 128, L], BF16)
    gidx = nc.dram_tensor("gidx", [128, 448], mybir.dt.uint32, kind="ExternalInput").ap()
    smask = nc.dram_tensor("smask", [128, 1], F32, kind="ExternalInput").ap()

    b = B()
    ctx = ExitStack()
    abf_t = ctx.enter_context(nc.sbuf_tensor("abf", [128, 59 * 1024], BF16))
    af_t = ctx.enter_context(nc.sbuf_tensor("af32", [128, 22 * 1024], F32))
    ABF = Arena(abf_t, 59 * 1024)
    AFA = Arena(af_t, 22 * 1024)

    gidx_t = ctx.enter_context(nc.sbuf_tensor("gidxs", [128, 448], mybir.dt.uint32))
    ps_mm = [ctx.enter_context(nc.psum_tensor(f"psmm{i}", [128, TC], F32)) for i in range(4)]
    ps_bc0 = ctx.enter_context(nc.psum_tensor("psbc0", [128, TC], F32))
    ps_bc1 = ctx.enter_context(nc.psum_tensor("psbc1", [128, TC], F32))
    ps_small = ctx.enter_context(nc.psum_tensor("pssmall", [128, TC], F32))
    ps_geb = ctx.enter_context(nc.psum_tensor("psgeb", [8, TC], BF16))
    ps_mm_b = [Buf(f"psmm{i}") for i in range(4)]
    ps_bc0_b, ps_bc1_b = Buf("psbc0"), Buf("psbc1")
    ps_small_b = Buf("pssmall")
    ps_geb_b = Buf("psgeb")

    bwT_s = ABF.alloc(KT * DM)
    rwT_s = ABF.alloc(KT * NE)
    lA_s = ABF.alloc(KT * NR)
    lB_s = ABF.alloc(DM, rows=NE * NR)
    wdt_s = ABF.alloc(DIN, rows=DTR)
    ident_s = ABF.alloc(128)
    repl_s = ABF.alloc(2 * NE * NR, rows=8)
    ones_row = ABF.alloc(TC, rows=1)
    onescol = ABF.alloc(1)
    bb_s = ABF.alloc(DM, rows=1)
    rb_s = ABF.alloc(NE, rows=1)

    onesrow_f = AFA.alloc(128, rows=1)
    vec_s = AFA.alloc(64)
    negA_s = AFA.alloc(DT2 * NS)
    posA_s = AFA.alloc(DT2 * NS)
    carry_s = AFA.alloc(DT2 * NS)
    hrun_s = AFA.alloc(DT2 * NS)
    smask_s = AFA.alloc(4)
    idf_s = AFA.alloc(128)
    atmp = AFA.alloc(NS)

    CONSTS = [0.0, 1.0, -1.0, 0.5, -0.5, 2.0, EPS, SM30, 1.0 / DM]
    const_t = AFA.alloc(len(CONSTS))
    cb = Buf("consts")
    for ci, cv in enumerate(CONSTS):
        nc.const_aps.register(F32, cv, const_t[:, ci:ci + 1])

    WB = {k: Buf(k) for k in ("bwT", "rwT", "lA", "lB", "wdt", "ident", "repl",
                              "ones", "bb", "rb", "onesf", "vec", "negA",
                              "posA", "carry", "hrun", "smask", "gidx",
                              "tails", "winbf", "woutbf", "wx",
                              "cdiag", "idf", "atmp")}

    def dma(eng, out, in_, reads=(), writes=(), tag="", dsem=None):
        return b.emit(eng, lambda e, o=out, i=in_: e.dma_start(out=o, in_=i),
                      reads=reads, writes=writes, dma=True, tag=tag, dsem=dsem)

    for k in range(KT):
        dma("g", bwT_s[:, k * DM:(k + 1) * DM], bwT[k], dsem="boot_g")
        dma("g", rwT_s[:, k * NE:(k + 1) * NE], rwT[k], dsem="boot_g")
        dma("g", lA_s[:, k * NR:(k + 1) * NR], lA[k], dsem="boot_g")
    dma("g", lB_s, lB, dsem="boot_g")
    dma("g", wdt_s, wdt, dsem="boot_g")
    dma("g", ident_s, ident_in, dsem="boot_g")
    dma("s", idf_s, ident_in, dsem="boot_s")
    dma("g", repl_s[0:8, 0:64], repl_in[0], dsem="boot_g")
    dma("g", repl_s[0:8, 64:128], repl_in[1], dsem="boot_g")
    dma("g", bb_s, bbrow, dsem="boot_g")
    dma("g", rb_s, rbrow, dsem="boot_g")
    dma("s", vec_s, vec, dsem="boot_s")
    dma("s", gidx_t.ap(), gidx, dsem="boot_s")
    dma("s", smask_s[:, 0:1], smask, dsem="boot_s")

    for ci, cv in enumerate(CONSTS):
        b.emit("v", lambda e, ci=ci, cv=cv: e.memset(const_t[:, ci:ci + 1], cv),
               writes=[cb])
    b.emit("v", lambda e: e.memset(ones_row, 1.0), writes=[WB["ones"]])
    b.emit("v", lambda e: e.memset(onescol, 1.0), writes=[WB["ones"]])
    b.emit("v", lambda e: e.memset(onesrow_f, 1.0), writes=[WB["onesf"]])
    b.emit("v", lambda e: e.memset(carry_s, 0.0), writes=[WB["carry"]])
    b.fence()
    for d in range(DT2):
        slp = posA_s[:, d * NS:(d + 1) * NS]
        sln = negA_s[:, d * NS:(d + 1) * NS]
        dma("s", atmp, alog[d], writes=[WB["atmp"]])
        b.emit("a", lambda e, slp=slp: e.activation(out=slp, in_=atmp, func=AF.Exp),
               reads=[WB["atmp"]], writes=[WB["posA"]])
        b.emit("a", lambda e, slp=slp, sln=sln: e.mul(out=sln, in_=slp, mul=-1.0),
               reads=[WB["posA"]], writes=[WB["negA"]])

    ident8f = idf_s[0:8, 0:8]
    ident_bf = ident_s[0:128, 0:128]

    def vcol(name, j):
        base = dict(conv_b=0, b_dt=16, D=32, gamma=48, beta=56)[name]
        return vec_s[:, base + j:base + j + 1]

    _lnrot = [0]

    def ln_rows(src_tiles, src_bufs, lnrows_sets, lnrows_bs, sq_t, sq_b, xbf_t, xbf_b, tag):
        _lnrot[0] ^= 1
        lnrows = lnrows_sets[_lnrot[0]] if isinstance(lnrows_sets, (list, tuple)) else lnrows_sets
        lnrows_b = lnrows_bs[_lnrot[0]] if isinstance(lnrows_bs, (list, tuple)) else lnrows_bs
        for k in range(KT):
            b.emit("v", lambda e, k=k: e.tensor_copy(out=xbf_t, in_=src_tiles[k]),
                   reads=[src_bufs[k]], writes=[xbf_b], tag=tag + "cast")
            b.emit("a", lambda e, k=k: e.activation(out=sq_t, in_=src_tiles[k],
                   func=AF.Square), reads=[src_bufs[k]], writes=[sq_b], tag=tag + "sq")
            b.emit("p", lambda e, st=(k == 0), last=(k == KT - 1): e.matmul(
                   out=ps_small[0:1, :], lhsT=onescol, rhs=xbf_t, start=st,
                   stop=last),
                   reads=[xbf_b, WB["ones"]] if k == 0 else [xbf_b],
                   war=[ps_small_b],
                   tag=tag + "mmx")
            b.emit("p", lambda e, st=(k == 0), last=(k == KT - 1): e.matmul(
                   out=ps_bc0[0:1, :], lhsT=onescol, rhs=sq_t, start=st,
                   stop=last),
                   reads=[sq_b],
                   war=[ps_bc0_b],
                   tag=tag + "mmq")
        b.close_group("p", ps_small_b, ps_bc0_b)
        mu = lnrows[0:1, 0:TC]
        m2 = lnrows[0:1, TC:2 * TC]
        var = lnrows[0:1, 2 * TC:3 * TC]
        arow = lnrows[0:1, 3 * TC:4 * TC]
        brow = lnrows[0:1, 4 * TC:5 * TC]
        b.emit("v", lambda e: e.tensor_scalar(out=mu, in0=ps_small[0:1, :],
               scalar1=1.0 / DM, scalar2=None, op0=ALU.mult),
               reads=[ps_small_b], writes=[lnrows_b], tag=tag)
        b.emit("v", lambda e: e.tensor_scalar(out=m2, in0=ps_bc0[0:1, :],
               scalar1=1.0 / DM, scalar2=None, op0=ALU.mult),
               reads=[ps_bc0_b], writes=[lnrows_b], tag=tag)
        b.emit("v", lambda e: e.tensor_tensor(out=var, in0=mu, in1=mu, op=ALU.mult),
               reads=[lnrows_b], writes=[lnrows_b], tag=tag)
        b.emit("v", lambda e: e.tensor_tensor(out=var, in0=m2, in1=var,
               op=ALU.subtract), reads=[lnrows_b], writes=[lnrows_b], tag=tag)
        b.emit("a", lambda e: e.activation(out=arow, in_=var, func=AF.Ln, bias=EPS),
               reads=[lnrows_b], writes=[lnrows_b], tag=tag)
        b.emit("a", lambda e: e.activation(out=arow, in_=arow, func=AF.Exp, scale=-0.5),
               reads=[lnrows_b], writes=[lnrows_b], tag=tag)
        b.emit("v", lambda e: e.tensor_tensor(out=brow, in0=mu, in1=arow, op=ALU.mult),
               reads=[lnrows_b], writes=[lnrows_b], tag=tag)
        b.emit("p", lambda e: e.matmul(out=ps_bc0[:], lhsT=onesrow_f, rhs=arow,
               start=True, stop=True),
               reads=[lnrows_b, WB["onesf"]], war=[ps_bc0_b], tag=tag + "bc0")
        b.close_group("p", ps_bc0_b)
        b.emit("p", lambda e: e.matmul(out=ps_bc1[:], lhsT=onesrow_f, rhs=brow,
               start=True, stop=True),
               reads=[lnrows_b], war=[ps_bc1_b], tag=tag + "bc1")
        b.close_group("p", ps_bc1_b)

    def ln_apply(src_tiles, src_bufs, out_tiles, out_bufs, tmp_t, tmp_b, tag):
        for k in range(KT):
            b.emit("v", lambda e, k=k: e.tensor_tensor(out=tmp_t, in0=src_tiles[k],
                   in1=ps_bc0[:], op=ALU.mult),
                   reads=[src_bufs[k], ps_bc0_b], writes=[tmp_b], tag=tag + "a")
            b.emit("v", lambda e, k=k: e.tensor_tensor(out=out_tiles[k], in0=tmp_t,
                   in1=ps_bc1[:], op=ALU.subtract),
                   reads=[tmp_b, ps_bc1_b], writes=[out_bufs[k]], tag=tag + "b")

    def make_ffn_work():
        return dict(
            lgt=(AFA.alloc(TC, rows=8), Buf("lgt")),
            mid=(ABF.alloc(TC, rows=8), Buf("mid")),
            gex=(AFA.alloc(32), Buf("gex")),
            gtb=(ABF.alloc(32), Buf("gtb")),
            ge=(ABF.alloc(TC, rows=8), Buf("ge")),
            grep=(ABF.alloc(TC, rows=64), Buf("grep")),
            wmid=(ABF.alloc(TC, rows=64), Buf("wmid")),
            rows=(AFA.alloc(2), Buf("rows")),
        )

    def ffn(xin_tiles, xin_bufs, res_tiles, res_bufs, out_tiles, out_bufs, W, tag):
        lgt_t, lgt_b = W["lgt"]
        mid_t, mid_b = W["mid"]
        gex_t, gex_b = W["gex"]
        gtb_t, gtb_b = W["gtb"]
        ge_t, ge_b = W["ge"]
        grep_t, grep_b = W["grep"]
        wmid_t, wmid_b = W["wmid"]
        rows_t, rows_b = W["rows"]
        for k in range(KT):
            b.emit("p", lambda e, k=k: e.matmul(out=ps_small[0:8, :],
                   lhsT=rwT_s[:, k * NE:(k + 1) * NE], rhs=xin_tiles[k],
                   start=(k == 0), stop=False),
                   reads=[xin_bufs[k]],
                   war=[ps_small_b],
                   tag=tag + "rout")
        b.emit("p", lambda e: e.matmul(out=ps_small[0:8, :], lhsT=rb_s,
               rhs=ones_row, start=False, stop=True),
               reads=[WB["rb"], WB["ones"]], tag=tag)
        for k in range(KT):
            b.emit("p", lambda e, k=k: e.matmul(out=ps_small[32:40, :],
                   lhsT=lA_s[:, k * NR:(k + 1) * NR], rhs=xin_tiles[k],
                   start=(k == 0), stop=(k == KT - 1), tile_position=(0, 32)),
                   reads=[xin_bufs[k]],
                   war=[ps_small_b],
                   tag=tag + "mid")
        b.close_group("p", ps_small_b)
        b.emit("v", lambda e: e.tensor_copy(out=lgt_t, in_=ps_small[0:8, :]),
               reads=[ps_small_b], writes=[lgt_b], tag=tag + "lgt")
        b.emit("v", lambda e: e.tensor_copy(out=mid_t, in_=ps_small[32:40, :]),
               reads=[ps_small_b], writes=[mid_b], tag=tag + "midc")
        nj = TC // 128
        for j in range(nj):
            b.emit("p", lambda e, j=j: e.transpose(
                   out=ps_small[:, j * 8:(j + 1) * 8],
                   in_=lgt_t[0:8, j * 128:(j + 1) * 128], identity=ident8f),
                   reads=[lgt_b, WB["idf"]] if j == 0 else [lgt_b],
                   war=[ps_small_b],
                   tag=tag + "tr")
        b.close_group("p", ps_small_b)
        mxr = rows_t[0:128, 0:1]
        smr = rows_t[0:128, 1:2]
        for j in range(nj):
            sl = ps_small[:, j * 8:(j + 1) * 8]
            b.emit("v", lambda e, sl=sl: e.reduce_max(out=mxr, in_=sl, axis=AX.X),
                   reads=[ps_small_b], writes=[rows_b], tag=tag + "mx")
            b.emit("v", lambda e: e.tensor_scalar(out=mxr, in0=mxr, scalar1=-1.0,
                   scalar2=None, op0=ALU.mult), reads=[rows_b], writes=[rows_b],
                   tag=tag)
            b.emit("a", lambda e, sl=sl, j=j: e.activation(
                   out=gex_t[:, j * 8:(j + 1) * 8], in_=sl, func=AF.Exp, bias=mxr),
                   reads=[ps_small_b, rows_b], writes=[gex_b], tag=tag + "exp")
            b.emit("v", lambda e, j=j: e.reduce_sum(out=smr,
                   in_=gex_t[:, j * 8:(j + 1) * 8], axis=AX.X),
                   reads=[gex_b], writes=[rows_b], tag=tag + "sum")
            b.emit("v", lambda e: e.reciprocal(out=smr, in_=smr),
                   reads=[rows_b], writes=[rows_b], tag=tag + "rcp")
            b.emit("v", lambda e, j=j: e.tensor_scalar(
                   out=gtb_t[:, j * 8:(j + 1) * 8],
                   in0=gex_t[:, j * 8:(j + 1) * 8], scalar1=smr, scalar2=None,
                   op0=ALU.mult), reads=[gex_b, rows_b], writes=[gtb_b],
                   tag=tag + "gmul")
        for j in range(nj):
            b.emit("p", lambda e, j=j: e.transpose(
                   out=ps_geb[0:8, j * 128:(j + 1) * 128],
                   in_=gtb_t[:, j * 8:(j + 1) * 8], identity=ident_bf),
                   reads=[gtb_b, WB["ident"]] if j == 0 else [gtb_b],
                   war=[ps_geb_b],
                   tag=tag + "trb")
        b.close_group("p", ps_geb_b)
        b.emit("v", lambda e: e.tensor_copy(out=ge_t, in_=ps_geb[0:8, :]),
               reads=[ps_geb_b], writes=[ge_b], tag=tag + "gec")
        b.emit("p", lambda e: e.matmul(out=ps_bc0[0:64, :],
               lhsT=repl_s[0:8, 64:128], rhs=mid_t, start=True, stop=True),
               reads=[mid_b, WB["repl"]], war=[ps_bc0_b], tag=tag + "repm")
        b.close_group("p", ps_bc0_b)
        b.emit("p", lambda e: e.matmul(out=ps_bc1[0:64, :],
               lhsT=repl_s[0:8, 0:64], rhs=ge_t, start=True, stop=True),
               reads=[ge_b], war=[ps_bc1_b], tag=tag + "repg")
        b.close_group("p", ps_bc1_b)
        b.emit("v", lambda e: e.tensor_copy(out=grep_t, in_=ps_bc1[0:64, :]),
               reads=[ps_bc1_b], writes=[grep_b], tag=tag + "grc")
        b.emit("v", lambda e: e.scalar_tensor_tensor(out=wmid_t,
               in0=ps_bc0[0:64, :], scalar=2.0, in1=grep_t, op0=ALU.mult,
               op1=ALU.mult), reads=[ps_bc0_b, grep_b], writes=[wmid_b],
               tag=tag + "wmid")
        for g0 in range(0, KT, 4):
            for m in range(g0, g0 + 4):
                ps, psb = ps_mm[m % 4], ps_mm_b[m % 4]
                for k in range(KT):
                    b.emit("p", lambda e, m=m, k=k, ps=ps: e.matmul(out=ps[:],
                           lhsT=bwT_s[:, k * DM + m * 128:k * DM + (m + 1) * 128],
                           rhs=xin_tiles[k], start=(k == 0), stop=False),
                           reads=[xin_bufs[k]],
                           war=[psb],
                           tag=tag + f"bs{m}")
                b.emit("p", lambda e, m=m, ps=ps: e.matmul(out=ps[:],
                       lhsT=bb_s[0:1, m * 128:(m + 1) * 128], rhs=ones_row,
                       start=False, stop=False), reads=[], war=[psb], tag=tag)
                b.emit("p", lambda e, m=m, ps=ps: e.matmul(out=ps[:],
                       lhsT=lB_s[:, m * 128:(m + 1) * 128], rhs=wmid_t,
                       start=False, stop=True), reads=[wmid_b], war=[psb],
                       tag=tag + "lora")
                b.close_group("p", psb)
            for m in range(g0, g0 + 4):
                ps, psb = ps_mm[m % 4], ps_mm_b[m % 4]
                b.emit("v", lambda e, m=m, ps=ps: e.scalar_tensor_tensor(
                       out=out_tiles[m], in0=ps[:], scalar=0.5, in1=res_tiles[m],
                       op0=ALU.mult, op1=ALU.add),
                       reads=[psb, res_bufs[m]], writes=[out_bufs[m]],
                       tag=tag + "ev")

    ffnw1 = make_ffn_work()

    ABF.mark(); AFA.mark()
    wx_s = ABF.alloc(DT2 * 96)
    cdiag_s = ABF.alloc(64 * 128)
    tails_s = ABF.alloc(DT2 * 4)
    for d in range(DT2):
        dma("g", wx_s[:, d * 96:(d + 1) * 96], wx[d], dsem="boot_g")
    for t in range(64):
        dma("g", cdiag_s[:, t * 128:(t + 1) * 128], cdiag[t],
            dsem="boot_g")
    b.emit("v", lambda e: e.memset(tails_s, 0.0), writes=[WB["tails"]])
    ABF.mark()
    stage = ABF.alloc(2 * DIN)
    stage_b = Buf("stage")
    for k in range(KT):
        dma("g", stage, win[k], writes=[stage_b])
        dma("s", win_bf.ap()[k], stage, reads=[stage_b], writes=[WB["winbf"]])
    for d in range(DT2):
        dma("g", stage[:, 0:DM], wout[d], writes=[stage_b])
        dma("s", wout_bf.ap()[d], stage[:, 0:DM], reads=[stage_b],
            writes=[WB["woutbf"]])
    ABF.reset()
    b.barrier()

    xf_t = [AFA.alloc(TC) for _ in range(KT)]
    xf_b = [Buf(f"xf{k}") for k in range(KT)]
    xbf_t = ABF.alloc(TC)
    xbf_b = Buf("xbf")
    sq_t = ABF.alloc(TC)
    sq_b = Buf("sq")
    lnrows_t = [AFA.alloc(5 * TC, rows=1) for _ in range(2)]
    lnrows_b = [Buf(f"lnrows{i}") for i in range(2)]
    tmp_t = AFA.alloc(TC)
    tmp_b = Buf("tmp")
    xln1_t = [ABF.alloc(TC) for _ in range(KT)]
    xln1_b = [Buf(f"xln1{k}") for k in range(KT)]
    x1_t = [AFA.alloc(TC) for _ in range(KT)]
    x1_b = [Buf(f"x1{k}") for k in range(KT)]
    xln2f_t = AFA.alloc(TC)
    xln2f_b = Buf("xln2f")
    xln2_t = [ABF.alloc(TC) for _ in range(KT)]
    xln2_b = [Buf(f"xln2{k}") for k in range(KT)]

    wg_t = [ABF.alloc(KT * 512) for _ in range(2)]
    wg_b = [Buf(f"wg{i}") for i in range(2)]
    xin_t = [ABF.alloc(TC + 4) for _ in range(2)]
    xin_b = [Buf(f"xin{i}") for i in range(2)]
    u_t = [ABF.alloc(TC) for _ in range(DT2)]
    u_b = [Buf(f"u{d}") for d in range(DT2)]
    gz_t = ABF.alloc(TC)
    gz_b = Buf("gz")
    dtb_t = ABF.alloc(TC, rows=64)
    dtb_b = Buf("dtb")
    bct_t = ABF.alloc(TC, rows=32)
    bct_b = Buf("bct")
    cbuf_t = AFA.alloc(TC)
    cbuf_b = Buf("cbuf")
    ebuf_t = AFA.alloc(TC)
    ebuf_b = Buf("ebuf")
    lbuf_t = AFA.alloc(TC)
    lbuf_b = Buf("lbuf")
    dl_t = ABF.alloc(TC)
    dl_b = Buf("dl")
    du2_t = ABF.alloc(TC)
    du2_b = Buf("du2")
    _spd = Buf("spd"); sb_spd = [_spd] * DT2
    _spdu = Buf("spdu"); sb_spdu = [_spdu] * DT2
    _spu = Buf("spu"); sb_spu = [_spu] * DT2
    _spg = Buf("spg"); sb_spg = [_spg] * DT2
    sb_spbc = Buf("spbc")
    _spx1 = Buf("spx1"); sb_spx1 = [_spx1] * KT
    _spx2 = Buf("spx2"); sb_spx2 = [_spx2] * KT

    for c in range(NCH):
        cs = slice(c * TC, (c + 1) * TC)
        for k in range(KT):
            dma("s", xf_t[k], xT[k][:, cs], writes=[xf_b[k]], tag=f"ldx{c}")
        ln_rows(xf_t, xf_b, lnrows_t, lnrows_b, sq_t, sq_b, xbf_t, xbf_b, f"L1{c}")
        ln_apply(xf_t, xf_b, xln1_t, xln1_b, tmp_t, tmp_b, f"L1a{c}")
        ffn(xln1_t, xln1_b, xf_t, xf_b, x1_t, x1_b, ffnw1, f"F1{c}")
        for k in range(KT):
            dma("s", sp_x1.ap()[k][:, cs], x1_t[k], reads=[x1_b[k]], writes=[sb_spx1[k]], tag=f"sx1{c}")
        ln_rows(x1_t, x1_b, lnrows_t, lnrows_b, sq_t, sq_b, xbf_t, xbf_b, f"L2{c}")
        for k in range(KT):
            b.emit("v", lambda e, k=k: e.tensor_tensor(out=tmp_t, in0=x1_t[k],
                   in1=ps_bc0[:], op=ALU.mult),
                   reads=[x1_b[k], ps_bc0_b], writes=[tmp_b], tag=f"L2a{c}")
            b.emit("v", lambda e: e.tensor_tensor(out=xln2f_t, in0=tmp_t,
                   in1=ps_bc1[:], op=ALU.subtract),
                   reads=[tmp_b, ps_bc1_b], writes=[xln2f_b], tag=f"L2b{c}")
            b.emit("v", lambda e, k=k: e.tensor_copy(out=xln2_t[k], in_=xln2f_t),
                   reads=[xln2f_b], writes=[xln2_b[k]], tag=f"L2c{c}")
            dma("s", sp_x2.ap()[k][:, cs], xln2f_t, reads=[xln2f_b], writes=[sb_spx2[k]], tag=f"sx2{c}")
        for g0 in range(0, 2 * DT2, 4):
            wgi = (g0 // 4) % 2
            for k in range(KT):
                dma("s" if k % 2 == 0 else "a", wg_t[wgi][:, k * 512:(k + 1) * 512],
                    win_bf.ap()[k][:, g0 * 128:(g0 + 4) * 128],
                    reads=[WB["winbf"]], writes=[wg_b[wgi]], tag=f"lw{c}")
            for mi in range(4):
                m = g0 + mi
                ps, psb = ps_mm[mi % 2], ps_mm_b[mi % 2]
                for k in range(KT):
                    b.emit("p", lambda e, k=k, mi=mi, wgi=wgi, ps=ps:
                           e.matmul(out=ps[:],
                                    lhsT=wg_t[wgi][:, k * 512 + mi * 128:k * 512 + (mi + 1) * 128],
                                    rhs=xln2_t[k], start=(k == 0), stop=(k == KT - 1)),
                           reads=[xln2_b[k], wg_b[wgi]],
                           war=[psb],
                           tag=f"wi{c}m{m}")
                b.close_group("p", psb)
                if m < DT2:
                    xt, xtb = xin_t[m % 2], xin_b[m % 2]
                    b.emit("v", lambda e, xt=xt, ps=ps: e.tensor_copy(
                           out=xt[:, 3:3 + TC], in_=ps[:]),
                           reads=[psb], writes=[xtb], tag=f"xc{c}")
                    b.emit("v", lambda e, xt=xt, m=m: e.tensor_copy(
                           out=xt[:, 0:3], in_=tails_s[:, m * 4:m * 4 + 3]),
                           reads=[WB["tails"]], writes=[xtb], tag=f"tl{c}")
                    psc, pscb = ps_mm[(mi + 2) % 4], ps_mm_b[(mi + 2) % 4]
                    for tap in range(4):
                        b.emit("p", lambda e, m=m, tap=tap, xt=xt, psc=psc:
                               e.matmul(out=psc[:],
                                        lhsT=cdiag_s[:, (m * 4 + tap) * 128:(m * 4 + tap + 1) * 128],
                                        rhs=xt[:, tap:tap + TC],
                                        start=(tap == 0), stop=(tap == 3)),
                               reads=[xtb],
                               war=[pscb],
                               tag=f"cv{c}m{m}")
                    b.close_group("p", pscb)
                    b.emit("v", lambda e, xt=xt, m=m: e.tensor_copy(
                           out=tails_s[:, m * 4:m * 4 + 3], in_=xt[:, TC:TC + 3]),
                           reads=[xtb], writes=[WB["tails"]], tag=f"tu{c}")
                    b.emit("a", lambda e, m=m, psc=psc: e.activation(
                           out=u_t[m], in_=psc[:], func=AF.Silu,
                           bias=vcol("conv_b", m)),
                           reads=[pscb, WB["vec"]], writes=[u_b[m]], tag=f"si{c}")
                    dma("s", sp_u.ap()[m][:, cs], u_t[m], reads=[u_b[m]],
                        writes=[sb_spu[m]], tag=f"su{c}")
                else:
                    dz = m - DT2
                    b.emit("a", lambda e, ps=ps: e.activation(out=gz_t, in_=ps[:],
                           func=AF.Silu), reads=[psb], writes=[gz_b], tag=f"sz{c}")
                    dma("s", sp_g.ap()[dz][:, cs], gz_t, reads=[gz_b], writes=[sb_spg[dz]], tag=f"sg{c}")
        for d in range(DT2):
            b.emit("p", lambda e, d=d: e.matmul(out=ps_small[0:96, :],
                   lhsT=wx_s[:, d * 96:(d + 1) * 96], rhs=u_t[d],
                   start=(d == 0), stop=(d == DT2 - 1)),
                   reads=[u_b[d]],
                   war=[ps_small_b],
                   tag=f"wx{c}")
        b.close_group("p", ps_small_b)
        b.emit("v", lambda e: e.tensor_copy(out=dtb_t, in_=ps_small[0:64, :]),
               reads=[ps_small_b], writes=[dtb_b], tag=f"dtb{c}")
        b.emit("v", lambda e: e.tensor_copy(out=bct_t, in_=ps_small[64:96, :]),
               reads=[ps_small_b], writes=[bct_b], tag=f"bct{c}")
        dma("s", sp_bc.ap()[:, cs], bct_t, reads=[bct_b], writes=[sb_spbc], tag=f"sbc{c}")
        for g0 in range(0, DT2, 4):
            for di in range(4):
                d = g0 + di
                ps, psb = ps_mm[di], ps_mm_b[di]
                b.emit("p", lambda e, d=d, ps=ps: e.matmul(out=ps[:],
                       lhsT=wdt_s[:, d * 128:(d + 1) * 128], rhs=dtb_t,
                       start=True, stop=True),
                       reads=[dtb_b], war=[psb], tag=f"wd{c}")
                b.close_group("p", psb)
            for di in range(4):
                d = g0 + di
                ps, psb = ps_mm[di], ps_mm_b[di]
                b.emit("v", lambda e, d=d, ps=ps: e.tensor_scalar(out=cbuf_t,
                       in0=ps[:], scalar1=vcol("b_dt", d), scalar2=SM30,
                       op0=ALU.add, op1=ALU.min),
                       reads=[psb, WB["vec"]], writes=[cbuf_b], tag=f"dm{c}")
                b.emit("a", lambda e: e.activation(out=ebuf_t, in_=cbuf_t,
                       func=AF.Exp), reads=[cbuf_b], writes=[ebuf_b], tag=f"de{c}")
                b.emit("a", lambda e: e.activation(out=lbuf_t, in_=ebuf_t,
                       func=AF.Ln, bias=1.0), reads=[ebuf_b], writes=[lbuf_b],
                       tag=f"dl{c}")
                b.emit("v", lambda e, d=d, ps=ps: e.scalar_tensor_tensor(
                       out=dl_t, in0=ps[:], scalar=vcol("b_dt", d), in1=lbuf_t,
                       op0=ALU.add, op1=ALU.max),
                       reads=[psb, lbuf_b, WB["vec"]], writes=[dl_b], tag=f"dx{c}")
                dma("s", sp_d.ap()[d][:, cs], dl_t, reads=[dl_b], writes=[sb_spd[d]], tag=f"sd{c}")
                b.emit("v", lambda e, d=d: e.tensor_tensor(out=du2_t, in0=dl_t,
                       in1=u_t[d], op=ALU.mult),
                       reads=[dl_b, u_b[d]], writes=[du2_b], tag=f"duz{c}")
                dma("a", sp_du.ap()[d][:, cs], du2_t, reads=[du2_b], writes=[sb_spdu[d]], tag=f"sdu{c}")

    ABF.reset(); AFA.reset()
    b.barrier()

    ABF.mark(); AFA.mark()
    NSP = L // 1024
    NCH2 = NCH // 2
    v_d = sp_d.ap().rearrange("d p (s n) -> (d p s) n", n=TC)
    v_du = sp_du.ap().rearrange("d p (s n) -> (d p s) n", n=TC)
    v_u = sp_u.ap().rearrange("d p (s n) -> (d p s) n", n=TC)
    v_g = sp_g.ap().rearrange("d p (s n) -> (d p s) n", n=TC)
    v_bc = sp_bc.ap().rearrange("r (c n) -> (r c) n", n=TC)
    v_x1 = sp_x1.ap().rearrange("k p (c n) -> (k p c) n", n=TC)
    v_x2 = sp_x2.ap().rearrange("k p (c n) -> (k p c) n", n=TC)
    v_yo = yout.rearrange("k p (c n) -> (k p c) n", n=TC)

    def gdt(dt, j):
        c0 = dt * 8 + j
        return gidx_t.ap()[:, c0:c0 + 1]

    def gbc(r, j):
        c0 = 128 + r * 8 + j
        return gidx_t.ap()[:, c0:c0 + 1]

    def gk(k, j):
        c0 = 384 + k * 8 + j
        return gidx_t.ap()[:, c0:c0 + 1]

    def ig(out, view, idx, reads=(), writes=(), tag=""):
        return b.emit("g", lambda e, o=out, v=view, i=idx: e.indirect_dma_start(
            out=o, out_offset=None, in_=v,
            in_offset=bass.IndirectOffsetOnAxis(ap=i, axis=0)),
            reads=reads, writes=writes, dma=True, tag=tag)

    def isc(view, idx, in_, reads=(), writes=(), tag=""):
        return b.emit("g", lambda e, v=view, i=idx, s=in_: e.indirect_dma_start(
            out=v, out_offset=bass.IndirectOffsetOnAxis(ap=i, axis=0),
            in_=s, in_offset=None),
            reads=reads, writes=writes, dma=True, tag=tag)

    Bbc_t = [ABF.alloc(TC) for _ in range(NS)]
    Bbc_b = [Buf(f"Bb{s}") for s in range(NS)]
    Cbc_t = [ABF.alloc(TC) for _ in range(NS)]
    Cbc_b = [Buf(f"Cb{s}") for s in range(NS)]
    bchain = [Buf(f"bch{i}") for i in range(4)]
    cchain = [Buf(f"cch{i}") for i in range(4)]
    dld_t = [ABF.alloc(TC) for _ in range(2)]
    dld_b = [Buf(f"dld{i}") for i in range(2)]
    duld_t = [ABF.alloc(TC) for _ in range(2)]
    duld_b = [Buf(f"duld{i}") for i in range(2)]
    uld_t = [ABF.alloc(TC) for _ in range(2)]
    uld_b = [Buf(f"uld{i}") for i in range(2)]
    gld_t = [ABF.alloc(TC) for _ in range(2)]
    gld_b = [Buf(f"gld{i}") for i in range(2)]
    wbuf_t = [ABF.alloc(TC) for _ in range(2)]
    wbuf_b = [Buf(f"wb{i}") for i in range(2)]
    scr_t = AFA.alloc(TC)
    scr_b = Buf("scr")
    dA_t = [AFA.alloc(TC) for _ in range(4)]
    dA_b = [Buf(f"dA{i}") for i in range(4)]
    dBu_t = [ABF.alloc(TC) for _ in range(4)]
    dBu_b = [Buf(f"dBu{i}") for i in range(4)]
    h_all = ABF.alloc(4 * TC)
    h_t = [h_all[:, i * TC:(i + 1) * TC] for i in range(4)]
    h_b = [Buf(f"h{i}") for i in range(4)]
    h_last = h_all.tensor.ap()

    p_t = [ABF.alloc(TC) for _ in range(4)]
    p_b = [Buf(f"p{s}") for s in range(4)]
    ym_t = [ABF.alloc(TC) for _ in range(DT2)]
    ym_b = [Buf(f"ym{d}") for d in range(DT2)]
    cdl_t = AFA.alloc(TC)
    cdl_b = Buf("cdl")
    rb_t2 = AFA.alloc(TC)
    rb_b2 = Buf("rb2")
    gam_t = AFA.alloc(NS)
    gam_b = Buf("gam")
    part_t = AFA.alloc(NS)
    part_b = Buf("part")
    t1_t = AFA.alloc(TC)
    t1_b = Buf("t1")
    wo2_t = [ABF.alloc(DT2 * 256) for _ in range(2)]
    wo2_b = [Buf(f"wo2{i}") for i in range(2)]
    x1l_t = [AFA.alloc(TC) for _ in range(2)]
    x1l_b = [Buf(f"x1l{i}") for i in range(2)]
    xl2l_t = [AFA.alloc(TC) for _ in range(2)]
    xl2l_b = [Buf(f"xl2l{i}") for i in range(2)]
    h3_t = [AFA.alloc(TC) for _ in range(KT)]
    h3_b = [Buf(f"h3{k}") for k in range(KT)]
    x2_t = [AFA.alloc(TC) for _ in range(KT)]
    x2_b = [Buf(f"x2{k}") for k in range(KT)]
    xln4_t = [ym_t[k] for k in range(KT)]
    xln4_b = [ym_b[k] for k in range(KT)]
    outst_t = AFA.alloc(TC)
    outst_b = Buf("outst")
    xbf2_t = ABF.alloc(TC)
    xbf2_b = Buf("xbf2")
    sq2_t = ABF.alloc(TC)
    sq2_b = Buf("sq2")
    lnrows2_t = [AFA.alloc(5 * TC, rows=1) for _ in range(2)]
    lnrows2_b = [Buf(f"lnrows2{i}") for i in range(2)]
    tmp2_t = AFA.alloc(TC)
    tmp2_b = Buf("tmp2")
    ffnw2 = ffnw1
    sb_yo = [Buf(f"yo{k}") for k in range(4)] * 2

    b.emit("v", lambda e: e.memset(hrun_s, 0.0), writes=[WB["hrun"]])

    for c in range(NCH2):
        cs = slice(c * TC, (c + 1) * TC)
        for s in range(NS):
            q = "s" if s % 2 == 0 else "a"
            dma(q, Bbc_t[s], sp_bc.ap()[s, cs].partition_broadcast(128),
                writes=[bchain[s % 4], Bbc_b[s]], tag=f"tBb{c}")
        for dt in range(DT2):
            sl = dt % 2
            dma("s", dld_t[sl], sp_d.ap()[dt][:, cs],
                writes=[dld_b[sl]], tag=f"tld{c}")
            dma("a", duld_t[sl], sp_du.ap()[dt][:, cs],
                writes=[duld_b[sl]], tag=f"tlu{c}")
            b.emit("v", lambda e, sl=sl: e.tensor_tensor_scan(
                   out=cdl_t, data0=dld_t[sl], data1=dld_t[sl],
                   initial=0.0, op0=ALU.add, op1=ALU.bypass),
                   reads=[dld_b[sl]], writes=[cdl_b], tag=f"tcs{c}")
            b.emit("v", lambda e: e.tensor_scalar(out=rb_t2, in0=cdl_t,
                   scalar1=cdl_t[:, TC - 1:TC], scalar2=None, op0=ALU.subtract),
                   reads=[cdl_b], writes=[rb_b2], tag=f"tr{c}")
            b.emit("v", lambda e, dt=dt: e.tensor_scalar(
                   out=gam_t, in0=posA_s[:, dt * NS:(dt + 1) * NS],
                   scalar1=cdl_t[:, TC - 1:TC], scalar2=-1.0,
                   op0=ALU.mult, op1=ALU.mult),
                   reads=[cdl_b, WB["posA"]], writes=[gam_b], tag=f"tg{c}")
            b.emit("a", lambda e: e.activation(out=gam_t, in_=gam_t, func=AF.Exp),
                   reads=[gam_b], writes=[gam_b], tag=f"tge{c}")
            hsl = hrun_s[:, dt * NS:(dt + 1) * NS]
            b.emit("v", lambda e, hsl=hsl: e.tensor_tensor(out=hsl, in0=hsl,
                   in1=gam_t, op=ALU.mult),
                   reads=[gam_b, WB["hrun"]], writes=[WB["hrun"]], tag=f"thg{c}")
            for s in range(NS):
                r2 = s % 4
                b.emit("a", lambda e, dt=dt, s=s, r2=r2 % 2: e.activation(
                       out=wbuf_t[r2 % 2], in_=rb_t2, func=AF.Exp,
                       scale=posA_s[:, dt * NS + s:dt * NS + s + 1]),
                       reads=[rb_b2, WB["posA"]], writes=[wbuf_b[r2 % 2]], tag=f"tw{c}")
                b.emit("v", lambda e, s=s, sl=sl, r2=r2: e.tensor_tensor(
                       out=dBu_t[r2], in0=duld_t[sl], in1=Bbc_t[s],
                       op=ALU.mult),
                       reads=[duld_b[sl], Bbc_b[s]], writes=[dBu_b[r2]],
                       tag=f"tdb{c}", cost=550)
                b.emit("v", lambda e, s=s, r2=r2: e.scalar_tensor_tensor(
                       out=scr_t, in0=wbuf_t[r2 % 2], scalar=1.0, in1=dBu_t[r2],
                       op0=ALU.mult, op1=ALU.mult,
                       accum_out=part_t[:, s:s + 1]),
                       reads=[wbuf_b[r2 % 2], dBu_b[r2]],
                       writes=[scr_b, part_b], tag=f"ttr{c}")
            hsl2 = hrun_s[:, dt * NS:(dt + 1) * NS]
            b.emit("v", lambda e, hsl2=hsl2: e.tensor_tensor(out=hsl2,
                   in0=hsl2, in1=part_t, op=ALU.add),
                   reads=[part_b, WB["hrun"]], writes=[WB["hrun"]],
                   tag=f"tha{c}")
    b.emit("v", lambda e: e.tensor_scalar(out=carry_s, in0=hrun_s,
           scalar1=smask_s[:, 0:1], scalar2=None, op0=ALU.mult),
           reads=[WB["hrun"], WB["smask"]], writes=[WB["carry"]], tag="cinit")

    for j in range(NCH2):
        for s in range(NS):
            ig(Bbc_t[s], v_bc, gbc(s, j),
               writes=[bchain[s % 4], Bbc_b[s]], tag=f"sBb{j}")
            ig(Cbc_t[s], v_bc, gbc(16 + s, j),
               writes=[cchain[s % 4], Cbc_b[s]], tag=f"sCb{j}")
        if True:
            for dt in range(DT2):
                sl = dt % 2
                psy, psyb = (ps_bc0, ps_bc0_b) if dt % 2 == 0 else (ps_bc1, ps_bc1_b)
                ig(dld_t[sl], v_d, gdt(dt, j), writes=[dld_b[sl]], tag=f"gd{j}")
                ig(duld_t[sl], v_du, gdt(dt, j), writes=[duld_b[sl]], tag=f"gdu{j}")
                ig(uld_t[sl], v_u, gdt(dt, j), writes=[uld_b[sl]], tag=f"gu{j}")
                ig(gld_t[sl], v_g, gdt(dt, j), writes=[gld_b[sl]], tag=f"gg{j}")
                for s in range(NS):
                    r2 = s % 4
                    b.emit("a", lambda e, dt=dt, s=s, sl=sl, r2=r2:
                           e.activation(out=dA_t[r2], in_=dld_t[sl],
                                        func=AF.Exp,
                                        scale=negA_s[:, dt * NS + s:dt * NS + s + 1]),
                           reads=[dld_b[sl], WB["negA"]], writes=[dA_b[r2]],
                           tag=f"dA{j}")
                    b.emit("v", lambda e, s=s, sl=sl, r2=r2: e.tensor_tensor(
                           out=dBu_t[r2], in0=duld_t[sl], in1=Bbc_t[s],
                           op=ALU.mult),
                           reads=[duld_b[sl], Bbc_b[s]], writes=[dBu_b[r2]],
                           tag=f"dB{j}", cost=550)
                    b.emit("v", lambda e, dt=dt, s=s, r2=r2: e.tensor_tensor_scan(
                           out=h_t[r2], data0=dA_t[r2], data1=dBu_t[r2],
                           initial=carry_s[:, dt * NS + s:dt * NS + s + 1],
                           op0=ALU.mult, op1=ALU.add),
                           reads=[dA_b[r2], dBu_b[r2], WB["carry"]],
                           writes=[h_b[r2]], tag=f"ts{j}")
                    if s % 4 == 3:
                        hoff = h_t[0].offset % (128 * 1024 * 64)
                        b.emit("v", lambda e, dt=dt, s=s: e.tensor_copy(
                               out=carry_s[:, dt * NS + s - 3:dt * NS + s + 1],
                               in_=h_all[:, TC - 1::TC]),
                               reads=[h_b[0], h_b[1], h_b[2], h_b[3]],
                               writes=[WB["carry"]], tag=f"cr{j}", cost=200)
                    b.emit("v", lambda e, s=s, r2=r2: e.tensor_tensor(
                           out=p_t[s % 4],
                           in0=h_t[r2], in1=Cbc_t[s], op=ALU.mult),
                           reads=[h_b[r2], Cbc_b[s]], writes=[p_b[s % 4]],
                           tag=f"cm{j}")
                    b.emit("p", lambda e, s=s, psy=psy: e.matmul(out=psy[:],
                           lhsT=ident_bf, rhs=p_t[s % 4], start=(s == 0),
                           stop=(s == NS - 1)),
                           reads=[p_b[s % 4]],
                           war=[psyb],
                           tag=f"sm{j}")
                b.close_group("p", psyb)
                b.emit("v", lambda e, dt=dt, sl=sl, psy=psy:
                       e.scalar_tensor_tensor(
                       out=t1_t, in0=uld_t[sl], scalar=vcol("D", dt),
                       in1=psy[:], op0=ALU.mult, op1=ALU.add),
                       reads=[uld_b[sl], psyb, WB["vec"]], writes=[t1_b],
                       tag=f"y1{j}")
                b.emit("v", lambda e, dt=dt, sl=sl: e.tensor_tensor(
                       out=ym_t[dt], in0=t1_t, in1=gld_t[sl], op=ALU.mult),
                       reads=[t1_b, gld_b[sl]], writes=[ym_b[dt]], tag=f"y2{j}")
            for m in range(KT):
                wgi = m % 2
                if m % 2 == 0:
                    for d in range(DT2):
                        dma("s" if d % 2 == 0 else "a",
                            wo2_t[(m // 2) % 2][:, d * 256:(d + 1) * 256],
                            wout_bf.ap()[d][:, m * 128:(m + 2) * 128],
                            reads=[WB["woutbf"]], writes=[wo2_b[(m // 2) % 2]],
                            tag=f"lwo{j}")
                ps, psb = ps_mm[m % 2], ps_mm_b[m % 2]
                for d in range(DT2):
                    b.emit("p", lambda e, d=d, m=m, ps=ps:
                           e.matmul(out=ps[:],
                                    lhsT=wo2_t[(m // 2) % 2][:, d * 256 + (m % 2) * 128:d * 256 + (m % 2 + 1) * 128],
                                    rhs=ym_t[d], start=(d == 0),
                                    stop=(d == DT2 - 1)),
                           reads=[ym_b[d], wo2_b[(m // 2) % 2]],
                           war=[psb],
                           tag=f"wo{j}m{m}")
                b.close_group("p", psb)
                mi = m % 2
                ig(xl2l_t[mi], v_x2, gk(m, j), writes=[xl2l_b[mi]], tag=f"lx2{j}")
                b.emit("v", lambda e, m=m, mi=mi, ps=ps: e.tensor_tensor(
                       out=h3_t[m], in0=ps[:], in1=xl2l_t[mi], op=ALU.add),
                       reads=[psb, xl2l_b[mi]], writes=[h3_b[m]], tag=f"h3{j}")
            ln_rows(h3_t, h3_b, lnrows2_t, lnrows2_b, sq2_t, sq2_b, xbf2_t,
                    xbf2_b, f"L3{j}")
            for k in range(KT):
                kl = k % 2
                ig(x1l_t[kl], v_x1, gk(k, j), writes=[x1l_b[kl]], tag=f"lx1{j}")
                b.emit("v", lambda e, k=k: e.tensor_tensor(out=tmp2_t,
                       in0=h3_t[k], in1=ps_bc0[:], op=ALU.mult),
                       reads=[h3_b[k], ps_bc0_b], writes=[tmp2_b], tag=f"L3a{j}")
                b.emit("v", lambda e: e.tensor_tensor(out=tmp2_t, in0=tmp2_t,
                       in1=ps_bc1[:], op=ALU.subtract),
                       reads=[tmp2_b, ps_bc1_b], writes=[tmp2_b], tag=f"L3b{j}")
                b.emit("v", lambda e, k=k, kl=kl: e.scalar_tensor_tensor(
                       out=x2_t[k], in0=tmp2_t, scalar=vcol("gamma", k),
                       in1=x1l_t[kl], op0=ALU.mult, op1=ALU.add),
                       reads=[tmp2_b, x1l_b[kl], WB["vec"]], writes=[x2_b[k]],
                       tag=f"L3g{j}")
                b.emit("v", lambda e, k=k: e.tensor_scalar(out=x2_t[k],
                       in0=x2_t[k], scalar1=vcol("beta", k), scalar2=None,
                       op0=ALU.add),
                       reads=[x2_b[k], WB["vec"]], writes=[x2_b[k]], tag=f"L3bt{j}")
            ln_rows(x2_t, x2_b, lnrows2_t, lnrows2_b, sq2_t, sq2_b, xbf2_t,
                    xbf2_b, f"L4{j}")
            ln_apply(x2_t, x2_b, xln4_t, xln4_b, tmp2_t, tmp2_b, f"L4a{j}")
            ffn(xln4_t, xln4_b, x2_t, x2_b, x2_t, x2_b, ffnw2, f"F2{j}")
            for k in range(KT):
                isc(v_yo, gk(k, j), x2_t[k], reads=[x2_b[k]],
                    writes=[sb_yo[k]], tag=f"out{j}")

    ABF.reset(); AFA.reset()
    b.fence()
    b.emit("g", lambda e: e.nop(), wait_all_dma=True, tag="final")
    order = b.schedule()
    b.lower(order)
    b.finalize(order)
    b.coalesce_waits(order)

    with ctx:
        with ExitStack() as sctx:
            sems = {}
            for n in b.all_sems:
                sems[n] = sctx.enter_context(nc.semaphore(n))
            with nc.Block() as block:
                def mk(eng_key):
                    def run(engine):
                        for op in order:
                            if op.eng != eng_key:
                                continue
                            for s, v in op.waits:
                                engine.wait_ge(sems[s], v)
                            inst = op.fn(engine)
                            if op.inc is not None and inst is not None:
                                inst.then_inc(sems[op.inc[0]], op.inc[1])
                    return run
                block.tensor(mk("p"))
                block.vector(mk("v"))
                block.scalar(mk("a"))
                block.gpsimd(mk("g"))
                block.sync(mk("s"))
    return nc


def prep_inputs(inputs, L=8192):
    f = np.ascontiguousarray
    W_in = np.asarray(inputs["L0_attn_W_in"])
    conv_w = np.asarray(inputs["L0_attn_conv_w"])
    d = {}
    d["win"] = f(W_in.reshape(KT, 128, 2 * DIN))
    d["bwT"] = f(np.asarray(inputs["L0_ffn_base_w"]).T.reshape(KT, 128, DM))
    d["rwT"] = f(np.asarray(inputs["L0_ffn_router_w"]).T.reshape(KT, 128, NE))
    d["lA"] = f(np.asarray(inputs["L0_ffn_lora_A"]).reshape(KT, 128, NR))
    d["lB"] = f(np.asarray(inputs["L0_ffn_lora_B"]).reshape(NE * NR, DM))
    d["wx"] = f(np.asarray(inputs["L0_attn_W_x"]).reshape(DT2, 128, 96))
    d["wdt"] = f(np.asarray(inputs["L0_attn_W_dt"]))
    d["wout"] = f(np.asarray(inputs["L0_attn_W_out"]).reshape(DT2, 128, DM))
    cd = np.zeros((DT2 * 4, 128, 128), np.float32)
    idx = np.arange(128)
    for dt in range(DT2):
        for k in range(4):
            cd[dt * 4 + k, idx, idx] = conv_w[dt * 128 + idx, k]
    d["cdiag"] = cd
    d["alog"] = f(np.asarray(inputs["L0_attn_A_log"]).reshape(DT2, 128, NS))
    vec = np.zeros((128, 64), np.float32)
    vec[:, 0:16] = np.asarray(inputs["L0_attn_conv_b"]).reshape(DT2, 128).T
    vec[:, 16:32] = np.asarray(inputs["L0_attn_b_dt"]).reshape(DT2, 128).T
    vec[:, 32:48] = np.asarray(inputs["L0_attn_D"]).reshape(DT2, 128).T
    vec[:, 48:56] = np.asarray(inputs["L0_attn_gamma"]).reshape(KT, 128).T
    vec[:, 56:64] = np.asarray(inputs["L0_attn_beta"]).reshape(KT, 128).T
    d["vec"] = vec
    d["bbrow"] = f(np.asarray(inputs["L0_ffn_base_b"]).reshape(1, DM))
    d["rbrow"] = f(np.asarray(inputs["L0_ffn_router_b"]).reshape(1, NE))
    d["ident_in"] = np.eye(128, dtype=np.float32)
    p = np.arange(128, dtype=np.uint32)
    nch = L // TC
    nc2 = nch // 2
    for sigma in (0, 1):
        gi = np.zeros((128, 448), np.uint32)
        for dt in range(16):
            for j in range(nc2):
                gi[:, dt * 8 + j] = (dt * 128 + p) * nch + (sigma * nc2 + j)
        for r in range(32):
            for j in range(nc2):
                gi[:, 128 + r * 8 + j] = r * nch + (sigma * nc2 + j)
        for k in range(8):
            for j in range(nc2):
                gi[:, 384 + k * 8 + j] = (k * 128 + p) * nch + (sigma * nc2 + j)
        d[f"gidx{sigma}"] = gi
        d[f"smask{sigma}"] = np.full((128, 1), float(sigma), np.float32)
    repl = np.zeros((2, 8, NE * NR), np.float32)
    for ee in range(NE):
        for r in range(NR):
            repl[0, ee, ee * NR + r] = 1.0
            repl[1, r, ee * NR + r] = 1.0
    d["repl_in"] = repl
    return d


def kernel(**inputs):
    x = np.asarray(inputs["L0_attn_x"])
    L = x.shape[1]
    shared = prep_inputs(inputs, L=L)
    nc = build_program(L=L)
    in_maps = []
    for core in range(8):
        bidx = core // 2
        sigma = core % 2
        m = {k: v for k, v in shared.items()
             if not k.startswith(("gidx", "smask"))}
        m["gidx"] = shared[f"gidx{sigma}"]
        m["smask"] = shared[f"smask{sigma}"]
        m["xT"] = np.ascontiguousarray(x[bidx].T).reshape(KT, 128, L)
        in_maps.append(m)
    res = run_bass_kernel_spmd(nc, in_maps, core_ids=list(range(8)))
    out = np.empty((x.shape[0], L, DM), np.float32)
    half = L // 2
    for bidx in range(x.shape[0]):
        y0 = np.asarray(res.results[2 * bidx]["yout"]).reshape(DM, L)
        y1 = np.asarray(res.results[2 * bidx + 1]["yout"]).reshape(DM, L)
        out[bidx, :half] = y0[:, :half].T
        out[bidx, half:] = y1[:, half:].T
    return out


# version 12
# speedup vs baseline: 1.3273x; 1.0028x over previous
import numpy as np
from contextlib import ExitStack

import concourse.bass as bass
import concourse.mybir as mybir
from concourse.bass_utils import run_bass_kernel_spmd

F32 = mybir.dt.float32
BF16 = mybir.dt.bfloat16
AF = mybir.ActivationFunctionType
ALU = mybir.AluOpType
AX = mybir.AxisListType

DM = 1024
DIN = 2048
NS = 16
DTR = 64
NE = 8
NR = 8
KT = DM // 128
DT2 = DIN // 128
EPS = 1e-5
SM30 = 30.0
TC = 512

COMPUTE = ("p", "v", "a", "g", "s")


class Buf:
    __slots__ = ("writer", "readers", "name", "dsem")

    def __init__(self, name=""):
        self.writer = None
        self.readers = []
        self.name = name
        self.dsem = {}


class Op:
    __slots__ = ("eng", "fn", "dma", "reads", "writes", "tag", "cost",
                 "epoch", "idx", "preds", "wait_all_dma", "start",
                 "waits", "inc", "dsem")


class B:

    ENGS = ("p", "v", "a", "g", "s")

    def __init__(self):
        self.ops = []
        self.epoch = 0

    def emit(self, eng, fn, reads=(), writes=(), dma=False, noinc=False,
             extra_waits=(), war=(), tag="", cost=700, wait_all_dma=False,
             dsem=None):
        op = Op()
        op.eng, op.fn, op.dma = eng, fn, dma
        op.reads = list(reads)
        op.writes = list(writes) + list(war)
        op.tag = tag
        op.cost = cost
        op.epoch = self.epoch
        op.wait_all_dma = wait_all_dma
        op.dsem = dsem
        op.idx = len(self.ops)
        op.preds = None
        self.ops.append(op)
        return op

    def fence(self):
        self.epoch += 1

    def barrier(self):
        self.fence()

    def close_group(self, eng, *bufs):
        pass

    def _build_graph(self):
        writer = {}
        readers = {}
        for op in self.ops:
            preds = set()
            for bb in op.reads:
                w = writer.get(id(bb))
                if w is not None:
                    preds.add(w)
            for bb in op.writes:
                w = writer.get(id(bb))
                if w is not None:
                    preds.add(w)
                for r in readers.get(id(bb), ()):
                    preds.add(r)
            for bb in op.writes:
                writer[id(bb)] = op.idx
                readers[id(bb)] = []
            for bb in op.reads:
                readers.setdefault(id(bb), []).append(op.idx)
            preds.discard(op.idx)
            op.preds = sorted(preds)

    COST = {"p": 320, "v": 750, "a": 780, "g": 1200, "s": 300}

    def _estimate_costs(self):
        for op in self.ops:
            if op.cost != 700:
                continue
            if op.dma:
                op.cost = 1200
            else:
                c = self.COST.get(op.eng, 700)
                t = op.tag
                if op.eng == "v":
                    if t.startswith("ts"):
                        c = 1300
                    elif t.startswith(("cm", "dB")):
                        c = 550
                    elif t.startswith("cr"):
                        c = 160
                    elif t.startswith("ttr"):
                        c = 1050
                op.cost = c

    def schedule(self):
        import heapq
        self._estimate_costs()
        self._build_graph()
        n = len(self.ops)
        nsucc_pred = [len(op.preds) for op in self.ops]
        succs = [[] for _ in range(n)]
        for op in self.ops:
            for p in op.preds:
                succs[p].append(op.idx)
        done_t = [0.0] * n
        eng_free = {e: 0.0 for e in self.ENGS}
        q_free = {e: 0.0 for e in ("s", "a", "g")}
        epoch_end = 0.0
        order = []
        by_epoch = {}
        for op in self.ops:
            by_epoch.setdefault(op.epoch, []).append(op)
        for ep in sorted(by_epoch):
            ops_e = by_epoch[ep]
            remaining = set(o.idx for o in ops_e)
            pend = {o.idx: sum(1 for p in self.ops[o.idx].preds
                               if self.ops[p].epoch == ep) for o in ops_e}
            ready = []
            for o in ops_e:
                if pend[o.idx] == 0:
                    heapq.heappush(ready, o.idx)
            scheduled_any = True
            while remaining:
                if not ready:
                    raise RuntimeError("scheduler deadlock (cyclic deps?)")
                cand = heapq.heappop(ready)
                op = self.ops[cand]
                start = max(epoch_end, eng_free[op.eng],
                            max((done_t[p] for p in op.preds), default=0.0))
                if op.dma:
                    issue = max(start, q_free[op.eng])
                    eng_free[op.eng] = issue + 150.0
                    q_free[op.eng] = issue + max(600.0, op.cost * 0.5)
                    done_t[cand] = issue + 1500.0 + op.cost
                else:
                    eng_free[op.eng] = start + op.cost
                    done_t[cand] = start + op.cost
                op.start = start
                order.append(op)
                remaining.discard(cand)
                for sidx in succs[cand]:
                    if self.ops[sidx].epoch == ep and sidx in remaining:
                        pend[sidx] -= 1
                        if pend[sidx] == 0:
                            heapq.heappush(ready, sidx)
            epoch_end = max(max(done_t[o.idx] for o in ops_e), epoch_end)
        return order

    def lower(self, order):
        self.dsem_n = 0
        self.posmap = {}
        cnt = {k: 0 for k in ("p", "v", "a", "g")}
        pos = {}
        wm = {e: {} for e in self.ENGS}

        def dma_sem(op):
            if op.dsem is None:
                tgt = None
                for bb in op.writes:
                    tgt = bb
                    break
                if tgt is None:
                    op.dsem = "dmisc_" + op.eng
                else:
                    if op.eng not in tgt.dsem:
                        tgt.dsem[op.eng] = f"d{self.dsem_n}_{op.eng}"
                        self.dsem_n += 1
                    op.dsem = tgt.dsem[op.eng]
            if op.dsem not in cnt:
                cnt[op.dsem] = 0
            return op.dsem

        for op in order:
            semkey = dma_sem(op) if op.dma else op.eng
            cnt[semkey] += 16 if op.dma else 1
            pos[op.idx] = (semkey, cnt[semkey])
            waits = {}
            for p in op.preds:
                s, v = pos[p]
                if s == "p" and op.eng == "p":
                    continue
                if wm[op.eng].get(s, 0) >= v:
                    continue
                waits[s] = max(waits.get(s, 0), v)
            if op.wait_all_dma:
                for s, v in cnt.items():
                    if s not in ("p", "v", "a", "g") and v > waits.get(s, 0):
                        waits[s] = v
            op.waits = sorted(waits.items())
            op.inc = (semkey, 16 if op.dma else 1)
            self.posmap[(semkey, cnt[semkey])] = op
            for s, v in waits.items():
                wm[op.eng][s] = v
        self.all_sems = sorted(cnt)
        return cnt

    def coalesce_waits(self, order, margin=4000.0):
        per_eng = {}
        for op in order:
            per_eng.setdefault(op.eng, []).append(op)
        for eng, ops in per_eng.items():
            carrier = None
            carrier_limit = 0
            for op in ops:
                if not op.waits:
                    continue
                keep = []
                hoist = []
                for s, v in op.waits:
                    p = self.posmap.get((s, v))
                    if (carrier is not None and p is not None
                            and p.start + p.cost + margin <= carrier.start
                            and carrier_limit < 24):
                        hoist.append((s, v))
                    else:
                        keep.append((s, v))
                if hoist and carrier is not None:
                    w = dict(carrier.waits)
                    for s, v in hoist:
                        w[s] = max(w.get(s, 0), v)
                    carrier.waits = sorted(w.items())
                    carrier_limit += len(hoist)
                op.waits = keep
                if keep:
                    carrier = op
                    carrier_limit = 0
                elif carrier is None:
                    carrier = op
                    carrier_limit = 0

    def finalize(self, order):
        cnt = {}
        epoch_counts = {}
        cur = order[0].epoch if order else 0
        for op in order:
            if op.epoch != cur:
                epoch_counts[cur] = dict(cnt)
                cur = op.epoch
            semkey = op.inc[0]
            cnt[semkey] = cnt.get(semkey, 0) + op.inc[1]
        seen = set()
        for op in order:
            key = (op.eng, op.epoch)
            if key in seen:
                continue
            seen.add(key)
            prev = op.epoch - 1
            if prev in epoch_counts:
                w = dict(op.waits)
                for s, v in epoch_counts[prev].items():
                    if v > 0 and w.get(s, 0) < v:
                        w[s] = v
                op.waits = sorted(w.items())


class Arena:
    def __init__(self, t, n):
        self.t = t
        self.n = n
        self.off = 0
        self.stack = []

    def alloc(self, cols, rows=128):
        off = (self.off + 3) & ~3
        assert off + cols <= self.n, f"arena overflow {off}+{cols}>{self.n}"
        self.off = off + cols
        return self.t.ap()[0:rows, off:off + cols]

    def mark(self):
        self.stack.append(self.off)

    def reset(self):
        self.off = self.stack.pop()


def build_program(L=8192):
    NCH = L // TC
    nc = bass.Bass("TRN2", target_bir_lowering=False, debug=False)

    xT = nc.dram_tensor("xT", [KT, 128, L], F32, kind="ExternalInput").ap()
    win = nc.dram_tensor("win", [KT, 128, 2 * DIN], F32, kind="ExternalInput").ap()
    bwT = nc.dram_tensor("bwT", [KT, 128, DM], F32, kind="ExternalInput").ap()
    rwT = nc.dram_tensor("rwT", [KT, 128, NE], F32, kind="ExternalInput").ap()
    lA = nc.dram_tensor("lA", [KT, 128, NR], F32, kind="ExternalInput").ap()
    lB = nc.dram_tensor("lB", [NE * NR, DM], F32, kind="ExternalInput").ap()
    wx = nc.dram_tensor("wx", [DT2, 128, 96], F32, kind="ExternalInput").ap()
    wdt = nc.dram_tensor("wdt", [DTR, DIN], F32, kind="ExternalInput").ap()
    wout = nc.dram_tensor("wout", [DT2, 128, DM], F32, kind="ExternalInput").ap()
    cdiag = nc.dram_tensor("cdiag", [DT2 * 4, 128, 128], F32, kind="ExternalInput").ap()
    alog = nc.dram_tensor("alog", [DT2, 128, NS], F32, kind="ExternalInput").ap()
    vec = nc.dram_tensor("vec", [128, 64], F32, kind="ExternalInput").ap()
    bbrow = nc.dram_tensor("bbrow", [1, DM], F32, kind="ExternalInput").ap()
    rbrow = nc.dram_tensor("rbrow", [1, NE], F32, kind="ExternalInput").ap()
    ident_in = nc.dram_tensor("ident_in", [128, 128], F32, kind="ExternalInput").ap()
    repl_in = nc.dram_tensor("repl_in", [2, 8, NE * NR], F32, kind="ExternalInput").ap()

    yout = nc.dram_tensor("yout", [KT, 128, L], F32, kind="ExternalOutput").ap()

    win_bf = nc.dram_tensor("win_bf", [KT, 128, 2 * DIN], BF16)
    wout_bf = nc.dram_tensor("wout_bf", [DT2, 128, DM], BF16)
    sp_d = nc.dram_tensor("sp_d", [DT2, 128, L], BF16)
    sp_u = nc.dram_tensor("sp_u", [DT2, 128, L], BF16)
    sp_bc = nc.dram_tensor("sp_bc", [32, L], BF16)
    sp_x1 = nc.dram_tensor("sp_x1", [KT, 128, L], F32)
    sp_x2 = nc.dram_tensor("sp_x2", [KT, 128, L], F32)
    sp_g = nc.dram_tensor("sp_g", [DT2, 128, L], BF16)
    sp_du = nc.dram_tensor("sp_du", [DT2, 128, L], BF16)
    gidx = nc.dram_tensor("gidx", [128, 448], mybir.dt.uint32, kind="ExternalInput").ap()
    smask = nc.dram_tensor("smask", [128, 1], F32, kind="ExternalInput").ap()

    b = B()
    ctx = ExitStack()
    abf_t = ctx.enter_context(nc.sbuf_tensor("abf", [128, 59 * 1024], BF16))
    af_t = ctx.enter_context(nc.sbuf_tensor("af32", [128, 22 * 1024], F32))
    ABF = Arena(abf_t, 59 * 1024)
    AFA = Arena(af_t, 22 * 1024)

    gidx_t = ctx.enter_context(nc.sbuf_tensor("gidxs", [128, 448], mybir.dt.uint32))
    ps_mm = [ctx.enter_context(nc.psum_tensor(f"psmm{i}", [128, TC], F32)) for i in range(4)]
    ps_bc0 = ctx.enter_context(nc.psum_tensor("psbc0", [128, TC], F32))
    ps_bc1 = ctx.enter_context(nc.psum_tensor("psbc1", [128, TC], F32))
    ps_small = ctx.enter_context(nc.psum_tensor("pssmall", [128, TC], F32))
    ps_geb = ctx.enter_context(nc.psum_tensor("psgeb", [8, TC], BF16))
    ps_mm_b = [Buf(f"psmm{i}") for i in range(4)]
    ps_bc0_b, ps_bc1_b = Buf("psbc0"), Buf("psbc1")
    ps_small_b = Buf("pssmall")
    ps_geb_b = Buf("psgeb")

    bwT_s = ABF.alloc(KT * DM)
    rwT_s = ABF.alloc(KT * NE)
    lA_s = ABF.alloc(KT * NR)
    lB_s = ABF.alloc(DM, rows=NE * NR)
    wdt_s = ABF.alloc(DIN, rows=DTR)
    ident_s = ABF.alloc(128)
    repl_s = ABF.alloc(2 * NE * NR, rows=8)
    ones_row = ABF.alloc(TC, rows=1)
    onescol = ABF.alloc(1)
    bb_s = ABF.alloc(DM, rows=1)
    rb_s = ABF.alloc(NE, rows=1)

    onesrow_f = AFA.alloc(128, rows=1)
    vec_s = AFA.alloc(64)
    negA_s = AFA.alloc(DT2 * NS)
    posA_s = AFA.alloc(DT2 * NS)
    carry_s = AFA.alloc(DT2 * NS)
    hrun_s = AFA.alloc(DT2 * NS)
    smask_s = AFA.alloc(4)
    idf_s = AFA.alloc(128)
    atmp = AFA.alloc(NS)

    CONSTS = [0.0, 1.0, -1.0, 0.5, -0.5, 2.0, EPS, SM30, 1.0 / DM]
    const_t = AFA.alloc(len(CONSTS))
    cb = Buf("consts")
    for ci, cv in enumerate(CONSTS):
        nc.const_aps.register(F32, cv, const_t[:, ci:ci + 1])

    WB = {k: Buf(k) for k in ("bwT", "rwT", "lA", "lB", "wdt", "ident", "repl",
                              "ones", "bb", "rb", "onesf", "vec", "negA",
                              "posA", "carry", "hrun", "smask", "gidx",
                              "tails", "winbf", "woutbf", "wx",
                              "cdiag", "idf", "atmp")}

    def dma(eng, out, in_, reads=(), writes=(), tag="", dsem=None):
        return b.emit(eng, lambda e, o=out, i=in_: e.dma_start(out=o, in_=i),
                      reads=reads, writes=writes, dma=True, tag=tag, dsem=dsem)

    for k in range(KT):
        dma("g", bwT_s[:, k * DM:(k + 1) * DM], bwT[k], dsem="boot_g")
        dma("g", rwT_s[:, k * NE:(k + 1) * NE], rwT[k], dsem="boot_g")
        dma("g", lA_s[:, k * NR:(k + 1) * NR], lA[k], dsem="boot_g")
    dma("g", lB_s, lB, dsem="boot_g")
    dma("g", wdt_s, wdt, dsem="boot_g")
    dma("g", ident_s, ident_in, dsem="boot_g")
    dma("s", idf_s, ident_in, dsem="boot_s")
    dma("g", repl_s[0:8, 0:64], repl_in[0], dsem="boot_g")
    dma("g", repl_s[0:8, 64:128], repl_in[1], dsem="boot_g")
    dma("g", bb_s, bbrow, dsem="boot_g")
    dma("g", rb_s, rbrow, dsem="boot_g")
    dma("s", vec_s, vec, dsem="boot_s")
    dma("s", gidx_t.ap(), gidx, dsem="boot_s")
    dma("s", smask_s[:, 0:1], smask, dsem="boot_s")

    for ci, cv in enumerate(CONSTS):
        b.emit("v", lambda e, ci=ci, cv=cv: e.memset(const_t[:, ci:ci + 1], cv),
               writes=[cb])
    b.emit("v", lambda e: e.memset(ones_row, 1.0), writes=[WB["ones"]])
    b.emit("v", lambda e: e.memset(onescol, 1.0), writes=[WB["ones"]])
    b.emit("v", lambda e: e.memset(onesrow_f, 1.0), writes=[WB["onesf"]])
    b.emit("v", lambda e: e.memset(carry_s, 0.0), writes=[WB["carry"]])
    b.fence()
    for d in range(DT2):
        slp = posA_s[:, d * NS:(d + 1) * NS]
        sln = negA_s[:, d * NS:(d + 1) * NS]
        dma("s", atmp, alog[d], writes=[WB["atmp"]])
        b.emit("a", lambda e, slp=slp: e.activation(out=slp, in_=atmp, func=AF.Exp),
               reads=[WB["atmp"]], writes=[WB["posA"]])
        b.emit("a", lambda e, slp=slp, sln=sln: e.mul(out=sln, in_=slp, mul=-1.0),
               reads=[WB["posA"]], writes=[WB["negA"]])

    ident8f = idf_s[0:8, 0:8]
    ident_bf = ident_s[0:128, 0:128]

    def vcol(name, j):
        base = dict(conv_b=0, b_dt=16, D=32, gamma=48, beta=56)[name]
        return vec_s[:, base + j:base + j + 1]

    _lnrot = [0]

    def ln_rows(src_tiles, src_bufs, lnrows_sets, lnrows_bs, sq_t, sq_b, xbf_t, xbf_b, tag):
        _lnrot[0] ^= 1
        lnrows = lnrows_sets[_lnrot[0]] if isinstance(lnrows_sets, (list, tuple)) else lnrows_sets
        lnrows_b = lnrows_bs[_lnrot[0]] if isinstance(lnrows_bs, (list, tuple)) else lnrows_bs
        wide = xbf_t.shape[1] >= 2 * TC
        xb2 = [xbf_t[0:128, 0:TC], xbf_t[0:128, TC:2 * TC] if wide else xbf_t[0:128, 0:TC]]
        sq2 = [sq_t[0:128, 0:TC], sq_t[0:128, TC:2 * TC] if wide else sq_t[0:128, 0:TC]]
        xb2b = [xbf_b, Buf(tag + "xb1") if wide else xbf_b]
        sq2b = [sq_b, Buf(tag + "sq1") if wide else sq_b]
        for k in range(KT):
            kk = k % 2
            b.emit("v", lambda e, k=k, kk=kk: e.tensor_copy(out=xb2[kk],
                   in_=src_tiles[k]),
                   reads=[src_bufs[k]], writes=[xb2b[kk]], tag=tag + "cast")
            b.emit("a", lambda e, k=k, kk=kk: e.activation(out=sq2[kk],
                   in_=src_tiles[k],
                   func=AF.Square), reads=[src_bufs[k]], writes=[sq2b[kk]],
                   tag=tag + "sq")
            b.emit("p", lambda e, kk=kk, st=(k == 0), last=(k == KT - 1): e.matmul(
                   out=ps_small[0:1, :], lhsT=onescol, rhs=xb2[kk], start=st,
                   stop=last),
                   reads=[xb2b[kk], WB["ones"]] if k == 0 else [xb2b[kk]],
                   war=[ps_small_b],
                   tag=tag + "mmx")
            b.emit("p", lambda e, kk=kk, st=(k == 0), last=(k == KT - 1): e.matmul(
                   out=ps_bc0[0:1, :], lhsT=onescol, rhs=sq2[kk], start=st,
                   stop=last),
                   reads=[sq2b[kk]],
                   war=[ps_bc0_b],
                   tag=tag + "mmq")
        b.close_group("p", ps_small_b, ps_bc0_b)
        mu = lnrows[0:1, 0:TC]
        m2 = lnrows[0:1, TC:2 * TC]
        var = lnrows[0:1, 2 * TC:3 * TC]
        arow = lnrows[0:1, 3 * TC:4 * TC]
        brow = lnrows[0:1, 4 * TC:5 * TC]
        b.emit("v", lambda e: e.tensor_scalar(out=mu, in0=ps_small[0:1, :],
               scalar1=1.0 / DM, scalar2=None, op0=ALU.mult),
               reads=[ps_small_b], writes=[lnrows_b], tag=tag)
        b.emit("v", lambda e: e.tensor_scalar(out=m2, in0=ps_bc0[0:1, :],
               scalar1=1.0 / DM, scalar2=None, op0=ALU.mult),
               reads=[ps_bc0_b], writes=[lnrows_b], tag=tag)
        b.emit("v", lambda e: e.tensor_tensor(out=var, in0=mu, in1=mu, op=ALU.mult),
               reads=[lnrows_b], writes=[lnrows_b], tag=tag)
        b.emit("v", lambda e: e.tensor_tensor(out=var, in0=m2, in1=var,
               op=ALU.subtract), reads=[lnrows_b], writes=[lnrows_b], tag=tag)
        b.emit("a", lambda e: e.activation(out=arow, in_=var, func=AF.Ln, bias=EPS),
               reads=[lnrows_b], writes=[lnrows_b], tag=tag)
        b.emit("a", lambda e: e.activation(out=arow, in_=arow, func=AF.Exp, scale=-0.5),
               reads=[lnrows_b], writes=[lnrows_b], tag=tag)
        b.emit("v", lambda e: e.tensor_tensor(out=brow, in0=mu, in1=arow, op=ALU.mult),
               reads=[lnrows_b], writes=[lnrows_b], tag=tag)
        b.emit("p", lambda e: e.matmul(out=ps_bc0[:], lhsT=onesrow_f, rhs=arow,
               start=True, stop=True),
               reads=[lnrows_b, WB["onesf"]], war=[ps_bc0_b], tag=tag + "bc0")
        b.close_group("p", ps_bc0_b)
        b.emit("p", lambda e: e.matmul(out=ps_bc1[:], lhsT=onesrow_f, rhs=brow,
               start=True, stop=True),
               reads=[lnrows_b], war=[ps_bc1_b], tag=tag + "bc1")
        b.close_group("p", ps_bc1_b)

    def ln_apply(src_tiles, src_bufs, out_tiles, out_bufs, tmp_t, tmp_b, tag):
        for k in range(KT):
            b.emit("v", lambda e, k=k: e.tensor_tensor(out=tmp_t, in0=src_tiles[k],
                   in1=ps_bc0[:], op=ALU.mult),
                   reads=[src_bufs[k], ps_bc0_b], writes=[tmp_b], tag=tag + "a")
            b.emit("v", lambda e, k=k: e.tensor_tensor(out=out_tiles[k], in0=tmp_t,
                   in1=ps_bc1[:], op=ALU.subtract),
                   reads=[tmp_b, ps_bc1_b], writes=[out_bufs[k]], tag=tag + "b")

    def make_ffn_work():
        return dict(
            lgt=(AFA.alloc(TC, rows=8), Buf("lgt")),
            mid=(ABF.alloc(TC, rows=8), Buf("mid")),
            gex=(AFA.alloc(32), Buf("gex")),
            gtb=(ABF.alloc(32), Buf("gtb")),
            ge=(ABF.alloc(TC, rows=8), Buf("ge")),
            grep=(ABF.alloc(TC, rows=64), Buf("grep")),
            wmid=(ABF.alloc(TC, rows=64), Buf("wmid")),
            rows=(AFA.alloc(2), Buf("rows")),
        )

    def ffn(xin_tiles, xin_bufs, res_tiles, res_bufs, out_tiles, out_bufs, W, tag):
        lgt_t, lgt_b = W["lgt"]
        mid_t, mid_b = W["mid"]
        gex_t, gex_b = W["gex"]
        gtb_t, gtb_b = W["gtb"]
        ge_t, ge_b = W["ge"]
        grep_t, grep_b = W["grep"]
        wmid_t, wmid_b = W["wmid"]
        rows_t, rows_b = W["rows"]
        for k in range(KT):
            b.emit("p", lambda e, k=k: e.matmul(out=ps_small[0:8, :],
                   lhsT=rwT_s[:, k * NE:(k + 1) * NE], rhs=xin_tiles[k],
                   start=(k == 0), stop=False),
                   reads=[xin_bufs[k]],
                   war=[ps_small_b],
                   tag=tag + "rout")
        b.emit("p", lambda e: e.matmul(out=ps_small[0:8, :], lhsT=rb_s,
               rhs=ones_row, start=False, stop=True),
               reads=[WB["rb"], WB["ones"]], tag=tag)
        for k in range(KT):
            b.emit("p", lambda e, k=k: e.matmul(out=ps_small[32:40, :],
                   lhsT=lA_s[:, k * NR:(k + 1) * NR], rhs=xin_tiles[k],
                   start=(k == 0), stop=(k == KT - 1), tile_position=(0, 32)),
                   reads=[xin_bufs[k]],
                   war=[ps_small_b],
                   tag=tag + "mid")
        b.close_group("p", ps_small_b)
        b.emit("v", lambda e: e.tensor_copy(out=lgt_t, in_=ps_small[0:8, :]),
               reads=[ps_small_b], writes=[lgt_b], tag=tag + "lgt")
        b.emit("v", lambda e: e.tensor_copy(out=mid_t, in_=ps_small[32:40, :]),
               reads=[ps_small_b], writes=[mid_b], tag=tag + "midc")
        nj = TC // 128
        for j in range(nj):
            b.emit("p", lambda e, j=j: e.transpose(
                   out=ps_small[:, j * 8:(j + 1) * 8],
                   in_=lgt_t[0:8, j * 128:(j + 1) * 128], identity=ident8f),
                   reads=[lgt_b, WB["idf"]] if j == 0 else [lgt_b],
                   war=[ps_small_b],
                   tag=tag + "tr")
        b.close_group("p", ps_small_b)
        mxr = rows_t[0:128, 0:1]
        smr = rows_t[0:128, 1:2]
        for j in range(nj):
            sl = ps_small[:, j * 8:(j + 1) * 8]
            b.emit("v", lambda e, sl=sl: e.reduce_max(out=mxr, in_=sl, axis=AX.X),
                   reads=[ps_small_b], writes=[rows_b], tag=tag + "mx")
            b.emit("v", lambda e: e.tensor_scalar(out=mxr, in0=mxr, scalar1=-1.0,
                   scalar2=None, op0=ALU.mult), reads=[rows_b], writes=[rows_b],
                   tag=tag)
            b.emit("a", lambda e, sl=sl, j=j: e.activation(
                   out=gex_t[:, j * 8:(j + 1) * 8], in_=sl, func=AF.Exp, bias=mxr),
                   reads=[ps_small_b, rows_b], writes=[gex_b], tag=tag + "exp")
            b.emit("v", lambda e, j=j: e.reduce_sum(out=smr,
                   in_=gex_t[:, j * 8:(j + 1) * 8], axis=AX.X),
                   reads=[gex_b], writes=[rows_b], tag=tag + "sum")
            b.emit("v", lambda e: e.reciprocal(out=smr, in_=smr),
                   reads=[rows_b], writes=[rows_b], tag=tag + "rcp")
            b.emit("v", lambda e, j=j: e.tensor_scalar(
                   out=gtb_t[:, j * 8:(j + 1) * 8],
                   in0=gex_t[:, j * 8:(j + 1) * 8], scalar1=smr, scalar2=None,
                   op0=ALU.mult), reads=[gex_b, rows_b], writes=[gtb_b],
                   tag=tag + "gmul")
        for j in range(nj):
            b.emit("p", lambda e, j=j: e.transpose(
                   out=ps_geb[0:8, j * 128:(j + 1) * 128],
                   in_=gtb_t[:, j * 8:(j + 1) * 8], identity=ident_bf),
                   reads=[gtb_b, WB["ident"]] if j == 0 else [gtb_b],
                   war=[ps_geb_b],
                   tag=tag + "trb")
        b.close_group("p", ps_geb_b)
        b.emit("v", lambda e: e.tensor_copy(out=ge_t, in_=ps_geb[0:8, :]),
               reads=[ps_geb_b], writes=[ge_b], tag=tag + "gec")
        b.emit("p", lambda e: e.matmul(out=ps_bc0[0:64, :],
               lhsT=repl_s[0:8, 64:128], rhs=mid_t, start=True, stop=True),
               reads=[mid_b, WB["repl"]], war=[ps_bc0_b], tag=tag + "repm")
        b.close_group("p", ps_bc0_b)
        b.emit("p", lambda e: e.matmul(out=ps_bc1[0:64, :],
               lhsT=repl_s[0:8, 0:64], rhs=ge_t, start=True, stop=True),
               reads=[ge_b], war=[ps_bc1_b], tag=tag + "repg")
        b.close_group("p", ps_bc1_b)
        b.emit("v", lambda e: e.tensor_copy(out=grep_t, in_=ps_bc1[0:64, :]),
               reads=[ps_bc1_b], writes=[grep_b], tag=tag + "grc")
        b.emit("v", lambda e: e.scalar_tensor_tensor(out=wmid_t,
               in0=ps_bc0[0:64, :], scalar=2.0, in1=grep_t, op0=ALU.mult,
               op1=ALU.mult), reads=[ps_bc0_b, grep_b], writes=[wmid_b],
               tag=tag + "wmid")
        for g0 in range(0, KT, 4):
            for m in range(g0, g0 + 4):
                ps, psb = ps_mm[m % 4], ps_mm_b[m % 4]
                for k in range(KT):
                    b.emit("p", lambda e, m=m, k=k, ps=ps: e.matmul(out=ps[:],
                           lhsT=bwT_s[:, k * DM + m * 128:k * DM + (m + 1) * 128],
                           rhs=xin_tiles[k], start=(k == 0), stop=False),
                           reads=[xin_bufs[k]],
                           war=[psb],
                           tag=tag + f"bs{m}")
                b.emit("p", lambda e, m=m, ps=ps: e.matmul(out=ps[:],
                       lhsT=bb_s[0:1, m * 128:(m + 1) * 128], rhs=ones_row,
                       start=False, stop=False), reads=[], war=[psb], tag=tag)
                b.emit("p", lambda e, m=m, ps=ps: e.matmul(out=ps[:],
                       lhsT=lB_s[:, m * 128:(m + 1) * 128], rhs=wmid_t,
                       start=False, stop=True), reads=[wmid_b], war=[psb],
                       tag=tag + "lora")
                b.close_group("p", psb)
            for m in range(g0, g0 + 4):
                ps, psb = ps_mm[m % 4], ps_mm_b[m % 4]
                b.emit("v", lambda e, m=m, ps=ps: e.scalar_tensor_tensor(
                       out=out_tiles[m], in0=ps[:], scalar=0.5, in1=res_tiles[m],
                       op0=ALU.mult, op1=ALU.add),
                       reads=[psb, res_bufs[m]], writes=[out_bufs[m]],
                       tag=tag + "ev")

    ffnw1 = make_ffn_work()

    ABF.mark(); AFA.mark()
    wx_s = ABF.alloc(DT2 * 96)
    cdiag_s = ABF.alloc(64 * 128)
    tails_s = ABF.alloc(DT2 * 4)
    for d in range(DT2):
        dma("g", wx_s[:, d * 96:(d + 1) * 96], wx[d], dsem="boot_g")
    for t in range(64):
        dma("g", cdiag_s[:, t * 128:(t + 1) * 128], cdiag[t],
            dsem="boot_g")
    b.emit("v", lambda e: e.memset(tails_s, 0.0), writes=[WB["tails"]])
    ABF.mark()
    stage = ABF.alloc(2 * DIN)
    stage_b = Buf("stage")
    for k in range(KT):
        dma("g", stage, win[k], writes=[stage_b])
        dma("s", win_bf.ap()[k], stage, reads=[stage_b], writes=[WB["winbf"]])
    for d in range(DT2):
        dma("g", stage[:, 0:DM], wout[d], writes=[stage_b])
        dma("s", wout_bf.ap()[d], stage[:, 0:DM], reads=[stage_b],
            writes=[WB["woutbf"]])
    ABF.reset()
    b.barrier()

    xf_t = [AFA.alloc(TC) for _ in range(KT)]
    xf_b = [Buf(f"xf{k}") for k in range(KT)]
    xbf_t = ABF.alloc(2 * TC)
    xbf_b = Buf("xbf")
    sq_t = ABF.alloc(2 * TC)
    sq_b = Buf("sq")
    lnrows_t = [AFA.alloc(5 * TC, rows=1) for _ in range(2)]
    lnrows_b = [Buf(f"lnrows{i}") for i in range(2)]
    tmp_t = AFA.alloc(TC)
    tmp_b = Buf("tmp")
    xln1_t = [ABF.alloc(TC) for _ in range(KT)]
    xln1_b = [Buf(f"xln1{k}") for k in range(KT)]
    x1_t = [AFA.alloc(TC) for _ in range(KT)]
    x1_b = [Buf(f"x1{k}") for k in range(KT)]
    xln2f_t = AFA.alloc(TC)
    xln2f_b = Buf("xln2f")
    xln2_t = [ABF.alloc(TC) for _ in range(KT)]
    xln2_b = [Buf(f"xln2{k}") for k in range(KT)]

    wg_t = [ABF.alloc(KT * 512) for _ in range(2)]
    wg_b = [Buf(f"wg{i}") for i in range(2)]
    xin_t = [ABF.alloc(TC + 4) for _ in range(2)]
    xin_b = [Buf(f"xin{i}") for i in range(2)]
    u_t = [ABF.alloc(TC) for _ in range(DT2)]
    u_b = [Buf(f"u{d}") for d in range(DT2)]
    gz_t = ABF.alloc(TC)
    gz_b = Buf("gz")
    dtb_t = ABF.alloc(TC, rows=64)
    dtb_b = Buf("dtb")
    bct_t = ABF.alloc(TC, rows=32)
    bct_b = Buf("bct")
    cbuf_t = AFA.alloc(TC)
    cbuf_b = Buf("cbuf")
    ebuf_t = AFA.alloc(TC)
    ebuf_b = Buf("ebuf")
    lbuf_t = AFA.alloc(TC)
    lbuf_b = Buf("lbuf")
    dl_t = ABF.alloc(TC)
    dl_b = Buf("dl")
    du2_t = ABF.alloc(TC)
    du2_b = Buf("du2")
    _spd = Buf("spd"); sb_spd = [_spd] * DT2
    _spdu = Buf("spdu"); sb_spdu = [_spdu] * DT2
    _spu = Buf("spu"); sb_spu = [_spu] * DT2
    _spg = Buf("spg"); sb_spg = [_spg] * DT2
    sb_spbc = Buf("spbc")
    _spx1 = Buf("spx1"); sb_spx1 = [_spx1] * KT
    _spx2 = Buf("spx2"); sb_spx2 = [_spx2] * KT

    for c in range(NCH):
        cs = slice(c * TC, (c + 1) * TC)
        for k in range(KT):
            dma("s", xf_t[k], xT[k][:, cs], writes=[xf_b[k]], tag=f"ldx{c}")
        ln_rows(xf_t, xf_b, lnrows_t, lnrows_b, sq_t, sq_b, xbf_t, xbf_b, f"L1{c}")
        ln_apply(xf_t, xf_b, xln1_t, xln1_b, tmp_t, tmp_b, f"L1a{c}")
        ffn(xln1_t, xln1_b, xf_t, xf_b, x1_t, x1_b, ffnw1, f"F1{c}")
        for k in range(KT):
            dma("s", sp_x1.ap()[k][:, cs], x1_t[k], reads=[x1_b[k]], writes=[sb_spx1[k]], tag=f"sx1{c}")
        ln_rows(x1_t, x1_b, lnrows_t, lnrows_b, sq_t, sq_b, xbf_t, xbf_b, f"L2{c}")
        for k in range(KT):
            b.emit("v", lambda e, k=k: e.tensor_tensor(out=tmp_t, in0=x1_t[k],
                   in1=ps_bc0[:], op=ALU.mult),
                   reads=[x1_b[k], ps_bc0_b], writes=[tmp_b], tag=f"L2a{c}")
            b.emit("v", lambda e: e.tensor_tensor(out=xln2f_t, in0=tmp_t,
                   in1=ps_bc1[:], op=ALU.subtract),
                   reads=[tmp_b, ps_bc1_b], writes=[xln2f_b], tag=f"L2b{c}")
            b.emit("v", lambda e, k=k: e.tensor_copy(out=xln2_t[k], in_=xln2f_t),
                   reads=[xln2f_b], writes=[xln2_b[k]], tag=f"L2c{c}")
            dma("s", sp_x2.ap()[k][:, cs], xln2f_t, reads=[xln2f_b], writes=[sb_spx2[k]], tag=f"sx2{c}")
        for g0 in range(0, 2 * DT2, 4):
            wgi = (g0 // 4) % 2
            for k in range(KT):
                dma("s" if k % 2 == 0 else "a", wg_t[wgi][:, k * 512:(k + 1) * 512],
                    win_bf.ap()[k][:, g0 * 128:(g0 + 4) * 128],
                    reads=[WB["winbf"]], writes=[wg_b[wgi]], tag=f"lw{c}")
            for mi in range(4):
                m = g0 + mi
                ps, psb = ps_mm[mi % 2], ps_mm_b[mi % 2]
                for k in range(KT):
                    b.emit("p", lambda e, k=k, mi=mi, wgi=wgi, ps=ps:
                           e.matmul(out=ps[:],
                                    lhsT=wg_t[wgi][:, k * 512 + mi * 128:k * 512 + (mi + 1) * 128],
                                    rhs=xln2_t[k], start=(k == 0), stop=(k == KT - 1)),
                           reads=[xln2_b[k], wg_b[wgi]],
                           war=[psb],
                           tag=f"wi{c}m{m}")
                b.close_group("p", psb)
                if m < DT2:
                    xt, xtb = xin_t[m % 2], xin_b[m % 2]
                    b.emit("v", lambda e, xt=xt, ps=ps: e.tensor_copy(
                           out=xt[:, 3:3 + TC], in_=ps[:]),
                           reads=[psb], writes=[xtb], tag=f"xc{c}")
                    b.emit("v", lambda e, xt=xt, m=m: e.tensor_copy(
                           out=xt[:, 0:3], in_=tails_s[:, m * 4:m * 4 + 3]),
                           reads=[WB["tails"]], writes=[xtb], tag=f"tl{c}")
                    psc, pscb = ps_mm[(mi + 2) % 4], ps_mm_b[(mi + 2) % 4]
                    for tap in range(4):
                        b.emit("p", lambda e, m=m, tap=tap, xt=xt, psc=psc:
                               e.matmul(out=psc[:],
                                        lhsT=cdiag_s[:, (m * 4 + tap) * 128:(m * 4 + tap + 1) * 128],
                                        rhs=xt[:, tap:tap + TC],
                                        start=(tap == 0), stop=(tap == 3)),
                               reads=[xtb],
                               war=[pscb],
                               tag=f"cv{c}m{m}")
                    b.close_group("p", pscb)
                    b.emit("v", lambda e, xt=xt, m=m: e.tensor_copy(
                           out=tails_s[:, m * 4:m * 4 + 3], in_=xt[:, TC:TC + 3]),
                           reads=[xtb], writes=[WB["tails"]], tag=f"tu{c}")
                    b.emit("a", lambda e, m=m, psc=psc: e.activation(
                           out=u_t[m], in_=psc[:], func=AF.Silu,
                           bias=vcol("conv_b", m)),
                           reads=[pscb, WB["vec"]], writes=[u_b[m]], tag=f"si{c}")
                    dma("s", sp_u.ap()[m][:, cs], u_t[m], reads=[u_b[m]],
                        writes=[sb_spu[m]], tag=f"su{c}")
                else:
                    dz = m - DT2
                    b.emit("a", lambda e, ps=ps: e.activation(out=gz_t, in_=ps[:],
                           func=AF.Silu), reads=[psb], writes=[gz_b], tag=f"sz{c}")
                    dma("s", sp_g.ap()[dz][:, cs], gz_t, reads=[gz_b], writes=[sb_spg[dz]], tag=f"sg{c}")
        for d in range(DT2):
            b.emit("p", lambda e, d=d: e.matmul(out=ps_mm[2][0:96, :],
                   lhsT=wx_s[:, d * 96:(d + 1) * 96], rhs=u_t[d],
                   start=(d == 0), stop=(d == DT2 - 1)),
                   reads=[u_b[d]],
                   war=[ps_mm_b[2]],
                   tag=f"wx{c}")
        b.close_group("p", ps_mm_b[2])
        b.emit("v", lambda e: e.tensor_copy(out=dtb_t, in_=ps_mm[2][0:64, :]),
               reads=[ps_mm_b[2]], writes=[dtb_b], tag=f"dtb{c}")
        b.emit("v", lambda e: e.tensor_copy(out=bct_t, in_=ps_mm[2][64:96, :]),
               reads=[ps_mm_b[2]], writes=[bct_b], tag=f"bct{c}")
        dma("s", sp_bc.ap()[:, cs], bct_t, reads=[bct_b], writes=[sb_spbc], tag=f"sbc{c}")
        for g0 in range(0, DT2, 4):
            for di in range(4):
                d = g0 + di
                ps, psb = ps_mm[di], ps_mm_b[di]
                b.emit("p", lambda e, d=d, ps=ps: e.matmul(out=ps[:],
                       lhsT=wdt_s[:, d * 128:(d + 1) * 128], rhs=dtb_t,
                       start=True, stop=True),
                       reads=[dtb_b], war=[psb], tag=f"wd{c}")
                b.close_group("p", psb)
            for di in range(4):
                d = g0 + di
                ps, psb = ps_mm[di], ps_mm_b[di]
                b.emit("v", lambda e, d=d, ps=ps: e.tensor_scalar(out=cbuf_t,
                       in0=ps[:], scalar1=vcol("b_dt", d), scalar2=SM30,
                       op0=ALU.add, op1=ALU.min),
                       reads=[psb, WB["vec"]], writes=[cbuf_b], tag=f"dm{c}")
                b.emit("a", lambda e: e.activation(out=ebuf_t, in_=cbuf_t,
                       func=AF.Exp), reads=[cbuf_b], writes=[ebuf_b], tag=f"de{c}")
                b.emit("a", lambda e: e.activation(out=lbuf_t, in_=ebuf_t,
                       func=AF.Ln, bias=1.0), reads=[ebuf_b], writes=[lbuf_b],
                       tag=f"dl{c}")
                b.emit("v", lambda e, d=d, ps=ps: e.scalar_tensor_tensor(
                       out=dl_t, in0=ps[:], scalar=vcol("b_dt", d), in1=lbuf_t,
                       op0=ALU.add, op1=ALU.max),
                       reads=[psb, lbuf_b, WB["vec"]], writes=[dl_b], tag=f"dx{c}")
                dma("s", sp_d.ap()[d][:, cs], dl_t, reads=[dl_b], writes=[sb_spd[d]], tag=f"sd{c}")
                b.emit("v", lambda e, d=d: e.tensor_tensor(out=du2_t, in0=dl_t,
                       in1=u_t[d], op=ALU.mult),
                       reads=[dl_b, u_b[d]], writes=[du2_b], tag=f"duz{c}")
                dma("a", sp_du.ap()[d][:, cs], du2_t, reads=[du2_b], writes=[sb_spdu[d]], tag=f"sdu{c}")

    ABF.reset(); AFA.reset()
    b.barrier()

    ABF.mark(); AFA.mark()
    NSP = L // 1024
    NCH2 = NCH // 2
    v_d = sp_d.ap().rearrange("d p (s n) -> (d p s) n", n=TC)
    v_du = sp_du.ap().rearrange("d p (s n) -> (d p s) n", n=TC)
    v_u = sp_u.ap().rearrange("d p (s n) -> (d p s) n", n=TC)
    v_g = sp_g.ap().rearrange("d p (s n) -> (d p s) n", n=TC)
    v_bc = sp_bc.ap().rearrange("r (c n) -> (r c) n", n=TC)
    v_x1 = sp_x1.ap().rearrange("k p (c n) -> (k p c) n", n=TC)
    v_x2 = sp_x2.ap().rearrange("k p (c n) -> (k p c) n", n=TC)
    v_yo = yout.rearrange("k p (c n) -> (k p c) n", n=TC)

    def gdt(dt, j):
        c0 = dt * 8 + j
        return gidx_t.ap()[:, c0:c0 + 1]

    def gbc(r, j):
        c0 = 128 + r * 8 + j
        return gidx_t.ap()[:, c0:c0 + 1]

    def gk(k, j):
        c0 = 384 + k * 8 + j
        return gidx_t.ap()[:, c0:c0 + 1]

    def ig(out, view, idx, reads=(), writes=(), tag=""):
        return b.emit("g", lambda e, o=out, v=view, i=idx: e.indirect_dma_start(
            out=o, out_offset=None, in_=v,
            in_offset=bass.IndirectOffsetOnAxis(ap=i, axis=0)),
            reads=reads, writes=writes, dma=True, tag=tag)

    def isc(view, idx, in_, reads=(), writes=(), tag=""):
        return b.emit("g", lambda e, v=view, i=idx, s=in_: e.indirect_dma_start(
            out=v, out_offset=bass.IndirectOffsetOnAxis(ap=i, axis=0),
            in_=s, in_offset=None),
            reads=reads, writes=writes, dma=True, tag=tag)

    Bbc_t = [ABF.alloc(TC) for _ in range(NS)]
    Bbc_b = [Buf(f"Bb{s}") for s in range(NS)]
    Cbc_t = [ABF.alloc(TC) for _ in range(NS)]
    Cbc_b = [Buf(f"Cb{s}") for s in range(NS)]
    bchain = [Buf(f"bch{i}") for i in range(4)]
    cchain = [Buf(f"cch{i}") for i in range(4)]
    dld_t = [ABF.alloc(TC) for _ in range(2)]
    dld_b = [Buf(f"dld{i}") for i in range(2)]
    duld_t = [ABF.alloc(TC) for _ in range(2)]
    duld_b = [Buf(f"duld{i}") for i in range(2)]
    uld_t = [ABF.alloc(TC) for _ in range(2)]
    uld_b = [Buf(f"uld{i}") for i in range(2)]
    gld_t = [ABF.alloc(TC) for _ in range(2)]
    gld_b = [Buf(f"gld{i}") for i in range(2)]
    wbuf_t = [ABF.alloc(TC) for _ in range(2)]
    wbuf_b = [Buf(f"wb{i}") for i in range(2)]
    scr_t = AFA.alloc(TC)
    scr_b = Buf("scr")
    dA_t = [AFA.alloc(TC) for _ in range(4)]
    dA_b = [Buf(f"dA{i}") for i in range(4)]
    dBu_t = [ABF.alloc(TC) for _ in range(4)]
    dBu_b = [Buf(f"dBu{i}") for i in range(4)]
    h_all = ABF.alloc(4 * TC)
    h_t = [h_all[:, i * TC:(i + 1) * TC] for i in range(4)]
    h_b = [Buf(f"h{i}") for i in range(4)]
    h_last = h_all.tensor.ap()

    p_t = [ABF.alloc(TC) for _ in range(4)]
    p_b = [Buf(f"p{s}") for s in range(4)]
    ym_t = [ABF.alloc(TC) for _ in range(DT2)]
    ym_b = [Buf(f"ym{d}") for d in range(DT2)]
    cdl_t = AFA.alloc(TC)
    cdl_b = Buf("cdl")
    rb_t2 = AFA.alloc(TC)
    rb_b2 = Buf("rb2")
    gam_t = AFA.alloc(NS)
    gam_b = Buf("gam")
    part_t = AFA.alloc(NS)
    part_b = Buf("part")
    t1_t = AFA.alloc(TC)
    t1_b = Buf("t1")
    wo2_t = [ABF.alloc(DT2 * 256) for _ in range(2)]
    wo2_b = [Buf(f"wo2{i}") for i in range(2)]
    x1l_t = [AFA.alloc(TC) for _ in range(2)]
    x1l_b = [Buf(f"x1l{i}") for i in range(2)]
    xl2l_t = [AFA.alloc(TC) for _ in range(2)]
    xl2l_b = [Buf(f"xl2l{i}") for i in range(2)]
    h3_t = [AFA.alloc(TC) for _ in range(KT)]
    h3_b = [Buf(f"h3{k}") for k in range(KT)]
    x2_t = [AFA.alloc(TC) for _ in range(KT)]
    x2_b = [Buf(f"x2{k}") for k in range(KT)]
    xln4_t = [ym_t[k] for k in range(KT)]
    xln4_b = [ym_b[k] for k in range(KT)]
    outst_t = AFA.alloc(TC)
    outst_b = Buf("outst")
    xbf2_t = ABF.alloc(TC)
    xbf2_b = Buf("xbf2")
    sq2_t = ABF.alloc(TC)
    sq2_b = Buf("sq2")
    lnrows2_t = [AFA.alloc(5 * TC, rows=1) for _ in range(2)]
    lnrows2_b = [Buf(f"lnrows2{i}") for i in range(2)]
    tmp2_t = AFA.alloc(TC)
    tmp2_b = Buf("tmp2")
    ffnw2 = ffnw1
    sb_yo = [Buf(f"yo{k}") for k in range(4)] * 2

    b.emit("v", lambda e: e.memset(hrun_s, 0.0), writes=[WB["hrun"]])

    for c in range(NCH2):
        cs = slice(c * TC, (c + 1) * TC)
        for s in range(NS):
            q = "s" if s % 2 == 0 else "a"
            dma(q, Bbc_t[s], sp_bc.ap()[s, cs].partition_broadcast(128),
                writes=[bchain[s % 4], Bbc_b[s]], tag=f"tBb{c}")
        for dt in range(DT2):
            sl = dt % 2
            dma("s", dld_t[sl], sp_d.ap()[dt][:, cs],
                writes=[dld_b[sl]], tag=f"tld{c}")
            dma("a", duld_t[sl], sp_du.ap()[dt][:, cs],
                writes=[duld_b[sl]], tag=f"tlu{c}")
            b.emit("v", lambda e, sl=sl: e.tensor_tensor_scan(
                   out=cdl_t, data0=dld_t[sl], data1=dld_t[sl],
                   initial=0.0, op0=ALU.add, op1=ALU.bypass),
                   reads=[dld_b[sl]], writes=[cdl_b], tag=f"tcs{c}")
            b.emit("v", lambda e: e.tensor_scalar(out=rb_t2, in0=cdl_t,
                   scalar1=cdl_t[:, TC - 1:TC], scalar2=None, op0=ALU.subtract),
                   reads=[cdl_b], writes=[rb_b2], tag=f"tr{c}")
            b.emit("v", lambda e, dt=dt: e.tensor_scalar(
                   out=gam_t, in0=posA_s[:, dt * NS:(dt + 1) * NS],
                   scalar1=cdl_t[:, TC - 1:TC], scalar2=-1.0,
                   op0=ALU.mult, op1=ALU.mult),
                   reads=[cdl_b, WB["posA"]], writes=[gam_b], tag=f"tg{c}")
            b.emit("a", lambda e: e.activation(out=gam_t, in_=gam_t, func=AF.Exp),
                   reads=[gam_b], writes=[gam_b], tag=f"tge{c}")
            hsl = hrun_s[:, dt * NS:(dt + 1) * NS]
            b.emit("v", lambda e, hsl=hsl: e.tensor_tensor(out=hsl, in0=hsl,
                   in1=gam_t, op=ALU.mult),
                   reads=[gam_b, WB["hrun"]], writes=[WB["hrun"]], tag=f"thg{c}")
            for s in range(NS):
                r2 = s % 4
                b.emit("a", lambda e, dt=dt, s=s, r2=r2 % 2: e.activation(
                       out=wbuf_t[r2 % 2], in_=rb_t2, func=AF.Exp,
                       scale=posA_s[:, dt * NS + s:dt * NS + s + 1]),
                       reads=[rb_b2, WB["posA"]], writes=[wbuf_b[r2 % 2]], tag=f"tw{c}")
                b.emit("v", lambda e, s=s, sl=sl, r2=r2: e.tensor_tensor(
                       out=dBu_t[r2], in0=duld_t[sl], in1=Bbc_t[s],
                       op=ALU.mult),
                       reads=[duld_b[sl], Bbc_b[s]], writes=[dBu_b[r2]],
                       tag=f"tdb{c}", cost=550)
                b.emit("v", lambda e, s=s, r2=r2: e.scalar_tensor_tensor(
                       out=scr_t, in0=wbuf_t[r2 % 2], scalar=1.0, in1=dBu_t[r2],
                       op0=ALU.mult, op1=ALU.mult,
                       accum_out=part_t[:, s:s + 1]),
                       reads=[wbuf_b[r2 % 2], dBu_b[r2]],
                       writes=[scr_b, part_b], tag=f"ttr{c}")
            hsl2 = hrun_s[:, dt * NS:(dt + 1) * NS]
            b.emit("v", lambda e, hsl2=hsl2: e.tensor_tensor(out=hsl2,
                   in0=hsl2, in1=part_t, op=ALU.add),
                   reads=[part_b, WB["hrun"]], writes=[WB["hrun"]],
                   tag=f"tha{c}")
    b.emit("v", lambda e: e.tensor_scalar(out=carry_s, in0=hrun_s,
           scalar1=smask_s[:, 0:1], scalar2=None, op0=ALU.mult),
           reads=[WB["hrun"], WB["smask"]], writes=[WB["carry"]], tag="cinit")

    for j in range(NCH2):
        for s in range(NS):
            ig(Bbc_t[s], v_bc, gbc(s, j),
               writes=[bchain[s % 4], Bbc_b[s]], tag=f"sBb{j}")
            ig(Cbc_t[s], v_bc, gbc(16 + s, j),
               writes=[cchain[s % 4], Cbc_b[s]], tag=f"sCb{j}")
        if True:
            for dt in range(DT2):
                sl = dt % 2
                psy, psyb = (ps_bc0, ps_bc0_b) if dt % 2 == 0 else (ps_bc1, ps_bc1_b)
                ig(dld_t[sl], v_d, gdt(dt, j), writes=[dld_b[sl]], tag=f"gd{j}")
                ig(duld_t[sl], v_du, gdt(dt, j), writes=[duld_b[sl]], tag=f"gdu{j}")
                ig(uld_t[sl], v_u, gdt(dt, j), writes=[uld_b[sl]], tag=f"gu{j}")
                ig(gld_t[sl], v_g, gdt(dt, j), writes=[gld_b[sl]], tag=f"gg{j}")
                for s in range(NS):
                    r2 = s % 4
                    b.emit("a", lambda e, dt=dt, s=s, sl=sl, r2=r2:
                           e.activation(out=dA_t[r2], in_=dld_t[sl],
                                        func=AF.Exp,
                                        scale=negA_s[:, dt * NS + s:dt * NS + s + 1]),
                           reads=[dld_b[sl], WB["negA"]], writes=[dA_b[r2]],
                           tag=f"dA{j}")
                    b.emit("v", lambda e, s=s, sl=sl, r2=r2: e.tensor_tensor(
                           out=dBu_t[r2], in0=duld_t[sl], in1=Bbc_t[s],
                           op=ALU.mult),
                           reads=[duld_b[sl], Bbc_b[s]], writes=[dBu_b[r2]],
                           tag=f"dB{j}", cost=550)
                    b.emit("v", lambda e, dt=dt, s=s, r2=r2: e.tensor_tensor_scan(
                           out=h_t[r2], data0=dA_t[r2], data1=dBu_t[r2],
                           initial=carry_s[:, dt * NS + s:dt * NS + s + 1],
                           op0=ALU.mult, op1=ALU.add),
                           reads=[dA_b[r2], dBu_b[r2], WB["carry"]],
                           writes=[h_b[r2]], tag=f"ts{j}")
                    if s % 4 == 3:
                        hoff = h_t[0].offset % (128 * 1024 * 64)
                        b.emit("v", lambda e, dt=dt, s=s: e.tensor_copy(
                               out=carry_s[:, dt * NS + s - 3:dt * NS + s + 1],
                               in_=h_all[:, TC - 1::TC]),
                               reads=[h_b[0], h_b[1], h_b[2], h_b[3]],
                               writes=[WB["carry"]], tag=f"cr{j}", cost=200)
                    b.emit("v", lambda e, s=s, r2=r2: e.tensor_tensor(
                           out=p_t[s % 4],
                           in0=h_t[r2], in1=Cbc_t[s], op=ALU.mult),
                           reads=[h_b[r2], Cbc_b[s]], writes=[p_b[s % 4]],
                           tag=f"cm{j}")
                    b.emit("p", lambda e, s=s, psy=psy: e.matmul(out=psy[:],
                           lhsT=ident_bf, rhs=p_t[s % 4], start=(s == 0),
                           stop=(s == NS - 1)),
                           reads=[p_b[s % 4]],
                           war=[psyb],
                           tag=f"sm{j}")
                b.close_group("p", psyb)
                b.emit("v", lambda e, dt=dt, sl=sl, psy=psy:
                       e.scalar_tensor_tensor(
                       out=t1_t, in0=uld_t[sl], scalar=vcol("D", dt),
                       in1=psy[:], op0=ALU.mult, op1=ALU.add),
                       reads=[uld_b[sl], psyb, WB["vec"]], writes=[t1_b],
                       tag=f"y1{j}")
                b.emit("v", lambda e, dt=dt, sl=sl: e.tensor_tensor(
                       out=ym_t[dt], in0=t1_t, in1=gld_t[sl], op=ALU.mult),
                       reads=[t1_b, gld_b[sl]], writes=[ym_b[dt]], tag=f"y2{j}")
            for m in range(KT):
                wgi = m % 2
                if m % 2 == 0:
                    for d in range(DT2):
                        dma("s" if d % 2 == 0 else "a",
                            wo2_t[(m // 2) % 2][:, d * 256:(d + 1) * 256],
                            wout_bf.ap()[d][:, m * 128:(m + 2) * 128],
                            reads=[WB["woutbf"]], writes=[wo2_b[(m // 2) % 2]],
                            tag=f"lwo{j}")
                ps, psb = ps_mm[m % 2], ps_mm_b[m % 2]
                for d in range(DT2):
                    b.emit("p", lambda e, d=d, m=m, ps=ps:
                           e.matmul(out=ps[:],
                                    lhsT=wo2_t[(m // 2) % 2][:, d * 256 + (m % 2) * 128:d * 256 + (m % 2 + 1) * 128],
                                    rhs=ym_t[d], start=(d == 0),
                                    stop=(d == DT2 - 1)),
                           reads=[ym_b[d], wo2_b[(m // 2) % 2]],
                           war=[psb],
                           tag=f"wo{j}m{m}")
                b.close_group("p", psb)
                mi = m % 2
                ig(xl2l_t[mi], v_x2, gk(m, j), writes=[xl2l_b[mi]], tag=f"lx2{j}")
                b.emit("v", lambda e, m=m, mi=mi, ps=ps: e.tensor_tensor(
                       out=h3_t[m], in0=ps[:], in1=xl2l_t[mi], op=ALU.add),
                       reads=[psb, xl2l_b[mi]], writes=[h3_b[m]], tag=f"h3{j}")
            ln_rows(h3_t, h3_b, lnrows2_t, lnrows2_b, sq2_t, sq2_b, xbf2_t,
                    xbf2_b, f"L3{j}")
            for k in range(KT):
                kl = k % 2
                ig(x1l_t[kl], v_x1, gk(k, j), writes=[x1l_b[kl]], tag=f"lx1{j}")
                b.emit("v", lambda e, k=k: e.tensor_tensor(out=tmp2_t,
                       in0=h3_t[k], in1=ps_bc0[:], op=ALU.mult),
                       reads=[h3_b[k], ps_bc0_b], writes=[tmp2_b], tag=f"L3a{j}")
                b.emit("v", lambda e: e.tensor_tensor(out=tmp2_t, in0=tmp2_t,
                       in1=ps_bc1[:], op=ALU.subtract),
                       reads=[tmp2_b, ps_bc1_b], writes=[tmp2_b], tag=f"L3b{j}")
                b.emit("v", lambda e, k=k, kl=kl: e.scalar_tensor_tensor(
                       out=x2_t[k], in0=tmp2_t, scalar=vcol("gamma", k),
                       in1=x1l_t[kl], op0=ALU.mult, op1=ALU.add),
                       reads=[tmp2_b, x1l_b[kl], WB["vec"]], writes=[x2_b[k]],
                       tag=f"L3g{j}")
                b.emit("v", lambda e, k=k: e.tensor_scalar(out=x2_t[k],
                       in0=x2_t[k], scalar1=vcol("beta", k), scalar2=None,
                       op0=ALU.add),
                       reads=[x2_b[k], WB["vec"]], writes=[x2_b[k]], tag=f"L3bt{j}")
            ln_rows(x2_t, x2_b, lnrows2_t, lnrows2_b, sq2_t, sq2_b, xbf2_t,
                    xbf2_b, f"L4{j}")
            ln_apply(x2_t, x2_b, xln4_t, xln4_b, tmp2_t, tmp2_b, f"L4a{j}")
            ffn(xln4_t, xln4_b, x2_t, x2_b, x2_t, x2_b, ffnw2, f"F2{j}")
            for k in range(KT):
                isc(v_yo, gk(k, j), x2_t[k], reads=[x2_b[k]],
                    writes=[sb_yo[k]], tag=f"out{j}")

    ABF.reset(); AFA.reset()
    b.fence()
    b.emit("g", lambda e: e.nop(), wait_all_dma=True, tag="final")
    order = b.schedule()
    b.lower(order)
    b.finalize(order)
    b.coalesce_waits(order)

    with ctx:
        with ExitStack() as sctx:
            sems = {}
            for n in b.all_sems:
                sems[n] = sctx.enter_context(nc.semaphore(n))
            with nc.Block() as block:
                def mk(eng_key):
                    def run(engine):
                        for op in order:
                            if op.eng != eng_key:
                                continue
                            for s, v in op.waits:
                                engine.wait_ge(sems[s], v)
                            inst = op.fn(engine)
                            if op.inc is not None and inst is not None:
                                inst.then_inc(sems[op.inc[0]], op.inc[1])
                    return run
                block.tensor(mk("p"))
                block.vector(mk("v"))
                block.scalar(mk("a"))
                block.gpsimd(mk("g"))
                block.sync(mk("s"))
    return nc


def prep_inputs(inputs, L=8192):
    f = np.ascontiguousarray
    W_in = np.asarray(inputs["L0_attn_W_in"])
    conv_w = np.asarray(inputs["L0_attn_conv_w"])
    d = {}
    d["win"] = f(W_in.reshape(KT, 128, 2 * DIN))
    d["bwT"] = f(np.asarray(inputs["L0_ffn_base_w"]).T.reshape(KT, 128, DM))
    d["rwT"] = f(np.asarray(inputs["L0_ffn_router_w"]).T.reshape(KT, 128, NE))
    d["lA"] = f(np.asarray(inputs["L0_ffn_lora_A"]).reshape(KT, 128, NR))
    d["lB"] = f(np.asarray(inputs["L0_ffn_lora_B"]).reshape(NE * NR, DM))
    d["wx"] = f(np.asarray(inputs["L0_attn_W_x"]).reshape(DT2, 128, 96))
    d["wdt"] = f(np.asarray(inputs["L0_attn_W_dt"]))
    d["wout"] = f(np.asarray(inputs["L0_attn_W_out"]).reshape(DT2, 128, DM))
    cd = np.zeros((DT2 * 4, 128, 128), np.float32)
    idx = np.arange(128)
    for dt in range(DT2):
        for k in range(4):
            cd[dt * 4 + k, idx, idx] = conv_w[dt * 128 + idx, k]
    d["cdiag"] = cd
    d["alog"] = f(np.asarray(inputs["L0_attn_A_log"]).reshape(DT2, 128, NS))
    vec = np.zeros((128, 64), np.float32)
    vec[:, 0:16] = np.asarray(inputs["L0_attn_conv_b"]).reshape(DT2, 128).T
    vec[:, 16:32] = np.asarray(inputs["L0_attn_b_dt"]).reshape(DT2, 128).T
    vec[:, 32:48] = np.asarray(inputs["L0_attn_D"]).reshape(DT2, 128).T
    vec[:, 48:56] = np.asarray(inputs["L0_attn_gamma"]).reshape(KT, 128).T
    vec[:, 56:64] = np.asarray(inputs["L0_attn_beta"]).reshape(KT, 128).T
    d["vec"] = vec
    d["bbrow"] = f(np.asarray(inputs["L0_ffn_base_b"]).reshape(1, DM))
    d["rbrow"] = f(np.asarray(inputs["L0_ffn_router_b"]).reshape(1, NE))
    d["ident_in"] = np.eye(128, dtype=np.float32)
    p = np.arange(128, dtype=np.uint32)
    nch = L // TC
    nc2 = nch // 2
    for sigma in (0, 1):
        gi = np.zeros((128, 448), np.uint32)
        for dt in range(16):
            for j in range(nc2):
                gi[:, dt * 8 + j] = (dt * 128 + p) * nch + (sigma * nc2 + j)
        for r in range(32):
            for j in range(nc2):
                gi[:, 128 + r * 8 + j] = r * nch + (sigma * nc2 + j)
        for k in range(8):
            for j in range(nc2):
                gi[:, 384 + k * 8 + j] = (k * 128 + p) * nch + (sigma * nc2 + j)
        d[f"gidx{sigma}"] = gi
        d[f"smask{sigma}"] = np.full((128, 1), float(sigma), np.float32)
    repl = np.zeros((2, 8, NE * NR), np.float32)
    for ee in range(NE):
        for r in range(NR):
            repl[0, ee, ee * NR + r] = 1.0
            repl[1, r, ee * NR + r] = 1.0
    d["repl_in"] = repl
    return d


def kernel(**inputs):
    x = np.asarray(inputs["L0_attn_x"])
    L = x.shape[1]
    shared = prep_inputs(inputs, L=L)
    nc = build_program(L=L)
    in_maps = []
    for core in range(8):
        bidx = core // 2
        sigma = core % 2
        m = {k: v for k, v in shared.items()
             if not k.startswith(("gidx", "smask"))}
        m["gidx"] = shared[f"gidx{sigma}"]
        m["smask"] = shared[f"smask{sigma}"]
        m["xT"] = np.ascontiguousarray(x[bidx].T).reshape(KT, 128, L)
        in_maps.append(m)
    res = run_bass_kernel_spmd(nc, in_maps, core_ids=list(range(8)))
    out = np.empty((x.shape[0], L, DM), np.float32)
    half = L // 2
    for bidx in range(x.shape[0]):
        y0 = np.asarray(res.results[2 * bidx]["yout"]).reshape(DM, L)
        y1 = np.asarray(res.results[2 * bidx + 1]["yout"]).reshape(DM, L)
        out[bidx, :half] = y0[:, :half].T
        out[bidx, half:] = y1[:, half:].T
    return out
